# version 01
# baseline (speedup 1.0000x reference)
import numpy as np
import ml_dtypes
import concourse.bass as bass
import concourse.mybir as mybir
from concourse.bass_utils import run_bass_kernel_spmd

F32 = mybir.dt.float32
BF16 = mybir.dt.bfloat16
I16 = mybir.dt.int16
I32 = mybir.dt.int32
AF = mybir.ActivationFunctionType
ALU = mybir.AluOpType
AX = mybir.AxisListType

T_CORE = 4096
D = 1024
NT = T_CORE // 128
LN_EPS = 1e-5
I_FFN = 2816
NI = I_FFN // 128

_SEC = {"T": "tensor", "V": "vector", "S": "scalar", "Y": "sync", "G": "gpsimd"}


class Buf:
    def __init__(self, h):
        self.h = h
        self.prod = None
        self.readers = {}

    def __getitem__(self, idx):
        return self.h[idx]

    @property
    def ap(self):
        return self.h[:]


class KB:
    def __init__(self):
        nc = bass.Bass("TRN2", target_bir_lowering=False, debug=False)
        self.nc = nc
        self.sem = {k: nc.alloc_semaphore("sem_" + k) for k in _SEC}
        self.cnt = {k: 0 for k in _SEC}
        self.waited = {c: {p: 0 for p in _SEC} for c in _SEC}
        self.prog = {k: [] for k in _SEC}
        self.psum = [Buf(nc.alloc_psum_tensor(f"ps{i}", [128, 512], F32))
                     for i in range(8)]
        self._uid = 0
        self._consts = {}
        self._const_tile = nc.alloc_sbuf_tensor("kb_consts", [128, 16], F32)

    def constcol(self, value):
        if value not in self._consts:
            idx = len(self._consts)
            assert idx < 16
            ap = self._const_tile[:, idx:idx + 1]
            self.op("V", lambda V, ap=ap, value=value: V.memset(ap, value))
            self.wait("S", "V")
            self._consts[value] = ap
        return self._consts[value]

    def raw(self, e, fn):
        self.prog[e].append(fn)

    def op(self, e, fn, inc=1):
        if inc:
            sem = self.sem[e]
            self.prog[e].append(lambda eng, fn=fn, sem=sem, inc=inc:
                                fn(eng).then_inc(sem, inc))
            self.cnt[e] += inc
        else:
            self.prog[e].append(fn)
        return self.cnt[e]

    def wait(self, consumer, producer, val=None):
        if val is None:
            val = self.cnt[producer]
        if val <= 0 or self.waited[consumer][producer] >= val:
            return
        sem = self.sem[producer]
        self.prog[consumer].append(
            lambda eng, sem=sem, val=val: eng.wait_ge(sem, val))
        self.waited[consumer][producer] = val

    def rd(self, e, *bufs):
        for b in bufs:
            if b.prod is not None:
                if b.prod[0] != e or not b.prod[2] or e in ("Y", "G"):
                    self.wait(e, b.prod[0], b.prod[1])

    def wr(self, e, *bufs):
        for b in bufs:
            if b.prod is not None and b.prod[0] != e:
                self.wait(e, b.prod[0], b.prod[1])
            for re, rc in b.readers.items():
                if re != e:
                    self.wait(e, re, rc)

    def did_w(self, e, *bufs, wide=False):
        for b in bufs:
            b.prod = (e, self.cnt[e], wide)
            b.readers = {}

    def did_r(self, e, *bufs):
        for b in bufs:
            b.readers[e] = self.cnt[e]

    def eop(self, e, fn, reads=(), writes=(), wide=False, inc=1):
        for b in reads:
            self.rd(e, b)
        self.wr(e, *writes)
        self.op(e, fn, inc)
        self.did_w(e, *writes, wide=wide)
        for b in reads:
            self.did_r(e, b)

    def sb(self, name, shape, dtype=F32):
        return Buf(self.nc.alloc_sbuf_tensor(name, shape, dtype))

    def psum_t(self, name, shape, dtype=F32):
        return Buf(self.nc.alloc_psum_tensor(name, shape, dtype))

    def dram(self, name, shape, dtype=F32, kind="Internal"):
        return self.nc.dram_tensor(name, shape, dtype, kind=kind)

    def finalize(self):
        for e in _SEC:
            v = self.cnt[e]
            if v > 0:
                sem = self.sem[e]
                self.prog[e].append(
                    lambda eng, sem=sem, v=v: eng.wait_ge(sem, v))
        with self.nc.Block() as blk:
            for e, sec in _SEC.items():
                thunks = self.prog[e]
                if not thunks:
                    continue
                getattr(blk, sec)(lambda eng, thunks=thunks:
                                  [t(eng) for t in thunks])
        return self.nc



def emit_ln_stats(kb, x, scratch, stats, col, xcols=None):
    x_ap = x.ap if xcols is None else x[:, xcols[0]:xcols[1]]
    st = stats.h
    kb.eop("S", lambda S, x_ap=x_ap, col=col: S.activation(
        scratch.ap, x_ap, AF.Square, accum_out=st[:, 8 + col:9 + col]),
        reads=(x,), writes=(scratch, stats), wide=True)
    kb.eop("V", lambda V, x_ap=x_ap, col=col: V.reduce_sum(
        st[:, col:col + 1], x_ap, AX.X), reads=(x,), writes=(stats,))


def emit_ln_finish(kb, stats, n):
    st = stats.h
    s1, s2 = st[:, 0:n], st[:, 8:8 + n]
    m, v = st[:, 16:16 + n], st[:, 24:24 + n]
    r, t = st[:, 32:32 + n], st[:, 40:40 + n]
    eps_ap = kb.constcol(LN_EPS)
    kb.eop("V", lambda V: V.tensor_scalar(m, s1, 1.0 / D, None, ALU.mult),
           reads=(stats,), writes=(stats,))
    kb.eop("V", lambda V: V.tensor_scalar(t, s2, 1.0 / D, None, ALU.mult),
           reads=(stats,), writes=(stats,))
    kb.eop("V", lambda V: V.tensor_tensor(v, m, m, ALU.mult),
           reads=(stats,), writes=(stats,))
    kb.eop("V", lambda V: V.tensor_tensor(v, t, v, ALU.subtract),
           reads=(stats,), writes=(stats,))
    kb.eop("S", lambda S: S.activation(t, v, AF.Ln, bias=eps_ap),
           reads=(stats,), writes=(stats,))
    kb.eop("S", lambda S: S.activation(r, t, AF.Exp, scale=-0.5),
           reads=(stats,), writes=(stats,))


def emit_ln_apply(kb, x, out, scratch, stats, col, xcols=None):
    x_ap = x.ap if xcols is None else x[:, xcols[0]:xcols[1]]
    st = stats.h
    kb.eop("V", lambda V, x_ap=x_ap, col=col: V.tensor_single_scalar(
        scratch.ap, x_ap, st[:, 16 + col:17 + col], ALU.subtract),
        reads=(x, stats), writes=(scratch,), wide=True)
    kb.eop("V", lambda V, col=col: V.tensor_single_scalar(
        out.ap, scratch.ap, st[:, 32 + col:33 + col], ALU.mult),
        reads=(scratch, stats), writes=(out,), wide=True)


def quant_scratch(kb, tag, F):
    return {
        "mx": kb.sb(f"q_mx_{tag}", [128, F // 16], F32),
        "rec": kb.sb(f"q_rec_{tag}", [128, F // 16], F32),
        "ab": kb.sb(f"q_ab_{tag}", [128, F], BF16),
        "v": kb.sb(f"q_v_{tag}", [128, F], BF16),
        "lo": kb.sb(f"q_lo_{tag}", [128, F], BF16),
        "sg": kb.sb(f"q_sg_{tag}", [128, F], BF16),
    }


def emit_fp4_quant(kb, x, out, F, sc):
    B = F // 16
    P = 128
    W = F >= 2048
    x_ap = x[:, 0:F]
    x3 = x_ap.rearrange("p (b k) -> p b k", k=16)
    mx, rec = sc["mx"], sc["rec"]
    ab, v, lo, sg = sc["ab"], sc["v"], sc["lo"], sc["sg"]
    kb.eop("S", lambda S: S.activation(sg[:, 0:F], x_ap, AF.Sign),
           reads=(x,), writes=(sg,), wide=True)
    kb.eop("V", lambda V: V.tensor_reduce(mx[:, 0:B], x3, AX.X, ALU.max,
                                          apply_absolute_value=True),
           reads=(x,), writes=(mx,))
    kb.eop("V", lambda V: V.tensor_scalar(mx[:, 0:B], mx[:, 0:B], 6e-12,
                                          None, ALU.max),
           reads=(mx,), writes=(mx,))
    kb.eop("V", lambda V: V.reciprocal(rec[:, 0:B], mx[:, 0:B]),
           reads=(mx,), writes=(rec,))
    kb.eop("V", lambda V: V.tensor_scalar(ab[:, 0:F].bitcast(I16),
                                          x_ap.bitcast(I16), 0x7FFF, None,
                                          ALU.bitwise_and),
           reads=(x,), writes=(ab,), wide=W)
    rec_b = rec[:, 0:B].unsqueeze(2).to_broadcast((P, B, 16))
    mx_b = mx[:, 0:B].unsqueeze(2).to_broadcast((P, B, 16))
    kb.eop("V", lambda V: V.scalar_tensor_tensor(
        v[:, 0:F].rearrange("p (b k) -> p b k", k=16),
        ab[:, 0:F].rearrange("p (b k) -> p b k", k=16),
        6.0, rec_b, ALU.mult, ALU.mult),
        reads=(ab, rec), writes=(v,), wide=W)
    kb.eop("V", lambda V: V.tensor_scalar(lo[:, 0:F], v[:, 0:F], 1.0, None,
                                          ALU.is_lt),
           reads=(v,), writes=(lo,), wide=W)
    kb.eop("V", lambda V: V.tensor_tensor(v[:, 0:F], v[:, 0:F], lo[:, 0:F],
                                          ALU.add),
           reads=(v, lo), writes=(v,), wide=W)
    kb.eop("V", lambda V: V.tensor_scalar(v[:, 0:F].bitcast(I16),
                                          v[:, 0:F].bitcast(I16), 0x20, None,
                                          ALU.add),
           reads=(v,), writes=(v,), wide=W)
    kb.eop("V", lambda V: V.tensor_scalar(v[:, 0:F].bitcast(I16),
                                          v[:, 0:F].bitcast(I16), -0x40,
                                          None, ALU.bitwise_and),
           reads=(v,), writes=(v,), wide=W)
    kb.eop("V", lambda V: V.tensor_tensor(v[:, 0:F], v[:, 0:F], lo[:, 0:F],
                                          ALU.subtract),
           reads=(v, lo), writes=(v,), wide=W)
    kb.eop("V", lambda V: V.scalar_tensor_tensor(
        v[:, 0:F].rearrange("p (b k) -> p b k", k=16),
        v[:, 0:F].rearrange("p (b k) -> p b k", k=16),
        1.0 / 6.0, mx_b, ALU.mult, ALU.mult),
        reads=(v, mx), writes=(v,), wide=W)
    kb.eop("V", lambda V: V.tensor_tensor(out[:, 0:F], v[:, 0:F],
                                          sg[:, 0:F], ALU.mult),
        reads=(v, sg), writes=(out,), wide=W)



def emit_l0ffn_weight_prep(kb, w_gate, w_up, w_down, scr_q, scr_guT, scr_dnT,
                           qsc, big_f32, big_bf_a, big_bf_b):
    wt, wtb, wqt = big_f32, big_bf_a, big_bf_b
    for w, row0, ncols, nrows in [(w_gate, 0, D, NI), (w_up, I_FFN, D, NI),
                                  (w_down, 5632, I_FFN, 8)]:
        for i in range(nrows):
            kb.wr("Y", wt)
            kb.op("Y", lambda Y, w=w, i=i, nc_=ncols: Y.dma_start(
                out=wt[:, 0:nc_], in_=w.ap()[i * 128:(i + 1) * 128, :]), 16)
            kb.did_w("Y", wt)
            kb.rd("V", wt); kb.wr("V", wtb)
            kb.op("V", lambda V, nc_=ncols: V.tensor_copy(wtb[:, 0:nc_],
                                                          wt[:, 0:nc_]))
            kb.did_r("V", wt); kb.did_w("V", wtb)
            emit_fp4_quant(kb, wtb, wqt, ncols, qsc)
            kb.rd("Y", wqt)
            kb.op("Y", lambda Y, row0=row0, i=i, nc_=ncols: Y.dma_start(
                out=scr_q.ap()[row0 + i * 128: row0 + (i + 1) * 128, 0:nc_],
                in_=wqt[:, 0:nc_]), 16)
            kb.did_r("Y", wqt)
    kb.wait("Y", "Y")
    stage = big_bf_a
    for c in range(8):
        for half, r0 in [(0, 0), (1, I_FFN)]:
            kb.wr("Y", stage)
            kb.op("Y", lambda Y, c=c, r0=r0: Y.dma_start(
                out=stage[:, 0:I_FFN],
                in_=scr_q.ap()[r0:r0 + I_FFN, c * 128:(c + 1) * 128],
                transpose=True), 16)
            kb.did_w("Y", stage)
            kb.rd("Y", stage)
            kb.op("Y", lambda Y, c=c, half=half: Y.dma_start(
                out=scr_guT.ap()[c, :, half * I_FFN:(half + 1) * I_FFN],
                in_=stage[:, 0:I_FFN]), 16)
            kb.did_r("Y", stage)
    for i in range(NI):
        kb.wr("Y", stage)
        kb.op("Y", lambda Y, i=i: Y.dma_start(
            out=stage[:, 0:D],
            in_=scr_q.ap()[5632:6656, i * 128:(i + 1) * 128],
            transpose=True), 16)
        kb.did_w("Y", stage)
        kb.rd("Y", stage)
        kb.op("Y", lambda Y, i=i: Y.dma_start(
            out=scr_dnT.ap()[i, :, :], in_=stage[:, 0:D]), 16)
        kb.did_r("Y", stage)


FPLAN = [(0, 512), (512, 512), (1024, 512), (1536, 512), (2048, 512),
         (2560, 256)]


def emit_l0ffn(kb, load_x, store_x, scr_guT, scr_dnT, hq_dram, ident,
               qsc, big_f32, big_bf_a, big_bf_b, grp_tiles=8):
    GRP = grp_tiles
    xln = kb.sb("f_xln", [128, D], BF16)
    lnscr = kb.sb("f_lnscr", [128, D], BF16)
    stats = kb.sb("f_stats", [128, 48], F32)
    xq = kb.sb("f_xq", [128, D], BF16)
    xqT = [kb.sb(f"f_xqT{g}", [128, 8 * 128], BF16) for g in range(GRP)]
    psT = Buf(None)
    psT.h = kb.psum[7].h[:].bitcast(BF16)
    psT.readers = kb.psum[7].readers
    hbufs = [kb.sb(f"f_hb{g}", [128, I_FFN], BF16) for g in range(GRP)]
    silu_t = kb.sb("f_silu", [128, 512], BF16)
    hq = big_bf_b
    wgu = [kb.sb(f"f_wgu{b}", [128, 16 * 512], BF16) for b in range(2)]
    WCOLS = 16 * 512

    kb.wait("Y", "Y")
    for grp in range(NT // GRP):
        xts = [load_x(kb, grp * GRP + g) for g in range(GRP)]
        for g in range(GRP):
            emit_ln_stats(kb, xts[g], lnscr, stats, g, xcols=(0, D))
        emit_ln_finish(kb, stats, GRP)
        for g in range(GRP):
            emit_ln_apply(kb, xts[g], xln, lnscr, stats, g, xcols=(0, D))
            emit_fp4_quant(kb, xln, xq, D, qsc)
            kb.rd("T", xq, ident); kb.wr("T", psT)
            def tr(T, xq=xq, psT=psT):
                for c in range(8):
                    ins = T.transpose(psT.h[:, c * 128:(c + 1) * 128],
                                      xq[:, c * 128:(c + 1) * 128],
                                      ident.h[:])
                return ins
            kb.op("T", tr)
            kb.did_r("T", xq); kb.did_w("T", psT)
            kb.rd("V", psT); kb.wr("V", xqT[g])
            kb.op("V", lambda V, g=g: V.tensor_copy(xqT[g].h[:], psT.h))
            kb.did_r("V", psT); kb.did_w("V", xqT[g])
        for fi, (f0, fw) in enumerate(FPLAN):
            wbuf = wgu[fi % 2]
            kb.wr("Y", wbuf)
            for h in range(2):
                kb.op("Y", lambda Y, wbuf=wbuf, f0=f0, fw=fw, h=h:
                      Y.dma_start(
                    out=wbuf.h[:, 0:16 * fw].rearrange(
                        "p (a f) -> p a f", f=fw)[:, h::2, :],
                    in_=scr_guT.ap()[:, :, h * I_FFN + f0:
                                     h * I_FFN + f0 + fw].rearrange(
                        "c p x -> p c x")), 16)
            kb.did_w("Y", wbuf)
            for g in range(GRP):
                bg, bu = kb.psum[g % 2 * 2], kb.psum[g % 2 * 2 + 1]
                kb.rd("T", xqT[g], wbuf); kb.wr("T", bg, bu)
                def mm(T, g=g, bg=bg, bu=bu, wbuf=wbuf, fw=fw):
                    for c in range(8):
                        T.matmul(bg[:, 0:fw],
                                 xqT[g][:, c * 128:(c + 1) * 128],
                                 wbuf[:, (2 * c) * fw:(2 * c + 1) * fw],
                                 start=(c == 0), stop=(c == 7))
                    for c in range(8):
                        ins = T.matmul(
                            bu[:, 0:fw], xqT[g][:, c * 128:(c + 1) * 128],
                            wbuf[:, (2 * c + 1) * fw:(2 * c + 2) * fw],
                            start=(c == 0), stop=(c == 7))
                    return ins
                kb.op("T", mm)
                kb.did_r("T", xqT[g], wbuf); kb.did_w("T", bg, bu)
                kb.rd("S", bg); kb.wr("S", silu_t)
                kb.op("S", lambda S, bg=bg, fw=fw: S.activation(
                    silu_t[:, 0:fw], bg[:, 0:fw], AF.Silu))
                kb.did_r("S", bg); kb.did_w("S", silu_t)
                kb.rd("V", bu, silu_t); kb.wr("V", hbufs[g])
                kb.op("V", lambda V, g=g, bu=bu, f0=f0, fw=fw:
                      V.tensor_tensor(hbufs[g][:, f0:f0 + fw], bu[:, 0:fw],
                                      silu_t[:, 0:fw], ALU.mult))
                kb.did_r("V", bu, silu_t)
                if fi == len(FPLAN) - 1:
                    kb.did_w("V", hbufs[g])
        for g in range(GRP):
            tt = grp * GRP + g
            emit_fp4_quant(kb, hbufs[g], hq, I_FFN, qsc)
            kb.rd("Y", hq)
            kb.op("Y", lambda Y, tt=tt: Y.dma_start(
                out=hq_dram.ap()[tt * 128:(tt + 1) * 128, :],
                in_=hq[:, 0:I_FFN]), 16)
            kb.did_r("Y", hq)

    kb.wait("Y", "Y")
    kb.psum[7].prod = psT.prod
    hqT = [kb.sb(f"f_hqT{b}", [128, 512], BF16) for b in range(2)]
    wdn = [kb.sb(f"f_wdn{b}", [128, D], BF16) for b in range(2)]
    for sup in range(8):
        t0 = sup * 512
        banks = kb.psum
        for i in range(NI):
            buf = hqT[i % 2]
            wbuf = wdn[i % 2]
            kb.wr("Y", buf, wbuf)
            kb.op("Y", lambda Y, buf=buf, t0=t0, i=i: Y.dma_start(
                out=buf.h[:],
                in_=hq_dram.ap()[t0:t0 + 512, i * 128:(i + 1) * 128],
                transpose=True), 16)
            kb.op("Y", lambda Y, wbuf=wbuf, i=i: Y.dma_start(
                out=wbuf.h[:], in_=scr_dnT.ap()[i, :, :]), 16)
            kb.did_w("Y", buf, wbuf)
            kb.rd("T", buf, wbuf)
            if i == 0:
                for b in banks:
                    kb.wr("T", b)
            def mmd(T, buf=buf, wbuf=wbuf, i=i, banks=banks):
                for ti in range(4):
                    for fh in range(2):
                        ins = T.matmul(
                            banks[ti * 2 + fh].h[:],
                            buf[:, ti * 128:(ti + 1) * 128],
                            wbuf[:, fh * 512:fh * 512 + 512],
                            start=(i == 0), stop=(i == NI - 1))
                return ins
            kb.op("T", mmd)
            kb.did_r("T", buf, wbuf)
            if i == NI - 1:
                for b in banks:
                    kb.did_w("T", b)
        for ti in range(4):
            tt = sup * 4 + ti
            xres = load_x(kb, tt)
            for fh in range(2):
                bank = banks[ti * 2 + fh]
                kb.rd("V", xres, bank)
                kb.op("V", lambda V, xres=xres, bank=bank, fh=fh:
                      V.scalar_tensor_tensor(
                          xres[:, fh * 512:fh * 512 + 512], bank.h[:], 0.5,
                          xres[:, fh * 512:fh * 512 + 512], ALU.mult,
                          ALU.add))
                kb.did_r("V", bank)
            kb.did_w("V", xres, wide=True)
            store_x(kb, tt, xres)



HD = 64
NHEAD = 16


def emit_l0attn(kb, load_x, store_x, w_qkv, b_qkv, w_out, b_out,
                qkT_dram, v_dram, ctxT_dram, ident, mask4, lnscr, stats):
    onesrow = kb.sb("a_ones", [1, 128], F32)
    onescol = kb.sb("a_onesc", [128, 1], F32)
    kb.op("V", lambda V: V.memset(onesrow.ap, 1.0))
    kb.op("V", lambda V: V.memset(onescol.ap, 1.0))
    kb.did_w("V", onesrow, onescol)
    wqkv_bf = kb.sb("a_wqkv", [128, 8 * 3072], BF16)
    wout_bf = kb.sb("a_wout", [128, 8 * 1024], BF16)
    bq_col = kb.sb("a_bqcol", [128, 24], F32)
    brow = kb.sb("a_brow", [1, 3072], F32)
    borow = kb.sb("a_borow", [1, 1024], F32)
    for c in range(8):
        kb.op("G", lambda G, c=c: G.dma_start(
            out=wqkv_bf[:, c * 3072:(c + 1) * 3072],
            in_=w_qkv.ap()[c * 128:(c + 1) * 128, :]), 16)
        kb.op("G", lambda G, c=c: G.dma_start(
            out=wout_bf[:, c * 1024:(c + 1) * 1024],
            in_=w_out.ap()[c * 128:(c + 1) * 128, :]), 16)
    kb.op("Y", lambda Y: Y.dma_start(
        out=bq_col.ap, in_=b_qkv.ap().rearrange("(a p) -> p a", p=128),
        allow_slow_non_contiguous=True), 16)
    kb.op("Y", lambda Y: Y.dma_start(
        out=brow.ap, in_=b_qkv.ap().unsqueeze(0)), 16)
    kb.op("Y", lambda Y: Y.dma_start(
        out=borow.ap, in_=b_out.ap().unsqueeze(0)), 16)
    kb.did_w("G", wqkv_bf, wout_bf)
    kb.did_w("Y", bq_col, brow, borow)

    xln = kb.sb("a_xln", [128, D], BF16)
    xlnT = kb.sb("a_xlnT", [128, 8 * 512], BF16)
    psT = Buf(None)
    psT.h = kb.psum[7].h[:].bitcast(BF16)
    psT.readers = kb.psum[7].readers
    qkbuf = kb.sb("a_qkbuf", [128, 512], BF16)
    vbuf = kb.sb("a_vbuf", [128, 512], BF16)

    for sup in range(8):
        xts = [load_x(kb, sup * 4 + g) for g in range(4)]
        for g in range(4):
            emit_ln_stats(kb, xts[g], lnscr, stats, g, xcols=(0, D))
        emit_ln_finish(kb, stats, 4)
        x3 = xlnT.h[:].rearrange("p (c t) -> p c t", c=8)
        for g in range(4):
            emit_ln_apply(kb, xts[g], xln, lnscr, stats, g, xcols=(0, D))
            kb.rd("T", xln, ident); kb.wr("T", psT)
            def tr(T, xln=xln, psT=psT):
                for c in range(8):
                    ins = T.transpose(psT.h[:, c * 128:(c + 1) * 128],
                                      xln[:, c * 128:(c + 1) * 128],
                                      ident.h[:])
                return ins
            kb.op("T", tr)
            kb.did_r("T", xln); kb.did_w("T", psT)
            kb.rd("V", psT); kb.wr("V", xlnT)
            kb.op("V", lambda V, g=g, x3=x3: V.tensor_copy(
                x3[:, :, g * 128:(g + 1) * 128],
                psT.h[:].rearrange("p (c t) -> p c t", c=8)))
            kb.did_r("V", psT)
        kb.did_w("V", xlnT, wide=True)
        for ft in range(16):
            bank = kb.psum[ft % 2]
            kb.rd("T", xlnT, wqkv_bf); kb.wr("T", bank)
            def mmqk(T, ft=ft, bank=bank, x3=x3):
                for c in range(8):
                    ins = T.matmul(bank.h[:],
                                   wqkv_bf[:, c * 3072 + ft * 128:
                                           c * 3072 + (ft + 1) * 128],
                                   x3[:, c, :], start=(c == 0), stop=(c == 7))
                return ins
            kb.op("T", mmqk)
            kb.did_w("T", bank)
            kb.rd("V", bank, bq_col); kb.wr("V", qkbuf)
            kb.op("V", lambda V, ft=ft, bank=bank: V.tensor_single_scalar(
                qkbuf.ap, bank.h[:], bq_col[:, ft:ft + 1], ALU.add))
            kb.did_r("V", bank); kb.did_w("V", qkbuf)
            kb.rd("Y", qkbuf)
            kb.op("Y", lambda Y, ft=ft, sup=sup: Y.dma_start(
                out=qkT_dram.ap()[ft * 128:(ft + 1) * 128,
                                  sup * 512:(sup + 1) * 512],
                in_=qkbuf.ap), 16)
            kb.did_r("Y", qkbuf)
        for g in range(4):
            for fh in range(2):
                bank = kb.psum[2 + (g * 2 + fh) % 2]
                kb.rd("T", xlnT, wqkv_bf, onesrow, brow); kb.wr("T", bank)
                def mmv(T, g=g, fh=fh, bank=bank, x3=x3):
                    for c in range(8):
                        T.matmul(bank.h[:],
                                 x3[:, c, g * 128:(g + 1) * 128],
                                 wqkv_bf[:, c * 3072 + 2048 + fh * 512:
                                         c * 3072 + 2048 + (fh + 1) * 512],
                                 start=(c == 0), stop=False)
                    return T.matmul(bank.h[:], onesrow.h[0:1, :],
                                    brow.h[0:1, 2048 + fh * 512:
                                           2048 + (fh + 1) * 512],
                                    start=False, stop=True)
                kb.op("T", mmv)
                kb.did_w("T", bank)
                kb.rd("V", bank); kb.wr("V", vbuf)
                kb.op("V", lambda V, bank=bank: V.tensor_copy(vbuf.ap,
                                                              bank.h[:]))
                kb.did_r("V", bank); kb.did_w("V", vbuf)
                kb.rd("Y", vbuf)
                kb.op("Y", lambda Y, sup=sup, g=g, fh=fh: Y.dma_start(
                    out=v_dram.ap()[sup * 512 + g * 128:
                                    sup * 512 + (g + 1) * 128,
                                    fh * 512:(fh + 1) * 512],
                    in_=vbuf.ap), 16)
                kb.did_r("Y", vbuf)

    kb.wait("Y", "Y")

    kT = kb.sb("a_kT", [64, T_CORE], BF16)
    qT = kb.sb("a_qT", [64, T_CORE], BF16)
    vb3 = kb.sb("a_vb3", [128, 32 * 72], BF16)
    pT = [kb.sb(f"a_pT{i}", [128, 512], BF16) for i in range(2)]
    md = kb.sb("a_md", [64, 2], F32)
    sc1 = kb.sb("a_sc1", [1, 8], F32)
    bias_col = kb.sb("a_biascol", [128, 1], F32)
    linv_row = kb.sb("a_linv", [1, 512], F32)
    linv_rep = kb.sb("a_linvrep", [64, 512], F32)
    ctx_bf = kb.sb("a_ctx", [64, 512], BF16)
    v3 = vb3.h[:].rearrange("p (c w) -> p c w", w=72)
    kb.op("V", lambda V: V.memset(vb3.ap, 0.0))
    kb.op("V", lambda V, v3=v3: V.memset(v3[:, :, 64:65], 1.0))
    kb.did_w("V", vb3)

    for h in range(NHEAD):
        kb.wr("Y", kT, qT, vb3)
        kb.op("Y", lambda Y, h=h: Y.dma_start(
            out=kT.ap,
            in_=qkT_dram.ap()[1024 + h * 64:1024 + (h + 1) * 64, :]), 16)
        kb.op("Y", lambda Y, h=h: Y.dma_start(
            out=qT.ap, in_=qkT_dram.ap()[h * 64:(h + 1) * 64, :]), 16)
        kb.op("Y", lambda Y, h=h, v3=v3: Y.dma_start(
            out=v3[:, :, 0:64],
            in_=v_dram.ap()[:, h * 64:(h + 1) * 64].rearrange(
                "(c p) d -> p c d", p=128)), 16)
        kb.did_w("Y", kT, qT, vb3)
        kb.eop("V", lambda V: V.tensor_reduce(
            md[:, 0:1], kT.ap, AX.X, ALU.max, apply_absolute_value=True),
            reads=(kT,), writes=(md,))
        kb.eop("V", lambda V: V.tensor_reduce(
            md[:, 1:2], qT.ap, AX.X, ALU.max, apply_absolute_value=True),
            reads=(md, qT), writes=(md,))
        kb.eop("V", lambda V: V.tensor_tensor(md.ap, md.ap, md.ap, ALU.mult),
               reads=(md,), writes=(md,))
        bank = kb.psum[6]
        kb.rd("T", md, onescol); kb.wr("T", bank)
        kb.op("T", lambda T, bank=bank: T.matmul(
            bank.h[0:1, 0:1], md.h[0:64, 0:1], onescol.h[0:64, 0:1],
            start=True, stop=True))
        kb.op("T", lambda T, bank=bank: T.matmul(
            bank.h[0:1, 1:2], md.h[0:64, 1:2], onescol.h[0:64, 0:1],
            start=False, stop=True))
        kb.did_w("T", bank)
        kb.eop("V", lambda V, bank=bank: V.tensor_copy(sc1[0:1, 0:2],
                                                       bank.h[0:1, 0:2]),
               reads=(bank,), writes=(sc1,))
        kb.did_r("V", bank)
        kb.eop("V", lambda V: V.tensor_tensor(sc1[0:1, 2:3], sc1[0:1, 0:1],
                                              sc1[0:1, 1:2], ALU.mult),
               reads=(sc1,), writes=(sc1,))
        kb.eop("S", lambda S: S.activation(sc1[0:1, 3:4], sc1[0:1, 2:3],
                                           AF.Ln), reads=(sc1,),
               writes=(sc1,))
        kb.eop("S", lambda S: S.activation(sc1[0:1, 4:5], sc1[0:1, 3:4],
                                           AF.Exp, scale=0.5),
               reads=(sc1,), writes=(sc1,))
        kb.eop("V", lambda V: V.tensor_scalar(sc1[0:1, 5:6], sc1[0:1, 4:5],
                                              -0.125, None, ALU.mult),
               reads=(sc1,), writes=(sc1,))
        rep_bank = kb.psum[5]
        kb.rd("T", sc1, onesrow); kb.wr("T", rep_bank)
        kb.op("T", lambda T, rep_bank=rep_bank: T.matmul(
            rep_bank.h[:, 0:1], onesrow.h[0:1, :], sc1.h[0:1, 5:6],
            start=True, stop=True))
        kb.did_w("T", rep_bank)
        kb.eop("V", lambda V, rep_bank=rep_bank: V.tensor_copy(
            bias_col.ap, rep_bank.h[:, 0:1]),
            reads=(rep_bank,), writes=(bias_col,))
        kb.did_r("V", rep_bank)

        for qc in range(8):
            nkc = 4 * (qc + 1)
            pv_bank = kb.psum[4]
            kb.wr("T", pv_bank)
            for kc in range(nkc):
                sb_ = kb.psum[kc % 2]
                kb.rd("T", kT, qT, ident); kb.wr("T", sb_)
                diag_a = kc * 128 - qc * 512
                def mmsc(T, kc=kc, qc=qc, sb_=sb_, diag_a=diag_a):
                    ins = T.matmul(sb_.h[:], kT[:, kc * 128:(kc + 1) * 128],
                                   qT[:, qc * 512:(qc + 1) * 512],
                                   start=True, stop=(diag_a < 0))
                    if diag_a >= 0:
                        ins = T.matmul(sb_.h[:], ident.h[:],
                                       mask4[diag_a // 128],
                                       start=False, stop=True)
                    return ins
                kb.op("T", mmsc)
                kb.did_w("T", sb_)
                pbuf = pT[kc % 2]
                kb.rd("S", sb_, bias_col); kb.wr("S", pbuf)
                kb.op("S", lambda S, sb_=sb_, pbuf=pbuf: S.activation(
                    pbuf.ap, sb_.h[:], AF.Exp, bias=bias_col.ap,
                    scale=0.125))
                kb.did_r("S", sb_); kb.did_w("S", pbuf, wide=True)
                kb.rd("T", vb3, pbuf)
                def mmpv(T, kc=kc, pbuf=pbuf, pv_bank=pv_bank, nkc=nkc,
                         v3=v3):
                    return T.matmul(pv_bank.h[0:65, :], v3[:, kc, 0:65],
                                    pbuf.ap, start=(kc == 0),
                                    stop=(kc == nkc - 1))
                kb.op("T", mmpv)
                kb.did_r("T", pbuf)
            kb.did_w("T", pv_bank)
            kb.eop("V", lambda V, pv_bank=pv_bank: V.reciprocal(
                linv_row.ap, pv_bank.h[64:65, :]),
                reads=(pv_bank,), writes=(linv_row,))
            rep2 = kb.psum[5]
            kb.rd("T", linv_row, onesrow); kb.wr("T", rep2)
            kb.op("T", lambda T, rep2=rep2: T.matmul(
                rep2.h[0:64, :], onesrow.h[0:1, 0:64], linv_row.ap,
                start=True, stop=True))
            kb.did_w("T", rep2)
            kb.rd("S", rep2); kb.wr("S", linv_rep)
            kb.op("S", lambda S, rep2=rep2: S.activation(
                linv_rep.ap, rep2.h[0:64, :], AF.Copy))
            kb.did_r("S", rep2); kb.did_w("S", linv_rep)
            kb.rd("V", pv_bank, linv_rep); kb.wr("V", ctx_bf)
            kb.op("V", lambda V, pv_bank=pv_bank: V.tensor_tensor(
                ctx_bf.ap, pv_bank.h[0:64, :], linv_rep.ap, ALU.mult))
            kb.did_r("V", pv_bank); kb.did_w("V", ctx_bf)
            kb.rd("Y", ctx_bf)
            kb.op("Y", lambda Y, h=h, qc=qc: Y.dma_start(
                out=ctxT_dram.ap()[h * 64:(h + 1) * 64,
                                   qc * 512:(qc + 1) * 512],
                in_=ctx_bf.ap), 16)
            kb.did_r("Y", ctx_bf)

    kb.wait("Y", "Y")

    ctxc = [kb.sb(f"a_ctxc{i}", [128, 512], BF16) for i in range(2)]
    banks = kb.psum
    for sup in range(8):
        for c in range(8):
            buf = ctxc[c % 2]
            kb.wr("Y", buf)
            kb.op("Y", lambda Y, c=c, sup=sup, buf=buf: Y.dma_start(
                out=buf.ap,
                in_=ctxT_dram.ap()[c * 128:(c + 1) * 128,
                                   sup * 512:(sup + 1) * 512]), 16)
            kb.did_w("Y", buf)
            kb.rd("T", buf, wout_bf, onesrow, borow)
            if c == 0:
                for b in banks:
                    kb.wr("T", b)
            def mmo(T, c=c, buf=buf):
                for ti in range(4):
                    for fh in range(2):
                        ins = T.matmul(
                            banks[ti * 2 + fh].h[:],
                            buf[:, ti * 128:(ti + 1) * 128],
                            wout_bf[:, c * 1024 + fh * 512:
                                    c * 1024 + (fh + 1) * 512],
                            start=(c == 0), stop=False)
                if c == 7:
                    for ti in range(4):
                        for fh in range(2):
                            ins = T.matmul(
                                banks[ti * 2 + fh].h[:], onesrow.h[0:1, :],
                                borow.h[0:1, fh * 512:(fh + 1) * 512],
                                start=False, stop=True)
                return ins
            kb.op("T", mmo)
            kb.did_r("T", buf)
            if c == 7:
                for b in banks:
                    kb.did_w("T", b)
        for ti in range(4):
            tt = sup * 4 + ti
            xres = load_x(kb, tt)
            for fh in range(2):
                bank = banks[ti * 2 + fh]
                kb.rd("V", xres, bank)
                kb.op("V", lambda V, xres=xres, bank=bank, fh=fh:
                      V.tensor_tensor(xres[:, fh * 512:fh * 512 + 512],
                                      bank.h[:],
                                      xres[:, fh * 512:fh * 512 + 512],
                                      ALU.add))
                kb.did_r("V", bank)
            kb.did_w("V", xres, wide=True)
            store_x(kb, tt, xres)



DIN = 2048
NDT = DIN // 128
TC = 1024
NCH = T_CORE // TC


def emit_l1mamba(kb, load_x, store_x, win, conv_w, conv_b, xproj, dtw, dtb,
                 alog, dpar, cw, z_dram, ident, lnscr, stats, dbg=None):
    def dump(name, buf, cols, ch=None):
        if dbg is None or name not in dbg:
            return
        if ch is not None and ch != 0:
            return
        kb.rd("Y", buf)
        kb.op("Y", lambda Y, name=name, buf=buf, cols=cols: Y.dma_start(
            out=dbg[name].ap()[:, 0:cols], in_=buf[:, 0:cols]), 16)
        kb.did_r("Y", buf)
    nc = kb.nc
    onesrow = kb.sb("m_ones", [1, 128], F32)
    onesb = kb.sb("m_onesb", [1, 128], BF16)
    kb.op("V", lambda V: V.memset(onesrow.ap, 1.0))
    kb.op("V", lambda V: V.memset(onesb.ap, 1.0))
    kb.did_w("V", onesrow, onesb)

    convw_c = kb.sb("m_convw", [128, 64], F32)
    convb_c = kb.sb("m_convb", [128, 16], F32)
    dtw_c = kb.sb("m_dtw", [128, 16], F32)
    dtb_c = kb.sb("m_dtb", [128, 16], F32)
    alog_c = kb.sb("m_alog", [128, 256], F32)
    nega = kb.sb("m_nega", [128, 256], F32)
    dpar_c = kb.sb("m_dpar", [128, 16], F32)
    xproj_b = kb.sb("m_xproj", [128, 16 * 33], BF16)
    kb.op("Y", lambda Y: Y.dma_start(
        out=convw_c.h[:].rearrange("p (i j) -> p i j", i=16),
        in_=conv_w.ap().rearrange("(i p) o j -> p i (o j)", p=128)), 16)
    kb.op("Y", lambda Y: Y.dma_start(
        out=convb_c.ap, in_=conv_b.ap().rearrange("(i p) -> p i", p=128),
        allow_slow_non_contiguous=True), 16)
    kb.op("Y", lambda Y: Y.dma_start(
        out=dtw_c.ap, in_=dtw.ap().rearrange("o (i p) -> p (o i)", p=128),
        allow_slow_non_contiguous=True), 16)
    kb.op("Y", lambda Y: Y.dma_start(
        out=dtb_c.ap, in_=dtb.ap().rearrange("(i p) -> p i", p=128),
        allow_slow_non_contiguous=True), 16)
    kb.op("Y", lambda Y: Y.dma_start(
        out=alog_c.h[:].rearrange("p (i s) -> p i s", i=16),
        in_=alog.ap().rearrange("(i p) s -> p i s", p=128)), 16)
    kb.op("Y", lambda Y: Y.dma_start(
        out=dpar_c.ap, in_=dpar.ap().rearrange("(i p) -> p i", p=128),
        allow_slow_non_contiguous=True), 16)
    kb.did_w("Y", convw_c, convb_c, dtw_c, dtb_c, alog_c, dpar_c)
    kb.op("G", lambda G: G.dma_start(
        out=xproj_b.h[:].rearrange("p (i k) -> p i k", i=16),
        in_=xproj.ap().rearrange("(i p) k -> p i k", p=128)), 16)
    kb.did_w("G", xproj_b)
    kb.eop("S", lambda S: S.activation(nega.ap, alog_c.ap, AF.Exp),
           reads=(alog_c,), writes=(nega,), wide=True)
    kb.eop("V", lambda V: V.tensor_scalar(nega.ap, nega.ap, -1.0, None,
                                          ALU.mult),
           reads=(nega,), writes=(nega,), wide=True)

    xln = kb.sb("m_xln", [128, D], BF16)
    xlnT = kb.sb("m_xlnT", [128, 8 * TC], BF16)
    psT = Buf(None)
    psT.h = kb.psum[7].h[:].bitcast(BF16)
    psT.readers = kb.psum[7].readers
    xbr = kb.sb("m_xbr", [128, NDT * TC], BF16)
    ybuf = xbr
    ytmp = kb.sb("m_ytmp", [128, TC], BF16)
    brep = kb.sb("m_brep", [128, 16 * TC], BF16)
    crep = kb.sb("m_crep", [128, 16 * TC], BF16)
    pad = kb.sb("m_pad", [128, TC + 3], BF16)
    convo = kb.sb("m_convo", [128, TC], BF16)
    hist = kb.sb("m_hist", [128, 16 * 3], BF16)
    hcarry = kb.sb("m_hcarry", [128, 256], F32)
    bcbuf = kb.sb("m_bc", [32, TC], BF16)
    dtraw = kb.sb("m_dtraw", [1, TC], F32)
    brow = kb.sb("m_brow", [1, TC], BF16)
    Rrep = kb.sb("m_Rrep", [128, TC], F32)
    dtb_f = kb.sb("m_dtfull", [128, TC], F32)
    ab_f = kb.sb("m_abar", [128, TC], F32)
    bx_f = kb.sb("m_bx", [128, TC], F32)
    hbuf = kb.sb("m_h", [128, TC], BF16)
    ubuf = kb.sb("m_u", [128, TC], BF16)
    wsl = [kb.sb(f"m_wsl{i}", [128, 8 * 128], BF16) for i in range(2)]
    cwsl = [kb.sb(f"m_cwsl{i}", [128, D], BF16) for i in range(2)]
    zbuf = kb.sb("m_zbuf", [128, 512], BF16)
    kb.op("V", lambda V: V.memset(hist.ap, 0.0))
    kb.op("V", lambda V: V.memset(hcarry.ap, 0.0))
    kb.did_w("V", hist, hcarry)

    for ch in range(NCH):
        t0 = ch * TC
        xts = [load_x(kb, ch * 8 + g) for g in range(8)]
        for g in range(8):
            emit_ln_stats(kb, xts[g], lnscr, stats, g, xcols=(0, D))
        emit_ln_finish(kb, stats, 8)
        x3 = xlnT.h[:].rearrange("p (c t) -> p c t", c=8)
        for g in range(8):
            emit_ln_apply(kb, xts[g], xln, lnscr, stats, g, xcols=(0, D))
            kb.rd("T", xln, ident); kb.wr("T", psT)
            def tr(T, xln=xln, psT=psT):
                for c in range(8):
                    ins = T.transpose(psT.h[:, c * 128:(c + 1) * 128],
                                      xln[:, c * 128:(c + 1) * 128],
                                      ident.h[:])
                return ins
            kb.op("T", tr)
            kb.did_r("T", xln); kb.did_w("T", psT)
            kb.rd("V", psT); kb.wr("V", xlnT)
            kb.op("V", lambda V, g=g, x3=x3: V.tensor_copy(
                x3[:, :, g * 128:(g + 1) * 128],
                psT.h[:].rearrange("p (c t) -> p c t", c=8)))
            kb.did_r("V", psT)
        kb.did_w("V", xlnT, wide=True)

        for i in range(NDT):
            wbuf = wsl[i % 2]
            kb.wr("G", wbuf)
            kb.op("G", lambda G, i=i, wbuf=wbuf: G.dma_start(
                out=wbuf.h[:].rearrange("p (c f) -> p c f", c=8),
                in_=win.ap()[:, i * 128:(i + 1) * 128].rearrange(
                    "(c p) f -> p c f", p=128)), 16)
            kb.did_w("G", wbuf)
            for th in range(2):
                bank = kb.psum[th]
                kb.rd("T", xlnT, wbuf); kb.wr("T", bank)
                def mmi(T, i=i, th=th, bank=bank, wbuf=wbuf, x3=x3):
                    for c in range(8):
                        ins = T.matmul(bank.h[:],
                                       wbuf[:, c * 128:(c + 1) * 128],
                                       x3[:, c, th * 512:(th + 1) * 512],
                                       start=(c == 0), stop=(c == 7))
                    return ins
                kb.op("T", mmi)
                kb.did_w("T", bank); kb.did_r("T", wbuf, xlnT)
                kb.rd("S", bank); kb.wr("S", pad)
                kb.op("S", lambda S, th=th, bank=bank: S.activation(
                    pad[:, 3 + th * 512:3 + (th + 1) * 512], bank.h[:],
                    AF.Copy))
                kb.did_r("S", bank)
            kb.rd("S", hist)
            kb.op("S", lambda S, i=i: S.activation(
                pad[:, 0:3], hist[:, i * 3:(i + 1) * 3], AF.Copy))
            kb.did_r("S", hist); kb.did_w("S", pad)
            kb.rd("V", pad, convw_c); kb.wr("V", convo)
            kb.op("V", lambda V, i=i: V.tensor_single_scalar(
                convo.ap, pad[:, 0:TC], convw_c[:, i * 4:i * 4 + 1],
                ALU.mult))
            for j in range(1, 4):
                kb.op("V", lambda V, i=i, j=j: V.scalar_tensor_tensor(
                    convo.ap, pad[:, j:j + TC], convw_c[:, i * 4 + j:
                                                        i * 4 + j + 1],
                    convo.ap, ALU.mult, ALU.add))
            kb.did_w("V", convo, wide=True)
            kb.wr("V", hist)
            kb.op("V", lambda V, i=i: V.tensor_copy(
                hist[:, i * 3:(i + 1) * 3], pad[:, TC:TC + 3]))
            kb.did_w("V", hist)
            kb.did_r("V", pad)
            kb.rd("S", convo, convb_c); kb.wr("S", xbr)
            kb.op("S", lambda S, i=i: S.activation(
                xbr[:, i * TC:(i + 1) * TC], convo.ap, AF.Silu,
                bias=convb_c[:, i:i + 1]))
            kb.did_w("S", xbr, wide=True)

        dump("xbr", xbr, NDT * TC, ch)
        for i in range(NDT):
            wbuf = wsl[i % 2]
            kb.wr("G", wbuf)
            kb.op("G", lambda G, i=i, wbuf=wbuf: G.dma_start(
                out=wbuf.h[:].rearrange("p (c f) -> p c f", c=8),
                in_=win.ap()[:, 2048 + i * 128:2048 + (i + 1) * 128].rearrange(
                    "(c p) f -> p c f", p=128)), 16)
            kb.did_w("G", wbuf)
            for th in range(2):
                bank = kb.psum[2 + th]
                kb.rd("T", xlnT, wbuf); kb.wr("T", bank)
                def mmz(T, i=i, th=th, bank=bank, wbuf=wbuf, x3=x3):
                    for c in range(8):
                        ins = T.matmul(bank.h[:],
                                       wbuf[:, c * 128:(c + 1) * 128],
                                       x3[:, c, th * 512:(th + 1) * 512],
                                       start=(c == 0), stop=(c == 7))
                    return ins
                kb.op("T", mmz)
                kb.did_w("T", bank); kb.did_r("T", wbuf, xlnT)
                kb.rd("S", bank); kb.wr("S", zbuf)
                kb.op("S", lambda S, bank=bank: S.activation(
                    zbuf.ap, bank.h[:], AF.Silu))
                kb.did_r("S", bank); kb.did_w("S", zbuf)
                kb.rd("Y", zbuf)
                kb.op("Y", lambda Y, i=i, th=th, t0=t0: Y.dma_start(
                    out=z_dram.ap()[i * 128:(i + 1) * 128,
                                    t0 + th * 512:t0 + (th + 1) * 512],
                    in_=zbuf.ap), 16)
                kb.did_r("Y", zbuf)

        for th in range(2):
            bank = kb.psum[4 + th]
            kb.rd("T", xbr, xproj_b); kb.wr("T", bank)
            def mms(T, th=th, bank=bank):
                for i in range(NDT):
                    ins = T.matmul(bank.h[0:33, :],
                                   xproj_b[:, i * 33:(i + 1) * 33],
                                   xbr[:, i * TC + th * 512:
                                       i * TC + (th + 1) * 512],
                                   start=(i == 0), stop=(i == NDT - 1))
                return ins
            kb.op("T", mms)
            kb.did_w("T", bank)
            kb.rd("V", bank); kb.wr("V", bcbuf, dtraw)
            kb.op("V", lambda V, th=th, bank=bank: V.tensor_copy(
                bcbuf[0:32, th * 512:(th + 1) * 512], bank.h[0:32, :]))
            kb.op("V", lambda V, th=th, bank=bank: V.tensor_copy(
                dtraw[0:1, th * 512:(th + 1) * 512], bank.h[32:33, :]))
            kb.did_r("V", bank); kb.did_w("V", bcbuf, dtraw)

        dump("bc", bcbuf, TC, ch)
        dump("dtraw", dtraw, TC, ch)
        for th in range(2):
            bank = kb.psum[4 + th]
            kb.rd("T", dtraw, onesrow); kb.wr("T", bank)
            kb.op("T", lambda T, th=th, bank=bank: T.matmul(
                bank.h[:], onesrow.h[0:1, :],
                dtraw[0:1, th * 512:(th + 1) * 512], start=True, stop=True))
            kb.did_w("T", bank)
            kb.rd("S", bank); kb.wr("S", Rrep)
            kb.op("S", lambda S, th=th, bank=bank: S.activation(
                Rrep[:, th * 512:(th + 1) * 512], bank.h[:], AF.Copy))
            kb.did_r("S", bank)
        kb.did_w("S", Rrep, wide=True)

        for s in range(16):
            for part, dst in [(0, brep), (16, crep)]:
                kb.rd("G", bcbuf)
                kb.op("G", lambda G, s=s, part=part: G.dma_start(
                    out=brow.ap, in_=bcbuf[part + s:part + s + 1, :]), 16)
                kb.did_w("G", brow)
                for th in range(2):
                    bank = kb.psum[6]
                    kb.rd("T", brow, onesb); kb.wr("T", bank)
                    kb.op("T", lambda T, th=th, bank=bank: T.matmul(
                        bank.h[:], onesb.h[0:1, :],
                        brow[0:1, th * 512:(th + 1) * 512],
                        start=True, stop=True))
                    kb.did_w("T", bank)
                    kb.rd("S", bank); kb.wr("S", dst)
                    kb.op("S", lambda S, s=s, th=th, bank=bank, dst=dst:
                          S.activation(dst[:, s * TC + th * 512:
                                           s * TC + (th + 1) * 512],
                                       bank.h[:], AF.Copy))
                    kb.did_r("S", bank)
        kb.did_w("S", brep, crep, wide=True)

        for i in range(NDT):
            kb.eop("V", lambda V, i=i: V.tensor_single_scalar(
                dtb_f.ap, Rrep.ap, dtw_c[:, i:i + 1], ALU.mult),
                reads=(Rrep, dtw_c), writes=(dtb_f,), wide=True)
            kb.eop("V", lambda V, i=i: V.tensor_single_scalar(
                dtb_f.ap, dtb_f.ap, dtb_c[:, i:i + 1], ALU.add),
                reads=(dtb_f, dtb_c), writes=(dtb_f,), wide=True)
            kb.eop("V", lambda V: V.tensor_scalar(
                ab_f.h[:].bitcast(I32), dtb_f.h[:].bitcast(
                    I32), 0x7FFFFFFF, None, ALU.bitwise_and),
                reads=(dtb_f,), writes=(ab_f,), wide=True)
            kb.eop("S", lambda S: S.activation(ab_f.ap, ab_f.ap, AF.Exp,
                                               scale=-1.0),
                   reads=(ab_f,), writes=(ab_f,), wide=True)
            eps1 = kb.constcol(1.0)
            kb.eop("S", lambda S: S.activation(ab_f.ap, ab_f.ap, AF.Ln,
                                               bias=eps1),
                   reads=(ab_f,), writes=(ab_f,), wide=True)
            kb.eop("V", lambda V: V.tensor_scalar(bx_f.ap, dtb_f.ap, 0.0,
                                                  None, ALU.max),
                   reads=(dtb_f,), writes=(bx_f,), wide=True)
            kb.eop("V", lambda V: V.tensor_tensor(dtb_f.ap, bx_f.ap,
                                                  ab_f.ap, ALU.add),
                   reads=(bx_f, ab_f), writes=(dtb_f,), wide=True)
            if i == 0:
                dump("dt0", dtb_f, TC, ch)
            kb.eop("V", lambda V, i=i: V.tensor_tensor(
                ubuf.ap, dtb_f.ap, xbr[:, i * TC:(i + 1) * TC], ALU.mult),
                reads=(dtb_f, xbr), writes=(ubuf,), wide=True)
            for s in range(16):
                kb.eop("S", lambda S, i=i, s=s: S.activation(
                    ab_f.ap, dtb_f.ap, AF.Exp,
                    scale=nega[:, i * 16 + s:i * 16 + s + 1]),
                    reads=(dtb_f, nega), writes=(ab_f,), wide=True)
                kb.eop("V", lambda V, s=s: V.tensor_tensor(
                    bx_f.ap, ubuf.ap, brep[:, s * TC:(s + 1) * TC],
                    ALU.mult),
                    reads=(ubuf, brep), writes=(bx_f,), wide=True)
                col = None
                kb.eop("V", lambda V, i=i, s=s: V.tensor_tensor_scan(
                    hbuf.ap, ab_f.ap, bx_f.ap,
                    hcarry[:, i * 16 + s:i * 16 + s + 1], ALU.mult, ALU.add),
                    reads=(ab_f, bx_f, hcarry), writes=(hbuf,), wide=True)
                kb.eop("V", lambda V, i=i, s=s: V.tensor_copy(
                    hcarry[:, i * 16 + s:i * 16 + s + 1],
                    hbuf[:, TC - 1:TC]),
                    reads=(hbuf,), writes=(hcarry,))
                if i == 0 and s == 0:
                    dump("h00", hbuf, TC, ch)
                    dump("ab00", ab_f, TC, ch)
                    dump("bx00", bx_f, TC, ch)
                if s == 0:
                    kb.eop("V", lambda V, i=i, s=s: V.tensor_tensor(
                        ytmp.ap, hbuf.ap,
                        crep[:, s * TC:(s + 1) * TC], ALU.mult),
                        reads=(hbuf, crep), writes=(ytmp,), wide=True)
                else:
                    kb.eop("V", lambda V, i=i, s=s: V.tensor_tensor(
                        bx_f.h[:].bitcast(BF16)[:, 0:TC], hbuf.ap,
                        crep[:, s * TC:(s + 1) * TC], ALU.mult),
                        reads=(hbuf, crep), writes=(bx_f,), wide=True)
                    kb.eop("V", lambda V, i=i: V.tensor_tensor(
                        ytmp.ap, ytmp.ap,
                        bx_f.h[:].bitcast(BF16)[:, 0:TC], ALU.add),
                        reads=(bx_f, ytmp), writes=(ytmp,), wide=True)
            kb.eop("V", lambda V, i=i: V.scalar_tensor_tensor(
                ybuf[:, i * TC:(i + 1) * TC], xbr[:, i * TC:(i + 1) * TC],
                dpar_c[:, i:i + 1], ytmp.ap,
                ALU.mult, ALU.add),
                reads=(xbr, dpar_c, ytmp), writes=(ybuf,), wide=True)
        kb.wait("Y", "Y")
        for i in range(NDT):
            for th in range(2):
                kb.wr("Y", zbuf)
                kb.op("Y", lambda Y, i=i, th=th, t0=t0: Y.dma_start(
                    out=zbuf.ap,
                    in_=z_dram.ap()[i * 128:(i + 1) * 128,
                                    t0 + th * 512:t0 + (th + 1) * 512]), 16)
                kb.did_w("Y", zbuf)
                kb.eop("V", lambda V, i=i, th=th: V.tensor_tensor(
                    ybuf[:, i * TC + th * 512:i * TC + (th + 1) * 512],
                    ybuf[:, i * TC + th * 512:i * TC + (th + 1) * 512],
                    zbuf.ap, ALU.mult),
                    reads=(zbuf, ybuf), writes=(ybuf,), wide=True)

        banks = kb.psum
        for i in range(NDT):
            cbuf = cwsl[i % 2]
            kb.wr("G", cbuf)
            kb.op("G", lambda G, i=i, cbuf=cbuf: G.dma_start(
                out=cbuf.ap, in_=cw.ap()[i * 128:(i + 1) * 128, :]), 16)
            kb.did_w("G", cbuf)
            kb.rd("T", ybuf, cbuf)
            if i == 0:
                for b in banks:
                    kb.wr("T", b)
            def mmc(T, i=i, cbuf=cbuf):
                for tsub in range(4):
                    for fh in range(2):
                        ins = T.matmul(
                            banks[tsub * 2 + fh].h[:],
                            ybuf[:, i * TC + tsub * 128:
                                 i * TC + (tsub + 1) * 128],
                            cbuf[:, fh * 512:(fh + 1) * 512],
                            start=(i == 0), stop=(i == NDT - 1))
                return ins
            kb.op("T", mmc)
            kb.did_r("T", ybuf, cbuf)
            if i == NDT - 1:
                for b in banks:
                    kb.did_w("T", b)
        for tsub in range(4):
            tt = ch * 8 + tsub
            xres = load_x(kb, tt)
            for fh in range(2):
                bank = banks[tsub * 2 + fh]
                kb.rd("V", xres, bank)
                kb.op("V", lambda V, xres=xres, bank=bank, fh=fh:
                      V.tensor_tensor(xres[:, fh * 512:(fh + 1) * 512],
                                      bank.h[:],
                                      xres[:, fh * 512:(fh + 1) * 512],
                                      ALU.add))
                kb.did_r("V", bank)
            kb.did_w("V", xres, wide=True)
            store_x(kb, tt, xres)
        for i in range(NDT):
            cbuf = cwsl[i % 2]
            kb.wr("G", cbuf)
            kb.op("G", lambda G, i=i, cbuf=cbuf: G.dma_start(
                out=cbuf.ap, in_=cw.ap()[i * 128:(i + 1) * 128, :]), 16)
            kb.did_w("G", cbuf)
            kb.rd("T", ybuf, cbuf)
            if i == 0:
                for b in banks:
                    kb.wr("T", b)
            def mmc2(T, i=i, cbuf=cbuf):
                for tsub in range(4):
                    for fh in range(2):
                        ins = T.matmul(
                            banks[tsub * 2 + fh].h[:],
                            ybuf[:, i * TC + 512 + tsub * 128:
                                 i * TC + 512 + (tsub + 1) * 128],
                            cbuf[:, fh * 512:(fh + 1) * 512],
                            start=(i == 0), stop=(i == NDT - 1))
                return ins
            kb.op("T", mmc2)
            kb.did_r("T", ybuf, cbuf)
            if i == NDT - 1:
                for b in banks:
                    kb.did_w("T", b)
        for tsub in range(4):
            tt = ch * 8 + 4 + tsub
            xres = load_x(kb, tt)
            for fh in range(2):
                bank = banks[tsub * 2 + fh]
                kb.rd("V", xres, bank)
                kb.op("V", lambda V, xres=xres, bank=bank, fh=fh:
                      V.tensor_tensor(xres[:, fh * 512:(fh + 1) * 512],
                                      bank.h[:],
                                      xres[:, fh * 512:(fh + 1) * 512],
                                      ALU.add))
                kb.did_r("V", bank)
            kb.did_w("V", xres, wide=True)
            store_x(kb, tt, xres)



RD48 = 48


def emit_lv_partial(kb, load_x, out_dram, lnscr, stats):
    lv = kb.sb("p_lv", [128, NT + 2], F32)
    onescol = kb.sb("p_ones", [128, 1], F32)
    kb.op("V", lambda V: V.memset(onescol.ap, 1.0))
    kb.did_w("V", onescol)
    hs = kb.sb("p_hs", [128, D], BF16)
    for grp in range(NT // 8):
        xts = [load_x(kb, grp * 8 + g) for g in range(8)]
        for g in range(8):
            emit_ln_stats(kb, xts[g], lnscr, stats, g, xcols=(0, D))
        emit_ln_finish(kb, stats, 8)
        for g in range(8):
            tt = grp * 8 + g
            emit_ln_apply(kb, xts[g], hs, lnscr, stats, g, xcols=(0, D))
            emit_lv_col(kb, hs, lnscr, stats, lv, tt)
    kb.eop("V", lambda V: V.tensor_reduce(lv[:, NT:NT + 1], lv[:, 0:NT],
                                          AX.X, ALU.add),
           reads=(lv,), writes=(lv,))
    bank = kb.psum[0]
    kb.rd("T", lv, onescol); kb.wr("T", bank)
    kb.op("T", lambda T, bank=bank: T.matmul(
        bank.h[0:1, 0:1], lv.h[:, NT:NT + 1], onescol.h[:, 0:1],
        start=True, stop=True))
    kb.did_w("T", bank)
    kb.eop("V", lambda V, bank=bank: V.tensor_copy(lv[0:1, NT + 1:NT + 2],
                                                   bank.h[0:1, 0:1]),
           reads=(bank,), writes=(lv,))
    kb.did_r("V", bank)
    kb.rd("Y", lv)
    kb.op("Y", lambda Y: Y.dma_start(out=out_dram.ap()[0:1, 0:1],
                                     in_=lv[0:1, NT + 1:NT + 2]), 16)
    kb.did_r("Y", lv)


def emit_lv_col(kb, hs, scratch, stats, lv, tt):
    st = stats.h
    s1c, s2c, mc, vc = st[:, 40:41], st[:, 41:42], st[:, 42:43], st[:, 43:44]
    kb.eop("S", lambda S: S.activation(scratch.ap, hs.ap, AF.Square,
                                       accum_out=s2c),
           reads=(hs,), writes=(scratch, stats), wide=True)
    kb.eop("V", lambda V: V.reduce_sum(s1c, hs.ap, AX.X),
           reads=(hs, stats), writes=(stats,))
    kb.eop("V", lambda V: V.tensor_scalar(mc, s1c, 1.0 / D, None, ALU.mult),
           reads=(stats,), writes=(stats,))
    kb.eop("V", lambda V: V.tensor_scalar(vc, s2c, 1.0 / D, None, ALU.mult),
           reads=(stats,), writes=(stats,))
    kb.eop("V", lambda V: V.tensor_tensor(mc, mc, mc, ALU.mult),
           reads=(stats,), writes=(stats,))
    kb.eop("V", lambda V: V.tensor_tensor(vc, vc, mc, ALU.subtract),
           reads=(stats,), writes=(stats,))
    one_ap = kb.constcol(1.0)
    kb.eop("S", lambda S, tt=tt: S.activation(lv[:, tt:tt + 1], vc, AF.Ln,
                                              bias=one_ap),
           reads=(stats,), writes=(lv,))


def emit_l1ffn(kb, load_x, store_x, qw_d, rw_d, ak_d, av_d, rel_d, w1_d,
               b1_d, w2_d, b2gwgb_d, partials8_d, w_dram, scr2_d, ident,
               lnscr, stats):
    onesrow = kb.sb("l_ones", [1, 128], F32)
    onescol = kb.sb("l_onesc", [128, 1], F32)
    onesrow_b = kb.sb("l_onesb", [1, 128], BF16)
    kb.op("V", lambda V: V.memset(onesrow.ap, 1.0))
    kb.op("V", lambda V: V.memset(onescol.ap, 1.0))
    kb.op("V", lambda V: V.memset(onesrow_b.ap, 1.0))
    kb.did_w("V", onesrow, onescol, onesrow_b)

    stg = kb.sb("l_stg", [128, D], BF16)
    qwT = kb.sb("l_qwT", [128, 8 * 128], BF16)
    rwT = kb.sb("l_rwT", [128, RD48], BF16)
    akT = kb.sb("l_akT", [128, T_CORE], BF16)
    rkaugT = kb.sb("l_rkaug", [49, T_CORE], BF16)
    w1T = kb.sb("l_w1T", [128, 8 * 256], BF16)
    b1row = kb.sb("l_b1row", [1, 256], F32)
    w2rep = kb.sb("l_w2rep", [128, 256], BF16)
    c3 = kb.sb("l_c3", [128, 4], F32)
    par8 = kb.sb("l_par8", [1, 8], F32)
    sc8 = kb.sb("l_sc8", [1, 8], F32)
    meanrep = kb.sb("l_meanrep", [128, 1], F32)

    kb.op("G", lambda G: G.dma_start(out=stg[:, 0:D], in_=qw_d.ap()), 16)
    kb.did_w("G", stg)
    kb.rd("Y", stg)
    kb.op("Y", lambda Y: Y.dma_start(out=scr2_d.ap()[0:128, 0:D],
                                     in_=stg[:, 0:D]), 16)
    kb.did_r("Y", stg)
    kb.wr("G", stg)
    kb.op("G", lambda G: G.dma_start(out=stg[0:48, 0:128], in_=rw_d.ap()), 16)
    kb.did_w("G", stg)
    kb.rd("Y", stg)
    kb.op("Y", lambda Y: Y.dma_start(out=scr2_d.ap()[128:176, 0:128],
                                     in_=stg[0:48, 0:128]), 16)
    kb.did_r("Y", stg)
    for i in range(2):
        kb.wr("G", stg)
        kb.op("G", lambda G, i=i: G.dma_start(
            out=stg[:, 0:D], in_=w1_d.ap()[i * 128:(i + 1) * 128, :]), 16)
        kb.did_w("G", stg)
        kb.rd("Y", stg)
        kb.op("Y", lambda Y, i=i: Y.dma_start(
            out=scr2_d.ap()[256 + i * 128:256 + (i + 1) * 128, 0:D],
            in_=stg[:, 0:D]), 16)
        kb.did_r("Y", stg)
    for i in range(32):
        kb.wr("G", stg)
        kb.op("G", lambda G, i=i: G.dma_start(
            out=stg[:, 0:128], in_=ak_d.ap()[i * 128:(i + 1) * 128, :]), 16)
        kb.did_w("G", stg)
        kb.rd("Y", stg)
        kb.op("Y", lambda Y, i=i: Y.dma_start(
            out=scr2_d.ap()[1024 + i * 128:1024 + (i + 1) * 128, 0:128],
            in_=stg[:, 0:128]), 16)
        kb.did_r("Y", stg)
    kb.wait("Y", "Y")
    for c in range(8):
        kb.op("Y", lambda Y, c=c: Y.dma_start(
            out=qwT[:, c * 128:(c + 1) * 128],
            in_=scr2_d.ap()[0:128, c * 128:(c + 1) * 128], transpose=True), 16)
    kb.op("Y", lambda Y: Y.dma_start(
        out=rwT[:, 0:RD48], in_=scr2_d.ap()[128:176, 0:128],
        transpose=True), 16)
    for c in range(8):
        kb.op("Y", lambda Y, c=c: Y.dma_start(
            out=w1T[:, c * 256:(c + 1) * 256],
            in_=scr2_d.ap()[256:512, c * 128:(c + 1) * 128], transpose=True),
            16)
    kb.op("Y", lambda Y: Y.dma_start(
        out=akT.ap, in_=scr2_d.ap()[1024:5120, 0:128], transpose=True), 16)
    kb.did_w("Y", qwT, rwT, w1T, akT)
    kb.op("Y", lambda Y: Y.dma_start(out=b1row.ap, in_=b1_d.ap().unsqueeze(0)),
          16)
    kb.op("Y", lambda Y: Y.dma_start(out=par8.ap,
                                     in_=partials8_d.ap().unsqueeze(0)), 16)
    kb.did_w("Y", b1row, par8)
    w2row = kb.sb("l_w2row", [1, 256], F32)
    sc3row = kb.sb("l_sc3row", [1, 4], F32)
    kb.op("Y", lambda Y: Y.dma_start(out=w2row.ap, in_=w2_d.ap()), 16)
    kb.op("Y", lambda Y: Y.dma_start(out=sc3row[0:1, 0:3],
                                     in_=b2gwgb_d.ap().unsqueeze(0)), 16)
    kb.did_w("Y", w2row, sc3row)
    bank = kb.psum[0]
    kb.rd("T", w2row, onesrow); kb.wr("T", bank)
    kb.op("T", lambda T, bank=bank: T.matmul(bank.h[:, 0:256],
                                             onesrow.h[0:1, :], w2row.ap,
                                             start=True, stop=True))
    kb.did_w("T", bank)
    kb.rd("S", bank); kb.wr("S", w2rep)
    kb.op("S", lambda S, bank=bank: S.activation(w2rep.ap, bank.h[:, 0:256],
                                                 AF.Copy))
    kb.did_r("S", bank); kb.did_w("S", w2rep)
    bank = kb.psum[1]
    kb.rd("T", sc3row, onesrow); kb.wr("T", bank)
    kb.op("T", lambda T, bank=bank: T.matmul(bank.h[:, 0:4], onesrow.h[0:1, :],
                                             sc3row.ap, start=True, stop=True))
    kb.did_w("T", bank)
    kb.eop("V", lambda V, bank=bank: V.tensor_copy(c3.ap, bank.h[:, 0:4]),
           reads=(bank,), writes=(c3,))
    kb.did_r("V", bank)
    kb.eop("V", lambda V: V.tensor_reduce(sc8[0:1, 0:1], par8.ap, AX.X,
                                          ALU.add),
           reads=(par8,), writes=(sc8,))
    kb.eop("V", lambda V: V.tensor_scalar(sc8[0:1, 1:2], sc8[0:1, 0:1],
                                          1.0 / 32768.0, 1e-6, ALU.mult,
                                          ALU.add),
           reads=(sc8,), writes=(sc8,))
    kb.eop("V", lambda V: V.reciprocal(sc8[0:1, 2:3], sc8[0:1, 1:2]),
           reads=(sc8,), writes=(sc8,))
    bank = kb.psum[2]
    kb.rd("T", sc8, onesrow); kb.wr("T", bank)
    kb.op("T", lambda T, bank=bank: T.matmul(bank.h[:, 0:1], onesrow.h[0:1, :],
                                             sc8[0:1, 2:3], start=True,
                                             stop=True))
    kb.did_w("T", bank)
    kb.eop("V", lambda V, bank=bank: V.tensor_copy(meanrep.ap,
                                                   bank.h[:, 0:1]),
           reads=(bank,), writes=(meanrep,))
    kb.did_r("V", bank)
    kb.op("G", lambda G: G.dma_start(out=rkaugT.h[48:49, :],
                                     in_=rel_d.ap().unsqueeze(0)), 16)
    kb.did_w("G", rkaugT)
    for nch in range(8):
        bank = kb.psum[3]
        kb.rd("T", rwT, akT); kb.wr("T", bank)
        kb.op("T", lambda T, nch=nch, bank=bank: T.matmul(
            bank.h[0:RD48, :], rwT[:, 0:RD48],
            akT[:, nch * 512:(nch + 1) * 512], start=True, stop=True))
        kb.did_w("T", bank)
        kb.rd("S", bank); kb.wr("S", rkaugT)
        kb.op("S", lambda S, nch=nch, bank=bank: S.activation(
            rkaugT.h[0:48, nch * 512:(nch + 1) * 512], bank.h[0:RD48, :],
            AF.Copy))
        kb.did_r("S", bank)
    kb.did_w("S", rkaugT, wide=True)

    hs_full = kb.sb("l_hs", [128, NT * D], BF16)
    hsT = kb.sb("l_hsT", [128, 8 * 512], BF16)
    psT = Buf(None)
    psT.h = kb.psum[7].h[:].bitcast(BF16)
    psT.readers = kb.psum[7].readers
    hstile = kb.sb("l_hstile", [128, D], BF16)
    qT_s = kb.sb("l_qTs", [128, 512], BF16)
    rqaugT = kb.sb("l_rqaug", [49, 512], BF16)
    ones512b = kb.sb("l_ones512b", [1, 512], BF16)
    kb.op("V", lambda V: V.memset(ones512b.ap, 1.0))
    kb.did_w("V", ones512b)
    kb.rd("Y", ones512b)
    kb.op("Y", lambda Y: Y.dma_start(out=rqaugT.h[48:49, :],
                                     in_=ones512b.ap), 16)
    kb.did_r("Y", ones512b)
    kb.did_w("Y", rqaugT)
    Sbuf = kb.sb("l_S", [128, T_CORE], BF16)
    m8 = kb.sb("l_m8", [128, 8], F32)
    mwork = kb.sb("l_mwork", [128, T_CORE], BF16)
    wrow = kb.sb("l_wrow", [128, T_CORE], BF16)
    l_all = kb.sb("l_lall", [128, NT], F32)
    sig_all = kb.sb("l_sigall", [128, NT], F32)
    lv_all = kb.sb("l_lvall", [128, NT], F32)
    h1buf = kb.sb("l_h1", [128, 256], BF16)
    u1 = kb.sb("l_u1", [128, NT], F32)

    x3 = hsT.h[:].rearrange("p (c t) -> p c t", c=8)
    for sup in range(8):
        xts = [load_x(kb, sup * 4 + g) for g in range(4)]
        for g in range(4):
            emit_ln_stats(kb, xts[g], lnscr, stats, g, xcols=(0, D))
        emit_ln_finish(kb, stats, 4)
        for g in range(4):
            tt = sup * 4 + g
            emit_ln_apply(kb, xts[g], hstile, lnscr, stats, g, xcols=(0, D))
            emit_lv_col(kb, hstile, lnscr, stats, lv_all, tt)
            kb.eop("V", lambda V, tt=tt: V.tensor_copy(
                hs_full[:, tt * D:(tt + 1) * D], hstile.ap),
                reads=(hstile,), writes=(hs_full,), wide=True)
            kb.rd("T", hstile, ident); kb.wr("T", psT)
            def tr(T, hstile=hstile, psT=psT):
                for c in range(8):
                    ins = T.transpose(psT.h[:, c * 128:(c + 1) * 128],
                                      hstile[:, c * 128:(c + 1) * 128],
                                      ident.h[:])
                return ins
            kb.op("T", tr)
            kb.did_r("T", hstile); kb.did_w("T", psT)
            kb.rd("V", psT); kb.wr("V", hsT)
            kb.op("V", lambda V, g=g, x3=x3: V.tensor_copy(
                x3[:, :, g * 128:(g + 1) * 128],
                psT.h[:].rearrange("p (c t) -> p c t", c=8)))
            kb.did_r("V", psT)
        kb.did_w("V", hsT, wide=True)
        bank = kb.psum[0]
        kb.rd("T", qwT, hsT); kb.wr("T", bank)
        def mmq(T, bank=bank, x3=x3):
            for c in range(8):
                ins = T.matmul(bank.h[:], qwT[:, c * 128:(c + 1) * 128],
                               x3[:, c, :], start=(c == 0), stop=(c == 7))
            return ins
        kb.op("T", mmq)
        kb.did_w("T", bank); kb.did_r("T", hsT)
        kb.rd("S", bank); kb.wr("S", qT_s)
        kb.op("S", lambda S, bank=bank: S.activation(qT_s.ap, bank.h[:],
                                                     AF.Copy))
        kb.did_r("S", bank); kb.did_w("S", qT_s)
        bank = kb.psum[1]
        kb.rd("T", rwT, qT_s); kb.wr("T", bank)
        kb.op("T", lambda T, bank=bank: T.matmul(
            bank.h[0:RD48, :], rwT[:, 0:RD48], qT_s.ap, start=True,
            stop=True))
        kb.did_w("T", bank)
        kb.rd("S", bank); kb.wr("S", rqaugT)
        kb.op("S", lambda S, bank=bank: S.activation(
            rqaugT.h[0:48, :], bank.h[0:RD48, :], AF.Copy,
            scale=1.0 / float(np.sqrt(48.0))))
        kb.did_r("S", bank); kb.did_w("S", rqaugT)
        for g in range(4):
            tt = sup * 4 + g
            bank = kb.psum[2]
            kb.rd("T", w1T, hsT, onesrow, b1row); kb.wr("T", bank)
            def mmh1(T, g=g, bank=bank, x3=x3):
                for c in range(8):
                    T.matmul(bank.h[:, 0:256],
                             x3[:, c, g * 128:(g + 1) * 128],
                             w1T[:, c * 256:(c + 1) * 256],
                             start=(c == 0), stop=False)
                return T.matmul(bank.h[:, 0:256], onesrow.h[0:1, :],
                                b1row.ap, start=False, stop=True)
            kb.op("T", mmh1)
            kb.did_w("T", bank)
            kb.rd("S", bank); kb.wr("S", h1buf)
            kb.op("S", lambda S, bank=bank: S.activation(
                h1buf.ap, bank.h[:, 0:256], AF.Gelu))
            kb.did_r("S", bank); kb.did_w("S", h1buf)
            kb.eop("V", lambda V: V.tensor_tensor(h1buf.ap, h1buf.ap,
                                                  w2rep.ap, ALU.mult),
                   reads=(h1buf, w2rep), writes=(h1buf,), wide=True)
            kb.eop("V", lambda V, tt=tt: V.reduce_sum(u1[:, tt:tt + 1],
                                                      h1buf.ap, AX.X),
                   reads=(h1buf, u1), writes=(u1,))
        for g in range(4):
            tt = sup * 4 + g
            for nch in range(8):
                bank = kb.psum[3 + nch % 2]
                kb.rd("T", rqaugT, rkaugT); kb.wr("T", bank)
                kb.op("T", lambda T, g=g, nch=nch, bank=bank: T.matmul(
                    bank.h[:], rqaugT.h[0:49, g * 128:(g + 1) * 128],
                    rkaugT.h[0:49, nch * 512:(nch + 1) * 512],
                    start=True, stop=True))
                kb.did_w("T", bank)
                kb.rd("S", bank); kb.wr("S", Sbuf)
                kb.op("S", lambda S, nch=nch, bank=bank: S.activation(
                    Sbuf[:, nch * 512:(nch + 1) * 512], bank.h[:], AF.Copy))
                kb.did_r("S", bank)
            kb.did_w("S", Sbuf, wide=True)
            kb.eop("V", lambda V: V.max(m8.ap, Sbuf.ap),
                   reads=(Sbuf,), writes=(m8,))
            kb.eop("V", lambda V: V.tensor_scalar(
                mwork.ap, Sbuf.ap, m8[:, 7:8], 30000.0, ALU.is_ge, ALU.mult),
                reads=(Sbuf, m8), writes=(mwork,), wide=True)
            kb.eop("V", lambda V: V.scalar_tensor_tensor(
                mwork.ap, Sbuf.ap, -30000.0, mwork.ap, ALU.add, ALU.add),
                reads=(Sbuf, mwork), writes=(mwork,), wide=True)
            kb.eop("V", lambda V: V.tensor_scalar(m8[:, 1:2], m8[:, 0:1],
                                                  -1.0, None, ALU.mult),
                   reads=(m8,), writes=(m8,))
            kb.rd("S", mwork, m8); kb.wr("S", wrow)
            kb.op("S", lambda S, tt=tt: S.activation(
                wrow.ap, mwork.ap, AF.Exp, bias=m8[:, 1:2],
                accum_out=l_all[:, tt:tt + 1]))
            kb.did_r("S", mwork); kb.did_w("S", wrow, wide=True)
            kb.did_w("S", l_all)
            kb.rd("Y", wrow)
            kb.op("Y", lambda Y, tt=tt: Y.dma_start(
                out=w_dram.ap()[tt * 128:(tt + 1) * 128, :], in_=wrow.ap), 16)
            kb.did_r("Y", wrow)

    kb.eop("V", lambda V: V.tensor_single_scalar(u1.ap, u1.ap, c3[:, 0:1],
                                                 ALU.add),
           reads=(u1, c3), writes=(u1,), wide=True)
    kb.eop("S", lambda S: S.activation(sig_all.ap, u1.ap, AF.Sigmoid),
           reads=(u1,), writes=(sig_all,), wide=True)
    kb.eop("V", lambda V: V.tensor_single_scalar(lv_all.ap, lv_all.ap,
                                                 meanrep[:, 0:1], ALU.mult),
           reads=(lv_all, meanrep), writes=(lv_all,), wide=True)
    kb.eop("V", lambda V: V.tensor_scalar(lv_all.ap, lv_all.ap, 0.5, None,
                                          ALU.mult),
           reads=(lv_all,), writes=(lv_all,))
    kb.eop("V", lambda V: V.scalar_tensor_tensor(
        sig_all.ap, sig_all.ap, 2.5, lv_all.ap, ALU.mult, ALU.add),
        reads=(sig_all, lv_all), writes=(sig_all,))
    kb.eop("V", lambda V: V.tensor_scalar(sig_all.ap, sig_all.ap, 0.0, 5.0,
                                          ALU.max, ALU.min),
           reads=(sig_all,), writes=(sig_all,))
    kb.eop("V", lambda V: V.tensor_single_scalar(sig_all.ap, sig_all.ap,
                                                 c3[:, 1:2], ALU.mult),
           reads=(sig_all, c3), writes=(sig_all,))
    kb.eop("V", lambda V: V.tensor_single_scalar(sig_all.ap, sig_all.ap,
                                                 c3[:, 2:3], ALU.add),
           reads=(sig_all, c3), writes=(sig_all,))
    kb.eop("S", lambda S: S.activation(sig_all.ap, sig_all.ap, AF.Sigmoid),
           reads=(sig_all,), writes=(sig_all,), wide=True)
    kb.eop("V", lambda V: V.reciprocal(l_all.ap, l_all.ap),
           reads=(l_all,), writes=(l_all,), wide=True)
    kb.eop("V", lambda V: V.tensor_tensor(sig_all.ap, sig_all.ap, l_all.ap,
                                          ALU.mult),
           reads=(sig_all, l_all), writes=(sig_all,))
    kb.eop("V", lambda V: V.tensor_scalar(sig_all.ap, sig_all.ap, 0.5, None,
                                          ALU.mult),
           reads=(sig_all,), writes=(sig_all,), wide=True)

    kb.wait("Y", "Y")
    wTs = [kb.sb(f"l_wT{i}", [128, 512], BF16) for i in range(2)]
    avb = [kb.sb(f"l_av{i}", [128, D], BF16) for i in range(2)]
    banks = kb.psum
    for sup in range(8):
        for n in range(32):
            wbuf = wTs[n % 2]
            abuf = avb[n % 2]
            kb.wr("Y", wbuf)
            kb.op("Y", lambda Y, sup=sup, n=n, wbuf=wbuf: Y.dma_start(
                out=wbuf.ap,
                in_=w_dram.ap()[sup * 512:(sup + 1) * 512,
                                n * 128:(n + 1) * 128], transpose=True), 16)
            kb.did_w("Y", wbuf)
            kb.wr("G", abuf)
            kb.op("G", lambda G, n=n, abuf=abuf: G.dma_start(
                out=abuf.ap, in_=av_d.ap()[n * 128:(n + 1) * 128, :]), 16)
            kb.did_w("G", abuf)
            kb.rd("T", wbuf, abuf)
            if n == 0:
                for b in banks:
                    kb.wr("T", b)
            def mma(T, n=n, wbuf=wbuf, abuf=abuf):
                for qt in range(4):
                    for hh in range(2):
                        ins = T.matmul(
                            banks[qt * 2 + hh].h[:],
                            wbuf[:, qt * 128:(qt + 1) * 128],
                            abuf[:, hh * 512:(hh + 1) * 512],
                            start=(n == 0), stop=(n == 31))
                return ins
            kb.op("T", mma)
            kb.did_r("T", wbuf, abuf)
            if n == 31:
                for b in banks:
                    kb.did_w("T", b)
        for qt in range(4):
            tt = sup * 4 + qt
            xres = load_x(kb, tt)
            for hh in range(2):
                bank = banks[qt * 2 + hh]
                kb.rd("V", xres, bank, sig_all, hs_full)
                kb.op("V", lambda V, xres=xres, bank=bank, hh=hh, tt=tt:
                      V.scalar_tensor_tensor(
                          xres[:, hh * 512:(hh + 1) * 512], bank.h[:],
                          sig_all[:, tt:tt + 1],
                          xres[:, hh * 512:(hh + 1) * 512],
                          ALU.mult, ALU.add))
                kb.op("V", lambda V, xres=xres, hh=hh, tt=tt:
                      V.scalar_tensor_tensor(
                          xres[:, hh * 512:(hh + 1) * 512],
                          hs_full[:, tt * D + hh * 512:
                                  tt * D + (hh + 1) * 512],
                          0.5, xres[:, hh * 512:(hh + 1) * 512],
                          ALU.mult, ALU.add))
                kb.did_r("V", bank, hs_full)
            kb.did_w("V", xres, wide=True)
            store_x(kb, tt, xres)



def build_l0ffn_launch():
    kb = KB()
    nc = kb.nc
    x_in = nc.dram_tensor("x_in", [T_CORE, D], F32, kind="ExternalInput")
    w_gate = nc.dram_tensor("w_gate", [I_FFN, D], F32, kind="ExternalInput")
    w_up = nc.dram_tensor("w_up", [I_FFN, D], F32, kind="ExternalInput")
    w_down = nc.dram_tensor("w_down", [D, I_FFN], F32, kind="ExternalInput")
    id_in = nc.dram_tensor("ident_in", [128, 128], BF16, kind="ExternalInput")
    x_out = nc.dram_tensor("x_out", [T_CORE, D], F32, kind="ExternalOutput")
    scr_q = nc.dram_tensor("ffn_scr", [6656, I_FFN], BF16)
    scr_guT = nc.dram_tensor("ffn_guT", [8, 128, 5632], BF16)
    scr_dnT = nc.dram_tensor("ffn_dnT", [NI, 128, D], BF16)
    import os
    _dbg = os.environ.get("FFN_DBG") == "1"
    hq_dram = nc.dram_tensor("hq_scr", [T_CORE, I_FFN], BF16,
                             kind="ExternalOutput" if _dbg else "Internal")

    ident = kb.sb("ident", [128, 128], BF16)
    kb.op("Y", lambda Y: Y.dma_start(out=ident.h[:], in_=id_in.ap()), 16)
    kb.did_w("Y", ident)

    qsc = quant_scratch(kb, "sh", I_FFN)
    big_f32 = kb.sb("big_f32", [128, I_FFN], F32)
    big_bf_a = kb.sb("big_bf_a", [128, I_FFN], BF16)
    big_bf_b = kb.sb("big_bf_b", [128, I_FFN], BF16)

    emit_l0ffn_weight_prep(kb, w_gate, w_up, w_down, scr_q, scr_guT, scr_dnT,
                           qsc, big_f32, big_bf_a, big_bf_b)

    xtiles = [kb.sb(f"xtile{i}", [128, D], F32) for i in range(9)]
    xslot = [0]

    def load_x(kb, tt):
        buf = xtiles[xslot[0] % len(xtiles)]
        xslot[0] += 1
        kb.wr("Y", buf)
        kb.op("Y", lambda Y, tt=tt, buf=buf: Y.dma_start(
            out=buf[:, 0:D], in_=x_in.ap()[tt * 128:(tt + 1) * 128, :]), 16)
        kb.did_w("Y", buf)
        return buf

    def store_x(kb, tt, buf):
        kb.rd("Y", buf)
        kb.op("Y", lambda Y, tt=tt, buf=buf: Y.dma_start(
            out=x_out.ap()[tt * 128:(tt + 1) * 128, :], in_=buf[:, 0:D]), 16)
        kb.did_r("Y", buf)

    emit_l0ffn(kb, load_x, store_x, scr_guT, scr_dnT, hq_dram, ident,
               qsc, big_f32, big_bf_a, big_bf_b)
    return kb.finalize()


def build_l0attn_launch():
    kb = KB()
    nc = kb.nc
    x_in = nc.dram_tensor("x_in", [T_CORE, D], F32, kind="ExternalInput")
    w_qkv = nc.dram_tensor("w_qkv", [D, 3072], F32, kind="ExternalInput")
    b_qkv = nc.dram_tensor("b_qkv", [3072], F32, kind="ExternalInput")
    w_out = nc.dram_tensor("w_out", [D, D], F32, kind="ExternalInput")
    b_out = nc.dram_tensor("b_out", [D], F32, kind="ExternalInput")
    id_in = nc.dram_tensor("ident_in", [128, 128], BF16, kind="ExternalInput")
    mask_in = nc.dram_tensor("mask_in", [4, 128, 512], BF16,
                             kind="ExternalInput")
    x_out = nc.dram_tensor("x_out", [T_CORE, D], F32, kind="ExternalOutput")
    qkT_dram = nc.dram_tensor("qkT_scr", [2048, T_CORE], BF16)
    v_dram = nc.dram_tensor("v_scr", [T_CORE, D], BF16)
    ctxT_dram = nc.dram_tensor("ctxT_scr", [D, T_CORE], BF16)

    ident = kb.sb("ident", [128, 128], BF16)
    maskb = kb.sb("maskb", [128, 4 * 512], BF16)
    kb.op("Y", lambda Y: Y.dma_start(out=ident.h[:], in_=id_in.ap()), 16)
    for a in range(4):
        kb.op("Y", lambda Y, a=a: Y.dma_start(
            out=maskb.h[:, a * 512:(a + 1) * 512], in_=mask_in.ap()[a]), 16)
    kb.did_w("Y", ident, maskb)
    mask4 = [maskb.h[:, a * 512:(a + 1) * 512] for a in range(4)]
    lnscr = kb.sb("lnscr", [128, D], BF16)
    stats = kb.sb("stats", [128, 48], F32)

    xtiles = [kb.sb(f"xtile{i}", [128, D], F32) for i in range(5)]
    xslot = [0]

    def load_x(kb, tt):
        buf = xtiles[xslot[0] % len(xtiles)]
        xslot[0] += 1
        kb.wr("Y", buf)
        kb.op("Y", lambda Y, tt=tt, buf=buf: Y.dma_start(
            out=buf[:, 0:D], in_=x_in.ap()[tt * 128:(tt + 1) * 128, :]), 16)
        kb.did_w("Y", buf)
        return buf

    def store_x(kb, tt, buf):
        kb.rd("Y", buf)
        kb.op("Y", lambda Y, tt=tt, buf=buf: Y.dma_start(
            out=x_out.ap()[tt * 128:(tt + 1) * 128, :], in_=buf[:, 0:D]), 16)
        kb.did_r("Y", buf)

    emit_l0attn(kb, load_x, store_x, w_qkv, b_qkv, w_out, b_out,
                qkT_dram, v_dram, ctxT_dram, ident, mask4, lnscr, stats)
    return kb.finalize()


def make_masks():
    m = np.zeros((4, 128, 512), np.float32)
    for a in range(4):
        for k in range(128):
            m[a, k, :a * 128 + k] = -1e9
    return m.astype(ml_dtypes.bfloat16)


def run_l0attn(x_full, w_qkv, b_qkv, w_out, b_out):
    nc = build_l0attn_launch()
    masks = make_masks()
    in_maps = [{"x_in": np.ascontiguousarray(x_full[c]), "w_qkv": w_qkv,
                "b_qkv": b_qkv, "w_out": w_out, "b_out": b_out,
                "ident_in": IDENT_BF16, "mask_in": masks} for c in range(8)]
    res = run_bass_kernel_spmd(nc, in_maps, core_ids=list(range(8)))
    return np.stack([res.results[c]["x_out"] for c in range(8)])


def build_mamba_launch():
    kb = KB()
    nc = kb.nc
    x_in = nc.dram_tensor("x_in", [T_CORE, D], F32, kind="ExternalInput")
    win = nc.dram_tensor("in_proj_w", [D, 2 * DIN], F32, kind="ExternalInput")
    conv_w = nc.dram_tensor("conv_w", [DIN, 1, 4], F32, kind="ExternalInput")
    conv_b = nc.dram_tensor("conv_b", [DIN], F32, kind="ExternalInput")
    xproj = nc.dram_tensor("x_proj_w", [DIN, 33], F32, kind="ExternalInput")
    dtw = nc.dram_tensor("dt_proj_w", [1, DIN], F32, kind="ExternalInput")
    dtb = nc.dram_tensor("dt_proj_b", [DIN], F32, kind="ExternalInput")
    alog = nc.dram_tensor("A_log", [DIN, 16], F32, kind="ExternalInput")
    dpar = nc.dram_tensor("D_param", [DIN], F32, kind="ExternalInput")
    cw = nc.dram_tensor("c_proj_w", [DIN, D], F32, kind="ExternalInput")
    id_in = nc.dram_tensor("ident_in", [128, 128], BF16, kind="ExternalInput")
    x_out = nc.dram_tensor("x_out", [T_CORE, D], F32, kind="ExternalOutput")
    z_dram = nc.dram_tensor("z_scr", [DIN, T_CORE], BF16)

    ident = kb.sb("ident", [128, 128], BF16)
    kb.op("Y", lambda Y: Y.dma_start(out=ident.h[:], in_=id_in.ap()), 16)
    kb.did_w("Y", ident)
    lnscr = kb.sb("lnscr", [128, D], BF16)
    stats = kb.sb("stats", [128, 48], F32)
    xtiles = [kb.sb(f"xtile{i}", [128, D], F32) for i in range(9)]
    xslot = [0]

    def load_x(kb, tt):
        buf = xtiles[xslot[0] % len(xtiles)]
        xslot[0] += 1
        kb.wr("Y", buf)
        kb.op("Y", lambda Y, tt=tt, buf=buf: Y.dma_start(
            out=buf[:, 0:D], in_=x_in.ap()[tt * 128:(tt + 1) * 128, :]), 16)
        kb.did_w("Y", buf)
        return buf

    def store_x(kb, tt, buf):
        kb.rd("Y", buf)
        kb.op("Y", lambda Y, tt=tt, buf=buf: Y.dma_start(
            out=x_out.ap()[tt * 128:(tt + 1) * 128, :], in_=buf[:, 0:D]), 16)
        kb.did_r("Y", buf)

    emit_l1mamba(kb, load_x, store_x, win, conv_w, conv_b, xproj, dtw, dtb,
                 alog, dpar, cw, z_dram, ident, lnscr, stats)
    return kb.finalize()


def run_mamba(x_full, z):
    nc = build_mamba_launch()
    in_maps = [{"x_in": np.ascontiguousarray(x_full[c]),
                "in_proj_w": z["L1_attn_in_proj_w"],
                "conv_w": z["L1_attn_conv_w"], "conv_b": z["L1_attn_conv_b"],
                "x_proj_w": z["L1_attn_x_proj_w"],
                "dt_proj_w": z["L1_attn_dt_proj_w"],
                "dt_proj_b": z["L1_attn_dt_proj_b"],
                "A_log": z["L1_attn_A_log"], "D_param": z["L1_attn_D_param"],
                "c_proj_w": z["L1_attn_c_proj_w"],
                "ident_in": IDENT_BF16} for c in range(8)]
    res = run_bass_kernel_spmd(nc, in_maps, core_ids=list(range(8)))
    return np.stack([res.results[c]["x_out"] for c in range(8)])


def _mk_loadstore(kb, x_in, x_out, ntiles=9):
    xtiles = [kb.sb(f"xtile{i}", [128, D], F32) for i in range(ntiles)]
    xslot = [0]

    def load_x(kb, tt):
        buf = xtiles[xslot[0] % len(xtiles)]
        xslot[0] += 1
        kb.wr("Y", buf)
        kb.op("Y", lambda Y, tt=tt, buf=buf: Y.dma_start(
            out=buf[:, 0:D], in_=x_in.ap()[tt * 128:(tt + 1) * 128, :]), 16)
        kb.did_w("Y", buf)
        return buf

    def store_x(kb, tt, buf):
        kb.rd("Y", buf)
        kb.op("Y", lambda Y, tt=tt, buf=buf: Y.dma_start(
            out=x_out.ap()[tt * 128:(tt + 1) * 128, :], in_=buf[:, 0:D]), 16)
        kb.did_r("Y", buf)
    return load_x, store_x


def build_lvpart_launch():
    kb = KB()
    nc = kb.nc
    x_in = nc.dram_tensor("x_in", [T_CORE, D], F32, kind="ExternalInput")
    out = nc.dram_tensor("lv_out", [1, 1], F32, kind="ExternalOutput")
    lnscr = kb.sb("lnscr", [128, D], BF16)
    stats = kb.sb("stats", [128, 48], F32)
    load_x, _ = _mk_loadstore(kb, x_in, None)
    emit_lv_partial(kb, load_x, out, lnscr, stats)
    return kb.finalize()


def build_l1ffn_launch():
    kb = KB()
    nc = kb.nc
    x_in = nc.dram_tensor("x_in", [T_CORE, D], F32, kind="ExternalInput")
    qw_d = nc.dram_tensor("q_proj_w", [128, D], F32, kind="ExternalInput")
    rw_d = nc.dram_tensor("router_w", [48, 128], F32, kind="ExternalInput")
    ak_d = nc.dram_tensor("aux_keys", [4096, 128], F32, kind="ExternalInput")
    av_d = nc.dram_tensor("aux_values", [4096, D], F32, kind="ExternalInput")
    rel_d = nc.dram_tensor("rel", [4096], F32, kind="ExternalInput")
    w1_d = nc.dram_tensor("unc_w1", [256, D], F32, kind="ExternalInput")
    b1_d = nc.dram_tensor("unc_b1", [256], F32, kind="ExternalInput")
    w2_d = nc.dram_tensor("unc_w2", [1, 256], F32, kind="ExternalInput")
    b2gwgb_d = nc.dram_tensor("b2gwgb", [3], F32, kind="ExternalInput")
    partials8_d = nc.dram_tensor("partials8", [8], F32, kind="ExternalInput")
    id_in = nc.dram_tensor("ident_in", [128, 128], BF16, kind="ExternalInput")
    x_out = nc.dram_tensor("x_out", [T_CORE, D], F32, kind="ExternalOutput")
    w_dram = nc.dram_tensor("w_scr", [T_CORE, 4096], BF16)
    scr2_d = nc.dram_tensor("scr2", [5120, D], BF16)

    ident = kb.sb("ident", [128, 128], BF16)
    kb.op("Y", lambda Y: Y.dma_start(out=ident.h[:], in_=id_in.ap()), 16)
    kb.did_w("Y", ident)
    lnscr = kb.sb("lnscr", [128, D], BF16)
    stats = kb.sb("stats", [128, 48], F32)
    load_x, store_x = _mk_loadstore(kb, x_in, x_out)
    emit_l1ffn(kb, load_x, store_x, qw_d, rw_d, ak_d, av_d, rel_d, w1_d,
               b1_d, w2_d, b2gwgb_d, partials8_d, w_dram, scr2_d, ident,
               lnscr, stats)
    return kb.finalize()


def run_l1ffn(x_full, z):
    ncA = build_lvpart_launch()
    in_mapsA = [{"x_in": np.ascontiguousarray(x_full[c])} for c in range(8)]
    resA = run_bass_kernel_spmd(ncA, in_mapsA, core_ids=list(range(8)))
    partials = np.concatenate([resA.results[c]["lv_out"].reshape(1)
                               for c in range(8)])
    ncB = build_l1ffn_launch()
    b2gwgb = np.stack([z["L1_ffn_unc_b2"].reshape(()),
                       z["L1_ffn_gate_w1"].reshape(()),
                       z["L1_ffn_gate_bias"].reshape(())]).astype(np.float32)
    in_mapsB = [{"x_in": np.ascontiguousarray(x_full[c]),
                 "q_proj_w": z["L1_ffn_q_proj_w"],
                 "router_w": z["L1_ffn_router_w"],
                 "aux_keys": z["L1_ffn_aux_keys"],
                 "aux_values": z["L1_ffn_aux_values"],
                 "rel": z["L1_ffn_reliability_mask"],
                 "unc_w1": z["L1_ffn_unc_w1"], "unc_b1": z["L1_ffn_unc_b1"],
                 "unc_w2": z["L1_ffn_unc_w2"], "b2gwgb": b2gwgb,
                 "partials8": partials, "ident_in": IDENT_BF16}
                for c in range(8)]
    resB = run_bass_kernel_spmd(ncB, in_mapsB, core_ids=list(range(8)))
    return np.stack([resB.results[c]["x_out"] for c in range(8)])


IDENT_BF16 = np.eye(128, dtype=ml_dtypes.bfloat16)


def run_l0ffn(x_full, w_gate, w_up, w_down):
    nc = build_l0ffn_launch()
    in_maps = [{"x_in": np.ascontiguousarray(x_full[c]),
                "w_gate": w_gate, "w_up": w_up, "w_down": w_down,
                "ident_in": IDENT_BF16} for c in range(8)]
    res = run_bass_kernel_spmd(nc, in_maps, core_ids=list(range(8)))
    return np.stack([res.results[c]["x_out"] for c in range(8)])


def kernel(**inputs):
    z = inputs
    x = np.ascontiguousarray(np.asarray(z["L0_attn_x"], np.float32))
    masks = make_masks()
    b2gwgb = np.stack([np.asarray(z["L1_ffn_unc_b2"]).reshape(-1)[0],
                       np.asarray(z["L1_ffn_gate_w1"]).reshape(-1)[0],
                       np.asarray(z["L1_ffn_gate_bias"]).reshape(-1)[0]]
                      ).astype(np.float32)

    def ffn(xc):
        nc = build_l0ffn_launch()
        im = [{"x_in": xc[c], "w_gate": z["L0_ffn_w_gate"],
               "w_up": z["L0_ffn_w_up"], "w_down": z["L0_ffn_w_down"],
               "ident_in": IDENT_BF16} for c in range(8)]
        r = run_bass_kernel_spmd(nc, im, core_ids=list(range(8)))
        return np.stack([r.results[c]["x_out"] for c in range(8)])

    def attn(xc):
        nc = build_l0attn_launch()
        im = [{"x_in": xc[c], "w_qkv": z["L0_attn_w_qkv"],
               "b_qkv": z["L0_attn_b_qkv"], "w_out": z["L0_attn_w_out"],
               "b_out": z["L0_attn_b_out"], "ident_in": IDENT_BF16,
               "mask_in": masks} for c in range(8)]
        r = run_bass_kernel_spmd(nc, im, core_ids=list(range(8)))
        return np.stack([r.results[c]["x_out"] for c in range(8)])

    def lvpart(xc):
        nc = build_lvpart_launch()
        im = [{"x_in": xc[c]} for c in range(8)]
        r = run_bass_kernel_spmd(nc, im, core_ids=list(range(8)))
        return np.concatenate([r.results[c]["lv_out"].reshape(1)
                               for c in range(8)]).astype(np.float32)

    def l1ffn(xc, partials):
        nc = build_l1ffn_launch()
        im = [{"x_in": xc[c], "q_proj_w": z["L1_ffn_q_proj_w"],
               "router_w": z["L1_ffn_router_w"],
               "aux_keys": z["L1_ffn_aux_keys"],
               "aux_values": z["L1_ffn_aux_values"],
               "rel": z["L1_ffn_reliability_mask"],
               "unc_w1": z["L1_ffn_unc_w1"], "unc_b1": z["L1_ffn_unc_b1"],
               "unc_w2": z["L1_ffn_unc_w2"], "b2gwgb": b2gwgb,
               "partials8": partials, "ident_in": IDENT_BF16}
              for c in range(8)]
        r = run_bass_kernel_spmd(nc, im, core_ids=list(range(8)))
        return np.stack([r.results[c]["x_out"] for c in range(8)])

    def mamba(xc):
        nc = build_mamba_launch()
        im = [{"x_in": xc[c], "in_proj_w": z["L1_attn_in_proj_w"],
               "conv_w": z["L1_attn_conv_w"], "conv_b": z["L1_attn_conv_b"],
               "x_proj_w": z["L1_attn_x_proj_w"],
               "dt_proj_w": z["L1_attn_dt_proj_w"],
               "dt_proj_b": z["L1_attn_dt_proj_b"],
               "A_log": z["L1_attn_A_log"], "D_param": z["L1_attn_D_param"],
               "c_proj_w": z["L1_attn_c_proj_w"], "ident_in": IDENT_BF16}
              for c in range(8)]
        r = run_bass_kernel_spmd(nc, im, core_ids=list(range(8)))
        return np.stack([r.results[c]["x_out"] for c in range(8)])

    x = ffn(x)
    x = attn(x)
    x = ffn(x)
    p = lvpart(x)
    x = l1ffn(x, p)
    x = mamba(x)
    p = lvpart(x)
    x = l1ffn(x, p)
    return np.ascontiguousarray(x.astype(np.float32))


# version 06
# speedup vs baseline: 1.0104x; 1.0104x over previous
import numpy as np
import ml_dtypes
import concourse.bass as bass
import concourse.mybir as mybir
from concourse.bass_utils import run_bass_kernel_spmd

F32 = mybir.dt.float32
BF16 = mybir.dt.bfloat16
I16 = mybir.dt.int16
I32 = mybir.dt.int32
AF = mybir.ActivationFunctionType
ALU = mybir.AluOpType
AX = mybir.AxisListType

T_CORE = 4096
D = 1024
NT = T_CORE // 128
LN_EPS = 1e-5
I_FFN = 2816
NI = I_FFN // 128

_SEC = {"T": "tensor", "V": "vector", "S": "scalar", "Y": "sync", "G": "gpsimd"}


class Buf:
    def __init__(self, h):
        self.h = h
        self.prod = None
        self.readers = {}

    def __getitem__(self, idx):
        return self.h[idx]

    @property
    def ap(self):
        try:
            return self.h[:]
        except Exception:
            return self.h


class KB:
    def __init__(self):
        nc = bass.Bass("TRN2", target_bir_lowering=False, debug=False)
        self.nc = nc
        self.sem = {k: nc.alloc_semaphore("sem_" + k) for k in _SEC}
        self.cnt = {k: 0 for k in _SEC}
        self.waited = {c: {p: 0 for p in _SEC} for c in _SEC}
        self.prog = {k: [] for k in _SEC}
        self.psum = [Buf(nc.alloc_psum_tensor(f"ps{i}", [128, 512], F32))
                     for i in range(8)]
        self._uid = 0
        self._consts = {}
        self._const_tile = nc.alloc_sbuf_tensor("kb_consts", [128, 16], F32)

    def constcol(self, value):
        if value not in self._consts:
            idx = len(self._consts)
            assert idx < 16
            ap = self._const_tile[:, idx:idx + 1]
            self.op("V", lambda V, ap=ap, value=value: V.memset(ap, value))
            self.wait("S", "V")
            self._consts[value] = ap
        return self._consts[value]

    def raw(self, e, fn):
        self.prog[e].append(fn)

    def op(self, e, fn, inc=1):
        if inc:
            sem = self.sem[e]
            self.prog[e].append(lambda eng, fn=fn, sem=sem, inc=inc:
                                fn(eng).then_inc(sem, inc))
            self.cnt[e] += inc
        else:
            self.prog[e].append(fn)
        return self.cnt[e]

    def wait(self, consumer, producer, val=None):
        if val is None:
            val = self.cnt[producer]
        if val <= 0 or self.waited[consumer][producer] >= val:
            return
        sem = self.sem[producer]
        self.prog[consumer].append(
            lambda eng, sem=sem, val=val: eng.wait_ge(sem, val))
        self.waited[consumer][producer] = val

    def rd(self, e, *bufs):
        for b in bufs:
            if b.prod is not None:
                if b.prod[0] != e or not b.prod[2] or e in ("Y", "G"):
                    self.wait(e, b.prod[0], b.prod[1])

    def wr(self, e, *bufs):
        for b in bufs:
            if b.prod is not None and b.prod[0] != e:
                self.wait(e, b.prod[0], b.prod[1])
            for re, rc in b.readers.items():
                if re != e:
                    self.wait(e, re, rc)

    def did_w(self, e, *bufs, wide=False):
        for b in bufs:
            b.prod = (e, self.cnt[e], wide)
            b.readers = {}

    def did_r(self, e, *bufs):
        for b in bufs:
            b.readers[e] = self.cnt[e]

    def eop(self, e, fn, reads=(), writes=(), wide=False, inc=1):
        for b in reads:
            self.rd(e, b)
        self.wr(e, *writes)
        self.op(e, fn, inc)
        self.did_w(e, *writes, wide=wide)
        for b in reads:
            self.did_r(e, b)

    def arena_init(self, kbytes):
        self._pool = self.nc.alloc_sbuf_tensor("kb_pool",
                                               [128, kbytes * 1024 // 4], F32)
        self._pool_off = 0
        self._arena_on = False

    def arena_begin(self):
        self._arena_on = True
        self._pool_off = 0

    def arena_end(self):
        self._arena_on = False

    def barrier(self):
        for c in _SEC:
            for p in _SEC:
                self.wait(c, p)

    def sb(self, name, shape, dtype=F32):
        if getattr(self, "_arena_on", False):
            esz = 2 if dtype in (BF16, I16) else 4
            ncols = int(np.prod(shape[1:]))
            nbytes = (ncols * esz + 63) // 64 * 64
            off = self._pool_off
            self._pool_off = off + nbytes
            maxb = self._pool.shape[1] * 4
            assert self._pool_off <= maxb, (
                f"arena overflow at {name}: {self._pool_off} > {maxb}")
            assert shape[0] <= 128 and len(shape) == 2
            view = self._pool[:].bitcast(dtype)[
                0:shape[0], off // esz: off // esz + ncols]
            b = Buf(None)
            b.h = view
            return b
        return Buf(self.nc.alloc_sbuf_tensor(name, shape, dtype))

    def psum_t(self, name, shape, dtype=F32):
        return Buf(self.nc.alloc_psum_tensor(name, shape, dtype))

    def dram(self, name, shape, dtype=F32, kind="Internal"):
        return self.nc.dram_tensor(name, shape, dtype, kind=kind)

    def finalize(self):
        for e in _SEC:
            v = self.cnt[e]
            if v > 0:
                sem = self.sem[e]
                self.prog[e].append(
                    lambda eng, sem=sem, v=v: eng.wait_ge(sem, v))
        with self.nc.Block() as blk:
            for e, sec in _SEC.items():
                thunks = self.prog[e]
                if not thunks:
                    continue
                getattr(blk, sec)(lambda eng, thunks=thunks:
                                  [t(eng) for t in thunks])
        return self.nc



def emit_ln_stats(kb, x, scratch, stats, col, xcols=None):
    x_ap = x.ap if xcols is None else x[:, xcols[0]:xcols[1]]
    st = stats.h
    kb.eop("S", lambda S, x_ap=x_ap, col=col: S.activation(
        scratch.ap, x_ap, AF.Square, accum_out=st[:, 8 + col:9 + col]),
        reads=(x,), writes=(scratch, stats), wide=True)
    kb.eop("V", lambda V, x_ap=x_ap, col=col: V.reduce_sum(
        st[:, col:col + 1], x_ap, AX.X), reads=(x,), writes=(stats,))


def emit_ln_finish(kb, stats, n):
    st = stats.h
    s1, s2 = st[:, 0:n], st[:, 8:8 + n]
    m, v = st[:, 16:16 + n], st[:, 24:24 + n]
    r, t = st[:, 32:32 + n], st[:, 40:40 + n]
    eps_ap = kb.constcol(LN_EPS)
    kb.eop("V", lambda V: V.tensor_scalar(m, s1, 1.0 / D, None, ALU.mult),
           reads=(stats,), writes=(stats,))
    kb.eop("V", lambda V: V.tensor_scalar(t, s2, 1.0 / D, None, ALU.mult),
           reads=(stats,), writes=(stats,))
    kb.eop("V", lambda V: V.tensor_tensor(v, m, m, ALU.mult),
           reads=(stats,), writes=(stats,))
    kb.eop("V", lambda V: V.tensor_tensor(v, t, v, ALU.subtract),
           reads=(stats,), writes=(stats,))
    kb.eop("S", lambda S: S.activation(t, v, AF.Ln, bias=eps_ap),
           reads=(stats,), writes=(stats,))
    kb.eop("S", lambda S: S.activation(r, t, AF.Exp, scale=-0.5),
           reads=(stats,), writes=(stats,))


def emit_ln_apply(kb, x, out, scratch, stats, col, xcols=None):
    x_ap = x.ap if xcols is None else x[:, xcols[0]:xcols[1]]
    st = stats.h
    kb.eop("V", lambda V, x_ap=x_ap, col=col: V.tensor_single_scalar(
        scratch.ap, x_ap, st[:, 16 + col:17 + col], ALU.subtract),
        reads=(x, stats), writes=(scratch,), wide=True)
    kb.eop("V", lambda V, col=col: V.tensor_single_scalar(
        out.ap, scratch.ap, st[:, 32 + col:33 + col], ALU.mult),
        reads=(scratch, stats), writes=(out,), wide=True)


def quant_scratch(kb, tag, F):
    return {
        "mx": kb.sb(f"q_mx_{tag}", [128, F // 16], F32),
        "rec": kb.sb(f"q_rec_{tag}", [128, F // 16], F32),
        "ab": kb.sb(f"q_ab_{tag}", [128, F], BF16),
        "v": kb.sb(f"q_v_{tag}", [128, F], BF16),
        "lo": kb.sb(f"q_lo_{tag}", [128, F], BF16),
        "sg": kb.sb(f"q_sg_{tag}", [128, F], BF16),
    }


def emit_fp4_quant(kb, x, out, F, sc):
    B = F // 16
    P = 128
    W = F >= 2048
    x_ap = x[:, 0:F]
    x3 = x_ap.rearrange("p (b k) -> p b k", k=16)
    mx, rec = sc["mx"], sc["rec"]
    ab, v, lo, sg = sc["ab"], sc["v"], sc["lo"], sc["sg"]
    kb.eop("S", lambda S: S.activation(sg[:, 0:F], x_ap, AF.Sign),
           reads=(x,), writes=(sg,), wide=True)
    kb.eop("V", lambda V: V.tensor_reduce(mx[:, 0:B], x3, AX.X, ALU.max,
                                          apply_absolute_value=True),
           reads=(x,), writes=(mx,))
    kb.eop("V", lambda V: V.tensor_scalar(mx[:, 0:B], mx[:, 0:B], 6e-12,
                                          None, ALU.max),
           reads=(mx,), writes=(mx,))
    kb.eop("V", lambda V: V.reciprocal(rec[:, 0:B], mx[:, 0:B]),
           reads=(mx,), writes=(rec,))
    kb.eop("V", lambda V: V.tensor_scalar(ab[:, 0:F].bitcast(I16),
                                          x_ap.bitcast(I16), 0x7FFF, None,
                                          ALU.bitwise_and),
           reads=(x,), writes=(ab,), wide=W)
    rec_b = rec[:, 0:B].unsqueeze(2).to_broadcast((P, B, 16))
    mx_b = mx[:, 0:B].unsqueeze(2).to_broadcast((P, B, 16))
    kb.eop("V", lambda V: V.scalar_tensor_tensor(
        v[:, 0:F].rearrange("p (b k) -> p b k", k=16),
        ab[:, 0:F].rearrange("p (b k) -> p b k", k=16),
        6.0, rec_b, ALU.mult, ALU.mult),
        reads=(ab, rec), writes=(v,), wide=W)
    kb.eop("V", lambda V: V.tensor_scalar(lo[:, 0:F], v[:, 0:F], 1.0, None,
                                          ALU.is_lt),
           reads=(v,), writes=(lo,), wide=W)
    kb.eop("V", lambda V: V.tensor_tensor(v[:, 0:F], v[:, 0:F], lo[:, 0:F],
                                          ALU.add),
           reads=(v, lo), writes=(v,), wide=W)
    kb.eop("V", lambda V: V.tensor_scalar(v[:, 0:F].bitcast(I16),
                                          v[:, 0:F].bitcast(I16), 0x20, None,
                                          ALU.add),
           reads=(v,), writes=(v,), wide=W)
    kb.eop("V", lambda V: V.tensor_scalar(v[:, 0:F].bitcast(I16),
                                          v[:, 0:F].bitcast(I16), -0x40,
                                          None, ALU.bitwise_and),
           reads=(v,), writes=(v,), wide=W)
    kb.eop("V", lambda V: V.tensor_tensor(v[:, 0:F], v[:, 0:F], lo[:, 0:F],
                                          ALU.subtract),
           reads=(v, lo), writes=(v,), wide=W)
    kb.eop("V", lambda V: V.scalar_tensor_tensor(
        v[:, 0:F].rearrange("p (b k) -> p b k", k=16),
        v[:, 0:F].rearrange("p (b k) -> p b k", k=16),
        1.0 / 6.0, mx_b, ALU.mult, ALU.mult),
        reads=(v, mx), writes=(v,), wide=W)
    kb.eop("V", lambda V: V.tensor_tensor(out[:, 0:F], v[:, 0:F],
                                          sg[:, 0:F], ALU.mult),
        reads=(v, sg), writes=(out,), wide=W)



def emit_l0ffn_weight_prep(kb, w_gate, w_up, w_down, scr_q, scr_guT, scr_dnT,
                           qsc, big_f32, big_bf_a, big_bf_b):
    wt, wtb, wqt = big_f32, big_bf_a, big_bf_b
    for w, row0, ncols, nrows in [(w_gate, 0, D, NI), (w_up, I_FFN, D, NI),
                                  (w_down, 5632, I_FFN, 8)]:
        for i in range(nrows):
            kb.wr("Y", wt)
            kb.op("Y", lambda Y, w=w, i=i, nc_=ncols: Y.dma_start(
                out=wt[:, 0:nc_], in_=w.ap()[i * 128:(i + 1) * 128, :]), 16)
            kb.did_w("Y", wt)
            kb.rd("V", wt); kb.wr("V", wtb)
            kb.op("V", lambda V, nc_=ncols: V.tensor_copy(wtb[:, 0:nc_],
                                                          wt[:, 0:nc_]))
            kb.did_r("V", wt); kb.did_w("V", wtb)
            emit_fp4_quant(kb, wtb, wqt, ncols, qsc)
            kb.rd("Y", wqt)
            kb.op("Y", lambda Y, row0=row0, i=i, nc_=ncols: Y.dma_start(
                out=scr_q.ap()[row0 + i * 128: row0 + (i + 1) * 128, 0:nc_],
                in_=wqt[:, 0:nc_]), 16)
            kb.did_r("Y", wqt)
    kb.wait("Y", "Y")
    stage = big_bf_a
    for c in range(8):
        for half, r0 in [(0, 0), (1, I_FFN)]:
            kb.wr("Y", stage)
            kb.op("Y", lambda Y, c=c, r0=r0: Y.dma_start(
                out=stage[:, 0:I_FFN],
                in_=scr_q.ap()[r0:r0 + I_FFN, c * 128:(c + 1) * 128],
                transpose=True), 16)
            kb.did_w("Y", stage)
            kb.rd("Y", stage)
            kb.op("Y", lambda Y, c=c, half=half: Y.dma_start(
                out=scr_guT.ap()[c, :, half * I_FFN:(half + 1) * I_FFN],
                in_=stage[:, 0:I_FFN]), 16)
            kb.did_r("Y", stage)
    for i in range(NI):
        kb.wr("Y", stage)
        kb.op("Y", lambda Y, i=i: Y.dma_start(
            out=stage[:, 0:D],
            in_=scr_q.ap()[5632:6656, i * 128:(i + 1) * 128],
            transpose=True), 16)
        kb.did_w("Y", stage)
        kb.rd("Y", stage)
        kb.op("Y", lambda Y, i=i: Y.dma_start(
            out=scr_dnT.ap()[i, :, :], in_=stage[:, 0:D]), 16)
        kb.did_r("Y", stage)


FPLAN = [(0, 512), (512, 512), (1024, 512), (1536, 512), (2048, 512),
         (2560, 256)]


def emit_l0ffn(kb, load_x, store_x, scr_guT, scr_dnT, hq_dram, ident,
               qsc, big_f32, big_bf_a, big_bf_b, grp_tiles=8):
    GRP = grp_tiles
    xln = kb.sb("f_xln", [128, D], BF16)
    lnscr = kb.sb("f_lnscr", [128, D], BF16)
    stats = kb.sb("f_stats", [128, 48], F32)
    xq = kb.sb("f_xq", [128, D], BF16)
    xqT = [kb.sb(f"f_xqT{g}", [128, 8 * 128], BF16) for g in range(GRP)]
    psT = Buf(None)
    psT.h = kb.psum[7].h[:].bitcast(BF16)
    psT.readers = kb.psum[7].readers
    hbufs = [kb.sb(f"f_hb{g}", [128, I_FFN], BF16) for g in range(GRP)]
    silu_t = kb.sb("f_silu", [128, 512], BF16)
    hq = big_bf_b
    wgu = [kb.sb(f"f_wgu{b}", [128, 16 * 512], BF16) for b in range(2)]
    WCOLS = 16 * 512

    kb.wait("Y", "Y")
    for grp in range(NT // GRP):
        xts = [load_x(kb, grp * GRP + g) for g in range(GRP)]
        for g in range(GRP):
            emit_ln_stats(kb, xts[g], lnscr, stats, g, xcols=(0, D))
        emit_ln_finish(kb, stats, GRP)
        for g in range(GRP):
            emit_ln_apply(kb, xts[g], xln, lnscr, stats, g, xcols=(0, D))
            emit_fp4_quant(kb, xln, xq, D, qsc)
            kb.rd("T", xq, ident); kb.wr("T", psT)
            def tr(T, xq=xq, psT=psT):
                for c in range(8):
                    ins = T.transpose(psT.h[:, c * 128:(c + 1) * 128],
                                      xq[:, c * 128:(c + 1) * 128],
                                      ident.h[:])
                return ins
            kb.op("T", tr)
            kb.did_r("T", xq); kb.did_w("T", psT)
            kb.rd("V", psT); kb.wr("V", xqT[g])
            kb.op("V", lambda V, g=g: V.tensor_copy(xqT[g].h[:], psT.h))
            kb.did_r("V", psT); kb.did_w("V", xqT[g])
        for fi, (f0, fw) in enumerate(FPLAN):
            wbuf = wgu[fi % 2]
            kb.wr("Y", wbuf)
            for h in range(2):
                kb.op("Y", lambda Y, wbuf=wbuf, f0=f0, fw=fw, h=h:
                      Y.dma_start(
                    out=wbuf.h[:, 0:16 * fw].rearrange(
                        "p (a f) -> p a f", f=fw)[:, h::2, :],
                    in_=scr_guT.ap()[:, :, h * I_FFN + f0:
                                     h * I_FFN + f0 + fw].rearrange(
                        "c p x -> p c x")), 16)
            kb.did_w("Y", wbuf)
            for g in range(GRP):
                bg, bu = kb.psum[g % 2 * 2], kb.psum[g % 2 * 2 + 1]
                kb.rd("T", xqT[g], wbuf); kb.wr("T", bg, bu)
                def mm(T, g=g, bg=bg, bu=bu, wbuf=wbuf, fw=fw):
                    for c in range(8):
                        T.matmul(bg[:, 0:fw],
                                 xqT[g][:, c * 128:(c + 1) * 128],
                                 wbuf[:, (2 * c) * fw:(2 * c + 1) * fw],
                                 start=(c == 0), stop=(c == 7))
                    for c in range(8):
                        ins = T.matmul(
                            bu[:, 0:fw], xqT[g][:, c * 128:(c + 1) * 128],
                            wbuf[:, (2 * c + 1) * fw:(2 * c + 2) * fw],
                            start=(c == 0), stop=(c == 7))
                    return ins
                kb.op("T", mm)
                kb.did_r("T", xqT[g], wbuf); kb.did_w("T", bg, bu)
                kb.rd("S", bg); kb.wr("S", silu_t)
                kb.op("S", lambda S, bg=bg, fw=fw: S.activation(
                    silu_t[:, 0:fw], bg[:, 0:fw], AF.Silu))
                kb.did_r("S", bg); kb.did_w("S", silu_t)
                kb.rd("V", bu, silu_t); kb.wr("V", hbufs[g])
                kb.op("V", lambda V, g=g, bu=bu, f0=f0, fw=fw:
                      V.tensor_tensor(hbufs[g][:, f0:f0 + fw], bu[:, 0:fw],
                                      silu_t[:, 0:fw], ALU.mult))
                kb.did_r("V", bu, silu_t)
                if fi == len(FPLAN) - 1:
                    kb.did_w("V", hbufs[g])
        for g in range(GRP):
            tt = grp * GRP + g
            emit_fp4_quant(kb, hbufs[g], hq, I_FFN, qsc)
            kb.rd("Y", hq)
            kb.op("Y", lambda Y, tt=tt: Y.dma_start(
                out=hq_dram.ap()[tt * 128:(tt + 1) * 128, :],
                in_=hq[:, 0:I_FFN]), 16)
            kb.did_r("Y", hq)

    kb.wait("Y", "Y")
    kb.psum[7].prod = psT.prod
    hqT = [kb.sb(f"f_hqT{b}", [128, 512], BF16) for b in range(2)]
    wdn = [kb.sb(f"f_wdn{b}", [128, D], BF16) for b in range(2)]
    for sup in range(8):
        t0 = sup * 512
        banks = kb.psum
        for i in range(NI):
            buf = hqT[i % 2]
            wbuf = wdn[i % 2]
            kb.wr("Y", buf, wbuf)
            kb.op("Y", lambda Y, buf=buf, t0=t0, i=i: Y.dma_start(
                out=buf.h[:],
                in_=hq_dram.ap()[t0:t0 + 512, i * 128:(i + 1) * 128],
                transpose=True), 16)
            kb.op("Y", lambda Y, wbuf=wbuf, i=i: Y.dma_start(
                out=wbuf.h[:], in_=scr_dnT.ap()[i, :, :]), 16)
            kb.did_w("Y", buf, wbuf)
            kb.rd("T", buf, wbuf)
            if i == 0:
                for b in banks:
                    kb.wr("T", b)
            def mmd(T, buf=buf, wbuf=wbuf, i=i, banks=banks):
                for ti in range(4):
                    for fh in range(2):
                        ins = T.matmul(
                            banks[ti * 2 + fh].h[:],
                            buf[:, ti * 128:(ti + 1) * 128],
                            wbuf[:, fh * 512:fh * 512 + 512],
                            start=(i == 0), stop=(i == NI - 1))
                return ins
            kb.op("T", mmd)
            kb.did_r("T", buf, wbuf)
            if i == NI - 1:
                for b in banks:
                    kb.did_w("T", b)
        for ti in range(4):
            tt = sup * 4 + ti
            xres = load_x(kb, tt)
            for fh in range(2):
                bank = banks[ti * 2 + fh]
                kb.rd("V", xres, bank)
                kb.op("V", lambda V, xres=xres, bank=bank, fh=fh:
                      V.scalar_tensor_tensor(
                          xres[:, fh * 512:fh * 512 + 512], bank.h[:], 0.5,
                          xres[:, fh * 512:fh * 512 + 512], ALU.mult,
                          ALU.add))
                kb.did_r("V", bank)
            kb.did_w("V", xres, wide=True)
            store_x(kb, tt, xres)



HD = 64
NHEAD = 16


def emit_l0attn(kb, load_x, store_x, w_qkv, b_qkv, w_out, b_out,
                qkT_dram, v_dram, ctxT_dram, ident, mask4, lnscr, stats):
    onesrow = kb.sb("a_ones", [1, 128], F32)
    onescol = kb.sb("a_onesc", [128, 1], F32)
    kb.op("V", lambda V: V.memset(onesrow.ap, 1.0))
    kb.op("V", lambda V: V.memset(onescol.ap, 1.0))
    kb.did_w("V", onesrow, onescol)
    wqkv_bf = kb.sb("a_wqkv", [128, 8 * 3072], BF16)
    wout_bf = kb.sb("a_wout", [128, 8 * 1024], BF16)
    bq_col = kb.sb("a_bqcol", [128, 24], F32)
    brow = kb.sb("a_brow", [1, 3072], F32)
    borow = kb.sb("a_borow", [1, 1024], F32)
    for c in range(8):
        kb.op("G", lambda G, c=c: G.dma_start(
            out=wqkv_bf[:, c * 3072:(c + 1) * 3072],
            in_=w_qkv.ap()[c * 128:(c + 1) * 128, :]), 16)
        kb.op("G", lambda G, c=c: G.dma_start(
            out=wout_bf[:, c * 1024:(c + 1) * 1024],
            in_=w_out.ap()[c * 128:(c + 1) * 128, :]), 16)
    kb.op("Y", lambda Y: Y.dma_start(
        out=bq_col.ap, in_=b_qkv.ap().rearrange("(a p) -> p a", p=128),
        allow_slow_non_contiguous=True), 16)
    kb.op("Y", lambda Y: Y.dma_start(
        out=brow.ap, in_=b_qkv.ap().unsqueeze(0)), 16)
    kb.op("Y", lambda Y: Y.dma_start(
        out=borow.ap, in_=b_out.ap().unsqueeze(0)), 16)
    kb.did_w("G", wqkv_bf, wout_bf)
    kb.did_w("Y", bq_col, brow, borow)

    xln = kb.sb("a_xln", [128, D], BF16)
    xlnT = kb.sb("a_xlnT", [128, 8 * 512], BF16)
    psT = Buf(None)
    psT.h = kb.psum[7].h[:].bitcast(BF16)
    psT.readers = kb.psum[7].readers
    qkbuf = kb.sb("a_qkbuf", [128, 512], BF16)
    vbuf = kb.sb("a_vbuf", [128, 512], BF16)

    for sup in range(8):
        xts = [load_x(kb, sup * 4 + g) for g in range(4)]
        for g in range(4):
            emit_ln_stats(kb, xts[g], lnscr, stats, g, xcols=(0, D))
        emit_ln_finish(kb, stats, 4)
        x3 = xlnT.h[:].rearrange("p (c t) -> p c t", c=8)
        for g in range(4):
            emit_ln_apply(kb, xts[g], xln, lnscr, stats, g, xcols=(0, D))
            kb.rd("T", xln, ident); kb.wr("T", psT)
            def tr(T, xln=xln, psT=psT):
                for c in range(8):
                    ins = T.transpose(psT.h[:, c * 128:(c + 1) * 128],
                                      xln[:, c * 128:(c + 1) * 128],
                                      ident.h[:])
                return ins
            kb.op("T", tr)
            kb.did_r("T", xln); kb.did_w("T", psT)
            kb.rd("V", psT); kb.wr("V", xlnT)
            kb.op("V", lambda V, g=g, x3=x3: V.tensor_copy(
                x3[:, :, g * 128:(g + 1) * 128],
                psT.h[:].rearrange("p (c t) -> p c t", c=8)))
            kb.did_r("V", psT)
        kb.did_w("V", xlnT, wide=True)
        for ft in range(16):
            bank = kb.psum[ft % 2]
            kb.rd("T", xlnT, wqkv_bf); kb.wr("T", bank)
            def mmqk(T, ft=ft, bank=bank, x3=x3):
                for c in range(8):
                    ins = T.matmul(bank.h[:],
                                   wqkv_bf[:, c * 3072 + ft * 128:
                                           c * 3072 + (ft + 1) * 128],
                                   x3[:, c, :], start=(c == 0), stop=(c == 7))
                return ins
            kb.op("T", mmqk)
            kb.did_w("T", bank)
            kb.rd("V", bank, bq_col); kb.wr("V", qkbuf)
            kb.op("V", lambda V, ft=ft, bank=bank: V.tensor_single_scalar(
                qkbuf.ap, bank.h[:], bq_col[:, ft:ft + 1], ALU.add))
            kb.did_r("V", bank); kb.did_w("V", qkbuf)
            kb.rd("Y", qkbuf)
            kb.op("Y", lambda Y, ft=ft, sup=sup: Y.dma_start(
                out=qkT_dram.ap()[ft * 128:(ft + 1) * 128,
                                  sup * 512:(sup + 1) * 512],
                in_=qkbuf.ap), 16)
            kb.did_r("Y", qkbuf)
        for g in range(4):
            for fh in range(2):
                bank = kb.psum[2 + (g * 2 + fh) % 2]
                kb.rd("T", xlnT, wqkv_bf, onesrow, brow); kb.wr("T", bank)
                def mmv(T, g=g, fh=fh, bank=bank, x3=x3):
                    for c in range(8):
                        T.matmul(bank.h[:],
                                 x3[:, c, g * 128:(g + 1) * 128],
                                 wqkv_bf[:, c * 3072 + 2048 + fh * 512:
                                         c * 3072 + 2048 + (fh + 1) * 512],
                                 start=(c == 0), stop=False)
                    return T.matmul(bank.h[:], onesrow.h[0:1, :],
                                    brow.h[0:1, 2048 + fh * 512:
                                           2048 + (fh + 1) * 512],
                                    start=False, stop=True)
                kb.op("T", mmv)
                kb.did_w("T", bank)
                kb.rd("V", bank); kb.wr("V", vbuf)
                kb.op("V", lambda V, bank=bank: V.tensor_copy(vbuf.ap,
                                                              bank.h[:]))
                kb.did_r("V", bank); kb.did_w("V", vbuf)
                kb.rd("Y", vbuf)
                kb.op("Y", lambda Y, sup=sup, g=g, fh=fh: Y.dma_start(
                    out=v_dram.ap()[sup * 512 + g * 128:
                                    sup * 512 + (g + 1) * 128,
                                    fh * 512:(fh + 1) * 512],
                    in_=vbuf.ap), 16)
                kb.did_r("Y", vbuf)

    kb.wait("Y", "Y")

    kT = kb.sb("a_kT", [64, T_CORE], BF16)
    qT = kb.sb("a_qT", [64, T_CORE], BF16)
    vb3 = kb.sb("a_vb3", [128, 32 * 72], BF16)
    pT = [kb.sb(f"a_pT{i}", [128, 512], BF16) for i in range(2)]
    md = kb.sb("a_md", [64, 2], F32)
    sc1 = kb.sb("a_sc1", [1, 8], F32)
    bias_col = kb.sb("a_biascol", [128, 1], F32)
    linv_row = kb.sb("a_linv", [1, 512], F32)
    linv_rep = kb.sb("a_linvrep", [64, 512], F32)
    ctx_bf = kb.sb("a_ctx", [64, 512], BF16)
    v3 = vb3.h[:].rearrange("p (c w) -> p c w", w=72)
    kb.op("V", lambda V: V.memset(vb3.ap, 0.0))
    kb.op("V", lambda V, v3=v3: V.memset(v3[:, :, 64:65], 1.0))
    kb.did_w("V", vb3)

    for h in range(NHEAD):
        kb.wr("Y", kT, qT, vb3)
        kb.op("Y", lambda Y, h=h: Y.dma_start(
            out=kT.ap,
            in_=qkT_dram.ap()[1024 + h * 64:1024 + (h + 1) * 64, :]), 16)
        kb.op("Y", lambda Y, h=h: Y.dma_start(
            out=qT.ap, in_=qkT_dram.ap()[h * 64:(h + 1) * 64, :]), 16)
        kb.op("Y", lambda Y, h=h, v3=v3: Y.dma_start(
            out=v3[:, :, 0:64],
            in_=v_dram.ap()[:, h * 64:(h + 1) * 64].rearrange(
                "(c p) d -> p c d", p=128)), 16)
        kb.did_w("Y", kT, qT, vb3)
        kb.eop("V", lambda V: V.tensor_reduce(
            md[:, 0:1], kT.ap, AX.X, ALU.max, apply_absolute_value=True),
            reads=(kT,), writes=(md,))
        kb.eop("V", lambda V: V.tensor_reduce(
            md[:, 1:2], qT.ap, AX.X, ALU.max, apply_absolute_value=True),
            reads=(md, qT), writes=(md,))
        kb.eop("V", lambda V: V.tensor_tensor(md.ap, md.ap, md.ap, ALU.mult),
               reads=(md,), writes=(md,))
        bank = kb.psum[6]
        kb.rd("T", md, onescol); kb.wr("T", bank)
        kb.op("T", lambda T, bank=bank: T.matmul(
            bank.h[0:1, 0:1], md.h[0:64, 0:1], onescol.h[0:64, 0:1],
            start=True, stop=True))
        kb.op("T", lambda T, bank=bank: T.matmul(
            bank.h[0:1, 1:2], md.h[0:64, 1:2], onescol.h[0:64, 0:1],
            start=False, stop=True))
        kb.did_w("T", bank)
        kb.eop("V", lambda V, bank=bank: V.tensor_copy(sc1[0:1, 0:2],
                                                       bank.h[0:1, 0:2]),
               reads=(bank,), writes=(sc1,))
        kb.did_r("V", bank)
        kb.eop("V", lambda V: V.tensor_tensor(sc1[0:1, 2:3], sc1[0:1, 0:1],
                                              sc1[0:1, 1:2], ALU.mult),
               reads=(sc1,), writes=(sc1,))
        kb.eop("S", lambda S: S.activation(sc1[0:1, 3:4], sc1[0:1, 2:3],
                                           AF.Ln), reads=(sc1,),
               writes=(sc1,))
        kb.eop("S", lambda S: S.activation(sc1[0:1, 4:5], sc1[0:1, 3:4],
                                           AF.Exp, scale=0.5),
               reads=(sc1,), writes=(sc1,))
        kb.eop("V", lambda V: V.tensor_scalar(sc1[0:1, 5:6], sc1[0:1, 4:5],
                                              -0.125, None, ALU.mult),
               reads=(sc1,), writes=(sc1,))
        rep_bank = kb.psum[5]
        kb.rd("T", sc1, onesrow); kb.wr("T", rep_bank)
        kb.op("T", lambda T, rep_bank=rep_bank: T.matmul(
            rep_bank.h[:, 0:1], onesrow.h[0:1, :], sc1.h[0:1, 5:6],
            start=True, stop=True))
        kb.did_w("T", rep_bank)
        kb.eop("V", lambda V, rep_bank=rep_bank: V.tensor_copy(
            bias_col.ap, rep_bank.h[:, 0:1]),
            reads=(rep_bank,), writes=(bias_col,))
        kb.did_r("V", rep_bank)

        for qc in range(8):
            nkc = 4 * (qc + 1)
            pv_bank = kb.psum[4]
            kb.wr("T", pv_bank)
            for kc in range(nkc):
                sb_ = kb.psum[kc % 2]
                kb.rd("T", kT, qT, ident); kb.wr("T", sb_)
                diag_a = kc * 128 - qc * 512
                def mmsc(T, kc=kc, qc=qc, sb_=sb_, diag_a=diag_a):
                    ins = T.matmul(sb_.h[:], kT[:, kc * 128:(kc + 1) * 128],
                                   qT[:, qc * 512:(qc + 1) * 512],
                                   start=True, stop=(diag_a < 0))
                    if diag_a >= 0:
                        ins = T.matmul(sb_.h[:], ident.h[:],
                                       mask4[diag_a // 128],
                                       start=False, stop=True)
                    return ins
                kb.op("T", mmsc)
                kb.did_w("T", sb_)
                pbuf = pT[kc % 2]
                kb.rd("S", sb_, bias_col); kb.wr("S", pbuf)
                kb.op("S", lambda S, sb_=sb_, pbuf=pbuf: S.activation(
                    pbuf.ap, sb_.h[:], AF.Exp, bias=bias_col.ap,
                    scale=0.125))
                kb.did_r("S", sb_); kb.did_w("S", pbuf, wide=True)
                kb.rd("T", vb3, pbuf)
                def mmpv(T, kc=kc, pbuf=pbuf, pv_bank=pv_bank, nkc=nkc,
                         v3=v3):
                    return T.matmul(pv_bank.h[0:65, :], v3[:, kc, 0:65],
                                    pbuf.ap, start=(kc == 0),
                                    stop=(kc == nkc - 1))
                kb.op("T", mmpv)
                kb.did_r("T", pbuf)
            kb.did_w("T", pv_bank)
            kb.eop("V", lambda V, pv_bank=pv_bank: V.reciprocal(
                linv_row.ap, pv_bank.h[64:65, :]),
                reads=(pv_bank,), writes=(linv_row,))
            rep2 = kb.psum[5]
            kb.rd("T", linv_row, onesrow); kb.wr("T", rep2)
            kb.op("T", lambda T, rep2=rep2: T.matmul(
                rep2.h[0:64, :], onesrow.h[0:1, 0:64], linv_row.ap,
                start=True, stop=True))
            kb.did_w("T", rep2)
            kb.rd("S", rep2); kb.wr("S", linv_rep)
            kb.op("S", lambda S, rep2=rep2: S.activation(
                linv_rep.ap, rep2.h[0:64, :], AF.Copy))
            kb.did_r("S", rep2); kb.did_w("S", linv_rep)
            kb.rd("V", pv_bank, linv_rep); kb.wr("V", ctx_bf)
            kb.op("V", lambda V, pv_bank=pv_bank: V.tensor_tensor(
                ctx_bf.ap, pv_bank.h[0:64, :], linv_rep.ap, ALU.mult))
            kb.did_r("V", pv_bank); kb.did_w("V", ctx_bf)
            kb.rd("Y", ctx_bf)
            kb.op("Y", lambda Y, h=h, qc=qc: Y.dma_start(
                out=ctxT_dram.ap()[h * 64:(h + 1) * 64,
                                   qc * 512:(qc + 1) * 512],
                in_=ctx_bf.ap), 16)
            kb.did_r("Y", ctx_bf)

    kb.wait("Y", "Y")

    ctxc = [kb.sb(f"a_ctxc{i}", [128, 512], BF16) for i in range(2)]
    banks = kb.psum
    for sup in range(8):
        for c in range(8):
            buf = ctxc[c % 2]
            kb.wr("Y", buf)
            kb.op("Y", lambda Y, c=c, sup=sup, buf=buf: Y.dma_start(
                out=buf.ap,
                in_=ctxT_dram.ap()[c * 128:(c + 1) * 128,
                                   sup * 512:(sup + 1) * 512]), 16)
            kb.did_w("Y", buf)
            kb.rd("T", buf, wout_bf, onesrow, borow)
            if c == 0:
                for b in banks:
                    kb.wr("T", b)
            def mmo(T, c=c, buf=buf):
                for ti in range(4):
                    for fh in range(2):
                        ins = T.matmul(
                            banks[ti * 2 + fh].h[:],
                            buf[:, ti * 128:(ti + 1) * 128],
                            wout_bf[:, c * 1024 + fh * 512:
                                    c * 1024 + (fh + 1) * 512],
                            start=(c == 0), stop=False)
                if c == 7:
                    for ti in range(4):
                        for fh in range(2):
                            ins = T.matmul(
                                banks[ti * 2 + fh].h[:], onesrow.h[0:1, :],
                                borow.h[0:1, fh * 512:(fh + 1) * 512],
                                start=False, stop=True)
                return ins
            kb.op("T", mmo)
            kb.did_r("T", buf)
            if c == 7:
                for b in banks:
                    kb.did_w("T", b)
        for ti in range(4):
            tt = sup * 4 + ti
            xres = load_x(kb, tt)
            for fh in range(2):
                bank = banks[ti * 2 + fh]
                kb.rd("V", xres, bank)
                kb.op("V", lambda V, xres=xres, bank=bank, fh=fh:
                      V.tensor_tensor(xres[:, fh * 512:fh * 512 + 512],
                                      bank.h[:],
                                      xres[:, fh * 512:fh * 512 + 512],
                                      ALU.add))
                kb.did_r("V", bank)
            kb.did_w("V", xres, wide=True)
            store_x(kb, tt, xres)



DIN = 2048
NDT = DIN // 128
TC = 512
NCH = T_CORE // TC


def emit_l1mamba(kb, load_x, store_x, win, conv_w, conv_b, xproj, dtw, dtb,
                 alog, dpar, cw, z_dram, ident, lnscr, stats, dbg=None):
    def dump(name, buf, cols, ch=None):
        if dbg is None or name not in dbg:
            return
        if ch is not None and ch != 0:
            return
        kb.rd("Y", buf)
        kb.op("Y", lambda Y, name=name, buf=buf, cols=cols: Y.dma_start(
            out=dbg[name].ap()[:, 0:cols], in_=buf[:, 0:cols]), 16)
        kb.did_r("Y", buf)
    nc = kb.nc
    onesrow = kb.sb("m_ones", [1, 128], F32)
    onesb = kb.sb("m_onesb", [1, 128], BF16)
    kb.op("V", lambda V: V.memset(onesrow.ap, 1.0))
    kb.op("V", lambda V: V.memset(onesb.ap, 1.0))
    kb.did_w("V", onesrow, onesb)

    convw_c = kb.sb("m_convw", [128, 64], F32)
    convb_c = kb.sb("m_convb", [128, 16], F32)
    dtw_c = kb.sb("m_dtw", [128, 16], F32)
    dtb_c = kb.sb("m_dtb", [128, 16], F32)
    alog_c = kb.sb("m_alog", [128, 256], F32)
    nega = kb.sb("m_nega", [128, 256], F32)
    dpar_c = kb.sb("m_dpar", [128, 16], F32)
    xproj_b = kb.sb("m_xproj", [128, 16 * 33], BF16)
    kb.op("Y", lambda Y: Y.dma_start(
        out=convw_c.h[:].rearrange("p (i j) -> p i j", i=16),
        in_=conv_w.ap().rearrange("(i p) o j -> p i (o j)", p=128)), 16)
    kb.op("Y", lambda Y: Y.dma_start(
        out=convb_c.ap, in_=conv_b.ap().rearrange("(i p) -> p i", p=128),
        allow_slow_non_contiguous=True), 16)
    kb.op("Y", lambda Y: Y.dma_start(
        out=dtw_c.ap, in_=dtw.ap().rearrange("o (i p) -> p (o i)", p=128),
        allow_slow_non_contiguous=True), 16)
    kb.op("Y", lambda Y: Y.dma_start(
        out=dtb_c.ap, in_=dtb.ap().rearrange("(i p) -> p i", p=128),
        allow_slow_non_contiguous=True), 16)
    kb.op("Y", lambda Y: Y.dma_start(
        out=alog_c.h[:].rearrange("p (i s) -> p i s", i=16),
        in_=alog.ap().rearrange("(i p) s -> p i s", p=128)), 16)
    kb.op("Y", lambda Y: Y.dma_start(
        out=dpar_c.ap, in_=dpar.ap().rearrange("(i p) -> p i", p=128),
        allow_slow_non_contiguous=True), 16)
    kb.did_w("Y", convw_c, convb_c, dtw_c, dtb_c, alog_c, dpar_c)
    kb.op("G", lambda G: G.dma_start(
        out=xproj_b.h[:].rearrange("p (i k) -> p i k", i=16),
        in_=xproj.ap().rearrange("(i p) k -> p i k", p=128)), 16)
    kb.did_w("G", xproj_b)
    kb.eop("S", lambda S: S.activation(nega.ap, alog_c.ap, AF.Exp),
           reads=(alog_c,), writes=(nega,), wide=True)
    kb.eop("V", lambda V: V.tensor_scalar(nega.ap, nega.ap, -1.0, None,
                                          ALU.mult),
           reads=(nega,), writes=(nega,), wide=True)

    xln = kb.sb("m_xln", [128, D], BF16)
    xlnT = kb.sb("m_xlnT", [128, 8 * TC], BF16)
    psT = Buf(None)
    psT.h = kb.psum[7].h[:].bitcast(BF16)
    psT.readers = kb.psum[7].readers
    xbr = kb.sb("m_xbr", [128, NDT * TC], BF16)
    ybuf = xbr
    ytmp = kb.sb("m_ytmp", [128, TC], BF16)
    brep = kb.sb("m_brep", [128, 16 * TC], BF16)
    crep = kb.sb("m_crep", [128, 16 * TC], BF16)
    pad = kb.sb("m_pad", [128, TC + 3], BF16)
    convo = kb.sb("m_convo", [128, TC], BF16)
    hist = kb.sb("m_hist", [128, 16 * 3], BF16)
    hcarry = kb.sb("m_hcarry", [128, 256], F32)
    bcbuf = kb.sb("m_bc", [32, TC], BF16)
    dtraw = kb.sb("m_dtraw", [1, TC], F32)
    brow = kb.sb("m_brow", [1, TC], BF16)
    Rrep = kb.sb("m_Rrep", [128, TC], F32)
    dtb_f = kb.sb("m_dtfull", [128, TC], F32)
    ab_f = kb.sb("m_abar", [128, TC], F32)
    bx_f = kb.sb("m_bx", [128, TC], F32)
    hbuf = kb.sb("m_h", [128, TC], BF16)
    ubuf = kb.sb("m_u", [128, TC], BF16)
    wsl = [kb.sb(f"m_wsl{i}", [128, 8 * 128], BF16) for i in range(2)]
    cwsl = [kb.sb(f"m_cwsl{i}", [128, D], BF16) for i in range(2)]
    zbuf = kb.sb("m_zbuf", [128, 512], BF16)
    kb.op("V", lambda V: V.memset(hist.ap, 0.0))
    kb.op("V", lambda V: V.memset(hcarry.ap, 0.0))
    kb.did_w("V", hist, hcarry)

    for ch in range(NCH):
        t0 = ch * TC
        NTPC = TC // 128
        xts = [load_x(kb, ch * NTPC + g) for g in range(NTPC)]
        for g in range(NTPC):
            emit_ln_stats(kb, xts[g], lnscr, stats, g, xcols=(0, D))
        emit_ln_finish(kb, stats, NTPC)
        x3 = xlnT.h[:].rearrange("p (c t) -> p c t", c=8)
        for g in range(NTPC):
            emit_ln_apply(kb, xts[g], xln, lnscr, stats, g, xcols=(0, D))
            kb.rd("T", xln, ident); kb.wr("T", psT)
            def tr(T, xln=xln, psT=psT):
                for c in range(8):
                    ins = T.transpose(psT.h[:, c * 128:(c + 1) * 128],
                                      xln[:, c * 128:(c + 1) * 128],
                                      ident.h[:])
                return ins
            kb.op("T", tr)
            kb.did_r("T", xln); kb.did_w("T", psT)
            kb.rd("V", psT); kb.wr("V", xlnT)
            kb.op("V", lambda V, g=g, x3=x3: V.tensor_copy(
                x3[:, :, g * 128:(g + 1) * 128],
                psT.h[:].rearrange("p (c t) -> p c t", c=8)))
            kb.did_r("V", psT)
        kb.did_w("V", xlnT, wide=True)

        for i in range(NDT):
            wbuf = wsl[i % 2]
            kb.wr("G", wbuf)
            kb.op("G", lambda G, i=i, wbuf=wbuf: G.dma_start(
                out=wbuf.h[:].rearrange("p (c f) -> p c f", c=8),
                in_=win.ap()[:, i * 128:(i + 1) * 128].rearrange(
                    "(c p) f -> p c f", p=128)), 16)
            kb.did_w("G", wbuf)
            for th in range(TC // 512):
                bank = kb.psum[th]
                kb.rd("T", xlnT, wbuf); kb.wr("T", bank)
                def mmi(T, i=i, th=th, bank=bank, wbuf=wbuf, x3=x3):
                    for c in range(8):
                        ins = T.matmul(bank.h[:],
                                       wbuf[:, c * 128:(c + 1) * 128],
                                       x3[:, c, th * 512:(th + 1) * 512],
                                       start=(c == 0), stop=(c == 7))
                    return ins
                kb.op("T", mmi)
                kb.did_w("T", bank); kb.did_r("T", wbuf, xlnT)
                kb.rd("S", bank); kb.wr("S", pad)
                kb.op("S", lambda S, th=th, bank=bank: S.activation(
                    pad[:, 3 + th * 512:3 + (th + 1) * 512], bank.h[:],
                    AF.Copy))
                kb.did_r("S", bank)
            kb.rd("S", hist)
            kb.op("S", lambda S, i=i: S.activation(
                pad[:, 0:3], hist[:, i * 3:(i + 1) * 3], AF.Copy))
            kb.did_r("S", hist); kb.did_w("S", pad)
            kb.rd("V", pad, convw_c); kb.wr("V", convo)
            kb.op("V", lambda V, i=i: V.tensor_single_scalar(
                convo.ap, pad[:, 0:TC], convw_c[:, i * 4:i * 4 + 1],
                ALU.mult))
            for j in range(1, 4):
                kb.op("V", lambda V, i=i, j=j: V.scalar_tensor_tensor(
                    convo.ap, pad[:, j:j + TC], convw_c[:, i * 4 + j:
                                                        i * 4 + j + 1],
                    convo.ap, ALU.mult, ALU.add))
            kb.did_w("V", convo, wide=True)
            kb.wr("V", hist)
            kb.op("V", lambda V, i=i: V.tensor_copy(
                hist[:, i * 3:(i + 1) * 3], pad[:, TC:TC + 3]))
            kb.did_w("V", hist)
            kb.did_r("V", pad)
            kb.rd("S", convo, convb_c); kb.wr("S", xbr)
            kb.op("S", lambda S, i=i: S.activation(
                xbr[:, i * TC:(i + 1) * TC], convo.ap, AF.Silu,
                bias=convb_c[:, i:i + 1]))
            kb.did_w("S", xbr, wide=True)

        dump("xbr", xbr, NDT * TC, ch)
        for i in range(NDT):
            wbuf = wsl[i % 2]
            kb.wr("G", wbuf)
            kb.op("G", lambda G, i=i, wbuf=wbuf: G.dma_start(
                out=wbuf.h[:].rearrange("p (c f) -> p c f", c=8),
                in_=win.ap()[:, 2048 + i * 128:2048 + (i + 1) * 128].rearrange(
                    "(c p) f -> p c f", p=128)), 16)
            kb.did_w("G", wbuf)
            for th in range(TC // 512):
                bank = kb.psum[2 + th]
                kb.rd("T", xlnT, wbuf); kb.wr("T", bank)
                def mmz(T, i=i, th=th, bank=bank, wbuf=wbuf, x3=x3):
                    for c in range(8):
                        ins = T.matmul(bank.h[:],
                                       wbuf[:, c * 128:(c + 1) * 128],
                                       x3[:, c, th * 512:(th + 1) * 512],
                                       start=(c == 0), stop=(c == 7))
                    return ins
                kb.op("T", mmz)
                kb.did_w("T", bank); kb.did_r("T", wbuf, xlnT)
                kb.rd("S", bank); kb.wr("S", zbuf)
                kb.op("S", lambda S, bank=bank: S.activation(
                    zbuf.ap, bank.h[:], AF.Silu))
                kb.did_r("S", bank); kb.did_w("S", zbuf)
                kb.rd("Y", zbuf)
                kb.op("Y", lambda Y, i=i, th=th, t0=t0: Y.dma_start(
                    out=z_dram.ap()[i * 128:(i + 1) * 128,
                                    t0 + th * 512:t0 + (th + 1) * 512],
                    in_=zbuf.ap), 16)
                kb.did_r("Y", zbuf)

        for th in range(TC // 512):
            bank = kb.psum[4 + th]
            kb.rd("T", xbr, xproj_b); kb.wr("T", bank)
            def mms(T, th=th, bank=bank):
                for i in range(NDT):
                    ins = T.matmul(bank.h[0:33, :],
                                   xproj_b[:, i * 33:(i + 1) * 33],
                                   xbr[:, i * TC + th * 512:
                                       i * TC + (th + 1) * 512],
                                   start=(i == 0), stop=(i == NDT - 1))
                return ins
            kb.op("T", mms)
            kb.did_w("T", bank)
            kb.rd("V", bank); kb.wr("V", bcbuf, dtraw)
            kb.op("V", lambda V, th=th, bank=bank: V.tensor_copy(
                bcbuf[0:32, th * 512:(th + 1) * 512], bank.h[0:32, :]))
            kb.op("V", lambda V, th=th, bank=bank: V.tensor_copy(
                dtraw[0:1, th * 512:(th + 1) * 512], bank.h[32:33, :]))
            kb.did_r("V", bank); kb.did_w("V", bcbuf, dtraw)

        dump("bc", bcbuf, TC, ch)
        dump("dtraw", dtraw, TC, ch)
        for th in range(TC // 512):
            bank = kb.psum[4 + th]
            kb.rd("T", dtraw, onesrow); kb.wr("T", bank)
            kb.op("T", lambda T, th=th, bank=bank: T.matmul(
                bank.h[:], onesrow.h[0:1, :],
                dtraw[0:1, th * 512:(th + 1) * 512], start=True, stop=True))
            kb.did_w("T", bank)
            kb.rd("S", bank); kb.wr("S", Rrep)
            kb.op("S", lambda S, th=th, bank=bank: S.activation(
                Rrep[:, th * 512:(th + 1) * 512], bank.h[:], AF.Copy))
            kb.did_r("S", bank)
        kb.did_w("S", Rrep, wide=True)

        for s in range(16):
            for part, dst in [(0, brep), (16, crep)]:
                kb.rd("G", bcbuf)
                kb.op("G", lambda G, s=s, part=part: G.dma_start(
                    out=brow.ap, in_=bcbuf[part + s:part + s + 1, :]), 16)
                kb.did_w("G", brow)
                for th in range(TC // 512):
                    bank = kb.psum[6]
                    kb.rd("T", brow, onesb); kb.wr("T", bank)
                    kb.op("T", lambda T, th=th, bank=bank: T.matmul(
                        bank.h[:], onesb.h[0:1, :],
                        brow[0:1, th * 512:(th + 1) * 512],
                        start=True, stop=True))
                    kb.did_w("T", bank)
                    kb.rd("S", bank); kb.wr("S", dst)
                    kb.op("S", lambda S, s=s, th=th, bank=bank, dst=dst:
                          S.activation(dst[:, s * TC + th * 512:
                                           s * TC + (th + 1) * 512],
                                       bank.h[:], AF.Copy))
                    kb.did_r("S", bank)
        kb.did_w("S", brep, crep, wide=True)

        for i in range(NDT):
            kb.eop("V", lambda V, i=i: V.tensor_single_scalar(
                dtb_f.ap, Rrep.ap, dtw_c[:, i:i + 1], ALU.mult),
                reads=(Rrep, dtw_c), writes=(dtb_f,), wide=True)
            kb.eop("V", lambda V, i=i: V.tensor_single_scalar(
                dtb_f.ap, dtb_f.ap, dtb_c[:, i:i + 1], ALU.add),
                reads=(dtb_f, dtb_c), writes=(dtb_f,), wide=True)
            kb.eop("V", lambda V: V.tensor_scalar(
                ab_f.h[:].bitcast(I32), dtb_f.h[:].bitcast(
                    I32), 0x7FFFFFFF, None, ALU.bitwise_and),
                reads=(dtb_f,), writes=(ab_f,), wide=True)
            kb.eop("S", lambda S: S.activation(ab_f.ap, ab_f.ap, AF.Exp,
                                               scale=-1.0),
                   reads=(ab_f,), writes=(ab_f,), wide=True)
            eps1 = kb.constcol(1.0)
            kb.eop("S", lambda S: S.activation(ab_f.ap, ab_f.ap, AF.Ln,
                                               bias=eps1),
                   reads=(ab_f,), writes=(ab_f,), wide=True)
            kb.eop("V", lambda V: V.tensor_scalar(bx_f.ap, dtb_f.ap, 0.0,
                                                  None, ALU.max),
                   reads=(dtb_f,), writes=(bx_f,), wide=True)
            kb.eop("V", lambda V: V.tensor_tensor(dtb_f.ap, bx_f.ap,
                                                  ab_f.ap, ALU.add),
                   reads=(bx_f, ab_f), writes=(dtb_f,), wide=True)
            if i == 0:
                dump("dt0", dtb_f, TC, ch)
            kb.eop("V", lambda V, i=i: V.tensor_tensor(
                ubuf.ap, dtb_f.ap, xbr[:, i * TC:(i + 1) * TC], ALU.mult),
                reads=(dtb_f, xbr), writes=(ubuf,), wide=True)
            for s in range(16):
                kb.eop("S", lambda S, i=i, s=s: S.activation(
                    ab_f.ap, dtb_f.ap, AF.Exp,
                    scale=nega[:, i * 16 + s:i * 16 + s + 1]),
                    reads=(dtb_f, nega), writes=(ab_f,), wide=True)
                kb.eop("V", lambda V, s=s: V.tensor_tensor(
                    bx_f.ap, ubuf.ap, brep[:, s * TC:(s + 1) * TC],
                    ALU.mult),
                    reads=(ubuf, brep), writes=(bx_f,), wide=True)
                col = None
                kb.eop("V", lambda V, i=i, s=s: V.tensor_tensor_scan(
                    hbuf.ap, ab_f.ap, bx_f.ap,
                    hcarry[:, i * 16 + s:i * 16 + s + 1], ALU.mult, ALU.add),
                    reads=(ab_f, bx_f, hcarry), writes=(hbuf,), wide=True)
                kb.eop("V", lambda V, i=i, s=s: V.tensor_copy(
                    hcarry[:, i * 16 + s:i * 16 + s + 1],
                    hbuf[:, TC - 1:TC]),
                    reads=(hbuf,), writes=(hcarry,))
                if i == 0 and s == 0:
                    dump("h00", hbuf, TC, ch)
                    dump("ab00", ab_f, TC, ch)
                    dump("bx00", bx_f, TC, ch)
                if s == 0:
                    kb.eop("V", lambda V, i=i, s=s: V.tensor_tensor(
                        ytmp.ap, hbuf.ap,
                        crep[:, s * TC:(s + 1) * TC], ALU.mult),
                        reads=(hbuf, crep), writes=(ytmp,), wide=True)
                else:
                    kb.eop("V", lambda V, i=i, s=s: V.tensor_tensor(
                        bx_f.h[:].bitcast(BF16)[:, 0:TC], hbuf.ap,
                        crep[:, s * TC:(s + 1) * TC], ALU.mult),
                        reads=(hbuf, crep), writes=(bx_f,), wide=True)
                    kb.eop("V", lambda V, i=i: V.tensor_tensor(
                        ytmp.ap, ytmp.ap,
                        bx_f.h[:].bitcast(BF16)[:, 0:TC], ALU.add),
                        reads=(bx_f, ytmp), writes=(ytmp,), wide=True)
            kb.eop("V", lambda V, i=i: V.scalar_tensor_tensor(
                ybuf[:, i * TC:(i + 1) * TC], xbr[:, i * TC:(i + 1) * TC],
                dpar_c[:, i:i + 1], ytmp.ap,
                ALU.mult, ALU.add),
                reads=(xbr, dpar_c, ytmp), writes=(ybuf,), wide=True)
        kb.wait("Y", "Y")
        for i in range(NDT):
            for th in range(TC // 512):
                kb.wr("Y", zbuf)
                kb.op("Y", lambda Y, i=i, th=th, t0=t0: Y.dma_start(
                    out=zbuf.ap,
                    in_=z_dram.ap()[i * 128:(i + 1) * 128,
                                    t0 + th * 512:t0 + (th + 1) * 512]), 16)
                kb.did_w("Y", zbuf)
                kb.eop("V", lambda V, i=i, th=th: V.tensor_tensor(
                    ybuf[:, i * TC + th * 512:i * TC + (th + 1) * 512],
                    ybuf[:, i * TC + th * 512:i * TC + (th + 1) * 512],
                    zbuf.ap, ALU.mult),
                    reads=(zbuf, ybuf), writes=(ybuf,), wide=True)

        banks = kb.psum
        for i in range(NDT):
            cbuf = cwsl[i % 2]
            kb.wr("G", cbuf)
            kb.op("G", lambda G, i=i, cbuf=cbuf: G.dma_start(
                out=cbuf.ap, in_=cw.ap()[i * 128:(i + 1) * 128, :]), 16)
            kb.did_w("G", cbuf)
            kb.rd("T", ybuf, cbuf)
            if i == 0:
                for b in banks:
                    kb.wr("T", b)
            def mmc(T, i=i, cbuf=cbuf):
                for tsub in range(4):
                    for fh in range(2):
                        ins = T.matmul(
                            banks[tsub * 2 + fh].h[:],
                            ybuf[:, i * TC + tsub * 128:
                                 i * TC + (tsub + 1) * 128],
                            cbuf[:, fh * 512:(fh + 1) * 512],
                            start=(i == 0), stop=(i == NDT - 1))
                return ins
            kb.op("T", mmc)
            kb.did_r("T", ybuf, cbuf)
            if i == NDT - 1:
                for b in banks:
                    kb.did_w("T", b)
        for tsub in range(4):
            tt = ch * (TC // 128) + tsub
            xres = load_x(kb, tt)
            for fh in range(2):
                bank = banks[tsub * 2 + fh]
                kb.rd("V", xres, bank)
                kb.op("V", lambda V, xres=xres, bank=bank, fh=fh:
                      V.tensor_tensor(xres[:, fh * 512:(fh + 1) * 512],
                                      bank.h[:],
                                      xres[:, fh * 512:(fh + 1) * 512],
                                      ALU.add))
                kb.did_r("V", bank)
            kb.did_w("V", xres, wide=True)
            store_x(kb, tt, xres)



RD48 = 48


def emit_lv_partial(kb, load_x, out_dram, lnscr, stats):
    lv = kb.sb("p_lv", [128, NT + 2], F32)
    onescol = kb.sb("p_ones", [128, 1], F32)
    kb.op("V", lambda V: V.memset(onescol.ap, 1.0))
    kb.did_w("V", onescol)
    hs = kb.sb("p_hs", [128, D], BF16)
    for grp in range(NT // 8):
        xts = [load_x(kb, grp * 8 + g) for g in range(8)]
        for g in range(8):
            emit_ln_stats(kb, xts[g], lnscr, stats, g, xcols=(0, D))
        emit_ln_finish(kb, stats, 8)
        for g in range(8):
            tt = grp * 8 + g
            emit_ln_apply(kb, xts[g], hs, lnscr, stats, g, xcols=(0, D))
            emit_lv_col(kb, hs, lnscr, stats, lv, tt)
    kb.eop("V", lambda V: V.tensor_reduce(lv[:, NT:NT + 1], lv[:, 0:NT],
                                          AX.X, ALU.add),
           reads=(lv,), writes=(lv,))
    bank = kb.psum[0]
    kb.rd("T", lv, onescol); kb.wr("T", bank)
    kb.op("T", lambda T, bank=bank: T.matmul(
        bank.h[0:1, 0:1], lv.h[:, NT:NT + 1], onescol.h[:, 0:1],
        start=True, stop=True))
    kb.did_w("T", bank)
    kb.eop("V", lambda V, bank=bank: V.tensor_copy(lv[0:1, NT + 1:NT + 2],
                                                   bank.h[0:1, 0:1]),
           reads=(bank,), writes=(lv,))
    kb.did_r("V", bank)
    kb.rd("Y", lv)
    kb.op("Y", lambda Y: Y.dma_start(out=out_dram.ap()[0:1, 0:1],
                                     in_=lv[0:1, NT + 1:NT + 2]), 16)
    kb.did_r("Y", lv)


def emit_lv_col(kb, hs, scratch, stats, lv, tt):
    st = stats.h
    s1c, s2c, mc, vc = st[:, 40:41], st[:, 41:42], st[:, 42:43], st[:, 43:44]
    kb.eop("S", lambda S: S.activation(scratch.ap, hs.ap, AF.Square,
                                       accum_out=s2c),
           reads=(hs,), writes=(scratch, stats), wide=True)
    kb.eop("V", lambda V: V.reduce_sum(s1c, hs.ap, AX.X),
           reads=(hs, stats), writes=(stats,))
    kb.eop("V", lambda V: V.tensor_scalar(mc, s1c, 1.0 / D, None, ALU.mult),
           reads=(stats,), writes=(stats,))
    kb.eop("V", lambda V: V.tensor_scalar(vc, s2c, 1.0 / D, None, ALU.mult),
           reads=(stats,), writes=(stats,))
    kb.eop("V", lambda V: V.tensor_tensor(mc, mc, mc, ALU.mult),
           reads=(stats,), writes=(stats,))
    kb.eop("V", lambda V: V.tensor_tensor(vc, vc, mc, ALU.subtract),
           reads=(stats,), writes=(stats,))
    one_ap = kb.constcol(1.0)
    kb.eop("S", lambda S, tt=tt: S.activation(lv[:, tt:tt + 1], vc, AF.Ln,
                                              bias=one_ap),
           reads=(stats,), writes=(lv,))


def emit_l1ffn(kb, load_x, store_x, qw_d, rw_d, ak_d, av_d, rel_d, w1_d,
               b1_d, w2_d, b2gwgb_d, partials8_d, w_dram, scr2_d, ident,
               lnscr, stats):
    onesrow = kb.sb("l_ones", [1, 128], F32)
    onescol = kb.sb("l_onesc", [128, 1], F32)
    onesrow_b = kb.sb("l_onesb", [1, 128], BF16)
    kb.op("V", lambda V: V.memset(onesrow.ap, 1.0))
    kb.op("V", lambda V: V.memset(onescol.ap, 1.0))
    kb.op("V", lambda V: V.memset(onesrow_b.ap, 1.0))
    kb.did_w("V", onesrow, onescol, onesrow_b)

    stg = kb.sb("l_stg", [128, D], BF16)
    qwT = kb.sb("l_qwT", [128, 8 * 128], BF16)
    rwT = kb.sb("l_rwT", [128, RD48], BF16)
    akT = kb.sb("l_akT", [128, T_CORE], BF16)
    rkaugT = kb.sb("l_rkaug", [49, T_CORE], BF16)
    w1T = kb.sb("l_w1T", [128, 8 * 256], BF16)
    b1row = kb.sb("l_b1row", [1, 256], F32)
    w2rep = kb.sb("l_w2rep", [128, 256], BF16)
    c3 = kb.sb("l_c3", [128, 4], F32)
    par8 = kb.sb("l_par8", [1, 8], F32)
    sc8 = kb.sb("l_sc8", [1, 8], F32)
    meanrep = kb.sb("l_meanrep", [128, 1], F32)

    kb.op("G", lambda G: G.dma_start(out=stg[:, 0:D], in_=qw_d.ap()), 16)
    kb.did_w("G", stg)
    kb.rd("Y", stg)
    kb.op("Y", lambda Y: Y.dma_start(out=scr2_d.ap()[0:128, 0:D],
                                     in_=stg[:, 0:D]), 16)
    kb.did_r("Y", stg)
    kb.wr("G", stg)
    kb.op("G", lambda G: G.dma_start(out=stg[0:48, 0:128], in_=rw_d.ap()), 16)
    kb.did_w("G", stg)
    kb.rd("Y", stg)
    kb.op("Y", lambda Y: Y.dma_start(out=scr2_d.ap()[128:176, 0:128],
                                     in_=stg[0:48, 0:128]), 16)
    kb.did_r("Y", stg)
    for i in range(2):
        kb.wr("G", stg)
        kb.op("G", lambda G, i=i: G.dma_start(
            out=stg[:, 0:D], in_=w1_d.ap()[i * 128:(i + 1) * 128, :]), 16)
        kb.did_w("G", stg)
        kb.rd("Y", stg)
        kb.op("Y", lambda Y, i=i: Y.dma_start(
            out=scr2_d.ap()[256 + i * 128:256 + (i + 1) * 128, 0:D],
            in_=stg[:, 0:D]), 16)
        kb.did_r("Y", stg)
    for i in range(32):
        kb.wr("G", stg)
        kb.op("G", lambda G, i=i: G.dma_start(
            out=stg[:, 0:128], in_=ak_d.ap()[i * 128:(i + 1) * 128, :]), 16)
        kb.did_w("G", stg)
        kb.rd("Y", stg)
        kb.op("Y", lambda Y, i=i: Y.dma_start(
            out=scr2_d.ap()[1024 + i * 128:1024 + (i + 1) * 128, 0:128],
            in_=stg[:, 0:128]), 16)
        kb.did_r("Y", stg)
    kb.wait("Y", "Y")
    for c in range(8):
        kb.op("Y", lambda Y, c=c: Y.dma_start(
            out=qwT[:, c * 128:(c + 1) * 128],
            in_=scr2_d.ap()[0:128, c * 128:(c + 1) * 128], transpose=True), 16)
    kb.op("Y", lambda Y: Y.dma_start(
        out=rwT[:, 0:RD48], in_=scr2_d.ap()[128:176, 0:128],
        transpose=True), 16)
    for c in range(8):
        kb.op("Y", lambda Y, c=c: Y.dma_start(
            out=w1T[:, c * 256:(c + 1) * 256],
            in_=scr2_d.ap()[256:512, c * 128:(c + 1) * 128], transpose=True),
            16)
    kb.op("Y", lambda Y: Y.dma_start(
        out=akT.ap, in_=scr2_d.ap()[1024:5120, 0:128], transpose=True), 16)
    kb.did_w("Y", qwT, rwT, w1T, akT)
    kb.op("Y", lambda Y: Y.dma_start(out=b1row.ap, in_=b1_d.ap().unsqueeze(0)),
          16)
    kb.op("Y", lambda Y: Y.dma_start(out=par8.ap,
                                     in_=partials8_d.ap().unsqueeze(0)), 16)
    kb.did_w("Y", b1row, par8)
    w2row = kb.sb("l_w2row", [1, 256], F32)
    sc3row = kb.sb("l_sc3row", [1, 4], F32)
    kb.op("Y", lambda Y: Y.dma_start(out=w2row.ap, in_=w2_d.ap()), 16)
    kb.op("Y", lambda Y: Y.dma_start(out=sc3row[0:1, 0:3],
                                     in_=b2gwgb_d.ap().unsqueeze(0)), 16)
    kb.did_w("Y", w2row, sc3row)
    bank = kb.psum[0]
    kb.rd("T", w2row, onesrow); kb.wr("T", bank)
    kb.op("T", lambda T, bank=bank: T.matmul(bank.h[:, 0:256],
                                             onesrow.h[0:1, :], w2row.ap,
                                             start=True, stop=True))
    kb.did_w("T", bank)
    kb.rd("S", bank); kb.wr("S", w2rep)
    kb.op("S", lambda S, bank=bank: S.activation(w2rep.ap, bank.h[:, 0:256],
                                                 AF.Copy))
    kb.did_r("S", bank); kb.did_w("S", w2rep)
    bank = kb.psum[1]
    kb.rd("T", sc3row, onesrow); kb.wr("T", bank)
    kb.op("T", lambda T, bank=bank: T.matmul(bank.h[:, 0:4], onesrow.h[0:1, :],
                                             sc3row.ap, start=True, stop=True))
    kb.did_w("T", bank)
    kb.eop("V", lambda V, bank=bank: V.tensor_copy(c3.ap, bank.h[:, 0:4]),
           reads=(bank,), writes=(c3,))
    kb.did_r("V", bank)
    kb.eop("V", lambda V: V.tensor_reduce(sc8[0:1, 0:1], par8.ap, AX.X,
                                          ALU.add),
           reads=(par8,), writes=(sc8,))
    kb.eop("V", lambda V: V.tensor_scalar(sc8[0:1, 1:2], sc8[0:1, 0:1],
                                          1.0 / 32768.0, 1e-6, ALU.mult,
                                          ALU.add),
           reads=(sc8,), writes=(sc8,))
    kb.eop("V", lambda V: V.reciprocal(sc8[0:1, 2:3], sc8[0:1, 1:2]),
           reads=(sc8,), writes=(sc8,))
    bank = kb.psum[2]
    kb.rd("T", sc8, onesrow); kb.wr("T", bank)
    kb.op("T", lambda T, bank=bank: T.matmul(bank.h[:, 0:1], onesrow.h[0:1, :],
                                             sc8[0:1, 2:3], start=True,
                                             stop=True))
    kb.did_w("T", bank)
    kb.eop("V", lambda V, bank=bank: V.tensor_copy(meanrep.ap,
                                                   bank.h[:, 0:1]),
           reads=(bank,), writes=(meanrep,))
    kb.did_r("V", bank)
    kb.op("G", lambda G: G.dma_start(out=rkaugT.h[48:49, :],
                                     in_=rel_d.ap().unsqueeze(0)), 16)
    kb.did_w("G", rkaugT)
    for nch in range(8):
        bank = kb.psum[3]
        kb.rd("T", rwT, akT); kb.wr("T", bank)
        kb.op("T", lambda T, nch=nch, bank=bank: T.matmul(
            bank.h[0:RD48, :], rwT[:, 0:RD48],
            akT[:, nch * 512:(nch + 1) * 512], start=True, stop=True))
        kb.did_w("T", bank)
        kb.rd("S", bank); kb.wr("S", rkaugT)
        kb.op("S", lambda S, nch=nch, bank=bank: S.activation(
            rkaugT.h[0:48, nch * 512:(nch + 1) * 512], bank.h[0:RD48, :],
            AF.Copy))
        kb.did_r("S", bank)
    kb.did_w("S", rkaugT, wide=True)

    hs_full = kb.sb("l_hs", [128, NT * D], BF16)
    hsT = kb.sb("l_hsT", [128, 8 * 512], BF16)
    psT = Buf(None)
    psT.h = kb.psum[7].h[:].bitcast(BF16)
    psT.readers = kb.psum[7].readers
    hstile = kb.sb("l_hstile", [128, D], BF16)
    qT_s = kb.sb("l_qTs", [128, 512], BF16)
    rqaugT = kb.sb("l_rqaug", [49, 512], BF16)
    ones512b = kb.sb("l_ones512b", [1, 512], BF16)
    kb.op("V", lambda V: V.memset(ones512b.ap, 1.0))
    kb.did_w("V", ones512b)
    kb.rd("Y", ones512b)
    kb.op("Y", lambda Y: Y.dma_start(out=rqaugT.h[48:49, :],
                                     in_=ones512b.ap), 16)
    kb.did_r("Y", ones512b)
    kb.did_w("Y", rqaugT)
    Sbuf = kb.sb("l_S", [128, T_CORE], BF16)
    m8 = kb.sb("l_m8", [128, 8], F32)
    mwork = kb.sb("l_mwork", [128, T_CORE], BF16)
    wrow = kb.sb("l_wrow", [128, T_CORE], BF16)
    l_all = kb.sb("l_lall", [128, NT], F32)
    sig_all = kb.sb("l_sigall", [128, NT], F32)
    lv_all = kb.sb("l_lvall", [128, NT], F32)
    h1buf = kb.sb("l_h1", [128, 256], BF16)
    u1 = kb.sb("l_u1", [128, NT], F32)

    x3 = hsT.h[:].rearrange("p (c t) -> p c t", c=8)
    for sup in range(8):
        xts = [load_x(kb, sup * 4 + g) for g in range(4)]
        for g in range(4):
            emit_ln_stats(kb, xts[g], lnscr, stats, g, xcols=(0, D))
        emit_ln_finish(kb, stats, 4)
        for g in range(4):
            tt = sup * 4 + g
            emit_ln_apply(kb, xts[g], hstile, lnscr, stats, g, xcols=(0, D))
            emit_lv_col(kb, hstile, lnscr, stats, lv_all, tt)
            kb.eop("V", lambda V, tt=tt: V.tensor_copy(
                hs_full[:, tt * D:(tt + 1) * D], hstile.ap),
                reads=(hstile,), writes=(hs_full,), wide=True)
            kb.rd("T", hstile, ident); kb.wr("T", psT)
            def tr(T, hstile=hstile, psT=psT):
                for c in range(8):
                    ins = T.transpose(psT.h[:, c * 128:(c + 1) * 128],
                                      hstile[:, c * 128:(c + 1) * 128],
                                      ident.h[:])
                return ins
            kb.op("T", tr)
            kb.did_r("T", hstile); kb.did_w("T", psT)
            kb.rd("V", psT); kb.wr("V", hsT)
            kb.op("V", lambda V, g=g, x3=x3: V.tensor_copy(
                x3[:, :, g * 128:(g + 1) * 128],
                psT.h[:].rearrange("p (c t) -> p c t", c=8)))
            kb.did_r("V", psT)
        kb.did_w("V", hsT, wide=True)
        bank = kb.psum[0]
        kb.rd("T", qwT, hsT); kb.wr("T", bank)
        def mmq(T, bank=bank, x3=x3):
            for c in range(8):
                ins = T.matmul(bank.h[:], qwT[:, c * 128:(c + 1) * 128],
                               x3[:, c, :], start=(c == 0), stop=(c == 7))
            return ins
        kb.op("T", mmq)
        kb.did_w("T", bank); kb.did_r("T", hsT)
        kb.rd("S", bank); kb.wr("S", qT_s)
        kb.op("S", lambda S, bank=bank: S.activation(qT_s.ap, bank.h[:],
                                                     AF.Copy))
        kb.did_r("S", bank); kb.did_w("S", qT_s)
        bank = kb.psum[1]
        kb.rd("T", rwT, qT_s); kb.wr("T", bank)
        kb.op("T", lambda T, bank=bank: T.matmul(
            bank.h[0:RD48, :], rwT[:, 0:RD48], qT_s.ap, start=True,
            stop=True))
        kb.did_w("T", bank)
        kb.rd("S", bank); kb.wr("S", rqaugT)
        kb.op("S", lambda S, bank=bank: S.activation(
            rqaugT.h[0:48, :], bank.h[0:RD48, :], AF.Copy,
            scale=1.0 / float(np.sqrt(48.0))))
        kb.did_r("S", bank); kb.did_w("S", rqaugT)
        for g in range(4):
            tt = sup * 4 + g
            bank = kb.psum[2]
            kb.rd("T", w1T, hsT, onesrow, b1row); kb.wr("T", bank)
            def mmh1(T, g=g, bank=bank, x3=x3):
                for c in range(8):
                    T.matmul(bank.h[:, 0:256],
                             x3[:, c, g * 128:(g + 1) * 128],
                             w1T[:, c * 256:(c + 1) * 256],
                             start=(c == 0), stop=False)
                return T.matmul(bank.h[:, 0:256], onesrow.h[0:1, :],
                                b1row.ap, start=False, stop=True)
            kb.op("T", mmh1)
            kb.did_w("T", bank)
            kb.rd("S", bank); kb.wr("S", h1buf)
            kb.op("S", lambda S, bank=bank: S.activation(
                h1buf.ap, bank.h[:, 0:256], AF.Gelu))
            kb.did_r("S", bank); kb.did_w("S", h1buf)
            kb.eop("V", lambda V: V.tensor_tensor(h1buf.ap, h1buf.ap,
                                                  w2rep.ap, ALU.mult),
                   reads=(h1buf, w2rep), writes=(h1buf,), wide=True)
            kb.eop("V", lambda V, tt=tt: V.reduce_sum(u1[:, tt:tt + 1],
                                                      h1buf.ap, AX.X),
                   reads=(h1buf, u1), writes=(u1,))
        for g in range(4):
            tt = sup * 4 + g
            for nch in range(8):
                bank = kb.psum[3 + nch % 2]
                kb.rd("T", rqaugT, rkaugT); kb.wr("T", bank)
                kb.op("T", lambda T, g=g, nch=nch, bank=bank: T.matmul(
                    bank.h[:], rqaugT.h[0:49, g * 128:(g + 1) * 128],
                    rkaugT.h[0:49, nch * 512:(nch + 1) * 512],
                    start=True, stop=True))
                kb.did_w("T", bank)
                kb.rd("S", bank); kb.wr("S", Sbuf)
                kb.op("S", lambda S, nch=nch, bank=bank: S.activation(
                    Sbuf[:, nch * 512:(nch + 1) * 512], bank.h[:], AF.Copy))
                kb.did_r("S", bank)
            kb.did_w("S", Sbuf, wide=True)
            kb.eop("V", lambda V: V.max(m8.ap, Sbuf.ap),
                   reads=(Sbuf,), writes=(m8,))
            kb.eop("V", lambda V: V.tensor_scalar(
                mwork.ap, Sbuf.ap, m8[:, 7:8], 30000.0, ALU.is_ge, ALU.mult),
                reads=(Sbuf, m8), writes=(mwork,), wide=True)
            kb.eop("V", lambda V: V.scalar_tensor_tensor(
                mwork.ap, Sbuf.ap, -30000.0, mwork.ap, ALU.add, ALU.add),
                reads=(Sbuf, mwork), writes=(mwork,), wide=True)
            kb.eop("V", lambda V: V.tensor_scalar(m8[:, 1:2], m8[:, 0:1],
                                                  -1.0, None, ALU.mult),
                   reads=(m8,), writes=(m8,))
            kb.rd("S", mwork, m8); kb.wr("S", wrow)
            kb.op("S", lambda S, tt=tt: S.activation(
                wrow.ap, mwork.ap, AF.Exp, bias=m8[:, 1:2],
                accum_out=l_all[:, tt:tt + 1]))
            kb.did_r("S", mwork); kb.did_w("S", wrow, wide=True)
            kb.did_w("S", l_all)
            kb.rd("Y", wrow)
            kb.op("Y", lambda Y, tt=tt: Y.dma_start(
                out=w_dram.ap()[tt * 128:(tt + 1) * 128, :], in_=wrow.ap), 16)
            kb.did_r("Y", wrow)

    kb.eop("V", lambda V: V.tensor_single_scalar(u1.ap, u1.ap, c3[:, 0:1],
                                                 ALU.add),
           reads=(u1, c3), writes=(u1,), wide=True)
    kb.eop("S", lambda S: S.activation(sig_all.ap, u1.ap, AF.Sigmoid),
           reads=(u1,), writes=(sig_all,), wide=True)
    kb.eop("V", lambda V: V.tensor_single_scalar(lv_all.ap, lv_all.ap,
                                                 meanrep[:, 0:1], ALU.mult),
           reads=(lv_all, meanrep), writes=(lv_all,), wide=True)
    kb.eop("V", lambda V: V.tensor_scalar(lv_all.ap, lv_all.ap, 0.5, None,
                                          ALU.mult),
           reads=(lv_all,), writes=(lv_all,))
    kb.eop("V", lambda V: V.scalar_tensor_tensor(
        sig_all.ap, sig_all.ap, 2.5, lv_all.ap, ALU.mult, ALU.add),
        reads=(sig_all, lv_all), writes=(sig_all,))
    kb.eop("V", lambda V: V.tensor_scalar(sig_all.ap, sig_all.ap, 0.0, 5.0,
                                          ALU.max, ALU.min),
           reads=(sig_all,), writes=(sig_all,))
    kb.eop("V", lambda V: V.tensor_single_scalar(sig_all.ap, sig_all.ap,
                                                 c3[:, 1:2], ALU.mult),
           reads=(sig_all, c3), writes=(sig_all,))
    kb.eop("V", lambda V: V.tensor_single_scalar(sig_all.ap, sig_all.ap,
                                                 c3[:, 2:3], ALU.add),
           reads=(sig_all, c3), writes=(sig_all,))
    kb.eop("S", lambda S: S.activation(sig_all.ap, sig_all.ap, AF.Sigmoid),
           reads=(sig_all,), writes=(sig_all,), wide=True)
    kb.eop("V", lambda V: V.reciprocal(l_all.ap, l_all.ap),
           reads=(l_all,), writes=(l_all,), wide=True)
    kb.eop("V", lambda V: V.tensor_tensor(sig_all.ap, sig_all.ap, l_all.ap,
                                          ALU.mult),
           reads=(sig_all, l_all), writes=(sig_all,))
    kb.eop("V", lambda V: V.tensor_scalar(sig_all.ap, sig_all.ap, 0.5, None,
                                          ALU.mult),
           reads=(sig_all,), writes=(sig_all,), wide=True)

    kb.wait("Y", "Y")
    wTs = [kb.sb(f"l_wT{i}", [128, 512], BF16) for i in range(2)]
    avb = [kb.sb(f"l_av{i}", [128, D], BF16) for i in range(2)]
    banks = kb.psum
    for sup in range(8):
        for n in range(32):
            wbuf = wTs[n % 2]
            abuf = avb[n % 2]
            kb.wr("Y", wbuf)
            kb.op("Y", lambda Y, sup=sup, n=n, wbuf=wbuf: Y.dma_start(
                out=wbuf.ap,
                in_=w_dram.ap()[sup * 512:(sup + 1) * 512,
                                n * 128:(n + 1) * 128], transpose=True), 16)
            kb.did_w("Y", wbuf)
            kb.wr("G", abuf)
            kb.op("G", lambda G, n=n, abuf=abuf: G.dma_start(
                out=abuf.ap, in_=av_d.ap()[n * 128:(n + 1) * 128, :]), 16)
            kb.did_w("G", abuf)
            kb.rd("T", wbuf, abuf)
            if n == 0:
                for b in banks:
                    kb.wr("T", b)
            def mma(T, n=n, wbuf=wbuf, abuf=abuf):
                for qt in range(4):
                    for hh in range(2):
                        ins = T.matmul(
                            banks[qt * 2 + hh].h[:],
                            wbuf[:, qt * 128:(qt + 1) * 128],
                            abuf[:, hh * 512:(hh + 1) * 512],
                            start=(n == 0), stop=(n == 31))
                return ins
            kb.op("T", mma)
            kb.did_r("T", wbuf, abuf)
            if n == 31:
                for b in banks:
                    kb.did_w("T", b)
        for qt in range(4):
            tt = sup * 4 + qt
            xres = load_x(kb, tt)
            for hh in range(2):
                bank = banks[qt * 2 + hh]
                kb.rd("V", xres, bank, sig_all, hs_full)
                kb.op("V", lambda V, xres=xres, bank=bank, hh=hh, tt=tt:
                      V.scalar_tensor_tensor(
                          xres[:, hh * 512:(hh + 1) * 512], bank.h[:],
                          sig_all[:, tt:tt + 1],
                          xres[:, hh * 512:(hh + 1) * 512],
                          ALU.mult, ALU.add))
                kb.op("V", lambda V, xres=xres, hh=hh, tt=tt:
                      V.scalar_tensor_tensor(
                          xres[:, hh * 512:(hh + 1) * 512],
                          hs_full[:, tt * D + hh * 512:
                                  tt * D + (hh + 1) * 512],
                          0.5, xres[:, hh * 512:(hh + 1) * 512],
                          ALU.mult, ALU.add))
                kb.did_r("V", bank, hs_full)
            kb.did_w("V", xres, wide=True)
            store_x(kb, tt, xres)



def build_l0ffn_launch():
    kb = KB()
    nc = kb.nc
    x_in = nc.dram_tensor("x_in", [T_CORE, D], F32, kind="ExternalInput")
    w_gate = nc.dram_tensor("w_gate", [I_FFN, D], F32, kind="ExternalInput")
    w_up = nc.dram_tensor("w_up", [I_FFN, D], F32, kind="ExternalInput")
    w_down = nc.dram_tensor("w_down", [D, I_FFN], F32, kind="ExternalInput")
    id_in = nc.dram_tensor("ident_in", [128, 128], BF16, kind="ExternalInput")
    x_out = nc.dram_tensor("x_out", [T_CORE, D], F32, kind="ExternalOutput")
    scr_q = nc.dram_tensor("ffn_scr", [6656, I_FFN], BF16)
    scr_guT = nc.dram_tensor("ffn_guT", [8, 128, 5632], BF16)
    scr_dnT = nc.dram_tensor("ffn_dnT", [NI, 128, D], BF16)
    import os
    _dbg = os.environ.get("FFN_DBG") == "1"
    hq_dram = nc.dram_tensor("hq_scr", [T_CORE, I_FFN], BF16,
                             kind="ExternalOutput" if _dbg else "Internal")

    ident = kb.sb("ident", [128, 128], BF16)
    kb.op("Y", lambda Y: Y.dma_start(out=ident.h[:], in_=id_in.ap()), 16)
    kb.did_w("Y", ident)

    qsc = quant_scratch(kb, "sh", I_FFN)
    big_f32 = kb.sb("big_f32", [128, I_FFN], F32)
    big_bf_a = kb.sb("big_bf_a", [128, I_FFN], BF16)
    big_bf_b = kb.sb("big_bf_b", [128, I_FFN], BF16)

    emit_l0ffn_weight_prep(kb, w_gate, w_up, w_down, scr_q, scr_guT, scr_dnT,
                           qsc, big_f32, big_bf_a, big_bf_b)

    xtiles = [kb.sb(f"xtile{i}", [128, D], F32) for i in range(9)]
    xslot = [0]

    def load_x(kb, tt):
        buf = xtiles[xslot[0] % len(xtiles)]
        xslot[0] += 1
        kb.wr("Y", buf)
        kb.op("Y", lambda Y, tt=tt, buf=buf: Y.dma_start(
            out=buf[:, 0:D], in_=x_in.ap()[tt * 128:(tt + 1) * 128, :]), 16)
        kb.did_w("Y", buf)
        return buf

    def store_x(kb, tt, buf):
        kb.rd("Y", buf)
        kb.op("Y", lambda Y, tt=tt, buf=buf: Y.dma_start(
            out=x_out.ap()[tt * 128:(tt + 1) * 128, :], in_=buf[:, 0:D]), 16)
        kb.did_r("Y", buf)

    emit_l0ffn(kb, load_x, store_x, scr_guT, scr_dnT, hq_dram, ident,
               qsc, big_f32, big_bf_a, big_bf_b)
    return kb.finalize()


def build_l0attn_launch():
    kb = KB()
    nc = kb.nc
    x_in = nc.dram_tensor("x_in", [T_CORE, D], F32, kind="ExternalInput")
    w_qkv = nc.dram_tensor("w_qkv", [D, 3072], F32, kind="ExternalInput")
    b_qkv = nc.dram_tensor("b_qkv", [3072], F32, kind="ExternalInput")
    w_out = nc.dram_tensor("w_out", [D, D], F32, kind="ExternalInput")
    b_out = nc.dram_tensor("b_out", [D], F32, kind="ExternalInput")
    id_in = nc.dram_tensor("ident_in", [128, 128], BF16, kind="ExternalInput")
    mask_in = nc.dram_tensor("mask_in", [4, 128, 512], BF16,
                             kind="ExternalInput")
    x_out = nc.dram_tensor("x_out", [T_CORE, D], F32, kind="ExternalOutput")
    qkT_dram = nc.dram_tensor("qkT_scr", [2048, T_CORE], BF16)
    v_dram = nc.dram_tensor("v_scr", [T_CORE, D], BF16)
    ctxT_dram = nc.dram_tensor("ctxT_scr", [D, T_CORE], BF16)

    ident = kb.sb("ident", [128, 128], BF16)
    maskb = kb.sb("maskb", [128, 4 * 512], BF16)
    kb.op("Y", lambda Y: Y.dma_start(out=ident.h[:], in_=id_in.ap()), 16)
    for a in range(4):
        kb.op("Y", lambda Y, a=a: Y.dma_start(
            out=maskb.h[:, a * 512:(a + 1) * 512], in_=mask_in.ap()[a]), 16)
    kb.did_w("Y", ident, maskb)
    mask4 = [maskb.h[:, a * 512:(a + 1) * 512] for a in range(4)]
    lnscr = kb.sb("lnscr", [128, D], BF16)
    stats = kb.sb("stats", [128, 48], F32)

    xtiles = [kb.sb(f"xtile{i}", [128, D], F32) for i in range(5)]
    xslot = [0]

    def load_x(kb, tt):
        buf = xtiles[xslot[0] % len(xtiles)]
        xslot[0] += 1
        kb.wr("Y", buf)
        kb.op("Y", lambda Y, tt=tt, buf=buf: Y.dma_start(
            out=buf[:, 0:D], in_=x_in.ap()[tt * 128:(tt + 1) * 128, :]), 16)
        kb.did_w("Y", buf)
        return buf

    def store_x(kb, tt, buf):
        kb.rd("Y", buf)
        kb.op("Y", lambda Y, tt=tt, buf=buf: Y.dma_start(
            out=x_out.ap()[tt * 128:(tt + 1) * 128, :], in_=buf[:, 0:D]), 16)
        kb.did_r("Y", buf)

    emit_l0attn(kb, load_x, store_x, w_qkv, b_qkv, w_out, b_out,
                qkT_dram, v_dram, ctxT_dram, ident, mask4, lnscr, stats)
    return kb.finalize()


def make_masks():
    m = np.zeros((4, 128, 512), np.float32)
    for a in range(4):
        for k in range(128):
            m[a, k, :a * 128 + k] = -1e9
    return m.astype(ml_dtypes.bfloat16)


def run_l0attn(x_full, w_qkv, b_qkv, w_out, b_out):
    nc = build_l0attn_launch()
    masks = make_masks()
    in_maps = [{"x_in": np.ascontiguousarray(x_full[c]), "w_qkv": w_qkv,
                "b_qkv": b_qkv, "w_out": w_out, "b_out": b_out,
                "ident_in": IDENT_BF16, "mask_in": masks} for c in range(8)]
    res = run_bass_kernel_spmd(nc, in_maps, core_ids=list(range(8)))
    return np.stack([res.results[c]["x_out"] for c in range(8)])


def build_mamba_launch():
    kb = KB()
    nc = kb.nc
    x_in = nc.dram_tensor("x_in", [T_CORE, D], F32, kind="ExternalInput")
    win = nc.dram_tensor("in_proj_w", [D, 2 * DIN], F32, kind="ExternalInput")
    conv_w = nc.dram_tensor("conv_w", [DIN, 1, 4], F32, kind="ExternalInput")
    conv_b = nc.dram_tensor("conv_b", [DIN], F32, kind="ExternalInput")
    xproj = nc.dram_tensor("x_proj_w", [DIN, 33], F32, kind="ExternalInput")
    dtw = nc.dram_tensor("dt_proj_w", [1, DIN], F32, kind="ExternalInput")
    dtb = nc.dram_tensor("dt_proj_b", [DIN], F32, kind="ExternalInput")
    alog = nc.dram_tensor("A_log", [DIN, 16], F32, kind="ExternalInput")
    dpar = nc.dram_tensor("D_param", [DIN], F32, kind="ExternalInput")
    cw = nc.dram_tensor("c_proj_w", [DIN, D], F32, kind="ExternalInput")
    id_in = nc.dram_tensor("ident_in", [128, 128], BF16, kind="ExternalInput")
    x_out = nc.dram_tensor("x_out", [T_CORE, D], F32, kind="ExternalOutput")
    z_dram = nc.dram_tensor("z_scr", [DIN, T_CORE], BF16)

    ident = kb.sb("ident", [128, 128], BF16)
    kb.op("Y", lambda Y: Y.dma_start(out=ident.h[:], in_=id_in.ap()), 16)
    kb.did_w("Y", ident)
    lnscr = kb.sb("lnscr", [128, D], BF16)
    stats = kb.sb("stats", [128, 48], F32)
    xtiles = [kb.sb(f"xtile{i}", [128, D], F32) for i in range(9)]
    xslot = [0]

    def load_x(kb, tt):
        buf = xtiles[xslot[0] % len(xtiles)]
        xslot[0] += 1
        kb.wr("Y", buf)
        kb.op("Y", lambda Y, tt=tt, buf=buf: Y.dma_start(
            out=buf[:, 0:D], in_=x_in.ap()[tt * 128:(tt + 1) * 128, :]), 16)
        kb.did_w("Y", buf)
        return buf

    def store_x(kb, tt, buf):
        kb.rd("Y", buf)
        kb.op("Y", lambda Y, tt=tt, buf=buf: Y.dma_start(
            out=x_out.ap()[tt * 128:(tt + 1) * 128, :], in_=buf[:, 0:D]), 16)
        kb.did_r("Y", buf)

    emit_l1mamba(kb, load_x, store_x, win, conv_w, conv_b, xproj, dtw, dtb,
                 alog, dpar, cw, z_dram, ident, lnscr, stats)
    return kb.finalize()


def run_mamba(x_full, z):
    nc = build_mamba_launch()
    in_maps = [{"x_in": np.ascontiguousarray(x_full[c]),
                "in_proj_w": z["L1_attn_in_proj_w"],
                "conv_w": z["L1_attn_conv_w"], "conv_b": z["L1_attn_conv_b"],
                "x_proj_w": z["L1_attn_x_proj_w"],
                "dt_proj_w": z["L1_attn_dt_proj_w"],
                "dt_proj_b": z["L1_attn_dt_proj_b"],
                "A_log": z["L1_attn_A_log"], "D_param": z["L1_attn_D_param"],
                "c_proj_w": z["L1_attn_c_proj_w"],
                "ident_in": IDENT_BF16} for c in range(8)]
    res = run_bass_kernel_spmd(nc, in_maps, core_ids=list(range(8)))
    return np.stack([res.results[c]["x_out"] for c in range(8)])


def _mk_loadstore(kb, x_in, x_out, ntiles=9):
    xtiles = [kb.sb(f"xtile{i}", [128, D], F32) for i in range(ntiles)]
    xslot = [0]

    def load_x(kb, tt):
        buf = xtiles[xslot[0] % len(xtiles)]
        xslot[0] += 1
        kb.wr("Y", buf)
        kb.op("Y", lambda Y, tt=tt, buf=buf: Y.dma_start(
            out=buf[:, 0:D], in_=x_in.ap()[tt * 128:(tt + 1) * 128, :]), 16)
        kb.did_w("Y", buf)
        return buf

    def store_x(kb, tt, buf):
        kb.rd("Y", buf)
        kb.op("Y", lambda Y, tt=tt, buf=buf: Y.dma_start(
            out=x_out.ap()[tt * 128:(tt + 1) * 128, :], in_=buf[:, 0:D]), 16)
        kb.did_r("Y", buf)
    return load_x, store_x


def build_lvpart_launch():
    kb = KB()
    nc = kb.nc
    x_in = nc.dram_tensor("x_in", [T_CORE, D], F32, kind="ExternalInput")
    out = nc.dram_tensor("lv_out", [1, 1], F32, kind="ExternalOutput")
    lnscr = kb.sb("lnscr", [128, D], BF16)
    stats = kb.sb("stats", [128, 48], F32)
    load_x, _ = _mk_loadstore(kb, x_in, None)
    emit_lv_partial(kb, load_x, out, lnscr, stats)
    return kb.finalize()


def build_l1ffn_launch():
    kb = KB()
    nc = kb.nc
    x_in = nc.dram_tensor("x_in", [T_CORE, D], F32, kind="ExternalInput")
    qw_d = nc.dram_tensor("q_proj_w", [128, D], F32, kind="ExternalInput")
    rw_d = nc.dram_tensor("router_w", [48, 128], F32, kind="ExternalInput")
    ak_d = nc.dram_tensor("aux_keys", [4096, 128], F32, kind="ExternalInput")
    av_d = nc.dram_tensor("aux_values", [4096, D], F32, kind="ExternalInput")
    rel_d = nc.dram_tensor("rel", [4096], F32, kind="ExternalInput")
    w1_d = nc.dram_tensor("unc_w1", [256, D], F32, kind="ExternalInput")
    b1_d = nc.dram_tensor("unc_b1", [256], F32, kind="ExternalInput")
    w2_d = nc.dram_tensor("unc_w2", [1, 256], F32, kind="ExternalInput")
    b2gwgb_d = nc.dram_tensor("b2gwgb", [3], F32, kind="ExternalInput")
    partials8_d = nc.dram_tensor("partials8", [8], F32, kind="ExternalInput")
    id_in = nc.dram_tensor("ident_in", [128, 128], BF16, kind="ExternalInput")
    x_out = nc.dram_tensor("x_out", [T_CORE, D], F32, kind="ExternalOutput")
    w_dram = nc.dram_tensor("w_scr", [T_CORE, 4096], BF16)
    scr2_d = nc.dram_tensor("scr2", [5120, D], BF16)

    ident = kb.sb("ident", [128, 128], BF16)
    kb.op("Y", lambda Y: Y.dma_start(out=ident.h[:], in_=id_in.ap()), 16)
    kb.did_w("Y", ident)
    lnscr = kb.sb("lnscr", [128, D], BF16)
    stats = kb.sb("stats", [128, 48], F32)
    load_x, store_x = _mk_loadstore(kb, x_in, x_out)
    emit_l1ffn(kb, load_x, store_x, qw_d, rw_d, ak_d, av_d, rel_d, w1_d,
               b1_d, w2_d, b2gwgb_d, partials8_d, w_dram, scr2_d, ident,
               lnscr, stats)
    return kb.finalize()


def run_l1ffn(x_full, z):
    ncA = build_lvpart_launch()
    in_mapsA = [{"x_in": np.ascontiguousarray(x_full[c])} for c in range(8)]
    resA = run_bass_kernel_spmd(ncA, in_mapsA, core_ids=list(range(8)))
    partials = np.concatenate([resA.results[c]["lv_out"].reshape(1)
                               for c in range(8)])
    ncB = build_l1ffn_launch()
    b2gwgb = np.stack([z["L1_ffn_unc_b2"].reshape(()),
                       z["L1_ffn_gate_w1"].reshape(()),
                       z["L1_ffn_gate_bias"].reshape(())]).astype(np.float32)
    in_mapsB = [{"x_in": np.ascontiguousarray(x_full[c]),
                 "q_proj_w": z["L1_ffn_q_proj_w"],
                 "router_w": z["L1_ffn_router_w"],
                 "aux_keys": z["L1_ffn_aux_keys"],
                 "aux_values": z["L1_ffn_aux_values"],
                 "rel": z["L1_ffn_reliability_mask"],
                 "unc_w1": z["L1_ffn_unc_w1"], "unc_b1": z["L1_ffn_unc_b1"],
                 "unc_w2": z["L1_ffn_unc_w2"], "b2gwgb": b2gwgb,
                 "partials8": partials, "ident_in": IDENT_BF16}
                for c in range(8)]
    resB = run_bass_kernel_spmd(ncB, in_mapsB, core_ids=list(range(8)))
    return np.stack([resB.results[c]["x_out"] for c in range(8)])


IDENT_BF16 = np.eye(128, dtype=ml_dtypes.bfloat16)


def run_l0ffn(x_full, w_gate, w_up, w_down):
    nc = build_l0ffn_launch()
    in_maps = [{"x_in": np.ascontiguousarray(x_full[c]),
                "w_gate": w_gate, "w_up": w_up, "w_down": w_down,
                "ident_in": IDENT_BF16} for c in range(8)]
    res = run_bass_kernel_spmd(nc, in_maps, core_ids=list(range(8)))
    return np.stack([res.results[c]["x_out"] for c in range(8)])



def _common_pre(kb, ntiles=9, pool_kb=150):
    nc = kb.nc
    kb.arena_init(pool_kb)
    id_in = nc.dram_tensor("ident_in", [128, 128], BF16, kind="ExternalInput")
    ident = kb.sb("ident", [128, 128], BF16)
    kb.op("Y", lambda Y: Y.dma_start(out=ident.h[:], in_=id_in.ap()), 16)
    kb.did_w("Y", ident)
    lnscr = kb.sb("lnscr", [128, D], BF16)
    stats = kb.sb("stats", [128, 48], F32)
    return ident, lnscr, stats


def _xio(kb, x_in_h=None, x_out_h=None, ntiles=9):
    xtiles = [kb.sb(f"xtile{i}", [128, D], F32) for i in range(ntiles)]
    state = {"in": x_in_h, "out": x_out_h, "slot": 0}

    def load_x(kb, tt):
        buf = xtiles[state["slot"] % len(xtiles)]
        state["slot"] += 1
        h = state["in"]
        kb.wr("Y", buf)
        kb.op("Y", lambda Y, tt=tt, buf=buf, h=h: Y.dma_start(
            out=buf[:, 0:D],
            in_=h.ap()[tt * 128:(tt + 1) * 128, :]), 16)
        kb.did_w("Y", buf)
        return buf

    def store_x(kb, tt, buf):
        h = state["out"]
        kb.rd("Y", buf)
        kb.op("Y", lambda Y, tt=tt, buf=buf, h=h: Y.dma_start(
            out=h.ap()[tt * 128:(tt + 1) * 128, :],
            in_=buf[:, 0:D]), 16)
        kb.did_r("Y", buf)
    return load_x, store_x, state


def build_launch1():
    kb = KB()
    nc = kb.nc
    x_in = nc.dram_tensor("x_in", [T_CORE, D], F32, kind="ExternalInput")
    w_gate = nc.dram_tensor("w_gate", [I_FFN, D], F32, kind="ExternalInput")
    w_up = nc.dram_tensor("w_up", [I_FFN, D], F32, kind="ExternalInput")
    w_down = nc.dram_tensor("w_down", [D, I_FFN], F32, kind="ExternalInput")
    w_qkv = nc.dram_tensor("w_qkv", [D, 3072], F32, kind="ExternalInput")
    b_qkv = nc.dram_tensor("b_qkv", [3072], F32, kind="ExternalInput")
    w_out = nc.dram_tensor("w_out", [D, D], F32, kind="ExternalInput")
    b_out = nc.dram_tensor("b_out", [D], F32, kind="ExternalInput")
    mask_in = nc.dram_tensor("mask_in", [4, 128, 512], BF16,
                             kind="ExternalInput")
    x_out3 = nc.dram_tensor("x_out3", [T_CORE, D], F32, kind="ExternalOutput")
    lv_out = nc.dram_tensor("lv_out", [1, 1], F32, kind="ExternalOutput")
    x_s1 = nc.dram_tensor("x_s1", [T_CORE, D], F32)
    x_s2 = nc.dram_tensor("x_s2", [T_CORE, D], F32)
    scr_q = nc.dram_tensor("ffn_scr", [6656, I_FFN], BF16)
    scr_guT = nc.dram_tensor("ffn_guT", [8, 128, 5632], BF16)
    scr_dnT = nc.dram_tensor("ffn_dnT", [NI, 128, D], BF16)
    hq_dram = nc.dram_tensor("hq_scr", [T_CORE, I_FFN], BF16)
    qkT_dram = nc.dram_tensor("qkT_scr", [2048, T_CORE], BF16)
    v_dram = nc.dram_tensor("v_scr", [T_CORE, D], BF16)
    ctxT_dram = nc.dram_tensor("ctxT_scr", [D, T_CORE], BF16)

    ident, lnscr, stats = _common_pre(kb)
    maskb = kb.sb("maskb", [128, 4 * 512], BF16)
    for a in range(4):
        kb.op("Y", lambda Y, a=a: Y.dma_start(
            out=maskb.h[:, a * 512:(a + 1) * 512], in_=mask_in.ap()[a]), 16)
    kb.did_w("Y", maskb)
    mask4 = [maskb.h[:, a * 512:(a + 1) * 512] for a in range(4)]
    load_x, store_x, state = _xio(kb)

    kb.arena_begin()
    qsc = quant_scratch(kb, "sh", I_FFN)
    big_f32 = kb.sb("big_f32", [128, I_FFN], F32)
    big_bf_a = kb.sb("big_bf_a", [128, I_FFN], BF16)
    big_bf_b = kb.sb("big_bf_b", [128, I_FFN], BF16)
    emit_l0ffn_weight_prep(kb, w_gate, w_up, w_down, scr_q, scr_guT, scr_dnT,
                           qsc, big_f32, big_bf_a, big_bf_b)
    state["in"], state["out"] = x_in, x_s1
    emit_l0ffn(kb, load_x, store_x, scr_guT, scr_dnT, hq_dram, ident,
               qsc, big_f32, big_bf_a, big_bf_b, grp_tiles=4)
    kb.arena_end(); kb.barrier()

    kb.arena_begin()
    state["in"], state["out"] = x_s1, x_s2
    emit_l0attn(kb, load_x, store_x, w_qkv, b_qkv, w_out, b_out,
                qkT_dram, v_dram, ctxT_dram, ident, mask4, lnscr, stats)
    kb.arena_end(); kb.barrier()

    kb.arena_begin()
    qsc = quant_scratch(kb, "sh2", I_FFN)
    big_f32 = kb.sb("big_f322", [128, I_FFN], F32)
    big_bf_a = kb.sb("big_bf_a2", [128, I_FFN], BF16)
    big_bf_b = kb.sb("big_bf_b2", [128, I_FFN], BF16)
    state["in"], state["out"] = x_s2, x_out3
    emit_l0ffn(kb, load_x, store_x, scr_guT, scr_dnT, hq_dram, ident,
               qsc, big_f32, big_bf_a, big_bf_b, grp_tiles=4)
    kb.arena_end(); kb.barrier()

    kb.arena_begin()
    state["in"] = x_out3
    emit_lv_partial(kb, load_x, lv_out, lnscr, stats)
    kb.arena_end()
    return kb.finalize()


def build_launch2():
    kb = KB()
    nc = kb.nc
    x_in = nc.dram_tensor("x_in", [T_CORE, D], F32, kind="ExternalInput")
    qw_d = nc.dram_tensor("q_proj_w", [128, D], F32, kind="ExternalInput")
    rw_d = nc.dram_tensor("router_w", [48, 128], F32, kind="ExternalInput")
    ak_d = nc.dram_tensor("aux_keys", [4096, 128], F32, kind="ExternalInput")
    av_d = nc.dram_tensor("aux_values", [4096, D], F32, kind="ExternalInput")
    rel_d = nc.dram_tensor("rel", [4096], F32, kind="ExternalInput")
    w1_d = nc.dram_tensor("unc_w1", [256, D], F32, kind="ExternalInput")
    b1_d = nc.dram_tensor("unc_b1", [256], F32, kind="ExternalInput")
    w2_d = nc.dram_tensor("unc_w2", [1, 256], F32, kind="ExternalInput")
    b2gwgb_d = nc.dram_tensor("b2gwgb", [3], F32, kind="ExternalInput")
    partials8_d = nc.dram_tensor("partials8", [8], F32, kind="ExternalInput")
    win = nc.dram_tensor("in_proj_w", [D, 2 * DIN], F32, kind="ExternalInput")
    conv_w = nc.dram_tensor("conv_w", [DIN, 1, 4], F32, kind="ExternalInput")
    conv_b = nc.dram_tensor("conv_b", [DIN], F32, kind="ExternalInput")
    xproj = nc.dram_tensor("x_proj_w", [DIN, 33], F32, kind="ExternalInput")
    dtw = nc.dram_tensor("dt_proj_w", [1, DIN], F32, kind="ExternalInput")
    dtb = nc.dram_tensor("dt_proj_b", [DIN], F32, kind="ExternalInput")
    alog = nc.dram_tensor("A_log", [DIN, 16], F32, kind="ExternalInput")
    dpar = nc.dram_tensor("D_param", [DIN], F32, kind="ExternalInput")
    cw = nc.dram_tensor("c_proj_w", [DIN, D], F32, kind="ExternalInput")
    x_out5 = nc.dram_tensor("x_out5", [T_CORE, D], F32, kind="ExternalOutput")
    lv_out = nc.dram_tensor("lv_out", [1, 1], F32, kind="ExternalOutput")
    x_s1 = nc.dram_tensor("x_s1", [T_CORE, D], F32)
    w_dram = nc.dram_tensor("w_scr", [T_CORE, 4096], BF16)
    scr2_d = nc.dram_tensor("scr2", [5120, D], BF16)
    z_dram = nc.dram_tensor("z_scr", [DIN, T_CORE], BF16)

    ident, lnscr, stats = _common_pre(kb)
    load_x, store_x, state = _xio(kb)

    kb.arena_begin()
    state["in"], state["out"] = x_in, x_s1
    emit_l1ffn(kb, load_x, store_x, qw_d, rw_d, ak_d, av_d, rel_d, w1_d,
               b1_d, w2_d, b2gwgb_d, partials8_d, w_dram, scr2_d, ident,
               lnscr, stats)
    kb.arena_end(); kb.barrier()

    kb.arena_begin()
    state["in"], state["out"] = x_s1, x_out5
    emit_l1mamba(kb, load_x, store_x, win, conv_w, conv_b, xproj, dtw, dtb,
                 alog, dpar, cw, z_dram, ident, lnscr, stats)
    kb.arena_end(); kb.barrier()

    kb.arena_begin()
    state["in"] = x_out5
    emit_lv_partial(kb, load_x, lv_out, lnscr, stats)
    kb.arena_end()
    return kb.finalize()


def build_launch3():
    kb = KB()
    nc = kb.nc
    x_in = nc.dram_tensor("x_in", [T_CORE, D], F32, kind="ExternalInput")
    qw_d = nc.dram_tensor("q_proj_w", [128, D], F32, kind="ExternalInput")
    rw_d = nc.dram_tensor("router_w", [48, 128], F32, kind="ExternalInput")
    ak_d = nc.dram_tensor("aux_keys", [4096, 128], F32, kind="ExternalInput")
    av_d = nc.dram_tensor("aux_values", [4096, D], F32, kind="ExternalInput")
    rel_d = nc.dram_tensor("rel", [4096], F32, kind="ExternalInput")
    w1_d = nc.dram_tensor("unc_w1", [256, D], F32, kind="ExternalInput")
    b1_d = nc.dram_tensor("unc_b1", [256], F32, kind="ExternalInput")
    w2_d = nc.dram_tensor("unc_w2", [1, 256], F32, kind="ExternalInput")
    b2gwgb_d = nc.dram_tensor("b2gwgb", [3], F32, kind="ExternalInput")
    partials8_d = nc.dram_tensor("partials8", [8], F32, kind="ExternalInput")
    x_out = nc.dram_tensor("x_out", [T_CORE, D], F32, kind="ExternalOutput")
    w_dram = nc.dram_tensor("w_scr", [T_CORE, 4096], BF16)
    scr2_d = nc.dram_tensor("scr2", [5120, D], BF16)
    ident, lnscr, stats = _common_pre(kb)
    load_x, store_x, state = _xio(kb)
    kb.arena_begin()
    state["in"], state["out"] = x_in, x_out
    emit_l1ffn(kb, load_x, store_x, qw_d, rw_d, ak_d, av_d, rel_d, w1_d,
               b1_d, w2_d, b2gwgb_d, partials8_d, w_dram, scr2_d, ident,
               lnscr, stats)
    kb.arena_end()
    return kb.finalize()


def kernel(**inputs):
    z = inputs
    x = np.ascontiguousarray(np.asarray(z["L0_attn_x"], np.float32))
    masks = make_masks()
    b2gwgb = np.stack([np.asarray(z["L1_ffn_unc_b2"]).reshape(-1)[0],
                       np.asarray(z["L1_ffn_gate_w1"]).reshape(-1)[0],
                       np.asarray(z["L1_ffn_gate_bias"]).reshape(-1)[0]]
                      ).astype(np.float32)

    nc1 = build_launch1()
    im = [{"x_in": x[c], "w_gate": z["L0_ffn_w_gate"],
           "w_up": z["L0_ffn_w_up"], "w_down": z["L0_ffn_w_down"],
           "w_qkv": z["L0_attn_w_qkv"], "b_qkv": z["L0_attn_b_qkv"],
           "w_out": z["L0_attn_w_out"], "b_out": z["L0_attn_b_out"],
           "mask_in": masks, "ident_in": IDENT_BF16} for c in range(8)]
    r1 = run_bass_kernel_spmd(nc1, im, core_ids=list(range(8)))
    x3 = [r1.results[c]["x_out3"] for c in range(8)]
    p1 = np.concatenate([r1.results[c]["lv_out"].reshape(1)
                         for c in range(8)]).astype(np.float32)

    nc2 = build_launch2()
    im = [{"x_in": x3[c], "q_proj_w": z["L1_ffn_q_proj_w"],
           "router_w": z["L1_ffn_router_w"], "aux_keys": z["L1_ffn_aux_keys"],
           "aux_values": z["L1_ffn_aux_values"],
           "rel": z["L1_ffn_reliability_mask"],
           "unc_w1": z["L1_ffn_unc_w1"], "unc_b1": z["L1_ffn_unc_b1"],
           "unc_w2": z["L1_ffn_unc_w2"], "b2gwgb": b2gwgb, "partials8": p1,
           "in_proj_w": z["L1_attn_in_proj_w"], "conv_w": z["L1_attn_conv_w"],
           "conv_b": z["L1_attn_conv_b"], "x_proj_w": z["L1_attn_x_proj_w"],
           "dt_proj_w": z["L1_attn_dt_proj_w"],
           "dt_proj_b": z["L1_attn_dt_proj_b"], "A_log": z["L1_attn_A_log"],
           "D_param": z["L1_attn_D_param"], "c_proj_w": z["L1_attn_c_proj_w"],
           "ident_in": IDENT_BF16} for c in range(8)]
    r2 = run_bass_kernel_spmd(nc2, im, core_ids=list(range(8)))
    x5 = [r2.results[c]["x_out5"] for c in range(8)]
    p2 = np.concatenate([r2.results[c]["lv_out"].reshape(1)
                         for c in range(8)]).astype(np.float32)

    nc3 = build_launch3()
    im = [{"x_in": x5[c], "q_proj_w": z["L1_ffn_q_proj_w"],
           "router_w": z["L1_ffn_router_w"], "aux_keys": z["L1_ffn_aux_keys"],
           "aux_values": z["L1_ffn_aux_values"],
           "rel": z["L1_ffn_reliability_mask"],
           "unc_w1": z["L1_ffn_unc_w1"], "unc_b1": z["L1_ffn_unc_b1"],
           "unc_w2": z["L1_ffn_unc_w2"], "b2gwgb": b2gwgb, "partials8": p2,
           "ident_in": IDENT_BF16} for c in range(8)]
    r3 = run_bass_kernel_spmd(nc3, im, core_ids=list(range(8)))
    out = np.stack([r3.results[c]["x_out"] for c in range(8)])
    return np.ascontiguousarray(out.astype(np.float32))


# version 07
# speedup vs baseline: 1.0833x; 1.0721x over previous
import numpy as np
import ml_dtypes
import concourse.bass as bass
import concourse.mybir as mybir
from concourse.bass_utils import run_bass_kernel_spmd

F32 = mybir.dt.float32
BF16 = mybir.dt.bfloat16
I16 = mybir.dt.int16
I32 = mybir.dt.int32
AF = mybir.ActivationFunctionType
ALU = mybir.AluOpType
AX = mybir.AxisListType

T_CORE = 4096
D = 1024
NT = T_CORE // 128
LN_EPS = 1e-5
I_FFN = 2816
NI = I_FFN // 128

_SEC = {"T": "tensor", "V": "vector", "S": "scalar", "Y": "sync", "G": "gpsimd"}


class Buf:
    def __init__(self, h):
        self.h = h
        self.prod = None
        self.readers = {}

    def __getitem__(self, idx):
        return self.h[idx]

    @property
    def ap(self):
        try:
            return self.h[:]
        except Exception:
            return self.h


class KB:
    def __init__(self):
        nc = bass.Bass("TRN2", target_bir_lowering=False, debug=False)
        self.nc = nc
        self.sem = {k: nc.alloc_semaphore("sem_" + k) for k in _SEC}
        self.cnt = {k: 0 for k in _SEC}
        self.waited = {c: {p: 0 for p in _SEC} for c in _SEC}
        self.prog = {k: [] for k in _SEC}
        self.psum = [Buf(nc.alloc_psum_tensor(f"ps{i}", [128, 512], F32))
                     for i in range(8)]
        self._uid = 0
        self._consts = {}
        self._const_tile = nc.alloc_sbuf_tensor("kb_consts", [128, 16], F32)

    def constcol(self, value):
        if value not in self._consts:
            idx = len(self._consts)
            assert idx < 16
            ap = self._const_tile[:, idx:idx + 1]
            self.op("V", lambda V, ap=ap, value=value: V.memset(ap, value))
            self.wait("S", "V")
            self._consts[value] = ap
        return self._consts[value]

    def raw(self, e, fn):
        self.prog[e].append(fn)

    def op(self, e, fn, inc=1):
        if inc:
            sem = self.sem[e]
            self.prog[e].append(lambda eng, fn=fn, sem=sem, inc=inc:
                                fn(eng).then_inc(sem, inc))
            self.cnt[e] += inc
        else:
            self.prog[e].append(fn)
        return self.cnt[e]

    def wait(self, consumer, producer, val=None):
        if val is None:
            val = self.cnt[producer]
        if val <= 0 or self.waited[consumer][producer] >= val:
            return
        sem = self.sem[producer]
        self.prog[consumer].append(
            lambda eng, sem=sem, val=val: eng.wait_ge(sem, val))
        self.waited[consumer][producer] = val

    def rd(self, e, *bufs):
        for b in bufs:
            if b.prod is not None:
                if b.prod[0] != e or not b.prod[2] or e in ("Y", "G"):
                    self.wait(e, b.prod[0], b.prod[1])

    def wr(self, e, *bufs):
        for b in bufs:
            if b.prod is not None and b.prod[0] != e:
                self.wait(e, b.prod[0], b.prod[1])
            for re, rc in b.readers.items():
                if re != e:
                    self.wait(e, re, rc)

    def did_w(self, e, *bufs, wide=False):
        for b in bufs:
            b.prod = (e, self.cnt[e], wide)
            b.readers = {}

    def did_r(self, e, *bufs):
        for b in bufs:
            b.readers[e] = self.cnt[e]

    def eop(self, e, fn, reads=(), writes=(), wide=False, inc=1):
        for b in reads:
            self.rd(e, b)
        self.wr(e, *writes)
        self.op(e, fn, inc)
        self.did_w(e, *writes, wide=wide)
        for b in reads:
            self.did_r(e, b)

    def arena_init(self, kbytes):
        self._pool = self.nc.alloc_sbuf_tensor("kb_pool",
                                               [128, kbytes * 1024 // 4], F32)
        self._pool_off = 0
        self._arena_on = False

    def arena_begin(self):
        self._arena_on = True
        self._pool_off = 0

    def arena_end(self):
        self._arena_on = False

    def barrier(self):
        for c in _SEC:
            for p in _SEC:
                self.wait(c, p)

    def sb(self, name, shape, dtype=F32):
        if getattr(self, "_arena_on", False):
            esz = 2 if dtype in (BF16, I16) else 4
            ncols = int(np.prod(shape[1:]))
            nbytes = (ncols * esz + 63) // 64 * 64
            off = self._pool_off
            self._pool_off = off + nbytes
            maxb = self._pool.shape[1] * 4
            assert self._pool_off <= maxb, (
                f"arena overflow at {name}: {self._pool_off} > {maxb}")
            assert shape[0] <= 128 and len(shape) == 2
            view = self._pool[:].bitcast(dtype)[
                0:shape[0], off // esz: off // esz + ncols]
            b = Buf(None)
            b.h = view
            return b
        return Buf(self.nc.alloc_sbuf_tensor(name, shape, dtype))

    def psum_t(self, name, shape, dtype=F32):
        return Buf(self.nc.alloc_psum_tensor(name, shape, dtype))

    def dram(self, name, shape, dtype=F32, kind="Internal"):
        return self.nc.dram_tensor(name, shape, dtype, kind=kind)

    def finalize(self):
        for e in _SEC:
            v = self.cnt[e]
            if v > 0:
                sem = self.sem[e]
                self.prog[e].append(
                    lambda eng, sem=sem, v=v: eng.wait_ge(sem, v))
        with self.nc.Block() as blk:
            for e, sec in _SEC.items():
                thunks = self.prog[e]
                if not thunks:
                    continue
                getattr(blk, sec)(lambda eng, thunks=thunks:
                                  [t(eng) for t in thunks])
        return self.nc



def emit_ln_stats(kb, x, scratch, stats, col, xcols=None):
    x_ap = x.ap if xcols is None else x[:, xcols[0]:xcols[1]]
    st = stats.h
    kb.eop("S", lambda S, x_ap=x_ap, col=col: S.activation(
        scratch.ap, x_ap, AF.Square, accum_out=st[:, 8 + col:9 + col]),
        reads=(x,), writes=(scratch, stats), wide=True)
    kb.eop("V", lambda V, x_ap=x_ap, col=col: V.reduce_sum(
        st[:, col:col + 1], x_ap, AX.X), reads=(x,), writes=(stats,))


def emit_ln_finish(kb, stats, n):
    st = stats.h
    s1, s2 = st[:, 0:n], st[:, 8:8 + n]
    m, v = st[:, 16:16 + n], st[:, 24:24 + n]
    r, t = st[:, 32:32 + n], st[:, 40:40 + n]
    eps_ap = kb.constcol(LN_EPS)
    kb.eop("V", lambda V: V.tensor_scalar(m, s1, 1.0 / D, None, ALU.mult),
           reads=(stats,), writes=(stats,))
    kb.eop("V", lambda V: V.tensor_scalar(t, s2, 1.0 / D, None, ALU.mult),
           reads=(stats,), writes=(stats,))
    kb.eop("V", lambda V: V.tensor_tensor(v, m, m, ALU.mult),
           reads=(stats,), writes=(stats,))
    kb.eop("V", lambda V: V.tensor_tensor(v, t, v, ALU.subtract),
           reads=(stats,), writes=(stats,))
    kb.eop("S", lambda S: S.activation(t, v, AF.Ln, bias=eps_ap),
           reads=(stats,), writes=(stats,))
    kb.eop("S", lambda S: S.activation(r, t, AF.Exp, scale=-0.5),
           reads=(stats,), writes=(stats,))


def emit_ln_apply(kb, x, out, scratch, stats, col, xcols=None):
    x_ap = x.ap if xcols is None else x[:, xcols[0]:xcols[1]]
    st = stats.h
    kb.eop("V", lambda V, x_ap=x_ap, col=col: V.tensor_single_scalar(
        scratch.ap, x_ap, st[:, 16 + col:17 + col], ALU.subtract),
        reads=(x, stats), writes=(scratch,), wide=True)
    kb.eop("V", lambda V, col=col: V.tensor_single_scalar(
        out.ap, scratch.ap, st[:, 32 + col:33 + col], ALU.mult),
        reads=(scratch, stats), writes=(out,), wide=True)


def quant_scratch(kb, tag, F):
    return {
        "mx": kb.sb(f"q_mx_{tag}", [128, F // 16], F32),
        "rec": kb.sb(f"q_rec_{tag}", [128, F // 16], F32),
        "ab": kb.sb(f"q_ab_{tag}", [128, F], BF16),
        "v": kb.sb(f"q_v_{tag}", [128, F], BF16),
        "lo": kb.sb(f"q_lo_{tag}", [128, F], BF16),
        "sg": kb.sb(f"q_sg_{tag}", [128, F], BF16),
    }


def emit_fp4_quant(kb, x, out, F, sc):
    B = F // 16
    P = 128
    W = F >= 2048
    x_ap = x[:, 0:F]
    x3 = x_ap.rearrange("p (b k) -> p b k", k=16)
    mx, rec = sc["mx"], sc["rec"]
    ab, v, lo, sg = sc["ab"], sc["v"], sc["lo"], sc["sg"]
    kb.eop("S", lambda S: S.activation(sg[:, 0:F], x_ap, AF.Sign),
           reads=(x,), writes=(sg,), wide=True)
    kb.eop("V", lambda V: V.tensor_reduce(mx[:, 0:B], x3, AX.X, ALU.max,
                                          apply_absolute_value=True),
           reads=(x,), writes=(mx,))
    kb.eop("V", lambda V: V.tensor_scalar(mx[:, 0:B], mx[:, 0:B], 6e-12,
                                          None, ALU.max),
           reads=(mx,), writes=(mx,))
    kb.eop("V", lambda V: V.reciprocal(rec[:, 0:B], mx[:, 0:B]),
           reads=(mx,), writes=(rec,))
    kb.eop("V", lambda V: V.tensor_scalar(ab[:, 0:F].bitcast(I16),
                                          x_ap.bitcast(I16), 0x7FFF, None,
                                          ALU.bitwise_and),
           reads=(x,), writes=(ab,), wide=W)
    rec_b = rec[:, 0:B].unsqueeze(2).to_broadcast((P, B, 16))
    mx_b = mx[:, 0:B].unsqueeze(2).to_broadcast((P, B, 16))
    kb.eop("V", lambda V: V.scalar_tensor_tensor(
        v[:, 0:F].rearrange("p (b k) -> p b k", k=16),
        ab[:, 0:F].rearrange("p (b k) -> p b k", k=16),
        6.0, rec_b, ALU.mult, ALU.mult),
        reads=(ab, rec), writes=(v,), wide=W)
    kb.eop("V", lambda V: V.tensor_scalar(lo[:, 0:F], v[:, 0:F], 1.0, None,
                                          ALU.is_lt),
           reads=(v,), writes=(lo,), wide=W)
    kb.eop("V", lambda V: V.tensor_tensor(v[:, 0:F], v[:, 0:F], lo[:, 0:F],
                                          ALU.add),
           reads=(v, lo), writes=(v,), wide=W)
    kb.eop("V", lambda V: V.tensor_scalar(v[:, 0:F].bitcast(I16),
                                          v[:, 0:F].bitcast(I16), 0x20, None,
                                          ALU.add),
           reads=(v,), writes=(v,), wide=W)
    kb.eop("V", lambda V: V.tensor_scalar(v[:, 0:F].bitcast(I16),
                                          v[:, 0:F].bitcast(I16), -0x40,
                                          None, ALU.bitwise_and),
           reads=(v,), writes=(v,), wide=W)
    kb.eop("V", lambda V: V.tensor_tensor(v[:, 0:F], v[:, 0:F], lo[:, 0:F],
                                          ALU.subtract),
           reads=(v, lo), writes=(v,), wide=W)
    kb.eop("V", lambda V: V.scalar_tensor_tensor(
        v[:, 0:F].rearrange("p (b k) -> p b k", k=16),
        v[:, 0:F].rearrange("p (b k) -> p b k", k=16),
        1.0 / 6.0, mx_b, ALU.mult, ALU.mult),
        reads=(v, mx), writes=(v,), wide=W)
    kb.eop("V", lambda V: V.tensor_tensor(out[:, 0:F], v[:, 0:F],
                                          sg[:, 0:F], ALU.mult),
        reads=(v, sg), writes=(out,), wide=W)



def emit_l0ffn_weight_prep(kb, w_gate, w_up, w_down, scr_q, scr_guT, scr_dnT,
                           qsc, big_f32, big_bf_a, big_bf_b):
    wt, wtb, wqt = big_f32, big_bf_a, big_bf_b
    for w, row0, ncols, nrows in [(w_gate, 0, D, NI), (w_up, I_FFN, D, NI),
                                  (w_down, 5632, I_FFN, 8)]:
        for i in range(nrows):
            kb.wr("Y", wt)
            kb.op("Y", lambda Y, w=w, i=i, nc_=ncols: Y.dma_start(
                out=wt[:, 0:nc_], in_=w.ap()[i * 128:(i + 1) * 128, :]), 16)
            kb.did_w("Y", wt)
            kb.rd("V", wt); kb.wr("V", wtb)
            kb.op("V", lambda V, nc_=ncols: V.tensor_copy(wtb[:, 0:nc_],
                                                          wt[:, 0:nc_]))
            kb.did_r("V", wt); kb.did_w("V", wtb)
            emit_fp4_quant(kb, wtb, wqt, ncols, qsc)
            kb.rd("Y", wqt)
            kb.op("Y", lambda Y, row0=row0, i=i, nc_=ncols: Y.dma_start(
                out=scr_q.ap()[row0 + i * 128: row0 + (i + 1) * 128, 0:nc_],
                in_=wqt[:, 0:nc_]), 16)
            kb.did_r("Y", wqt)
    kb.wait("Y", "Y")
    stage = big_bf_a
    for c in range(8):
        for half, r0 in [(0, 0), (1, I_FFN)]:
            kb.wr("Y", stage)
            kb.op("Y", lambda Y, c=c, r0=r0: Y.dma_start(
                out=stage[:, 0:I_FFN],
                in_=scr_q.ap()[r0:r0 + I_FFN, c * 128:(c + 1) * 128],
                transpose=True), 16)
            kb.did_w("Y", stage)
            kb.rd("Y", stage)
            kb.op("Y", lambda Y, c=c, half=half: Y.dma_start(
                out=scr_guT.ap()[c, :, half * I_FFN:(half + 1) * I_FFN],
                in_=stage[:, 0:I_FFN]), 16)
            kb.did_r("Y", stage)
    for i in range(NI):
        kb.wr("Y", stage)
        kb.op("Y", lambda Y, i=i: Y.dma_start(
            out=stage[:, 0:D],
            in_=scr_q.ap()[5632:6656, i * 128:(i + 1) * 128],
            transpose=True), 16)
        kb.did_w("Y", stage)
        kb.rd("Y", stage)
        kb.op("Y", lambda Y, i=i: Y.dma_start(
            out=scr_dnT.ap()[i, :, :], in_=stage[:, 0:D]), 16)
        kb.did_r("Y", stage)


FPLAN = [(0, 512), (512, 512), (1024, 512), (1536, 512), (2048, 512),
         (2560, 256)]


def emit_l0ffn(kb, load_x, store_x, scr_guT, scr_dnT, hq_dram, ident,
               qsc, big_f32, big_bf_a, big_bf_b, grp_tiles=8):
    GRP = grp_tiles
    xln2 = [kb.sb(f"f_xln{i}", [128, D], BF16) for i in range(2)]
    lnscr = kb.sb("f_lnscr", [128, D], BF16)
    stats = kb.sb("f_stats", [128, 48], F32)
    xq2 = [kb.sb(f"f_xq{i}", [128, D], BF16) for i in range(2)]
    xqT = [kb.sb(f"f_xqT{g}", [128, 8 * 128], BF16) for g in range(GRP)]
    psT = Buf(None)
    psT.h = kb.psum[7].h[:].bitcast(BF16)
    psT.readers = kb.psum[7].readers
    hbufs = [kb.sb(f"f_hb{g}", [128, I_FFN], BF16) for g in range(GRP)]
    silu_t2 = [kb.sb(f"f_silu{i}", [128, 512], BF16) for i in range(2)]
    hq2 = [big_bf_b, kb.sb("f_hqb", [128, I_FFN], BF16)]
    wgu = [kb.sb(f"f_wgu{b}", [128, 16 * 512], BF16) for b in range(2)]
    WCOLS = 16 * 512

    kb.wait("Y", "Y")
    for grp in range(NT // GRP):
        xts = [load_x(kb, grp * GRP + g) for g in range(GRP)]
        for g in range(GRP):
            emit_ln_stats(kb, xts[g], lnscr, stats, g, xcols=(0, D))
        emit_ln_finish(kb, stats, GRP)
        for g in range(GRP):
            xln = xln2[g % 2]
            xq = xq2[g % 2]
            emit_ln_apply(kb, xts[g], xln, lnscr, stats, g, xcols=(0, D))
            emit_fp4_quant(kb, xln, xq, D, qsc)
            kb.rd("T", xq, ident); kb.wr("T", psT)
            def tr(T, xq=xq, psT=psT):
                for c in range(8):
                    ins = T.transpose(psT.h[:, c * 128:(c + 1) * 128],
                                      xq[:, c * 128:(c + 1) * 128],
                                      ident.h[:])
                return ins
            kb.op("T", tr)
            kb.did_r("T", xq); kb.did_w("T", psT)
            kb.rd("V", psT); kb.wr("V", xqT[g])
            kb.op("V", lambda V, g=g: V.tensor_copy(xqT[g].h[:], psT.h))
            kb.did_r("V", psT); kb.did_w("V", xqT[g])
        for fi, (f0, fw) in enumerate(FPLAN):
            wbuf = wgu[fi % 2]
            kb.wr("Y", wbuf)
            for h in range(2):
                kb.op("Y", lambda Y, wbuf=wbuf, f0=f0, fw=fw, h=h:
                      Y.dma_start(
                    out=wbuf.h[:, 0:16 * fw].rearrange(
                        "p (a f) -> p a f", f=fw)[:, h::2, :],
                    in_=scr_guT.ap()[:, :, h * I_FFN + f0:
                                     h * I_FFN + f0 + fw].rearrange(
                        "c p x -> p c x")), 16)
            kb.did_w("Y", wbuf)
            for g in range(GRP):
                bg, bu = kb.psum[g % 2 * 2], kb.psum[g % 2 * 2 + 1]
                kb.rd("T", xqT[g], wbuf); kb.wr("T", bg, bu)
                def mm(T, g=g, bg=bg, bu=bu, wbuf=wbuf, fw=fw):
                    for c in range(8):
                        T.matmul(bg[:, 0:fw],
                                 xqT[g][:, c * 128:(c + 1) * 128],
                                 wbuf[:, (2 * c) * fw:(2 * c + 1) * fw],
                                 start=(c == 0), stop=(c == 7))
                    for c in range(8):
                        ins = T.matmul(
                            bu[:, 0:fw], xqT[g][:, c * 128:(c + 1) * 128],
                            wbuf[:, (2 * c + 1) * fw:(2 * c + 2) * fw],
                            start=(c == 0), stop=(c == 7))
                    return ins
                kb.op("T", mm)
                kb.did_r("T", xqT[g], wbuf); kb.did_w("T", bg, bu)
                silu_t = silu_t2[g % 2]
                kb.rd("S", bg); kb.wr("S", silu_t)
                kb.op("S", lambda S, bg=bg, fw=fw, silu_t=silu_t:
                      S.activation(silu_t[:, 0:fw], bg[:, 0:fw], AF.Silu))
                kb.did_r("S", bg); kb.did_w("S", silu_t)
                kb.rd("V", bu, silu_t); kb.wr("V", hbufs[g])
                kb.op("V", lambda V, g=g, bu=bu, f0=f0, fw=fw,
                      silu_t=silu_t: V.tensor_tensor(
                          hbufs[g][:, f0:f0 + fw], bu[:, 0:fw],
                          silu_t[:, 0:fw], ALU.mult))
                kb.did_r("V", bu, silu_t)
                if fi == len(FPLAN) - 1:
                    kb.did_w("V", hbufs[g])
        for g in range(GRP):
            tt = grp * GRP + g
            hq = hq2[g % 2]
            emit_fp4_quant(kb, hbufs[g], hq, I_FFN, qsc)
            kb.rd("Y", hq)
            kb.op("Y", lambda Y, tt=tt, hq=hq: Y.dma_start(
                out=hq_dram.ap()[tt * 128:(tt + 1) * 128, :],
                in_=hq[:, 0:I_FFN]), 16)
            kb.did_r("Y", hq)

    kb.wait("Y", "Y")
    kb.psum[7].prod = psT.prod
    hqT = [kb.sb(f"f_hqT{b}", [128, 512], BF16) for b in range(2)]
    wdn = [kb.sb(f"f_wdn{b}", [128, D], BF16) for b in range(2)]
    for sup in range(8):
        t0 = sup * 512
        banks = kb.psum
        for i in range(NI):
            buf = hqT[i % 2]
            wbuf = wdn[i % 2]
            kb.wr("Y", buf, wbuf)
            kb.op("Y", lambda Y, buf=buf, t0=t0, i=i: Y.dma_start(
                out=buf.h[:],
                in_=hq_dram.ap()[t0:t0 + 512, i * 128:(i + 1) * 128],
                transpose=True), 16)
            kb.op("Y", lambda Y, wbuf=wbuf, i=i: Y.dma_start(
                out=wbuf.h[:], in_=scr_dnT.ap()[i, :, :]), 16)
            kb.did_w("Y", buf, wbuf)
            kb.rd("T", buf, wbuf)
            if i == 0:
                for b in banks:
                    kb.wr("T", b)
            def mmd(T, buf=buf, wbuf=wbuf, i=i, banks=banks):
                for ti in range(4):
                    for fh in range(2):
                        ins = T.matmul(
                            banks[ti * 2 + fh].h[:],
                            buf[:, ti * 128:(ti + 1) * 128],
                            wbuf[:, fh * 512:fh * 512 + 512],
                            start=(i == 0), stop=(i == NI - 1))
                return ins
            kb.op("T", mmd)
            kb.did_r("T", buf, wbuf)
            if i == NI - 1:
                for b in banks:
                    kb.did_w("T", b)
        for ti in range(4):
            tt = sup * 4 + ti
            xres = load_x(kb, tt)
            for fh in range(2):
                bank = banks[ti * 2 + fh]
                kb.rd("V", xres, bank)
                kb.op("V", lambda V, xres=xres, bank=bank, fh=fh:
                      V.scalar_tensor_tensor(
                          xres[:, fh * 512:fh * 512 + 512], bank.h[:], 0.5,
                          xres[:, fh * 512:fh * 512 + 512], ALU.mult,
                          ALU.add))
                kb.did_r("V", bank)
            kb.did_w("V", xres, wide=True)
            store_x(kb, tt, xres)



HD = 64
NHEAD = 16


def emit_l0attn(kb, load_x, store_x, w_qkv, b_qkv, w_out, b_out,
                qkT_dram, v_dram, ctxT_dram, ident, mask4, lnscr, stats):
    onesrow = kb.sb("a_ones", [1, 128], F32)
    onescol = kb.sb("a_onesc", [128, 1], F32)
    kb.op("V", lambda V: V.memset(onesrow.ap, 1.0))
    kb.op("V", lambda V: V.memset(onescol.ap, 1.0))
    kb.did_w("V", onesrow, onescol)
    wqkv_bf = kb.sb("a_wqkv", [128, 8 * 3072], BF16)
    wout_bf = kb.sb("a_wout", [128, 8 * 1024], BF16)
    bq_col = kb.sb("a_bqcol", [128, 24], F32)
    brow = kb.sb("a_brow", [1, 3072], F32)
    borow = kb.sb("a_borow", [1, 1024], F32)
    for c in range(8):
        kb.op("G", lambda G, c=c: G.dma_start(
            out=wqkv_bf[:, c * 3072:(c + 1) * 3072],
            in_=w_qkv.ap()[c * 128:(c + 1) * 128, :]), 16)
        kb.op("G", lambda G, c=c: G.dma_start(
            out=wout_bf[:, c * 1024:(c + 1) * 1024],
            in_=w_out.ap()[c * 128:(c + 1) * 128, :]), 16)
    kb.op("Y", lambda Y: Y.dma_start(
        out=bq_col.ap, in_=b_qkv.ap().rearrange("(a p) -> p a", p=128),
        allow_slow_non_contiguous=True), 16)
    kb.op("Y", lambda Y: Y.dma_start(
        out=brow.ap, in_=b_qkv.ap().unsqueeze(0)), 16)
    kb.op("Y", lambda Y: Y.dma_start(
        out=borow.ap, in_=b_out.ap().unsqueeze(0)), 16)
    kb.did_w("G", wqkv_bf, wout_bf)
    kb.did_w("Y", bq_col, brow, borow)

    xln = kb.sb("a_xln", [128, D], BF16)
    xlnT = kb.sb("a_xlnT", [128, 8 * 512], BF16)
    psT = Buf(None)
    psT.h = kb.psum[7].h[:].bitcast(BF16)
    psT.readers = kb.psum[7].readers
    qkbuf = kb.sb("a_qkbuf", [128, 512], BF16)
    vbuf = kb.sb("a_vbuf", [128, 512], BF16)

    for sup in range(8):
        xts = [load_x(kb, sup * 4 + g) for g in range(4)]
        for g in range(4):
            emit_ln_stats(kb, xts[g], lnscr, stats, g, xcols=(0, D))
        emit_ln_finish(kb, stats, 4)
        x3 = xlnT.h[:].rearrange("p (c t) -> p c t", c=8)
        for g in range(4):
            emit_ln_apply(kb, xts[g], xln, lnscr, stats, g, xcols=(0, D))
            kb.rd("T", xln, ident); kb.wr("T", psT)
            def tr(T, xln=xln, psT=psT):
                for c in range(8):
                    ins = T.transpose(psT.h[:, c * 128:(c + 1) * 128],
                                      xln[:, c * 128:(c + 1) * 128],
                                      ident.h[:])
                return ins
            kb.op("T", tr)
            kb.did_r("T", xln); kb.did_w("T", psT)
            kb.rd("V", psT); kb.wr("V", xlnT)
            kb.op("V", lambda V, g=g, x3=x3: V.tensor_copy(
                x3[:, :, g * 128:(g + 1) * 128],
                psT.h[:].rearrange("p (c t) -> p c t", c=8)))
            kb.did_r("V", psT)
        kb.did_w("V", xlnT, wide=True)
        for ft in range(16):
            bank = kb.psum[ft % 2]
            kb.rd("T", xlnT, wqkv_bf); kb.wr("T", bank)
            def mmqk(T, ft=ft, bank=bank, x3=x3):
                for c in range(8):
                    ins = T.matmul(bank.h[:],
                                   wqkv_bf[:, c * 3072 + ft * 128:
                                           c * 3072 + (ft + 1) * 128],
                                   x3[:, c, :], start=(c == 0), stop=(c == 7))
                return ins
            kb.op("T", mmqk)
            kb.did_w("T", bank)
            kb.rd("V", bank, bq_col); kb.wr("V", qkbuf)
            kb.op("V", lambda V, ft=ft, bank=bank: V.tensor_single_scalar(
                qkbuf.ap, bank.h[:], bq_col[:, ft:ft + 1], ALU.add))
            kb.did_r("V", bank); kb.did_w("V", qkbuf)
            kb.rd("Y", qkbuf)
            kb.op("Y", lambda Y, ft=ft, sup=sup: Y.dma_start(
                out=qkT_dram.ap()[ft * 128:(ft + 1) * 128,
                                  sup * 512:(sup + 1) * 512],
                in_=qkbuf.ap), 16)
            kb.did_r("Y", qkbuf)
        for g in range(4):
            for fh in range(2):
                bank = kb.psum[2 + (g * 2 + fh) % 2]
                kb.rd("T", xlnT, wqkv_bf, onesrow, brow); kb.wr("T", bank)
                def mmv(T, g=g, fh=fh, bank=bank, x3=x3):
                    for c in range(8):
                        T.matmul(bank.h[:],
                                 x3[:, c, g * 128:(g + 1) * 128],
                                 wqkv_bf[:, c * 3072 + 2048 + fh * 512:
                                         c * 3072 + 2048 + (fh + 1) * 512],
                                 start=(c == 0), stop=False)
                    return T.matmul(bank.h[:], onesrow.h[0:1, :],
                                    brow.h[0:1, 2048 + fh * 512:
                                           2048 + (fh + 1) * 512],
                                    start=False, stop=True)
                kb.op("T", mmv)
                kb.did_w("T", bank)
                kb.rd("V", bank); kb.wr("V", vbuf)
                kb.op("V", lambda V, bank=bank: V.tensor_copy(vbuf.ap,
                                                              bank.h[:]))
                kb.did_r("V", bank); kb.did_w("V", vbuf)
                kb.rd("Y", vbuf)
                kb.op("Y", lambda Y, sup=sup, g=g, fh=fh: Y.dma_start(
                    out=v_dram.ap()[sup * 512 + g * 128:
                                    sup * 512 + (g + 1) * 128,
                                    fh * 512:(fh + 1) * 512],
                    in_=vbuf.ap), 16)
                kb.did_r("Y", vbuf)

    kb.wait("Y", "Y")

    kT = kb.sb("a_kT", [64, T_CORE], BF16)
    qT = kb.sb("a_qT", [64, T_CORE], BF16)
    vb3 = kb.sb("a_vb3", [128, 32 * 72], BF16)
    pT = [kb.sb(f"a_pT{i}", [128, 512], BF16) for i in range(2)]
    md = kb.sb("a_md", [64, 2], F32)
    sc1 = kb.sb("a_sc1", [1, 8], F32)
    bias_col = kb.sb("a_biascol", [128, 1], F32)
    linv_row = kb.sb("a_linv", [1, 512], F32)
    linv_rep = kb.sb("a_linvrep", [64, 512], F32)
    ctx_bf = kb.sb("a_ctx", [64, 512], BF16)
    v3 = vb3.h[:].rearrange("p (c w) -> p c w", w=72)
    kb.op("V", lambda V: V.memset(vb3.ap, 0.0))
    kb.op("V", lambda V, v3=v3: V.memset(v3[:, :, 64:65], 1.0))
    kb.did_w("V", vb3)

    for h in range(NHEAD):
        kb.wr("Y", kT, qT, vb3)
        kb.op("Y", lambda Y, h=h: Y.dma_start(
            out=kT.ap,
            in_=qkT_dram.ap()[1024 + h * 64:1024 + (h + 1) * 64, :]), 16)
        kb.op("Y", lambda Y, h=h: Y.dma_start(
            out=qT.ap, in_=qkT_dram.ap()[h * 64:(h + 1) * 64, :]), 16)
        kb.op("Y", lambda Y, h=h, v3=v3: Y.dma_start(
            out=v3[:, :, 0:64],
            in_=v_dram.ap()[:, h * 64:(h + 1) * 64].rearrange(
                "(c p) d -> p c d", p=128)), 16)
        kb.did_w("Y", kT, qT, vb3)
        kb.eop("V", lambda V: V.tensor_reduce(
            md[:, 0:1], kT.ap, AX.X, ALU.max, apply_absolute_value=True),
            reads=(kT,), writes=(md,))
        kb.eop("V", lambda V: V.tensor_reduce(
            md[:, 1:2], qT.ap, AX.X, ALU.max, apply_absolute_value=True),
            reads=(md, qT), writes=(md,))
        kb.eop("V", lambda V: V.tensor_tensor(md.ap, md.ap, md.ap, ALU.mult),
               reads=(md,), writes=(md,))
        bank = kb.psum[6]
        kb.rd("T", md, onescol); kb.wr("T", bank)
        kb.op("T", lambda T, bank=bank: T.matmul(
            bank.h[0:1, 0:1], md.h[0:64, 0:1], onescol.h[0:64, 0:1],
            start=True, stop=True))
        kb.op("T", lambda T, bank=bank: T.matmul(
            bank.h[0:1, 1:2], md.h[0:64, 1:2], onescol.h[0:64, 0:1],
            start=False, stop=True))
        kb.did_w("T", bank)
        kb.eop("V", lambda V, bank=bank: V.tensor_copy(sc1[0:1, 0:2],
                                                       bank.h[0:1, 0:2]),
               reads=(bank,), writes=(sc1,))
        kb.did_r("V", bank)
        kb.eop("V", lambda V: V.tensor_tensor(sc1[0:1, 2:3], sc1[0:1, 0:1],
                                              sc1[0:1, 1:2], ALU.mult),
               reads=(sc1,), writes=(sc1,))
        kb.eop("S", lambda S: S.activation(sc1[0:1, 3:4], sc1[0:1, 2:3],
                                           AF.Ln), reads=(sc1,),
               writes=(sc1,))
        kb.eop("S", lambda S: S.activation(sc1[0:1, 4:5], sc1[0:1, 3:4],
                                           AF.Exp, scale=0.5),
               reads=(sc1,), writes=(sc1,))
        kb.eop("V", lambda V: V.tensor_scalar(sc1[0:1, 5:6], sc1[0:1, 4:5],
                                              -0.125, None, ALU.mult),
               reads=(sc1,), writes=(sc1,))
        rep_bank = kb.psum[5]
        kb.rd("T", sc1, onesrow); kb.wr("T", rep_bank)
        kb.op("T", lambda T, rep_bank=rep_bank: T.matmul(
            rep_bank.h[:, 0:1], onesrow.h[0:1, :], sc1.h[0:1, 5:6],
            start=True, stop=True))
        kb.did_w("T", rep_bank)
        kb.eop("V", lambda V, rep_bank=rep_bank: V.tensor_copy(
            bias_col.ap, rep_bank.h[:, 0:1]),
            reads=(rep_bank,), writes=(bias_col,))
        kb.did_r("V", rep_bank)

        for qc in range(8):
            nkc = 4 * (qc + 1)
            pv_bank = kb.psum[4]
            kb.wr("T", pv_bank)
            def emit_sc(kc):
                sb_ = kb.psum[kc % 2]
                kb.rd("T", kT, qT, ident); kb.wr("T", sb_)
                diag_a = kc * 128 - qc * 512
                def mmsc(T, kc=kc, qc=qc, sb_=sb_, diag_a=diag_a):
                    ins = T.matmul(sb_.h[:], kT[:, kc * 128:(kc + 1) * 128],
                                   qT[:, qc * 512:(qc + 1) * 512],
                                   start=True, stop=(diag_a < 0))
                    if diag_a >= 0:
                        ins = T.matmul(sb_.h[:], ident.h[:],
                                       mask4[diag_a // 128],
                                       start=False, stop=True)
                    return ins
                kb.op("T", mmsc)
                kb.did_w("T", sb_)
                pbuf = pT[kc % 2]
                kb.rd("S", sb_, bias_col); kb.wr("S", pbuf)
                kb.op("S", lambda S, sb_=sb_, pbuf=pbuf: S.activation(
                    pbuf.ap, sb_.h[:], AF.Exp, bias=bias_col.ap,
                    scale=0.125))
                kb.did_r("S", sb_); kb.did_w("S", pbuf, wide=True)

            def emit_pv(kc):
                pbuf = pT[kc % 2]
                kb.rd("T", vb3, pbuf)
                def mmpv(T, kc=kc, pbuf=pbuf, pv_bank=pv_bank, nkc=nkc,
                         v3=v3):
                    return T.matmul(pv_bank.h[0:65, :], v3[:, kc, 0:65],
                                    pbuf.ap, start=(kc == 0),
                                    stop=(kc == nkc - 1))
                kb.op("T", mmpv)
                kb.did_r("T", pbuf)

            for kc in range(nkc):
                emit_sc(kc)
                if kc >= 1:
                    emit_pv(kc - 1)
            emit_pv(nkc - 1)
            kb.did_w("T", pv_bank)
            kb.eop("V", lambda V, pv_bank=pv_bank: V.reciprocal(
                linv_row.ap, pv_bank.h[64:65, :]),
                reads=(pv_bank,), writes=(linv_row,))
            rep2 = kb.psum[5]
            kb.rd("T", linv_row, onesrow); kb.wr("T", rep2)
            kb.op("T", lambda T, rep2=rep2: T.matmul(
                rep2.h[0:64, :], onesrow.h[0:1, 0:64], linv_row.ap,
                start=True, stop=True))
            kb.did_w("T", rep2)
            kb.rd("S", rep2); kb.wr("S", linv_rep)
            kb.op("S", lambda S, rep2=rep2: S.activation(
                linv_rep.ap, rep2.h[0:64, :], AF.Copy))
            kb.did_r("S", rep2); kb.did_w("S", linv_rep)
            kb.rd("V", pv_bank, linv_rep); kb.wr("V", ctx_bf)
            kb.op("V", lambda V, pv_bank=pv_bank: V.tensor_tensor(
                ctx_bf.ap, pv_bank.h[0:64, :], linv_rep.ap, ALU.mult))
            kb.did_r("V", pv_bank); kb.did_w("V", ctx_bf)
            kb.rd("Y", ctx_bf)
            kb.op("Y", lambda Y, h=h, qc=qc: Y.dma_start(
                out=ctxT_dram.ap()[h * 64:(h + 1) * 64,
                                   qc * 512:(qc + 1) * 512],
                in_=ctx_bf.ap), 16)
            kb.did_r("Y", ctx_bf)

    kb.wait("Y", "Y")

    ctxc = [kb.sb(f"a_ctxc{i}", [128, 512], BF16) for i in range(2)]
    banks = kb.psum
    for sup in range(8):
        for c in range(8):
            buf = ctxc[c % 2]
            kb.wr("Y", buf)
            kb.op("Y", lambda Y, c=c, sup=sup, buf=buf: Y.dma_start(
                out=buf.ap,
                in_=ctxT_dram.ap()[c * 128:(c + 1) * 128,
                                   sup * 512:(sup + 1) * 512]), 16)
            kb.did_w("Y", buf)
            kb.rd("T", buf, wout_bf, onesrow, borow)
            if c == 0:
                for b in banks:
                    kb.wr("T", b)
            def mmo(T, c=c, buf=buf):
                for ti in range(4):
                    for fh in range(2):
                        ins = T.matmul(
                            banks[ti * 2 + fh].h[:],
                            buf[:, ti * 128:(ti + 1) * 128],
                            wout_bf[:, c * 1024 + fh * 512:
                                    c * 1024 + (fh + 1) * 512],
                            start=(c == 0), stop=False)
                if c == 7:
                    for ti in range(4):
                        for fh in range(2):
                            ins = T.matmul(
                                banks[ti * 2 + fh].h[:], onesrow.h[0:1, :],
                                borow.h[0:1, fh * 512:(fh + 1) * 512],
                                start=False, stop=True)
                return ins
            kb.op("T", mmo)
            kb.did_r("T", buf)
            if c == 7:
                for b in banks:
                    kb.did_w("T", b)
        for ti in range(4):
            tt = sup * 4 + ti
            xres = load_x(kb, tt)
            for fh in range(2):
                bank = banks[ti * 2 + fh]
                kb.rd("V", xres, bank)
                kb.op("V", lambda V, xres=xres, bank=bank, fh=fh:
                      V.tensor_tensor(xres[:, fh * 512:fh * 512 + 512],
                                      bank.h[:],
                                      xres[:, fh * 512:fh * 512 + 512],
                                      ALU.add))
                kb.did_r("V", bank)
            kb.did_w("V", xres, wide=True)
            store_x(kb, tt, xres)



DIN = 2048
NDT = DIN // 128
TC = 512
NCH = T_CORE // TC


def emit_l1mamba(kb, load_x, store_x, win, conv_w, conv_b, xproj, dtw, dtb,
                 alog, dpar, cw, z_dram, ident, lnscr, stats, dbg=None):
    def dump(name, buf, cols, ch=None):
        if dbg is None or name not in dbg:
            return
        if ch is not None and ch != 0:
            return
        kb.rd("Y", buf)
        kb.op("Y", lambda Y, name=name, buf=buf, cols=cols: Y.dma_start(
            out=dbg[name].ap()[:, 0:cols], in_=buf[:, 0:cols]), 16)
        kb.did_r("Y", buf)
    nc = kb.nc
    onesrow = kb.sb("m_ones", [1, 128], F32)
    onesb = kb.sb("m_onesb", [1, 128], BF16)
    kb.op("V", lambda V: V.memset(onesrow.ap, 1.0))
    kb.op("V", lambda V: V.memset(onesb.ap, 1.0))
    kb.did_w("V", onesrow, onesb)

    convw_c = kb.sb("m_convw", [128, 64], F32)
    convb_c = kb.sb("m_convb", [128, 16], F32)
    dtw_c = kb.sb("m_dtw", [128, 16], F32)
    dtb_c = kb.sb("m_dtb", [128, 16], F32)
    alog_c = kb.sb("m_alog", [128, 256], F32)
    nega = kb.sb("m_nega", [128, 256], F32)
    dpar_c = kb.sb("m_dpar", [128, 16], F32)
    xproj_b = kb.sb("m_xproj", [128, 16 * 33], BF16)
    kb.op("Y", lambda Y: Y.dma_start(
        out=convw_c.h[:].rearrange("p (i j) -> p i j", i=16),
        in_=conv_w.ap().rearrange("(i p) o j -> p i (o j)", p=128)), 16)
    kb.op("Y", lambda Y: Y.dma_start(
        out=convb_c.ap, in_=conv_b.ap().rearrange("(i p) -> p i", p=128),
        allow_slow_non_contiguous=True), 16)
    kb.op("Y", lambda Y: Y.dma_start(
        out=dtw_c.ap, in_=dtw.ap().rearrange("o (i p) -> p (o i)", p=128),
        allow_slow_non_contiguous=True), 16)
    kb.op("Y", lambda Y: Y.dma_start(
        out=dtb_c.ap, in_=dtb.ap().rearrange("(i p) -> p i", p=128),
        allow_slow_non_contiguous=True), 16)
    kb.op("Y", lambda Y: Y.dma_start(
        out=alog_c.h[:].rearrange("p (i s) -> p i s", i=16),
        in_=alog.ap().rearrange("(i p) s -> p i s", p=128)), 16)
    kb.op("Y", lambda Y: Y.dma_start(
        out=dpar_c.ap, in_=dpar.ap().rearrange("(i p) -> p i", p=128),
        allow_slow_non_contiguous=True), 16)
    kb.did_w("Y", convw_c, convb_c, dtw_c, dtb_c, alog_c, dpar_c)
    kb.op("G", lambda G: G.dma_start(
        out=xproj_b.h[:].rearrange("p (i k) -> p i k", i=16),
        in_=xproj.ap().rearrange("(i p) k -> p i k", p=128)), 16)
    kb.did_w("G", xproj_b)
    kb.eop("S", lambda S: S.activation(nega.ap, alog_c.ap, AF.Exp),
           reads=(alog_c,), writes=(nega,), wide=True)
    kb.eop("V", lambda V: V.tensor_scalar(nega.ap, nega.ap, -1.0, None,
                                          ALU.mult),
           reads=(nega,), writes=(nega,), wide=True)

    xln = kb.sb("m_xln", [128, D], BF16)
    xlnT = kb.sb("m_xlnT", [128, 8 * TC], BF16)
    psT = Buf(None)
    psT.h = kb.psum[7].h[:].bitcast(BF16)
    psT.readers = kb.psum[7].readers
    xbr = kb.sb("m_xbr", [128, NDT * TC], BF16)
    ybuf = xbr
    ytmp = kb.sb("m_ytmp", [128, TC], BF16)
    brep = kb.sb("m_brep", [128, 16 * TC], BF16)
    crep = kb.sb("m_crep", [128, 16 * TC], BF16)
    pad = kb.sb("m_pad", [128, TC + 3], BF16)
    convo = kb.sb("m_convo", [128, TC], BF16)
    hist = kb.sb("m_hist", [128, 16 * 3], BF16)
    hcarry = kb.sb("m_hcarry", [128, 256], F32)
    bcbuf = kb.sb("m_bc", [32, TC], BF16)
    dtraw = kb.sb("m_dtraw", [1, TC], F32)
    brow = kb.sb("m_brow", [1, TC], BF16)
    Rrep = kb.sb("m_Rrep", [128, TC], F32)
    dtb_f = kb.sb("m_dtfull", [128, TC], F32)
    ab_f2 = [kb.sb(f"m_abar{i}", [128, TC], F32) for i in range(2)]
    ab_f = ab_f2[0]
    bx_f = kb.sb("m_bx", [128, TC], F32)
    hbuf = kb.sb("m_h", [128, TC], BF16)
    ubuf = kb.sb("m_u", [128, TC], BF16)
    wsl = [kb.sb(f"m_wsl{i}", [128, 8 * 128], BF16) for i in range(2)]
    cwsl = [kb.sb(f"m_cwsl{i}", [128, D], BF16) for i in range(2)]
    zbuf = kb.sb("m_zbuf", [128, 512], BF16)
    kb.op("V", lambda V: V.memset(hist.ap, 0.0))
    kb.op("V", lambda V: V.memset(hcarry.ap, 0.0))
    kb.did_w("V", hist, hcarry)

    for ch in range(NCH):
        t0 = ch * TC
        NTPC = TC // 128
        xts = [load_x(kb, ch * NTPC + g) for g in range(NTPC)]
        for g in range(NTPC):
            emit_ln_stats(kb, xts[g], lnscr, stats, g, xcols=(0, D))
        emit_ln_finish(kb, stats, NTPC)
        x3 = xlnT.h[:].rearrange("p (c t) -> p c t", c=8)
        for g in range(NTPC):
            emit_ln_apply(kb, xts[g], xln, lnscr, stats, g, xcols=(0, D))
            kb.rd("T", xln, ident); kb.wr("T", psT)
            def tr(T, xln=xln, psT=psT):
                for c in range(8):
                    ins = T.transpose(psT.h[:, c * 128:(c + 1) * 128],
                                      xln[:, c * 128:(c + 1) * 128],
                                      ident.h[:])
                return ins
            kb.op("T", tr)
            kb.did_r("T", xln); kb.did_w("T", psT)
            kb.rd("V", psT); kb.wr("V", xlnT)
            kb.op("V", lambda V, g=g, x3=x3: V.tensor_copy(
                x3[:, :, g * 128:(g + 1) * 128],
                psT.h[:].rearrange("p (c t) -> p c t", c=8)))
            kb.did_r("V", psT)
        kb.did_w("V", xlnT, wide=True)

        for i in range(NDT):
            wbuf = wsl[i % 2]
            kb.wr("G", wbuf)
            kb.op("G", lambda G, i=i, wbuf=wbuf: G.dma_start(
                out=wbuf.h[:].rearrange("p (c f) -> p c f", c=8),
                in_=win.ap()[:, i * 128:(i + 1) * 128].rearrange(
                    "(c p) f -> p c f", p=128)), 16)
            kb.did_w("G", wbuf)
            for th in range(TC // 512):
                bank = kb.psum[th]
                kb.rd("T", xlnT, wbuf); kb.wr("T", bank)
                def mmi(T, i=i, th=th, bank=bank, wbuf=wbuf, x3=x3):
                    for c in range(8):
                        ins = T.matmul(bank.h[:],
                                       wbuf[:, c * 128:(c + 1) * 128],
                                       x3[:, c, th * 512:(th + 1) * 512],
                                       start=(c == 0), stop=(c == 7))
                    return ins
                kb.op("T", mmi)
                kb.did_w("T", bank); kb.did_r("T", wbuf, xlnT)
                kb.rd("S", bank); kb.wr("S", pad)
                kb.op("S", lambda S, th=th, bank=bank: S.activation(
                    pad[:, 3 + th * 512:3 + (th + 1) * 512], bank.h[:],
                    AF.Copy))
                kb.did_r("S", bank)
            kb.rd("S", hist)
            kb.op("S", lambda S, i=i: S.activation(
                pad[:, 0:3], hist[:, i * 3:(i + 1) * 3], AF.Copy))
            kb.did_r("S", hist); kb.did_w("S", pad)
            kb.rd("V", pad, convw_c); kb.wr("V", convo)
            kb.op("V", lambda V, i=i: V.tensor_single_scalar(
                convo.ap, pad[:, 0:TC], convw_c[:, i * 4:i * 4 + 1],
                ALU.mult))
            for j in range(1, 4):
                kb.op("V", lambda V, i=i, j=j: V.scalar_tensor_tensor(
                    convo.ap, pad[:, j:j + TC], convw_c[:, i * 4 + j:
                                                        i * 4 + j + 1],
                    convo.ap, ALU.mult, ALU.add))
            kb.did_w("V", convo, wide=True)
            kb.wr("V", hist)
            kb.op("V", lambda V, i=i: V.tensor_copy(
                hist[:, i * 3:(i + 1) * 3], pad[:, TC:TC + 3]))
            kb.did_w("V", hist)
            kb.did_r("V", pad)
            kb.rd("S", convo, convb_c); kb.wr("S", xbr)
            kb.op("S", lambda S, i=i: S.activation(
                xbr[:, i * TC:(i + 1) * TC], convo.ap, AF.Silu,
                bias=convb_c[:, i:i + 1]))
            kb.did_w("S", xbr, wide=True)

        dump("xbr", xbr, NDT * TC, ch)
        for i in range(NDT):
            wbuf = wsl[i % 2]
            kb.wr("G", wbuf)
            kb.op("G", lambda G, i=i, wbuf=wbuf: G.dma_start(
                out=wbuf.h[:].rearrange("p (c f) -> p c f", c=8),
                in_=win.ap()[:, 2048 + i * 128:2048 + (i + 1) * 128].rearrange(
                    "(c p) f -> p c f", p=128)), 16)
            kb.did_w("G", wbuf)
            for th in range(TC // 512):
                bank = kb.psum[2 + th]
                kb.rd("T", xlnT, wbuf); kb.wr("T", bank)
                def mmz(T, i=i, th=th, bank=bank, wbuf=wbuf, x3=x3):
                    for c in range(8):
                        ins = T.matmul(bank.h[:],
                                       wbuf[:, c * 128:(c + 1) * 128],
                                       x3[:, c, th * 512:(th + 1) * 512],
                                       start=(c == 0), stop=(c == 7))
                    return ins
                kb.op("T", mmz)
                kb.did_w("T", bank); kb.did_r("T", wbuf, xlnT)
                kb.rd("S", bank); kb.wr("S", zbuf)
                kb.op("S", lambda S, bank=bank: S.activation(
                    zbuf.ap, bank.h[:], AF.Silu))
                kb.did_r("S", bank); kb.did_w("S", zbuf)
                kb.rd("Y", zbuf)
                kb.op("Y", lambda Y, i=i, th=th, t0=t0: Y.dma_start(
                    out=z_dram.ap()[i * 128:(i + 1) * 128,
                                    t0 + th * 512:t0 + (th + 1) * 512],
                    in_=zbuf.ap), 16)
                kb.did_r("Y", zbuf)

        for th in range(TC // 512):
            bank = kb.psum[4 + th]
            kb.rd("T", xbr, xproj_b); kb.wr("T", bank)
            def mms(T, th=th, bank=bank):
                for i in range(NDT):
                    ins = T.matmul(bank.h[0:33, :],
                                   xproj_b[:, i * 33:(i + 1) * 33],
                                   xbr[:, i * TC + th * 512:
                                       i * TC + (th + 1) * 512],
                                   start=(i == 0), stop=(i == NDT - 1))
                return ins
            kb.op("T", mms)
            kb.did_w("T", bank)
            kb.rd("V", bank); kb.wr("V", bcbuf, dtraw)
            kb.op("V", lambda V, th=th, bank=bank: V.tensor_copy(
                bcbuf[0:32, th * 512:(th + 1) * 512], bank.h[0:32, :]))
            kb.op("V", lambda V, th=th, bank=bank: V.tensor_copy(
                dtraw[0:1, th * 512:(th + 1) * 512], bank.h[32:33, :]))
            kb.did_r("V", bank); kb.did_w("V", bcbuf, dtraw)

        dump("bc", bcbuf, TC, ch)
        dump("dtraw", dtraw, TC, ch)
        for th in range(TC // 512):
            bank = kb.psum[4 + th]
            kb.rd("T", dtraw, onesrow); kb.wr("T", bank)
            kb.op("T", lambda T, th=th, bank=bank: T.matmul(
                bank.h[:], onesrow.h[0:1, :],
                dtraw[0:1, th * 512:(th + 1) * 512], start=True, stop=True))
            kb.did_w("T", bank)
            kb.rd("S", bank); kb.wr("S", Rrep)
            kb.op("S", lambda S, th=th, bank=bank: S.activation(
                Rrep[:, th * 512:(th + 1) * 512], bank.h[:], AF.Copy))
            kb.did_r("S", bank)
        kb.did_w("S", Rrep, wide=True)

        for s in range(16):
            for part, dst in [(0, brep), (16, crep)]:
                kb.rd("G", bcbuf)
                kb.op("G", lambda G, s=s, part=part: G.dma_start(
                    out=brow.ap, in_=bcbuf[part + s:part + s + 1, :]), 16)
                kb.did_w("G", brow)
                for th in range(TC // 512):
                    bank = kb.psum[6]
                    kb.rd("T", brow, onesb); kb.wr("T", bank)
                    kb.op("T", lambda T, th=th, bank=bank: T.matmul(
                        bank.h[:], onesb.h[0:1, :],
                        brow[0:1, th * 512:(th + 1) * 512],
                        start=True, stop=True))
                    kb.did_w("T", bank)
                    kb.rd("S", bank); kb.wr("S", dst)
                    kb.op("S", lambda S, s=s, th=th, bank=bank, dst=dst:
                          S.activation(dst[:, s * TC + th * 512:
                                           s * TC + (th + 1) * 512],
                                       bank.h[:], AF.Copy))
                    kb.did_r("S", bank)
        kb.did_w("S", brep, crep, wide=True)

        for i in range(NDT):
            kb.eop("V", lambda V, i=i: V.tensor_single_scalar(
                dtb_f.ap, Rrep.ap, dtw_c[:, i:i + 1], ALU.mult),
                reads=(Rrep, dtw_c), writes=(dtb_f,), wide=True)
            kb.eop("V", lambda V, i=i: V.tensor_single_scalar(
                dtb_f.ap, dtb_f.ap, dtb_c[:, i:i + 1], ALU.add),
                reads=(dtb_f, dtb_c), writes=(dtb_f,), wide=True)
            kb.eop("V", lambda V: V.tensor_scalar(
                ab_f.h[:].bitcast(I32), dtb_f.h[:].bitcast(
                    I32), 0x7FFFFFFF, None, ALU.bitwise_and),
                reads=(dtb_f,), writes=(ab_f,), wide=True)
            kb.eop("S", lambda S: S.activation(ab_f.ap, ab_f.ap, AF.Exp,
                                               scale=-1.0),
                   reads=(ab_f,), writes=(ab_f,), wide=True)
            eps1 = kb.constcol(1.0)
            kb.eop("S", lambda S: S.activation(ab_f.ap, ab_f.ap, AF.Ln,
                                               bias=eps1),
                   reads=(ab_f,), writes=(ab_f,), wide=True)
            kb.eop("V", lambda V: V.tensor_scalar(bx_f.ap, dtb_f.ap, 0.0,
                                                  None, ALU.max),
                   reads=(dtb_f,), writes=(bx_f,), wide=True)
            kb.eop("V", lambda V: V.tensor_tensor(dtb_f.ap, bx_f.ap,
                                                  ab_f.ap, ALU.add),
                   reads=(bx_f, ab_f), writes=(dtb_f,), wide=True)
            if i == 0:
                dump("dt0", dtb_f, TC, ch)
            kb.eop("V", lambda V, i=i: V.tensor_tensor(
                ubuf.ap, dtb_f.ap, xbr[:, i * TC:(i + 1) * TC], ALU.mult),
                reads=(dtb_f, xbr), writes=(ubuf,), wide=True)
            for s in range(16):
                abv = ab_f2[s % 2]
                kb.eop("S", lambda S, i=i, s=s, abv=abv: S.activation(
                    abv.ap, dtb_f.ap, AF.Exp,
                    scale=nega[:, i * 16 + s:i * 16 + s + 1]),
                    reads=(dtb_f, nega), writes=(abv,), wide=True)
                kb.eop("V", lambda V, s=s: V.tensor_tensor(
                    bx_f.ap, ubuf.ap, brep[:, s * TC:(s + 1) * TC],
                    ALU.mult),
                    reads=(ubuf, brep), writes=(bx_f,), wide=True)
                kb.eop("V", lambda V, i=i, s=s, abv=abv: V.tensor_tensor_scan(
                    hbuf.ap, abv.ap, bx_f.ap,
                    hcarry[:, i * 16 + s:i * 16 + s + 1], ALU.mult, ALU.add),
                    reads=(abv, bx_f, hcarry), writes=(hbuf,), wide=True)
                kb.eop("V", lambda V, i=i, s=s: V.tensor_copy(
                    hcarry[:, i * 16 + s:i * 16 + s + 1],
                    hbuf[:, TC - 1:TC]),
                    reads=(hbuf,), writes=(hcarry,))
                if i == 0 and s == 0:
                    dump("h00", hbuf, TC, ch)
                    dump("ab00", ab_f, TC, ch)
                    dump("bx00", bx_f, TC, ch)
                if s == 0:
                    kb.eop("V", lambda V, i=i, s=s: V.tensor_tensor(
                        ytmp.ap, hbuf.ap,
                        crep[:, s * TC:(s + 1) * TC], ALU.mult),
                        reads=(hbuf, crep), writes=(ytmp,), wide=True)
                else:
                    kb.eop("V", lambda V, i=i, s=s: V.tensor_tensor(
                        bx_f.h[:].bitcast(BF16)[:, 0:TC], hbuf.ap,
                        crep[:, s * TC:(s + 1) * TC], ALU.mult),
                        reads=(hbuf, crep), writes=(bx_f,), wide=True)
                    kb.eop("V", lambda V, i=i: V.tensor_tensor(
                        ytmp.ap, ytmp.ap,
                        bx_f.h[:].bitcast(BF16)[:, 0:TC], ALU.add),
                        reads=(bx_f, ytmp), writes=(ytmp,), wide=True)
            kb.eop("V", lambda V, i=i: V.scalar_tensor_tensor(
                ybuf[:, i * TC:(i + 1) * TC], xbr[:, i * TC:(i + 1) * TC],
                dpar_c[:, i:i + 1], ytmp.ap,
                ALU.mult, ALU.add),
                reads=(xbr, dpar_c, ytmp), writes=(ybuf,), wide=True)
        kb.wait("Y", "Y")
        for i in range(NDT):
            for th in range(TC // 512):
                kb.wr("Y", zbuf)
                kb.op("Y", lambda Y, i=i, th=th, t0=t0: Y.dma_start(
                    out=zbuf.ap,
                    in_=z_dram.ap()[i * 128:(i + 1) * 128,
                                    t0 + th * 512:t0 + (th + 1) * 512]), 16)
                kb.did_w("Y", zbuf)
                kb.eop("V", lambda V, i=i, th=th: V.tensor_tensor(
                    ybuf[:, i * TC + th * 512:i * TC + (th + 1) * 512],
                    ybuf[:, i * TC + th * 512:i * TC + (th + 1) * 512],
                    zbuf.ap, ALU.mult),
                    reads=(zbuf, ybuf), writes=(ybuf,), wide=True)

        banks = kb.psum
        for i in range(NDT):
            cbuf = cwsl[i % 2]
            kb.wr("G", cbuf)
            kb.op("G", lambda G, i=i, cbuf=cbuf: G.dma_start(
                out=cbuf.ap, in_=cw.ap()[i * 128:(i + 1) * 128, :]), 16)
            kb.did_w("G", cbuf)
            kb.rd("T", ybuf, cbuf)
            if i == 0:
                for b in banks:
                    kb.wr("T", b)
            def mmc(T, i=i, cbuf=cbuf):
                for tsub in range(4):
                    for fh in range(2):
                        ins = T.matmul(
                            banks[tsub * 2 + fh].h[:],
                            ybuf[:, i * TC + tsub * 128:
                                 i * TC + (tsub + 1) * 128],
                            cbuf[:, fh * 512:(fh + 1) * 512],
                            start=(i == 0), stop=(i == NDT - 1))
                return ins
            kb.op("T", mmc)
            kb.did_r("T", ybuf, cbuf)
            if i == NDT - 1:
                for b in banks:
                    kb.did_w("T", b)
        for tsub in range(4):
            tt = ch * (TC // 128) + tsub
            xres = load_x(kb, tt)
            for fh in range(2):
                bank = banks[tsub * 2 + fh]
                kb.rd("V", xres, bank)
                kb.op("V", lambda V, xres=xres, bank=bank, fh=fh:
                      V.tensor_tensor(xres[:, fh * 512:(fh + 1) * 512],
                                      bank.h[:],
                                      xres[:, fh * 512:(fh + 1) * 512],
                                      ALU.add))
                kb.did_r("V", bank)
            kb.did_w("V", xres, wide=True)
            store_x(kb, tt, xres)



RD48 = 48


def emit_lv_partial(kb, load_x, out_dram, lnscr, stats):
    lv = kb.sb("p_lv", [128, NT + 2], F32)
    onescol = kb.sb("p_ones", [128, 1], F32)
    kb.op("V", lambda V: V.memset(onescol.ap, 1.0))
    kb.did_w("V", onescol)
    hs = kb.sb("p_hs", [128, D], BF16)
    for grp in range(NT // 8):
        xts = [load_x(kb, grp * 8 + g) for g in range(8)]
        for g in range(8):
            emit_ln_stats(kb, xts[g], lnscr, stats, g, xcols=(0, D))
        emit_ln_finish(kb, stats, 8)
        for g in range(8):
            tt = grp * 8 + g
            emit_ln_apply(kb, xts[g], hs, lnscr, stats, g, xcols=(0, D))
            emit_lv_col(kb, hs, lnscr, stats, lv, tt)
    kb.eop("V", lambda V: V.tensor_reduce(lv[:, NT:NT + 1], lv[:, 0:NT],
                                          AX.X, ALU.add),
           reads=(lv,), writes=(lv,))
    bank = kb.psum[0]
    kb.rd("T", lv, onescol); kb.wr("T", bank)
    kb.op("T", lambda T, bank=bank: T.matmul(
        bank.h[0:1, 0:1], lv.h[:, NT:NT + 1], onescol.h[:, 0:1],
        start=True, stop=True))
    kb.did_w("T", bank)
    kb.eop("V", lambda V, bank=bank: V.tensor_copy(lv[0:1, NT + 1:NT + 2],
                                                   bank.h[0:1, 0:1]),
           reads=(bank,), writes=(lv,))
    kb.did_r("V", bank)
    kb.rd("Y", lv)
    kb.op("Y", lambda Y: Y.dma_start(out=out_dram.ap()[0:1, 0:1],
                                     in_=lv[0:1, NT + 1:NT + 2]), 16)
    kb.did_r("Y", lv)


def emit_lv_col(kb, hs, scratch, stats, lv, tt):
    st = stats.h
    s1c, s2c, mc, vc = st[:, 40:41], st[:, 41:42], st[:, 42:43], st[:, 43:44]
    kb.eop("S", lambda S: S.activation(scratch.ap, hs.ap, AF.Square,
                                       accum_out=s2c),
           reads=(hs,), writes=(scratch, stats), wide=True)
    kb.eop("V", lambda V: V.reduce_sum(s1c, hs.ap, AX.X),
           reads=(hs, stats), writes=(stats,))
    kb.eop("V", lambda V: V.tensor_scalar(mc, s1c, 1.0 / D, None, ALU.mult),
           reads=(stats,), writes=(stats,))
    kb.eop("V", lambda V: V.tensor_scalar(vc, s2c, 1.0 / D, None, ALU.mult),
           reads=(stats,), writes=(stats,))
    kb.eop("V", lambda V: V.tensor_tensor(mc, mc, mc, ALU.mult),
           reads=(stats,), writes=(stats,))
    kb.eop("V", lambda V: V.tensor_tensor(vc, vc, mc, ALU.subtract),
           reads=(stats,), writes=(stats,))
    one_ap = kb.constcol(1.0)
    kb.eop("S", lambda S, tt=tt: S.activation(lv[:, tt:tt + 1], vc, AF.Ln,
                                              bias=one_ap),
           reads=(stats,), writes=(lv,))


def emit_l1ffn(kb, load_x, store_x, qw_d, rw_d, ak_d, av_d, rel_d, w1_d,
               b1_d, w2_d, b2gwgb_d, partials8_d, w_dram, scr2_d, ident,
               lnscr, stats):
    onesrow = kb.sb("l_ones", [1, 128], F32)
    onescol = kb.sb("l_onesc", [128, 1], F32)
    onesrow_b = kb.sb("l_onesb", [1, 128], BF16)
    kb.op("V", lambda V: V.memset(onesrow.ap, 1.0))
    kb.op("V", lambda V: V.memset(onescol.ap, 1.0))
    kb.op("V", lambda V: V.memset(onesrow_b.ap, 1.0))
    kb.did_w("V", onesrow, onescol, onesrow_b)

    stg = kb.sb("l_stg", [128, D], BF16)
    qwT = kb.sb("l_qwT", [128, 8 * 128], BF16)
    rwT = kb.sb("l_rwT", [128, RD48], BF16)
    akT = kb.sb("l_akT", [128, T_CORE], BF16)
    rkaugT = kb.sb("l_rkaug", [49, T_CORE], BF16)
    w1T = kb.sb("l_w1T", [128, 8 * 256], BF16)
    b1row = kb.sb("l_b1row", [1, 256], F32)
    w2rep = kb.sb("l_w2rep", [128, 256], BF16)
    c3 = kb.sb("l_c3", [128, 4], F32)
    par8 = kb.sb("l_par8", [1, 8], F32)
    sc8 = kb.sb("l_sc8", [1, 8], F32)
    meanrep = kb.sb("l_meanrep", [128, 1], F32)

    kb.op("G", lambda G: G.dma_start(out=stg[:, 0:D], in_=qw_d.ap()), 16)
    kb.did_w("G", stg)
    kb.rd("Y", stg)
    kb.op("Y", lambda Y: Y.dma_start(out=scr2_d.ap()[0:128, 0:D],
                                     in_=stg[:, 0:D]), 16)
    kb.did_r("Y", stg)
    kb.wr("G", stg)
    kb.op("G", lambda G: G.dma_start(out=stg[0:48, 0:128], in_=rw_d.ap()), 16)
    kb.did_w("G", stg)
    kb.rd("Y", stg)
    kb.op("Y", lambda Y: Y.dma_start(out=scr2_d.ap()[128:176, 0:128],
                                     in_=stg[0:48, 0:128]), 16)
    kb.did_r("Y", stg)
    for i in range(2):
        kb.wr("G", stg)
        kb.op("G", lambda G, i=i: G.dma_start(
            out=stg[:, 0:D], in_=w1_d.ap()[i * 128:(i + 1) * 128, :]), 16)
        kb.did_w("G", stg)
        kb.rd("Y", stg)
        kb.op("Y", lambda Y, i=i: Y.dma_start(
            out=scr2_d.ap()[256 + i * 128:256 + (i + 1) * 128, 0:D],
            in_=stg[:, 0:D]), 16)
        kb.did_r("Y", stg)
    for i in range(32):
        kb.wr("G", stg)
        kb.op("G", lambda G, i=i: G.dma_start(
            out=stg[:, 0:128], in_=ak_d.ap()[i * 128:(i + 1) * 128, :]), 16)
        kb.did_w("G", stg)
        kb.rd("Y", stg)
        kb.op("Y", lambda Y, i=i: Y.dma_start(
            out=scr2_d.ap()[1024 + i * 128:1024 + (i + 1) * 128, 0:128],
            in_=stg[:, 0:128]), 16)
        kb.did_r("Y", stg)
    kb.wait("Y", "Y")
    for c in range(8):
        kb.op("Y", lambda Y, c=c: Y.dma_start(
            out=qwT[:, c * 128:(c + 1) * 128],
            in_=scr2_d.ap()[0:128, c * 128:(c + 1) * 128], transpose=True), 16)
    kb.op("Y", lambda Y: Y.dma_start(
        out=rwT[:, 0:RD48], in_=scr2_d.ap()[128:176, 0:128],
        transpose=True), 16)
    for c in range(8):
        kb.op("Y", lambda Y, c=c: Y.dma_start(
            out=w1T[:, c * 256:(c + 1) * 256],
            in_=scr2_d.ap()[256:512, c * 128:(c + 1) * 128], transpose=True),
            16)
    kb.op("Y", lambda Y: Y.dma_start(
        out=akT.ap, in_=scr2_d.ap()[1024:5120, 0:128], transpose=True), 16)
    kb.did_w("Y", qwT, rwT, w1T, akT)
    kb.op("Y", lambda Y: Y.dma_start(out=b1row.ap, in_=b1_d.ap().unsqueeze(0)),
          16)
    kb.op("Y", lambda Y: Y.dma_start(out=par8.ap,
                                     in_=partials8_d.ap().unsqueeze(0)), 16)
    kb.did_w("Y", b1row, par8)
    w2row = kb.sb("l_w2row", [1, 256], F32)
    sc3row = kb.sb("l_sc3row", [1, 4], F32)
    kb.op("Y", lambda Y: Y.dma_start(out=w2row.ap, in_=w2_d.ap()), 16)
    kb.op("Y", lambda Y: Y.dma_start(out=sc3row[0:1, 0:3],
                                     in_=b2gwgb_d.ap().unsqueeze(0)), 16)
    kb.did_w("Y", w2row, sc3row)
    bank = kb.psum[0]
    kb.rd("T", w2row, onesrow); kb.wr("T", bank)
    kb.op("T", lambda T, bank=bank: T.matmul(bank.h[:, 0:256],
                                             onesrow.h[0:1, :], w2row.ap,
                                             start=True, stop=True))
    kb.did_w("T", bank)
    kb.rd("S", bank); kb.wr("S", w2rep)
    kb.op("S", lambda S, bank=bank: S.activation(w2rep.ap, bank.h[:, 0:256],
                                                 AF.Copy))
    kb.did_r("S", bank); kb.did_w("S", w2rep)
    bank = kb.psum[1]
    kb.rd("T", sc3row, onesrow); kb.wr("T", bank)
    kb.op("T", lambda T, bank=bank: T.matmul(bank.h[:, 0:4], onesrow.h[0:1, :],
                                             sc3row.ap, start=True, stop=True))
    kb.did_w("T", bank)
    kb.eop("V", lambda V, bank=bank: V.tensor_copy(c3.ap, bank.h[:, 0:4]),
           reads=(bank,), writes=(c3,))
    kb.did_r("V", bank)
    kb.eop("V", lambda V: V.tensor_reduce(sc8[0:1, 0:1], par8.ap, AX.X,
                                          ALU.add),
           reads=(par8,), writes=(sc8,))
    kb.eop("V", lambda V: V.tensor_scalar(sc8[0:1, 1:2], sc8[0:1, 0:1],
                                          1.0 / 32768.0, 1e-6, ALU.mult,
                                          ALU.add),
           reads=(sc8,), writes=(sc8,))
    kb.eop("V", lambda V: V.reciprocal(sc8[0:1, 2:3], sc8[0:1, 1:2]),
           reads=(sc8,), writes=(sc8,))
    bank = kb.psum[2]
    kb.rd("T", sc8, onesrow); kb.wr("T", bank)
    kb.op("T", lambda T, bank=bank: T.matmul(bank.h[:, 0:1], onesrow.h[0:1, :],
                                             sc8[0:1, 2:3], start=True,
                                             stop=True))
    kb.did_w("T", bank)
    kb.eop("V", lambda V, bank=bank: V.tensor_copy(meanrep.ap,
                                                   bank.h[:, 0:1]),
           reads=(bank,), writes=(meanrep,))
    kb.did_r("V", bank)
    kb.op("G", lambda G: G.dma_start(out=rkaugT.h[48:49, :],
                                     in_=rel_d.ap().unsqueeze(0)), 16)
    kb.did_w("G", rkaugT)
    for nch in range(8):
        bank = kb.psum[3]
        kb.rd("T", rwT, akT); kb.wr("T", bank)
        kb.op("T", lambda T, nch=nch, bank=bank: T.matmul(
            bank.h[0:RD48, :], rwT[:, 0:RD48],
            akT[:, nch * 512:(nch + 1) * 512], start=True, stop=True))
        kb.did_w("T", bank)
        kb.rd("S", bank); kb.wr("S", rkaugT)
        kb.op("S", lambda S, nch=nch, bank=bank: S.activation(
            rkaugT.h[0:48, nch * 512:(nch + 1) * 512], bank.h[0:RD48, :],
            AF.Copy))
        kb.did_r("S", bank)
    kb.did_w("S", rkaugT, wide=True)

    hs_full = kb.sb("l_hs", [128, NT * D], BF16)
    hsT = kb.sb("l_hsT", [128, 8 * 512], BF16)
    psT = Buf(None)
    psT.h = kb.psum[7].h[:].bitcast(BF16)
    psT.readers = kb.psum[7].readers
    hstile = kb.sb("l_hstile", [128, D], BF16)
    qT_s = kb.sb("l_qTs", [128, 512], BF16)
    rqaugT = kb.sb("l_rqaug", [49, 512], BF16)
    ones512b = kb.sb("l_ones512b", [1, 512], BF16)
    kb.op("V", lambda V: V.memset(ones512b.ap, 1.0))
    kb.did_w("V", ones512b)
    kb.rd("Y", ones512b)
    kb.op("Y", lambda Y: Y.dma_start(out=rqaugT.h[48:49, :],
                                     in_=ones512b.ap), 16)
    kb.did_r("Y", ones512b)
    kb.did_w("Y", rqaugT)
    Sbuf = kb.sb("l_S", [128, T_CORE], BF16)
    m8 = kb.sb("l_m8", [128, 8], F32)
    mwork = kb.sb("l_mwork", [128, T_CORE], BF16)
    wrow = kb.sb("l_wrow", [128, T_CORE], BF16)
    l_all = kb.sb("l_lall", [128, NT], F32)
    sig_all = kb.sb("l_sigall", [128, NT], F32)
    lv_all = kb.sb("l_lvall", [128, NT], F32)
    h1buf = kb.sb("l_h1", [128, 256], BF16)
    u1 = kb.sb("l_u1", [128, NT], F32)

    x3 = hsT.h[:].rearrange("p (c t) -> p c t", c=8)
    for sup in range(8):
        xts = [load_x(kb, sup * 4 + g) for g in range(4)]
        for g in range(4):
            emit_ln_stats(kb, xts[g], lnscr, stats, g, xcols=(0, D))
        emit_ln_finish(kb, stats, 4)
        for g in range(4):
            tt = sup * 4 + g
            emit_ln_apply(kb, xts[g], hstile, lnscr, stats, g, xcols=(0, D))
            emit_lv_col(kb, hstile, lnscr, stats, lv_all, tt)
            kb.eop("V", lambda V, tt=tt: V.tensor_copy(
                hs_full[:, tt * D:(tt + 1) * D], hstile.ap),
                reads=(hstile,), writes=(hs_full,), wide=True)
            kb.rd("T", hstile, ident); kb.wr("T", psT)
            def tr(T, hstile=hstile, psT=psT):
                for c in range(8):
                    ins = T.transpose(psT.h[:, c * 128:(c + 1) * 128],
                                      hstile[:, c * 128:(c + 1) * 128],
                                      ident.h[:])
                return ins
            kb.op("T", tr)
            kb.did_r("T", hstile); kb.did_w("T", psT)
            kb.rd("V", psT); kb.wr("V", hsT)
            kb.op("V", lambda V, g=g, x3=x3: V.tensor_copy(
                x3[:, :, g * 128:(g + 1) * 128],
                psT.h[:].rearrange("p (c t) -> p c t", c=8)))
            kb.did_r("V", psT)
        kb.did_w("V", hsT, wide=True)
        bank = kb.psum[0]
        kb.rd("T", qwT, hsT); kb.wr("T", bank)
        def mmq(T, bank=bank, x3=x3):
            for c in range(8):
                ins = T.matmul(bank.h[:], qwT[:, c * 128:(c + 1) * 128],
                               x3[:, c, :], start=(c == 0), stop=(c == 7))
            return ins
        kb.op("T", mmq)
        kb.did_w("T", bank); kb.did_r("T", hsT)
        kb.rd("S", bank); kb.wr("S", qT_s)
        kb.op("S", lambda S, bank=bank: S.activation(qT_s.ap, bank.h[:],
                                                     AF.Copy))
        kb.did_r("S", bank); kb.did_w("S", qT_s)
        bank = kb.psum[1]
        kb.rd("T", rwT, qT_s); kb.wr("T", bank)
        kb.op("T", lambda T, bank=bank: T.matmul(
            bank.h[0:RD48, :], rwT[:, 0:RD48], qT_s.ap, start=True,
            stop=True))
        kb.did_w("T", bank)
        kb.rd("S", bank); kb.wr("S", rqaugT)
        kb.op("S", lambda S, bank=bank: S.activation(
            rqaugT.h[0:48, :], bank.h[0:RD48, :], AF.Copy,
            scale=1.0 / float(np.sqrt(48.0))))
        kb.did_r("S", bank); kb.did_w("S", rqaugT)
        for g in range(4):
            tt = sup * 4 + g
            bank = kb.psum[2]
            kb.rd("T", w1T, hsT, onesrow, b1row); kb.wr("T", bank)
            def mmh1(T, g=g, bank=bank, x3=x3):
                for c in range(8):
                    T.matmul(bank.h[:, 0:256],
                             x3[:, c, g * 128:(g + 1) * 128],
                             w1T[:, c * 256:(c + 1) * 256],
                             start=(c == 0), stop=False)
                return T.matmul(bank.h[:, 0:256], onesrow.h[0:1, :],
                                b1row.ap, start=False, stop=True)
            kb.op("T", mmh1)
            kb.did_w("T", bank)
            kb.rd("S", bank); kb.wr("S", h1buf)
            kb.op("S", lambda S, bank=bank: S.activation(
                h1buf.ap, bank.h[:, 0:256], AF.Gelu))
            kb.did_r("S", bank); kb.did_w("S", h1buf)
            kb.eop("V", lambda V: V.tensor_tensor(h1buf.ap, h1buf.ap,
                                                  w2rep.ap, ALU.mult),
                   reads=(h1buf, w2rep), writes=(h1buf,), wide=True)
            kb.eop("V", lambda V, tt=tt: V.reduce_sum(u1[:, tt:tt + 1],
                                                      h1buf.ap, AX.X),
                   reads=(h1buf, u1), writes=(u1,))
        for g in range(4):
            tt = sup * 4 + g
            for nch in range(8):
                bank = kb.psum[3 + nch % 2]
                kb.rd("T", rqaugT, rkaugT); kb.wr("T", bank)
                kb.op("T", lambda T, g=g, nch=nch, bank=bank: T.matmul(
                    bank.h[:], rqaugT.h[0:49, g * 128:(g + 1) * 128],
                    rkaugT.h[0:49, nch * 512:(nch + 1) * 512],
                    start=True, stop=True))
                kb.did_w("T", bank)
                kb.rd("S", bank); kb.wr("S", Sbuf)
                kb.op("S", lambda S, nch=nch, bank=bank: S.activation(
                    Sbuf[:, nch * 512:(nch + 1) * 512], bank.h[:], AF.Copy))
                kb.did_r("S", bank)
            kb.did_w("S", Sbuf, wide=True)
            kb.eop("V", lambda V: V.max(m8.ap, Sbuf.ap),
                   reads=(Sbuf,), writes=(m8,))
            kb.eop("V", lambda V: V.tensor_scalar(
                mwork.ap, Sbuf.ap, m8[:, 7:8], 30000.0, ALU.is_ge, ALU.mult),
                reads=(Sbuf, m8), writes=(mwork,), wide=True)
            kb.eop("V", lambda V: V.scalar_tensor_tensor(
                mwork.ap, Sbuf.ap, -30000.0, mwork.ap, ALU.add, ALU.add),
                reads=(Sbuf, mwork), writes=(mwork,), wide=True)
            kb.eop("V", lambda V: V.tensor_scalar(m8[:, 1:2], m8[:, 0:1],
                                                  -1.0, None, ALU.mult),
                   reads=(m8,), writes=(m8,))
            kb.rd("S", mwork, m8); kb.wr("S", wrow)
            kb.op("S", lambda S, tt=tt: S.activation(
                wrow.ap, mwork.ap, AF.Exp, bias=m8[:, 1:2],
                accum_out=l_all[:, tt:tt + 1]))
            kb.did_r("S", mwork); kb.did_w("S", wrow, wide=True)
            kb.did_w("S", l_all)
            kb.rd("Y", wrow)
            kb.op("Y", lambda Y, tt=tt: Y.dma_start(
                out=w_dram.ap()[tt * 128:(tt + 1) * 128, :], in_=wrow.ap), 16)
            kb.did_r("Y", wrow)

    kb.eop("V", lambda V: V.tensor_single_scalar(u1.ap, u1.ap, c3[:, 0:1],
                                                 ALU.add),
           reads=(u1, c3), writes=(u1,), wide=True)
    kb.eop("S", lambda S: S.activation(sig_all.ap, u1.ap, AF.Sigmoid),
           reads=(u1,), writes=(sig_all,), wide=True)
    kb.eop("V", lambda V: V.tensor_single_scalar(lv_all.ap, lv_all.ap,
                                                 meanrep[:, 0:1], ALU.mult),
           reads=(lv_all, meanrep), writes=(lv_all,), wide=True)
    kb.eop("V", lambda V: V.tensor_scalar(lv_all.ap, lv_all.ap, 0.5, None,
                                          ALU.mult),
           reads=(lv_all,), writes=(lv_all,))
    kb.eop("V", lambda V: V.scalar_tensor_tensor(
        sig_all.ap, sig_all.ap, 2.5, lv_all.ap, ALU.mult, ALU.add),
        reads=(sig_all, lv_all), writes=(sig_all,))
    kb.eop("V", lambda V: V.tensor_scalar(sig_all.ap, sig_all.ap, 0.0, 5.0,
                                          ALU.max, ALU.min),
           reads=(sig_all,), writes=(sig_all,))
    kb.eop("V", lambda V: V.tensor_single_scalar(sig_all.ap, sig_all.ap,
                                                 c3[:, 1:2], ALU.mult),
           reads=(sig_all, c3), writes=(sig_all,))
    kb.eop("V", lambda V: V.tensor_single_scalar(sig_all.ap, sig_all.ap,
                                                 c3[:, 2:3], ALU.add),
           reads=(sig_all, c3), writes=(sig_all,))
    kb.eop("S", lambda S: S.activation(sig_all.ap, sig_all.ap, AF.Sigmoid),
           reads=(sig_all,), writes=(sig_all,), wide=True)
    kb.eop("V", lambda V: V.reciprocal(l_all.ap, l_all.ap),
           reads=(l_all,), writes=(l_all,), wide=True)
    kb.eop("V", lambda V: V.tensor_tensor(sig_all.ap, sig_all.ap, l_all.ap,
                                          ALU.mult),
           reads=(sig_all, l_all), writes=(sig_all,))
    kb.eop("V", lambda V: V.tensor_scalar(sig_all.ap, sig_all.ap, 0.5, None,
                                          ALU.mult),
           reads=(sig_all,), writes=(sig_all,), wide=True)

    kb.wait("Y", "Y")
    wTs = [kb.sb(f"l_wT{i}", [128, 512], BF16) for i in range(2)]
    avb = [kb.sb(f"l_av{i}", [128, D], BF16) for i in range(2)]
    banks = kb.psum
    for sup in range(8):
        for n in range(32):
            wbuf = wTs[n % 2]
            abuf = avb[n % 2]
            kb.wr("Y", wbuf)
            kb.op("Y", lambda Y, sup=sup, n=n, wbuf=wbuf: Y.dma_start(
                out=wbuf.ap,
                in_=w_dram.ap()[sup * 512:(sup + 1) * 512,
                                n * 128:(n + 1) * 128], transpose=True), 16)
            kb.did_w("Y", wbuf)
            kb.wr("G", abuf)
            kb.op("G", lambda G, n=n, abuf=abuf: G.dma_start(
                out=abuf.ap, in_=av_d.ap()[n * 128:(n + 1) * 128, :]), 16)
            kb.did_w("G", abuf)
            kb.rd("T", wbuf, abuf)
            if n == 0:
                for b in banks:
                    kb.wr("T", b)
            def mma(T, n=n, wbuf=wbuf, abuf=abuf):
                for qt in range(4):
                    for hh in range(2):
                        ins = T.matmul(
                            banks[qt * 2 + hh].h[:],
                            wbuf[:, qt * 128:(qt + 1) * 128],
                            abuf[:, hh * 512:(hh + 1) * 512],
                            start=(n == 0), stop=(n == 31))
                return ins
            kb.op("T", mma)
            kb.did_r("T", wbuf, abuf)
            if n == 31:
                for b in banks:
                    kb.did_w("T", b)
        for qt in range(4):
            tt = sup * 4 + qt
            xres = load_x(kb, tt)
            for hh in range(2):
                bank = banks[qt * 2 + hh]
                kb.rd("V", xres, bank, sig_all, hs_full)
                kb.op("V", lambda V, xres=xres, bank=bank, hh=hh, tt=tt:
                      V.scalar_tensor_tensor(
                          xres[:, hh * 512:(hh + 1) * 512], bank.h[:],
                          sig_all[:, tt:tt + 1],
                          xres[:, hh * 512:(hh + 1) * 512],
                          ALU.mult, ALU.add))
                kb.op("V", lambda V, xres=xres, hh=hh, tt=tt:
                      V.scalar_tensor_tensor(
                          xres[:, hh * 512:(hh + 1) * 512],
                          hs_full[:, tt * D + hh * 512:
                                  tt * D + (hh + 1) * 512],
                          0.5, xres[:, hh * 512:(hh + 1) * 512],
                          ALU.mult, ALU.add))
                kb.did_r("V", bank, hs_full)
            kb.did_w("V", xres, wide=True)
            store_x(kb, tt, xres)



def build_l0ffn_launch():
    kb = KB()
    nc = kb.nc
    x_in = nc.dram_tensor("x_in", [T_CORE, D], F32, kind="ExternalInput")
    w_gate = nc.dram_tensor("w_gate", [I_FFN, D], F32, kind="ExternalInput")
    w_up = nc.dram_tensor("w_up", [I_FFN, D], F32, kind="ExternalInput")
    w_down = nc.dram_tensor("w_down", [D, I_FFN], F32, kind="ExternalInput")
    id_in = nc.dram_tensor("ident_in", [128, 128], BF16, kind="ExternalInput")
    x_out = nc.dram_tensor("x_out", [T_CORE, D], F32, kind="ExternalOutput")
    scr_q = nc.dram_tensor("ffn_scr", [6656, I_FFN], BF16)
    scr_guT = nc.dram_tensor("ffn_guT", [8, 128, 5632], BF16)
    scr_dnT = nc.dram_tensor("ffn_dnT", [NI, 128, D], BF16)
    import os
    _dbg = os.environ.get("FFN_DBG") == "1"
    hq_dram = nc.dram_tensor("hq_scr", [T_CORE, I_FFN], BF16,
                             kind="ExternalOutput" if _dbg else "Internal")

    ident = kb.sb("ident", [128, 128], BF16)
    kb.op("Y", lambda Y: Y.dma_start(out=ident.h[:], in_=id_in.ap()), 16)
    kb.did_w("Y", ident)

    qsc = quant_scratch(kb, "sh", I_FFN)
    big_f32 = kb.sb("big_f32", [128, I_FFN], F32)
    big_bf_a = kb.sb("big_bf_a", [128, I_FFN], BF16)
    big_bf_b = kb.sb("big_bf_b", [128, I_FFN], BF16)

    emit_l0ffn_weight_prep(kb, w_gate, w_up, w_down, scr_q, scr_guT, scr_dnT,
                           qsc, big_f32, big_bf_a, big_bf_b)

    xtiles = [kb.sb(f"xtile{i}", [128, D], F32) for i in range(9)]
    xslot = [0]

    def load_x(kb, tt):
        buf = xtiles[xslot[0] % len(xtiles)]
        xslot[0] += 1
        kb.wr("Y", buf)
        kb.op("Y", lambda Y, tt=tt, buf=buf: Y.dma_start(
            out=buf[:, 0:D], in_=x_in.ap()[tt * 128:(tt + 1) * 128, :]), 16)
        kb.did_w("Y", buf)
        return buf

    def store_x(kb, tt, buf):
        kb.rd("Y", buf)
        kb.op("Y", lambda Y, tt=tt, buf=buf: Y.dma_start(
            out=x_out.ap()[tt * 128:(tt + 1) * 128, :], in_=buf[:, 0:D]), 16)
        kb.did_r("Y", buf)

    emit_l0ffn(kb, load_x, store_x, scr_guT, scr_dnT, hq_dram, ident,
               qsc, big_f32, big_bf_a, big_bf_b)
    return kb.finalize()


def build_l0attn_launch():
    kb = KB()
    nc = kb.nc
    x_in = nc.dram_tensor("x_in", [T_CORE, D], F32, kind="ExternalInput")
    w_qkv = nc.dram_tensor("w_qkv", [D, 3072], F32, kind="ExternalInput")
    b_qkv = nc.dram_tensor("b_qkv", [3072], F32, kind="ExternalInput")
    w_out = nc.dram_tensor("w_out", [D, D], F32, kind="ExternalInput")
    b_out = nc.dram_tensor("b_out", [D], F32, kind="ExternalInput")
    id_in = nc.dram_tensor("ident_in", [128, 128], BF16, kind="ExternalInput")
    mask_in = nc.dram_tensor("mask_in", [4, 128, 512], BF16,
                             kind="ExternalInput")
    x_out = nc.dram_tensor("x_out", [T_CORE, D], F32, kind="ExternalOutput")
    qkT_dram = nc.dram_tensor("qkT_scr", [2048, T_CORE], BF16)
    v_dram = nc.dram_tensor("v_scr", [T_CORE, D], BF16)
    ctxT_dram = nc.dram_tensor("ctxT_scr", [D, T_CORE], BF16)

    ident = kb.sb("ident", [128, 128], BF16)
    maskb = kb.sb("maskb", [128, 4 * 512], BF16)
    kb.op("Y", lambda Y: Y.dma_start(out=ident.h[:], in_=id_in.ap()), 16)
    for a in range(4):
        kb.op("Y", lambda Y, a=a: Y.dma_start(
            out=maskb.h[:, a * 512:(a + 1) * 512], in_=mask_in.ap()[a]), 16)
    kb.did_w("Y", ident, maskb)
    mask4 = [maskb.h[:, a * 512:(a + 1) * 512] for a in range(4)]
    lnscr = kb.sb("lnscr", [128, D], BF16)
    stats = kb.sb("stats", [128, 48], F32)

    xtiles = [kb.sb(f"xtile{i}", [128, D], F32) for i in range(5)]
    xslot = [0]

    def load_x(kb, tt):
        buf = xtiles[xslot[0] % len(xtiles)]
        xslot[0] += 1
        kb.wr("Y", buf)
        kb.op("Y", lambda Y, tt=tt, buf=buf: Y.dma_start(
            out=buf[:, 0:D], in_=x_in.ap()[tt * 128:(tt + 1) * 128, :]), 16)
        kb.did_w("Y", buf)
        return buf

    def store_x(kb, tt, buf):
        kb.rd("Y", buf)
        kb.op("Y", lambda Y, tt=tt, buf=buf: Y.dma_start(
            out=x_out.ap()[tt * 128:(tt + 1) * 128, :], in_=buf[:, 0:D]), 16)
        kb.did_r("Y", buf)

    emit_l0attn(kb, load_x, store_x, w_qkv, b_qkv, w_out, b_out,
                qkT_dram, v_dram, ctxT_dram, ident, mask4, lnscr, stats)
    return kb.finalize()


def make_masks():
    m = np.zeros((4, 128, 512), np.float32)
    for a in range(4):
        for k in range(128):
            m[a, k, :a * 128 + k] = -1e9
    return m.astype(ml_dtypes.bfloat16)


def run_l0attn(x_full, w_qkv, b_qkv, w_out, b_out):
    nc = build_l0attn_launch()
    masks = make_masks()
    in_maps = [{"x_in": np.ascontiguousarray(x_full[c]), "w_qkv": w_qkv,
                "b_qkv": b_qkv, "w_out": w_out, "b_out": b_out,
                "ident_in": IDENT_BF16, "mask_in": masks} for c in range(8)]
    res = run_bass_kernel_spmd(nc, in_maps, core_ids=list(range(8)))
    return np.stack([res.results[c]["x_out"] for c in range(8)])


def build_mamba_launch():
    kb = KB()
    nc = kb.nc
    x_in = nc.dram_tensor("x_in", [T_CORE, D], F32, kind="ExternalInput")
    win = nc.dram_tensor("in_proj_w", [D, 2 * DIN], F32, kind="ExternalInput")
    conv_w = nc.dram_tensor("conv_w", [DIN, 1, 4], F32, kind="ExternalInput")
    conv_b = nc.dram_tensor("conv_b", [DIN], F32, kind="ExternalInput")
    xproj = nc.dram_tensor("x_proj_w", [DIN, 33], F32, kind="ExternalInput")
    dtw = nc.dram_tensor("dt_proj_w", [1, DIN], F32, kind="ExternalInput")
    dtb = nc.dram_tensor("dt_proj_b", [DIN], F32, kind="ExternalInput")
    alog = nc.dram_tensor("A_log", [DIN, 16], F32, kind="ExternalInput")
    dpar = nc.dram_tensor("D_param", [DIN], F32, kind="ExternalInput")
    cw = nc.dram_tensor("c_proj_w", [DIN, D], F32, kind="ExternalInput")
    id_in = nc.dram_tensor("ident_in", [128, 128], BF16, kind="ExternalInput")
    x_out = nc.dram_tensor("x_out", [T_CORE, D], F32, kind="ExternalOutput")
    z_dram = nc.dram_tensor("z_scr", [DIN, T_CORE], BF16)

    ident = kb.sb("ident", [128, 128], BF16)
    kb.op("Y", lambda Y: Y.dma_start(out=ident.h[:], in_=id_in.ap()), 16)
    kb.did_w("Y", ident)
    lnscr = kb.sb("lnscr", [128, D], BF16)
    stats = kb.sb("stats", [128, 48], F32)
    xtiles = [kb.sb(f"xtile{i}", [128, D], F32) for i in range(9)]
    xslot = [0]

    def load_x(kb, tt):
        buf = xtiles[xslot[0] % len(xtiles)]
        xslot[0] += 1
        kb.wr("Y", buf)
        kb.op("Y", lambda Y, tt=tt, buf=buf: Y.dma_start(
            out=buf[:, 0:D], in_=x_in.ap()[tt * 128:(tt + 1) * 128, :]), 16)
        kb.did_w("Y", buf)
        return buf

    def store_x(kb, tt, buf):
        kb.rd("Y", buf)
        kb.op("Y", lambda Y, tt=tt, buf=buf: Y.dma_start(
            out=x_out.ap()[tt * 128:(tt + 1) * 128, :], in_=buf[:, 0:D]), 16)
        kb.did_r("Y", buf)

    emit_l1mamba(kb, load_x, store_x, win, conv_w, conv_b, xproj, dtw, dtb,
                 alog, dpar, cw, z_dram, ident, lnscr, stats)
    return kb.finalize()


def run_mamba(x_full, z):
    nc = build_mamba_launch()
    in_maps = [{"x_in": np.ascontiguousarray(x_full[c]),
                "in_proj_w": z["L1_attn_in_proj_w"],
                "conv_w": z["L1_attn_conv_w"], "conv_b": z["L1_attn_conv_b"],
                "x_proj_w": z["L1_attn_x_proj_w"],
                "dt_proj_w": z["L1_attn_dt_proj_w"],
                "dt_proj_b": z["L1_attn_dt_proj_b"],
                "A_log": z["L1_attn_A_log"], "D_param": z["L1_attn_D_param"],
                "c_proj_w": z["L1_attn_c_proj_w"],
                "ident_in": IDENT_BF16} for c in range(8)]
    res = run_bass_kernel_spmd(nc, in_maps, core_ids=list(range(8)))
    return np.stack([res.results[c]["x_out"] for c in range(8)])


def _mk_loadstore(kb, x_in, x_out, ntiles=9):
    xtiles = [kb.sb(f"xtile{i}", [128, D], F32) for i in range(ntiles)]
    xslot = [0]

    def load_x(kb, tt):
        buf = xtiles[xslot[0] % len(xtiles)]
        xslot[0] += 1
        kb.wr("Y", buf)
        kb.op("Y", lambda Y, tt=tt, buf=buf: Y.dma_start(
            out=buf[:, 0:D], in_=x_in.ap()[tt * 128:(tt + 1) * 128, :]), 16)
        kb.did_w("Y", buf)
        return buf

    def store_x(kb, tt, buf):
        kb.rd("Y", buf)
        kb.op("Y", lambda Y, tt=tt, buf=buf: Y.dma_start(
            out=x_out.ap()[tt * 128:(tt + 1) * 128, :], in_=buf[:, 0:D]), 16)
        kb.did_r("Y", buf)
    return load_x, store_x


def build_lvpart_launch():
    kb = KB()
    nc = kb.nc
    x_in = nc.dram_tensor("x_in", [T_CORE, D], F32, kind="ExternalInput")
    out = nc.dram_tensor("lv_out", [1, 1], F32, kind="ExternalOutput")
    lnscr = kb.sb("lnscr", [128, D], BF16)
    stats = kb.sb("stats", [128, 48], F32)
    load_x, _ = _mk_loadstore(kb, x_in, None)
    emit_lv_partial(kb, load_x, out, lnscr, stats)
    return kb.finalize()


def build_l1ffn_launch():
    kb = KB()
    nc = kb.nc
    x_in = nc.dram_tensor("x_in", [T_CORE, D], F32, kind="ExternalInput")
    qw_d = nc.dram_tensor("q_proj_w", [128, D], F32, kind="ExternalInput")
    rw_d = nc.dram_tensor("router_w", [48, 128], F32, kind="ExternalInput")
    ak_d = nc.dram_tensor("aux_keys", [4096, 128], F32, kind="ExternalInput")
    av_d = nc.dram_tensor("aux_values", [4096, D], F32, kind="ExternalInput")
    rel_d = nc.dram_tensor("rel", [4096], F32, kind="ExternalInput")
    w1_d = nc.dram_tensor("unc_w1", [256, D], F32, kind="ExternalInput")
    b1_d = nc.dram_tensor("unc_b1", [256], F32, kind="ExternalInput")
    w2_d = nc.dram_tensor("unc_w2", [1, 256], F32, kind="ExternalInput")
    b2gwgb_d = nc.dram_tensor("b2gwgb", [3], F32, kind="ExternalInput")
    partials8_d = nc.dram_tensor("partials8", [8], F32, kind="ExternalInput")
    id_in = nc.dram_tensor("ident_in", [128, 128], BF16, kind="ExternalInput")
    x_out = nc.dram_tensor("x_out", [T_CORE, D], F32, kind="ExternalOutput")
    w_dram = nc.dram_tensor("w_scr", [T_CORE, 4096], BF16)
    scr2_d = nc.dram_tensor("scr2", [5120, D], BF16)

    ident = kb.sb("ident", [128, 128], BF16)
    kb.op("Y", lambda Y: Y.dma_start(out=ident.h[:], in_=id_in.ap()), 16)
    kb.did_w("Y", ident)
    lnscr = kb.sb("lnscr", [128, D], BF16)
    stats = kb.sb("stats", [128, 48], F32)
    load_x, store_x = _mk_loadstore(kb, x_in, x_out)
    emit_l1ffn(kb, load_x, store_x, qw_d, rw_d, ak_d, av_d, rel_d, w1_d,
               b1_d, w2_d, b2gwgb_d, partials8_d, w_dram, scr2_d, ident,
               lnscr, stats)
    return kb.finalize()


def run_l1ffn(x_full, z):
    ncA = build_lvpart_launch()
    in_mapsA = [{"x_in": np.ascontiguousarray(x_full[c])} for c in range(8)]
    resA = run_bass_kernel_spmd(ncA, in_mapsA, core_ids=list(range(8)))
    partials = np.concatenate([resA.results[c]["lv_out"].reshape(1)
                               for c in range(8)])
    ncB = build_l1ffn_launch()
    b2gwgb = np.stack([z["L1_ffn_unc_b2"].reshape(()),
                       z["L1_ffn_gate_w1"].reshape(()),
                       z["L1_ffn_gate_bias"].reshape(())]).astype(np.float32)
    in_mapsB = [{"x_in": np.ascontiguousarray(x_full[c]),
                 "q_proj_w": z["L1_ffn_q_proj_w"],
                 "router_w": z["L1_ffn_router_w"],
                 "aux_keys": z["L1_ffn_aux_keys"],
                 "aux_values": z["L1_ffn_aux_values"],
                 "rel": z["L1_ffn_reliability_mask"],
                 "unc_w1": z["L1_ffn_unc_w1"], "unc_b1": z["L1_ffn_unc_b1"],
                 "unc_w2": z["L1_ffn_unc_w2"], "b2gwgb": b2gwgb,
                 "partials8": partials, "ident_in": IDENT_BF16}
                for c in range(8)]
    resB = run_bass_kernel_spmd(ncB, in_mapsB, core_ids=list(range(8)))
    return np.stack([resB.results[c]["x_out"] for c in range(8)])


IDENT_BF16 = np.eye(128, dtype=ml_dtypes.bfloat16)


def run_l0ffn(x_full, w_gate, w_up, w_down):
    nc = build_l0ffn_launch()
    in_maps = [{"x_in": np.ascontiguousarray(x_full[c]),
                "w_gate": w_gate, "w_up": w_up, "w_down": w_down,
                "ident_in": IDENT_BF16} for c in range(8)]
    res = run_bass_kernel_spmd(nc, in_maps, core_ids=list(range(8)))
    return np.stack([res.results[c]["x_out"] for c in range(8)])



def _common_pre(kb, ntiles=9, pool_kb=150):
    nc = kb.nc
    kb.arena_init(pool_kb)
    id_in = nc.dram_tensor("ident_in", [128, 128], BF16, kind="ExternalInput")
    ident = kb.sb("ident", [128, 128], BF16)
    kb.op("Y", lambda Y: Y.dma_start(out=ident.h[:], in_=id_in.ap()), 16)
    kb.did_w("Y", ident)
    lnscr = kb.sb("lnscr", [128, D], BF16)
    stats = kb.sb("stats", [128, 48], F32)
    return ident, lnscr, stats


def _xio(kb, x_in_h=None, x_out_h=None, ntiles=9):
    xtiles = [kb.sb(f"xtile{i}", [128, D], F32) for i in range(ntiles)]
    state = {"in": x_in_h, "out": x_out_h, "slot": 0}

    def load_x(kb, tt):
        buf = xtiles[state["slot"] % len(xtiles)]
        state["slot"] += 1
        h = state["in"]
        kb.wr("Y", buf)
        kb.op("Y", lambda Y, tt=tt, buf=buf, h=h: Y.dma_start(
            out=buf[:, 0:D],
            in_=h.ap()[tt * 128:(tt + 1) * 128, :]), 16)
        kb.did_w("Y", buf)
        return buf

    def store_x(kb, tt, buf):
        h = state["out"]
        kb.rd("Y", buf)
        kb.op("Y", lambda Y, tt=tt, buf=buf, h=h: Y.dma_start(
            out=h.ap()[tt * 128:(tt + 1) * 128, :],
            in_=buf[:, 0:D]), 16)
        kb.did_r("Y", buf)
    return load_x, store_x, state


def build_launch1():
    kb = KB()
    nc = kb.nc
    x_in = nc.dram_tensor("x_in", [T_CORE, D], F32, kind="ExternalInput")
    w_gate = nc.dram_tensor("w_gate", [I_FFN, D], F32, kind="ExternalInput")
    w_up = nc.dram_tensor("w_up", [I_FFN, D], F32, kind="ExternalInput")
    w_down = nc.dram_tensor("w_down", [D, I_FFN], F32, kind="ExternalInput")
    w_qkv = nc.dram_tensor("w_qkv", [D, 3072], F32, kind="ExternalInput")
    b_qkv = nc.dram_tensor("b_qkv", [3072], F32, kind="ExternalInput")
    w_out = nc.dram_tensor("w_out", [D, D], F32, kind="ExternalInput")
    b_out = nc.dram_tensor("b_out", [D], F32, kind="ExternalInput")
    mask_in = nc.dram_tensor("mask_in", [4, 128, 512], BF16,
                             kind="ExternalInput")
    x_out3 = nc.dram_tensor("x_out3", [T_CORE, D], F32, kind="ExternalOutput")
    lv_out = nc.dram_tensor("lv_out", [1, 1], F32, kind="ExternalOutput")
    x_s1 = nc.dram_tensor("x_s1", [T_CORE, D], F32)
    x_s2 = nc.dram_tensor("x_s2", [T_CORE, D], F32)
    scr_q = nc.dram_tensor("ffn_scr", [6656, I_FFN], BF16)
    scr_guT = nc.dram_tensor("ffn_guT", [8, 128, 5632], BF16)
    scr_dnT = nc.dram_tensor("ffn_dnT", [NI, 128, D], BF16)
    hq_dram = nc.dram_tensor("hq_scr", [T_CORE, I_FFN], BF16)
    qkT_dram = nc.dram_tensor("qkT_scr", [2048, T_CORE], BF16)
    v_dram = nc.dram_tensor("v_scr", [T_CORE, D], BF16)
    ctxT_dram = nc.dram_tensor("ctxT_scr", [D, T_CORE], BF16)

    ident, lnscr, stats = _common_pre(kb)
    maskb = kb.sb("maskb", [128, 4 * 512], BF16)
    for a in range(4):
        kb.op("Y", lambda Y, a=a: Y.dma_start(
            out=maskb.h[:, a * 512:(a + 1) * 512], in_=mask_in.ap()[a]), 16)
    kb.did_w("Y", maskb)
    mask4 = [maskb.h[:, a * 512:(a + 1) * 512] for a in range(4)]
    load_x, store_x, state = _xio(kb)

    kb.arena_begin()
    qsc = quant_scratch(kb, "sh", I_FFN)
    big_f32 = kb.sb("big_f32", [128, I_FFN], F32)
    big_bf_a = kb.sb("big_bf_a", [128, I_FFN], BF16)
    big_bf_b = kb.sb("big_bf_b", [128, I_FFN], BF16)
    emit_l0ffn_weight_prep(kb, w_gate, w_up, w_down, scr_q, scr_guT, scr_dnT,
                           qsc, big_f32, big_bf_a, big_bf_b)
    state["in"], state["out"] = x_in, x_s1
    emit_l0ffn(kb, load_x, store_x, scr_guT, scr_dnT, hq_dram, ident,
               qsc, big_f32, big_bf_a, big_bf_b, grp_tiles=4)
    kb.arena_end(); kb.barrier()

    kb.arena_begin()
    state["in"], state["out"] = x_s1, x_s2
    emit_l0attn(kb, load_x, store_x, w_qkv, b_qkv, w_out, b_out,
                qkT_dram, v_dram, ctxT_dram, ident, mask4, lnscr, stats)
    kb.arena_end(); kb.barrier()

    kb.arena_begin()
    qsc = quant_scratch(kb, "sh2", I_FFN)
    big_f32 = kb.sb("big_f322", [128, I_FFN], F32)
    big_bf_a = kb.sb("big_bf_a2", [128, I_FFN], BF16)
    big_bf_b = kb.sb("big_bf_b2", [128, I_FFN], BF16)
    state["in"], state["out"] = x_s2, x_out3
    emit_l0ffn(kb, load_x, store_x, scr_guT, scr_dnT, hq_dram, ident,
               qsc, big_f32, big_bf_a, big_bf_b, grp_tiles=4)
    kb.arena_end(); kb.barrier()

    kb.arena_begin()
    state["in"] = x_out3
    emit_lv_partial(kb, load_x, lv_out, lnscr, stats)
    kb.arena_end()
    return kb.finalize()


def build_launch2():
    kb = KB()
    nc = kb.nc
    x_in = nc.dram_tensor("x_in", [T_CORE, D], F32, kind="ExternalInput")
    qw_d = nc.dram_tensor("q_proj_w", [128, D], F32, kind="ExternalInput")
    rw_d = nc.dram_tensor("router_w", [48, 128], F32, kind="ExternalInput")
    ak_d = nc.dram_tensor("aux_keys", [4096, 128], F32, kind="ExternalInput")
    av_d = nc.dram_tensor("aux_values", [4096, D], F32, kind="ExternalInput")
    rel_d = nc.dram_tensor("rel", [4096], F32, kind="ExternalInput")
    w1_d = nc.dram_tensor("unc_w1", [256, D], F32, kind="ExternalInput")
    b1_d = nc.dram_tensor("unc_b1", [256], F32, kind="ExternalInput")
    w2_d = nc.dram_tensor("unc_w2", [1, 256], F32, kind="ExternalInput")
    b2gwgb_d = nc.dram_tensor("b2gwgb", [3], F32, kind="ExternalInput")
    partials8_d = nc.dram_tensor("partials8", [8], F32, kind="ExternalInput")
    win = nc.dram_tensor("in_proj_w", [D, 2 * DIN], F32, kind="ExternalInput")
    conv_w = nc.dram_tensor("conv_w", [DIN, 1, 4], F32, kind="ExternalInput")
    conv_b = nc.dram_tensor("conv_b", [DIN], F32, kind="ExternalInput")
    xproj = nc.dram_tensor("x_proj_w", [DIN, 33], F32, kind="ExternalInput")
    dtw = nc.dram_tensor("dt_proj_w", [1, DIN], F32, kind="ExternalInput")
    dtb = nc.dram_tensor("dt_proj_b", [DIN], F32, kind="ExternalInput")
    alog = nc.dram_tensor("A_log", [DIN, 16], F32, kind="ExternalInput")
    dpar = nc.dram_tensor("D_param", [DIN], F32, kind="ExternalInput")
    cw = nc.dram_tensor("c_proj_w", [DIN, D], F32, kind="ExternalInput")
    x_out5 = nc.dram_tensor("x_out5", [T_CORE, D], F32, kind="ExternalOutput")
    lv_out = nc.dram_tensor("lv_out", [1, 1], F32, kind="ExternalOutput")
    x_s1 = nc.dram_tensor("x_s1", [T_CORE, D], F32)
    w_dram = nc.dram_tensor("w_scr", [T_CORE, 4096], BF16)
    scr2_d = nc.dram_tensor("scr2", [5120, D], BF16)
    z_dram = nc.dram_tensor("z_scr", [DIN, T_CORE], BF16)

    ident, lnscr, stats = _common_pre(kb)
    load_x, store_x, state = _xio(kb)

    kb.arena_begin()
    state["in"], state["out"] = x_in, x_s1
    emit_l1ffn(kb, load_x, store_x, qw_d, rw_d, ak_d, av_d, rel_d, w1_d,
               b1_d, w2_d, b2gwgb_d, partials8_d, w_dram, scr2_d, ident,
               lnscr, stats)
    kb.arena_end(); kb.barrier()

    kb.arena_begin()
    state["in"], state["out"] = x_s1, x_out5
    emit_l1mamba(kb, load_x, store_x, win, conv_w, conv_b, xproj, dtw, dtb,
                 alog, dpar, cw, z_dram, ident, lnscr, stats)
    kb.arena_end(); kb.barrier()

    kb.arena_begin()
    state["in"] = x_out5
    emit_lv_partial(kb, load_x, lv_out, lnscr, stats)
    kb.arena_end()
    return kb.finalize()


def build_launch3():
    kb = KB()
    nc = kb.nc
    x_in = nc.dram_tensor("x_in", [T_CORE, D], F32, kind="ExternalInput")
    qw_d = nc.dram_tensor("q_proj_w", [128, D], F32, kind="ExternalInput")
    rw_d = nc.dram_tensor("router_w", [48, 128], F32, kind="ExternalInput")
    ak_d = nc.dram_tensor("aux_keys", [4096, 128], F32, kind="ExternalInput")
    av_d = nc.dram_tensor("aux_values", [4096, D], F32, kind="ExternalInput")
    rel_d = nc.dram_tensor("rel", [4096], F32, kind="ExternalInput")
    w1_d = nc.dram_tensor("unc_w1", [256, D], F32, kind="ExternalInput")
    b1_d = nc.dram_tensor("unc_b1", [256], F32, kind="ExternalInput")
    w2_d = nc.dram_tensor("unc_w2", [1, 256], F32, kind="ExternalInput")
    b2gwgb_d = nc.dram_tensor("b2gwgb", [3], F32, kind="ExternalInput")
    partials8_d = nc.dram_tensor("partials8", [8], F32, kind="ExternalInput")
    x_out = nc.dram_tensor("x_out", [T_CORE, D], F32, kind="ExternalOutput")
    w_dram = nc.dram_tensor("w_scr", [T_CORE, 4096], BF16)
    scr2_d = nc.dram_tensor("scr2", [5120, D], BF16)
    ident, lnscr, stats = _common_pre(kb)
    load_x, store_x, state = _xio(kb)
    kb.arena_begin()
    state["in"], state["out"] = x_in, x_out
    emit_l1ffn(kb, load_x, store_x, qw_d, rw_d, ak_d, av_d, rel_d, w1_d,
               b1_d, w2_d, b2gwgb_d, partials8_d, w_dram, scr2_d, ident,
               lnscr, stats)
    kb.arena_end()
    return kb.finalize()


def kernel(**inputs):
    z = inputs
    x = np.ascontiguousarray(np.asarray(z["L0_attn_x"], np.float32))
    masks = make_masks()
    b2gwgb = np.stack([np.asarray(z["L1_ffn_unc_b2"]).reshape(-1)[0],
                       np.asarray(z["L1_ffn_gate_w1"]).reshape(-1)[0],
                       np.asarray(z["L1_ffn_gate_bias"]).reshape(-1)[0]]
                      ).astype(np.float32)

    nc1 = build_launch1()
    im = [{"x_in": x[c], "w_gate": z["L0_ffn_w_gate"],
           "w_up": z["L0_ffn_w_up"], "w_down": z["L0_ffn_w_down"],
           "w_qkv": z["L0_attn_w_qkv"], "b_qkv": z["L0_attn_b_qkv"],
           "w_out": z["L0_attn_w_out"], "b_out": z["L0_attn_b_out"],
           "mask_in": masks, "ident_in": IDENT_BF16} for c in range(8)]
    r1 = run_bass_kernel_spmd(nc1, im, core_ids=list(range(8)))
    x3 = [r1.results[c]["x_out3"] for c in range(8)]
    p1 = np.concatenate([r1.results[c]["lv_out"].reshape(1)
                         for c in range(8)]).astype(np.float32)

    nc2 = build_launch2()
    im = [{"x_in": x3[c], "q_proj_w": z["L1_ffn_q_proj_w"],
           "router_w": z["L1_ffn_router_w"], "aux_keys": z["L1_ffn_aux_keys"],
           "aux_values": z["L1_ffn_aux_values"],
           "rel": z["L1_ffn_reliability_mask"],
           "unc_w1": z["L1_ffn_unc_w1"], "unc_b1": z["L1_ffn_unc_b1"],
           "unc_w2": z["L1_ffn_unc_w2"], "b2gwgb": b2gwgb, "partials8": p1,
           "in_proj_w": z["L1_attn_in_proj_w"], "conv_w": z["L1_attn_conv_w"],
           "conv_b": z["L1_attn_conv_b"], "x_proj_w": z["L1_attn_x_proj_w"],
           "dt_proj_w": z["L1_attn_dt_proj_w"],
           "dt_proj_b": z["L1_attn_dt_proj_b"], "A_log": z["L1_attn_A_log"],
           "D_param": z["L1_attn_D_param"], "c_proj_w": z["L1_attn_c_proj_w"],
           "ident_in": IDENT_BF16} for c in range(8)]
    r2 = run_bass_kernel_spmd(nc2, im, core_ids=list(range(8)))
    x5 = [r2.results[c]["x_out5"] for c in range(8)]
    p2 = np.concatenate([r2.results[c]["lv_out"].reshape(1)
                         for c in range(8)]).astype(np.float32)

    nc3 = build_launch3()
    im = [{"x_in": x5[c], "q_proj_w": z["L1_ffn_q_proj_w"],
           "router_w": z["L1_ffn_router_w"], "aux_keys": z["L1_ffn_aux_keys"],
           "aux_values": z["L1_ffn_aux_values"],
           "rel": z["L1_ffn_reliability_mask"],
           "unc_w1": z["L1_ffn_unc_w1"], "unc_b1": z["L1_ffn_unc_b1"],
           "unc_w2": z["L1_ffn_unc_w2"], "b2gwgb": b2gwgb, "partials8": p2,
           "ident_in": IDENT_BF16} for c in range(8)]
    r3 = run_bass_kernel_spmd(nc3, im, core_ids=list(range(8)))
    out = np.stack([r3.results[c]["x_out"] for c in range(8)])
    return np.ascontiguousarray(out.astype(np.float32))


# version 08
# speedup vs baseline: 1.1138x; 1.0282x over previous
import numpy as np
import ml_dtypes
import concourse.bass as bass
import concourse.mybir as mybir
from concourse.bass_utils import run_bass_kernel_spmd

F32 = mybir.dt.float32
BF16 = mybir.dt.bfloat16
I16 = mybir.dt.int16
I32 = mybir.dt.int32
AF = mybir.ActivationFunctionType
ALU = mybir.AluOpType
AX = mybir.AxisListType

T_CORE = 4096
D = 1024
NT = T_CORE // 128
LN_EPS = 1e-5
I_FFN = 2816
NI = I_FFN // 128

_SEC = {"T": "tensor", "V": "vector", "S": "scalar", "Y": "sync", "G": "gpsimd"}


class Buf:
    def __init__(self, h):
        self.h = h
        self.prod = None
        self.readers = {}

    def __getitem__(self, idx):
        return self.h[idx]

    @property
    def ap(self):
        try:
            return self.h[:]
        except Exception:
            return self.h


class KB:
    def __init__(self):
        nc = bass.Bass("TRN2", target_bir_lowering=False, debug=False)
        self.nc = nc
        self.sem = {k: nc.alloc_semaphore("sem_" + k) for k in _SEC}
        self.cnt = {k: 0 for k in _SEC}
        self.waited = {c: {p: 0 for p in _SEC} for c in _SEC}
        self.prog = {k: [] for k in _SEC}
        self.psum = [Buf(nc.alloc_psum_tensor(f"ps{i}", [128, 512], F32))
                     for i in range(8)]
        self._uid = 0
        self._consts = {}
        self._const_tile = nc.alloc_sbuf_tensor("kb_consts", [128, 16], F32)

    def constcol(self, value):
        if value not in self._consts:
            idx = len(self._consts)
            assert idx < 16
            ap = self._const_tile[:, idx:idx + 1]
            self.op("V", lambda V, ap=ap, value=value: V.memset(ap, value))
            self.wait("S", "V")
            self._consts[value] = ap
        return self._consts[value]

    def raw(self, e, fn):
        self.prog[e].append(fn)

    def op(self, e, fn, inc=1):
        if inc:
            sem = self.sem[e]
            self.prog[e].append(lambda eng, fn=fn, sem=sem, inc=inc:
                                fn(eng).then_inc(sem, inc))
            self.cnt[e] += inc
        else:
            self.prog[e].append(fn)
        return self.cnt[e]

    def wait(self, consumer, producer, val=None):
        if val is None:
            val = self.cnt[producer]
        if val <= 0 or self.waited[consumer][producer] >= val:
            return
        sem = self.sem[producer]
        self.prog[consumer].append(
            lambda eng, sem=sem, val=val: eng.wait_ge(sem, val))
        self.waited[consumer][producer] = val

    def rd(self, e, *bufs):
        for b in bufs:
            if b.prod is not None:
                if b.prod[0] != e or not b.prod[2] or e in ("Y", "G"):
                    self.wait(e, b.prod[0], b.prod[1])

    def wr(self, e, *bufs):
        for b in bufs:
            if b.prod is not None and b.prod[0] != e:
                self.wait(e, b.prod[0], b.prod[1])
            for re, rc in b.readers.items():
                if re != e:
                    self.wait(e, re, rc)

    def did_w(self, e, *bufs, wide=False):
        for b in bufs:
            b.prod = (e, self.cnt[e], wide)
            b.readers = {}

    def did_r(self, e, *bufs):
        for b in bufs:
            b.readers[e] = self.cnt[e]

    def eop(self, e, fn, reads=(), writes=(), wide=False, inc=1):
        for b in reads:
            self.rd(e, b)
        self.wr(e, *writes)
        self.op(e, fn, inc)
        self.did_w(e, *writes, wide=wide)
        for b in reads:
            self.did_r(e, b)

    def arena_init(self, kbytes):
        self._pool = self.nc.alloc_sbuf_tensor("kb_pool",
                                               [128, kbytes * 1024 // 4], F32)
        self._pool_off = 0
        self._arena_on = False

    def arena_begin(self):
        self._arena_on = True
        self._pool_off = 0

    def arena_end(self):
        self._arena_on = False

    def barrier(self):
        for c in _SEC:
            for p in _SEC:
                self.wait(c, p)

    def sb(self, name, shape, dtype=F32):
        if getattr(self, "_arena_on", False):
            esz = 2 if dtype in (BF16, I16) else 4
            ncols = int(np.prod(shape[1:]))
            nbytes = (ncols * esz + 63) // 64 * 64
            off = self._pool_off
            self._pool_off = off + nbytes
            maxb = self._pool.shape[1] * 4
            assert self._pool_off <= maxb, (
                f"arena overflow at {name}: {self._pool_off} > {maxb}")
            assert shape[0] <= 128 and len(shape) == 2
            view = self._pool[:].bitcast(dtype)[
                0:shape[0], off // esz: off // esz + ncols]
            b = Buf(None)
            b.h = view
            return b
        return Buf(self.nc.alloc_sbuf_tensor(name, shape, dtype))

    def psum_t(self, name, shape, dtype=F32):
        return Buf(self.nc.alloc_psum_tensor(name, shape, dtype))

    def dram(self, name, shape, dtype=F32, kind="Internal"):
        return self.nc.dram_tensor(name, shape, dtype, kind=kind)

    def finalize(self):
        for e in _SEC:
            v = self.cnt[e]
            if v > 0:
                sem = self.sem[e]
                self.prog[e].append(
                    lambda eng, sem=sem, v=v: eng.wait_ge(sem, v))
        with self.nc.Block() as blk:
            for e, sec in _SEC.items():
                thunks = self.prog[e]
                if not thunks:
                    continue
                getattr(blk, sec)(lambda eng, thunks=thunks:
                                  [t(eng) for t in thunks])
        return self.nc



def emit_ln_stats(kb, x, scratch, stats, col, xcols=None):
    x_ap = x.ap if xcols is None else x[:, xcols[0]:xcols[1]]
    st = stats.h
    kb.eop("S", lambda S, x_ap=x_ap, col=col: S.activation(
        scratch.ap, x_ap, AF.Square, accum_out=st[:, 8 + col:9 + col]),
        reads=(x,), writes=(scratch, stats), wide=True)
    kb.eop("V", lambda V, x_ap=x_ap, col=col: V.reduce_sum(
        st[:, col:col + 1], x_ap, AX.X), reads=(x,), writes=(stats,))


def emit_ln_finish(kb, stats, n):
    st = stats.h
    s1, s2 = st[:, 0:n], st[:, 8:8 + n]
    m, v = st[:, 16:16 + n], st[:, 24:24 + n]
    r, t = st[:, 32:32 + n], st[:, 40:40 + n]
    eps_ap = kb.constcol(LN_EPS)
    kb.eop("V", lambda V: V.tensor_scalar(m, s1, 1.0 / D, None, ALU.mult),
           reads=(stats,), writes=(stats,))
    kb.eop("V", lambda V: V.tensor_scalar(t, s2, 1.0 / D, None, ALU.mult),
           reads=(stats,), writes=(stats,))
    kb.eop("V", lambda V: V.tensor_tensor(v, m, m, ALU.mult),
           reads=(stats,), writes=(stats,))
    kb.eop("V", lambda V: V.tensor_tensor(v, t, v, ALU.subtract),
           reads=(stats,), writes=(stats,))
    kb.eop("S", lambda S: S.activation(t, v, AF.Ln, bias=eps_ap),
           reads=(stats,), writes=(stats,))
    kb.eop("S", lambda S: S.activation(r, t, AF.Exp, scale=-0.5),
           reads=(stats,), writes=(stats,))


def emit_ln_apply(kb, x, out, scratch, stats, col, xcols=None):
    x_ap = x.ap if xcols is None else x[:, xcols[0]:xcols[1]]
    st = stats.h
    kb.eop("V", lambda V, x_ap=x_ap, col=col: V.tensor_single_scalar(
        scratch.ap, x_ap, st[:, 16 + col:17 + col], ALU.subtract),
        reads=(x, stats), writes=(scratch,), wide=True)
    kb.eop("V", lambda V, col=col: V.tensor_single_scalar(
        out.ap, scratch.ap, st[:, 32 + col:33 + col], ALU.mult),
        reads=(scratch, stats), writes=(out,), wide=True)


def quant_scratch(kb, tag, F):
    return {
        "mx": kb.sb(f"q_mx_{tag}", [128, F // 16], F32),
        "rec": kb.sb(f"q_rec_{tag}", [128, F // 16], F32),
        "ab": kb.sb(f"q_ab_{tag}", [128, F], BF16),
        "v": kb.sb(f"q_v_{tag}", [128, F], BF16),
        "lo": kb.sb(f"q_lo_{tag}", [128, F], BF16),
        "sg": kb.sb(f"q_sg_{tag}", [128, F], BF16),
    }


def emit_fp4_quant(kb, x, out, F, sc):
    B = F // 16
    P = 128
    W = F >= 2048
    x_ap = x[:, 0:F]
    x3 = x_ap.rearrange("p (b k) -> p b k", k=16)
    mx, rec = sc["mx"], sc["rec"]
    ab, v, lo, sg = sc["ab"], sc["v"], sc["lo"], sc["sg"]
    kb.eop("S", lambda S: S.activation(sg[:, 0:F], x_ap, AF.Sign),
           reads=(x,), writes=(sg,), wide=True)
    kb.eop("V", lambda V: V.tensor_reduce(mx[:, 0:B], x3, AX.X, ALU.max,
                                          apply_absolute_value=True),
           reads=(x,), writes=(mx,))
    kb.eop("V", lambda V: V.tensor_scalar(mx[:, 0:B], mx[:, 0:B], 6e-12,
                                          None, ALU.max),
           reads=(mx,), writes=(mx,))
    kb.eop("V", lambda V: V.reciprocal(rec[:, 0:B], mx[:, 0:B]),
           reads=(mx,), writes=(rec,))
    kb.eop("V", lambda V: V.tensor_scalar(ab[:, 0:F].bitcast(I16),
                                          x_ap.bitcast(I16), 0x7FFF, None,
                                          ALU.bitwise_and),
           reads=(x,), writes=(ab,), wide=W)
    rec_b = rec[:, 0:B].unsqueeze(2).to_broadcast((P, B, 16))
    mx_b = mx[:, 0:B].unsqueeze(2).to_broadcast((P, B, 16))
    kb.eop("V", lambda V: V.scalar_tensor_tensor(
        v[:, 0:F].rearrange("p (b k) -> p b k", k=16),
        ab[:, 0:F].rearrange("p (b k) -> p b k", k=16),
        6.0, rec_b, ALU.mult, ALU.mult),
        reads=(ab, rec), writes=(v,), wide=W)
    kb.eop("V", lambda V: V.tensor_scalar(lo[:, 0:F], v[:, 0:F], 1.0, None,
                                          ALU.is_lt),
           reads=(v,), writes=(lo,), wide=W)
    kb.eop("V", lambda V: V.tensor_tensor(v[:, 0:F], v[:, 0:F], lo[:, 0:F],
                                          ALU.add),
           reads=(v, lo), writes=(v,), wide=W)
    kb.eop("V", lambda V: V.tensor_scalar(v[:, 0:F].bitcast(I16),
                                          v[:, 0:F].bitcast(I16), 0x20, None,
                                          ALU.add),
           reads=(v,), writes=(v,), wide=W)
    kb.eop("V", lambda V: V.tensor_scalar(v[:, 0:F].bitcast(I16),
                                          v[:, 0:F].bitcast(I16), -0x40,
                                          None, ALU.bitwise_and),
           reads=(v,), writes=(v,), wide=W)
    kb.eop("V", lambda V: V.tensor_tensor(v[:, 0:F], v[:, 0:F], lo[:, 0:F],
                                          ALU.subtract),
           reads=(v, lo), writes=(v,), wide=W)
    kb.eop("V", lambda V: V.scalar_tensor_tensor(
        v[:, 0:F].rearrange("p (b k) -> p b k", k=16),
        v[:, 0:F].rearrange("p (b k) -> p b k", k=16),
        1.0 / 6.0, mx_b, ALU.mult, ALU.mult),
        reads=(v, mx), writes=(v,), wide=W)
    kb.eop("V", lambda V: V.tensor_tensor(out[:, 0:F], v[:, 0:F],
                                          sg[:, 0:F], ALU.mult),
        reads=(v, sg), writes=(out,), wide=W)



def emit_l0ffn_weight_prep(kb, w_gate, w_up, w_down, scr_q, scr_guT, scr_dnT,
                           qsc, big_f32, big_bf_a, big_bf_b):
    wt, wtb, wqt = big_f32, big_bf_a, big_bf_b
    for w, row0, ncols, nrows in [(w_gate, 0, D, NI), (w_up, I_FFN, D, NI),
                                  (w_down, 5632, I_FFN, 8)]:
        for i in range(nrows):
            kb.wr("Y", wt)
            kb.op("Y", lambda Y, w=w, i=i, nc_=ncols: Y.dma_start(
                out=wt[:, 0:nc_], in_=w.ap()[i * 128:(i + 1) * 128, :]), 16)
            kb.did_w("Y", wt)
            kb.rd("V", wt); kb.wr("V", wtb)
            kb.op("V", lambda V, nc_=ncols: V.tensor_copy(wtb[:, 0:nc_],
                                                          wt[:, 0:nc_]))
            kb.did_r("V", wt); kb.did_w("V", wtb)
            emit_fp4_quant(kb, wtb, wqt, ncols, qsc)
            kb.rd("Y", wqt)
            kb.op("Y", lambda Y, row0=row0, i=i, nc_=ncols: Y.dma_start(
                out=scr_q.ap()[row0 + i * 128: row0 + (i + 1) * 128, 0:nc_],
                in_=wqt[:, 0:nc_]), 16)
            kb.did_r("Y", wqt)
    kb.wait("Y", "Y")
    stage = big_bf_a
    for c in range(8):
        for half, r0 in [(0, 0), (1, I_FFN)]:
            kb.wr("Y", stage)
            kb.op("Y", lambda Y, c=c, r0=r0: Y.dma_start(
                out=stage[:, 0:I_FFN],
                in_=scr_q.ap()[r0:r0 + I_FFN, c * 128:(c + 1) * 128],
                transpose=True), 16)
            kb.did_w("Y", stage)
            kb.rd("Y", stage)
            kb.op("Y", lambda Y, c=c, half=half: Y.dma_start(
                out=scr_guT.ap()[c, :, half * I_FFN:(half + 1) * I_FFN],
                in_=stage[:, 0:I_FFN]), 16)
            kb.did_r("Y", stage)
    for i in range(NI):
        kb.wr("Y", stage)
        kb.op("Y", lambda Y, i=i: Y.dma_start(
            out=stage[:, 0:D],
            in_=scr_q.ap()[5632:6656, i * 128:(i + 1) * 128],
            transpose=True), 16)
        kb.did_w("Y", stage)
        kb.rd("Y", stage)
        kb.op("Y", lambda Y, i=i: Y.dma_start(
            out=scr_dnT.ap()[i, :, :], in_=stage[:, 0:D]), 16)
        kb.did_r("Y", stage)


FPLAN = [(0, 512), (512, 512), (1024, 512), (1536, 512), (2048, 512),
         (2560, 256)]


def emit_l0ffn(kb, load_x, store_x, scr_guT, scr_dnT, hq_dram, ident,
               qsc, big_f32, big_bf_a, big_bf_b, grp_tiles=8):
    GRP = grp_tiles
    xln2 = [kb.sb(f"f_xln{i}", [128, D], BF16) for i in range(2)]
    lnscr = kb.sb("f_lnscr", [128, D], BF16)
    stats = kb.sb("f_stats", [128, 48], F32)
    xq2 = [kb.sb(f"f_xq{i}", [128, D], BF16) for i in range(2)]
    xqT = [kb.sb(f"f_xqT{g}", [128, 8 * 128], BF16) for g in range(2 * GRP)]
    psT = Buf(None)
    psT.h = kb.psum[7].h[:].bitcast(BF16)
    psT.readers = kb.psum[7].readers
    hbufs = [kb.sb(f"f_hb{g}", [128, I_FFN], BF16) for g in range(GRP)]
    silu_t2 = [kb.sb(f"f_silu{i}", [128, 512], BF16) for i in range(2)]
    hq2 = [big_bf_b, kb.sb("f_hqb", [128, I_FFN], BF16)]
    wgu = [kb.sb(f"f_wgu{b}", [128, 16 * 512], BF16) for b in range(2)]
    WCOLS = 16 * 512

    kb.wait("Y", "Y")
    NGRP = NT // GRP

    def phaseA(grp):
        xts = [load_x(kb, grp * GRP + g) for g in range(GRP)]
        for g in range(GRP):
            emit_ln_stats(kb, xts[g], lnscr, stats, g, xcols=(0, D))
        emit_ln_finish(kb, stats, GRP)
        for g in range(GRP):
            xln = xln2[g % 2]
            xq = xq2[g % 2]
            emit_ln_apply(kb, xts[g], xln, lnscr, stats, g, xcols=(0, D))
            emit_fp4_quant(kb, xln, xq, D, qsc)
            kb.rd("T", xq, ident); kb.wr("T", psT)
            def tr(T, xq=xq, psT=psT):
                for c in range(8):
                    ins = T.transpose(psT.h[:, c * 128:(c + 1) * 128],
                                      xq[:, c * 128:(c + 1) * 128],
                                      ident.h[:])
                return ins
            kb.op("T", tr)
            kb.did_r("T", xq); kb.did_w("T", psT)
            xqg = xqT[(grp % 2) * GRP + g]
            kb.rd("V", psT); kb.wr("V", xqg)
            kb.op("V", lambda V, xqg=xqg: V.tensor_copy(xqg.h[:], psT.h))
            kb.did_r("V", psT); kb.did_w("V", xqg)

    def phaseBC(grp):
        for fi, (f0, fw) in enumerate(FPLAN):
            wbuf = wgu[fi % 2]
            kb.wr("Y", wbuf)
            for h in range(2):
                kb.op("Y", lambda Y, wbuf=wbuf, f0=f0, fw=fw, h=h:
                      Y.dma_start(
                    out=wbuf.h[:, 0:16 * fw].rearrange(
                        "p (a f) -> p a f", f=fw)[:, h::2, :],
                    in_=scr_guT.ap()[:, :, h * I_FFN + f0:
                                     h * I_FFN + f0 + fw].rearrange(
                        "c p x -> p c x")), 16)
            kb.did_w("Y", wbuf)
            for g in range(GRP):
                xqg = xqT[(grp % 2) * GRP + g]
                bg, bu = kb.psum[g % 2 * 2], kb.psum[g % 2 * 2 + 1]
                kb.rd("T", xqg, wbuf); kb.wr("T", bg, bu)
                def mm(T, xqg=xqg, bg=bg, bu=bu, wbuf=wbuf, fw=fw):
                    for c in range(8):
                        T.matmul(bg[:, 0:fw],
                                 xqg[:, c * 128:(c + 1) * 128],
                                 wbuf[:, (2 * c) * fw:(2 * c + 1) * fw],
                                 start=(c == 0), stop=(c == 7))
                    for c in range(8):
                        ins = T.matmul(
                            bu[:, 0:fw], xqg[:, c * 128:(c + 1) * 128],
                            wbuf[:, (2 * c + 1) * fw:(2 * c + 2) * fw],
                            start=(c == 0), stop=(c == 7))
                    return ins
                kb.op("T", mm)
                kb.did_r("T", xqg, wbuf); kb.did_w("T", bg, bu)
                silu_t = silu_t2[g % 2]
                kb.rd("S", bg); kb.wr("S", silu_t)
                kb.op("S", lambda S, bg=bg, fw=fw, silu_t=silu_t:
                      S.activation(silu_t[:, 0:fw], bg[:, 0:fw], AF.Silu))
                kb.did_r("S", bg); kb.did_w("S", silu_t)
                kb.rd("V", bu, silu_t); kb.wr("V", hbufs[g])
                kb.op("V", lambda V, g=g, bu=bu, f0=f0, fw=fw,
                      silu_t=silu_t: V.tensor_tensor(
                          hbufs[g][:, f0:f0 + fw], bu[:, 0:fw],
                          silu_t[:, 0:fw], ALU.mult))
                kb.did_r("V", bu, silu_t)
                if fi == len(FPLAN) - 1:
                    kb.did_w("V", hbufs[g])
        for g in range(GRP):
            tt = grp * GRP + g
            hq = hq2[g % 2]
            emit_fp4_quant(kb, hbufs[g], hq, I_FFN, qsc)
            kb.rd("Y", hq)
            kb.op("Y", lambda Y, tt=tt, hq=hq: Y.dma_start(
                out=hq_dram.ap()[tt * 128:(tt + 1) * 128, :],
                in_=hq[:, 0:I_FFN]), 16)
            kb.did_r("Y", hq)

    phaseA(0)
    for grp in range(NGRP):
        if grp + 1 < NGRP:
            phaseA(grp + 1)
        phaseBC(grp)

    kb.wait("Y", "Y")
    kb.psum[7].prod = psT.prod
    hqT = [kb.sb(f"f_hqT{b}", [128, 512], BF16) for b in range(2)]
    wdn = [kb.sb(f"f_wdn{b}", [128, D], BF16) for b in range(2)]
    for sup in range(8):
        t0 = sup * 512
        banks = kb.psum
        for i in range(NI):
            buf = hqT[i % 2]
            wbuf = wdn[i % 2]
            kb.wr("Y", buf, wbuf)
            kb.op("Y", lambda Y, buf=buf, t0=t0, i=i: Y.dma_start(
                out=buf.h[:],
                in_=hq_dram.ap()[t0:t0 + 512, i * 128:(i + 1) * 128],
                transpose=True), 16)
            kb.op("Y", lambda Y, wbuf=wbuf, i=i: Y.dma_start(
                out=wbuf.h[:], in_=scr_dnT.ap()[i, :, :]), 16)
            kb.did_w("Y", buf, wbuf)
            kb.rd("T", buf, wbuf)
            if i == 0:
                for b in banks:
                    kb.wr("T", b)
            def mmd(T, buf=buf, wbuf=wbuf, i=i, banks=banks):
                for ti in range(4):
                    for fh in range(2):
                        ins = T.matmul(
                            banks[ti * 2 + fh].h[:],
                            buf[:, ti * 128:(ti + 1) * 128],
                            wbuf[:, fh * 512:fh * 512 + 512],
                            start=(i == 0), stop=(i == NI - 1))
                return ins
            kb.op("T", mmd)
            kb.did_r("T", buf, wbuf)
            if i == NI - 1:
                for b in banks:
                    kb.did_w("T", b)
        for ti in range(4):
            tt = sup * 4 + ti
            xres = load_x(kb, tt)
            for fh in range(2):
                bank = banks[ti * 2 + fh]
                kb.rd("V", xres, bank)
                kb.op("V", lambda V, xres=xres, bank=bank, fh=fh:
                      V.scalar_tensor_tensor(
                          xres[:, fh * 512:fh * 512 + 512], bank.h[:], 0.5,
                          xres[:, fh * 512:fh * 512 + 512], ALU.mult,
                          ALU.add))
                kb.did_r("V", bank)
            kb.did_w("V", xres, wide=True)
            store_x(kb, tt, xres)



HD = 64
NHEAD = 16


def emit_l0attn(kb, load_x, store_x, w_qkv, b_qkv, w_out, b_out,
                qkT_dram, v_dram, ctxT_dram, ident, mask4, lnscr, stats):
    onesrow = kb.sb("a_ones", [1, 128], F32)
    onescol = kb.sb("a_onesc", [128, 1], F32)
    kb.op("V", lambda V: V.memset(onesrow.ap, 1.0))
    kb.op("V", lambda V: V.memset(onescol.ap, 1.0))
    kb.did_w("V", onesrow, onescol)
    wqkv_bf = kb.sb("a_wqkv", [128, 8 * 3072], BF16)
    wout_bf = kb.sb("a_wout", [128, 8 * 1024], BF16)
    bq_col = kb.sb("a_bqcol", [128, 24], F32)
    brow = kb.sb("a_brow", [1, 3072], F32)
    borow = kb.sb("a_borow", [1, 1024], F32)
    for c in range(8):
        kb.op("G", lambda G, c=c: G.dma_start(
            out=wqkv_bf[:, c * 3072:(c + 1) * 3072],
            in_=w_qkv.ap()[c * 128:(c + 1) * 128, :]), 16)
        kb.op("G", lambda G, c=c: G.dma_start(
            out=wout_bf[:, c * 1024:(c + 1) * 1024],
            in_=w_out.ap()[c * 128:(c + 1) * 128, :]), 16)
    kb.op("Y", lambda Y: Y.dma_start(
        out=bq_col.ap, in_=b_qkv.ap().rearrange("(a p) -> p a", p=128),
        allow_slow_non_contiguous=True), 16)
    kb.op("Y", lambda Y: Y.dma_start(
        out=brow.ap, in_=b_qkv.ap().unsqueeze(0)), 16)
    kb.op("Y", lambda Y: Y.dma_start(
        out=borow.ap, in_=b_out.ap().unsqueeze(0)), 16)
    kb.did_w("G", wqkv_bf, wout_bf)
    kb.did_w("Y", bq_col, brow, borow)

    xln = kb.sb("a_xln", [128, D], BF16)
    xlnT = kb.sb("a_xlnT", [128, 8 * 512], BF16)
    psT = Buf(None)
    psT.h = kb.psum[7].h[:].bitcast(BF16)
    psT.readers = kb.psum[7].readers
    qkbuf = kb.sb("a_qkbuf", [128, 512], BF16)
    vbuf = kb.sb("a_vbuf", [128, 512], BF16)

    for sup in range(8):
        xts = [load_x(kb, sup * 4 + g) for g in range(4)]
        for g in range(4):
            emit_ln_stats(kb, xts[g], lnscr, stats, g, xcols=(0, D))
        emit_ln_finish(kb, stats, 4)
        x3 = xlnT.h[:].rearrange("p (c t) -> p c t", c=8)
        for g in range(4):
            emit_ln_apply(kb, xts[g], xln, lnscr, stats, g, xcols=(0, D))
            kb.rd("T", xln, ident); kb.wr("T", psT)
            def tr(T, xln=xln, psT=psT):
                for c in range(8):
                    ins = T.transpose(psT.h[:, c * 128:(c + 1) * 128],
                                      xln[:, c * 128:(c + 1) * 128],
                                      ident.h[:])
                return ins
            kb.op("T", tr)
            kb.did_r("T", xln); kb.did_w("T", psT)
            kb.rd("V", psT); kb.wr("V", xlnT)
            kb.op("V", lambda V, g=g, x3=x3: V.tensor_copy(
                x3[:, :, g * 128:(g + 1) * 128],
                psT.h[:].rearrange("p (c t) -> p c t", c=8)))
            kb.did_r("V", psT)
        kb.did_w("V", xlnT, wide=True)
        for ft in range(16):
            bank = kb.psum[ft % 2]
            kb.rd("T", xlnT, wqkv_bf); kb.wr("T", bank)
            def mmqk(T, ft=ft, bank=bank, x3=x3):
                for c in range(8):
                    ins = T.matmul(bank.h[:],
                                   wqkv_bf[:, c * 3072 + ft * 128:
                                           c * 3072 + (ft + 1) * 128],
                                   x3[:, c, :], start=(c == 0), stop=(c == 7))
                return ins
            kb.op("T", mmqk)
            kb.did_w("T", bank)
            kb.rd("V", bank, bq_col); kb.wr("V", qkbuf)
            kb.op("V", lambda V, ft=ft, bank=bank: V.tensor_single_scalar(
                qkbuf.ap, bank.h[:], bq_col[:, ft:ft + 1], ALU.add))
            kb.did_r("V", bank); kb.did_w("V", qkbuf)
            kb.rd("Y", qkbuf)
            kb.op("Y", lambda Y, ft=ft, sup=sup: Y.dma_start(
                out=qkT_dram.ap()[ft * 128:(ft + 1) * 128,
                                  sup * 512:(sup + 1) * 512],
                in_=qkbuf.ap), 16)
            kb.did_r("Y", qkbuf)
        for g in range(4):
            for fh in range(2):
                bank = kb.psum[2 + (g * 2 + fh) % 2]
                kb.rd("T", xlnT, wqkv_bf, onesrow, brow); kb.wr("T", bank)
                def mmv(T, g=g, fh=fh, bank=bank, x3=x3):
                    for c in range(8):
                        T.matmul(bank.h[:],
                                 x3[:, c, g * 128:(g + 1) * 128],
                                 wqkv_bf[:, c * 3072 + 2048 + fh * 512:
                                         c * 3072 + 2048 + (fh + 1) * 512],
                                 start=(c == 0), stop=False)
                    return T.matmul(bank.h[:], onesrow.h[0:1, :],
                                    brow.h[0:1, 2048 + fh * 512:
                                           2048 + (fh + 1) * 512],
                                    start=False, stop=True)
                kb.op("T", mmv)
                kb.did_w("T", bank)
                kb.rd("V", bank); kb.wr("V", vbuf)
                kb.op("V", lambda V, bank=bank: V.tensor_copy(vbuf.ap,
                                                              bank.h[:]))
                kb.did_r("V", bank); kb.did_w("V", vbuf)
                kb.rd("Y", vbuf)
                kb.op("Y", lambda Y, sup=sup, g=g, fh=fh: Y.dma_start(
                    out=v_dram.ap()[sup * 512 + g * 128:
                                    sup * 512 + (g + 1) * 128,
                                    fh * 512:(fh + 1) * 512],
                    in_=vbuf.ap), 16)
                kb.did_r("Y", vbuf)

    kb.wait("Y", "Y")

    kT = kb.sb("a_kT", [64, T_CORE], BF16)
    qT = kb.sb("a_qT", [64, T_CORE], BF16)
    vb3 = kb.sb("a_vb3", [128, 32 * 72], BF16)
    pT = [kb.sb(f"a_pT{i}", [128, 512], BF16) for i in range(2)]
    md = kb.sb("a_md", [64, 2], F32)
    sc1 = kb.sb("a_sc1", [1, 8], F32)
    bias_col = kb.sb("a_biascol", [128, 1], F32)
    linv_row = kb.sb("a_linv", [1, 512], F32)
    linv_rep = kb.sb("a_linvrep", [64, 512], F32)
    ctx_bf = kb.sb("a_ctx", [64, 512], BF16)
    v3 = vb3.h[:].rearrange("p (c w) -> p c w", w=72)
    kb.op("V", lambda V: V.memset(vb3.ap, 0.0))
    kb.op("V", lambda V, v3=v3: V.memset(v3[:, :, 64:65], 1.0))
    kb.did_w("V", vb3)

    for h in range(NHEAD):
        kb.wr("Y", kT, qT, vb3)
        kb.op("Y", lambda Y, h=h: Y.dma_start(
            out=kT.ap,
            in_=qkT_dram.ap()[1024 + h * 64:1024 + (h + 1) * 64, :]), 16)
        kb.op("Y", lambda Y, h=h: Y.dma_start(
            out=qT.ap, in_=qkT_dram.ap()[h * 64:(h + 1) * 64, :]), 16)
        kb.op("Y", lambda Y, h=h, v3=v3: Y.dma_start(
            out=v3[:, :, 0:64],
            in_=v_dram.ap()[:, h * 64:(h + 1) * 64].rearrange(
                "(c p) d -> p c d", p=128)), 16)
        kb.did_w("Y", kT, qT, vb3)
        kb.eop("V", lambda V: V.tensor_reduce(
            md[:, 0:1], kT.ap, AX.X, ALU.max, apply_absolute_value=True),
            reads=(kT,), writes=(md,))
        kb.eop("V", lambda V: V.tensor_reduce(
            md[:, 1:2], qT.ap, AX.X, ALU.max, apply_absolute_value=True),
            reads=(md, qT), writes=(md,))
        kb.eop("V", lambda V: V.tensor_tensor(md.ap, md.ap, md.ap, ALU.mult),
               reads=(md,), writes=(md,))
        bank = kb.psum[6]
        kb.rd("T", md, onescol); kb.wr("T", bank)
        kb.op("T", lambda T, bank=bank: T.matmul(
            bank.h[0:1, 0:1], md.h[0:64, 0:1], onescol.h[0:64, 0:1],
            start=True, stop=True))
        kb.op("T", lambda T, bank=bank: T.matmul(
            bank.h[0:1, 1:2], md.h[0:64, 1:2], onescol.h[0:64, 0:1],
            start=False, stop=True))
        kb.did_w("T", bank)
        kb.eop("V", lambda V, bank=bank: V.tensor_copy(sc1[0:1, 0:2],
                                                       bank.h[0:1, 0:2]),
               reads=(bank,), writes=(sc1,))
        kb.did_r("V", bank)
        kb.eop("V", lambda V: V.tensor_tensor(sc1[0:1, 2:3], sc1[0:1, 0:1],
                                              sc1[0:1, 1:2], ALU.mult),
               reads=(sc1,), writes=(sc1,))
        kb.eop("S", lambda S: S.activation(sc1[0:1, 3:4], sc1[0:1, 2:3],
                                           AF.Ln), reads=(sc1,),
               writes=(sc1,))
        kb.eop("S", lambda S: S.activation(sc1[0:1, 4:5], sc1[0:1, 3:4],
                                           AF.Exp, scale=0.5),
               reads=(sc1,), writes=(sc1,))
        kb.eop("V", lambda V: V.tensor_scalar(sc1[0:1, 5:6], sc1[0:1, 4:5],
                                              -0.125, None, ALU.mult),
               reads=(sc1,), writes=(sc1,))
        rep_bank = kb.psum[5]
        kb.rd("T", sc1, onesrow); kb.wr("T", rep_bank)
        kb.op("T", lambda T, rep_bank=rep_bank: T.matmul(
            rep_bank.h[:, 0:1], onesrow.h[0:1, :], sc1.h[0:1, 5:6],
            start=True, stop=True))
        kb.did_w("T", rep_bank)
        kb.eop("V", lambda V, rep_bank=rep_bank: V.tensor_copy(
            bias_col.ap, rep_bank.h[:, 0:1]),
            reads=(rep_bank,), writes=(bias_col,))
        kb.did_r("V", rep_bank)

        for qc in range(8):
            nkc = 4 * (qc + 1)
            pv_bank = kb.psum[4]
            kb.wr("T", pv_bank)
            def emit_sc(kc):
                sb_ = kb.psum[kc % 2]
                kb.rd("T", kT, qT, ident); kb.wr("T", sb_)
                diag_a = kc * 128 - qc * 512
                def mmsc(T, kc=kc, qc=qc, sb_=sb_, diag_a=diag_a):
                    ins = T.matmul(sb_.h[:], kT[:, kc * 128:(kc + 1) * 128],
                                   qT[:, qc * 512:(qc + 1) * 512],
                                   start=True, stop=(diag_a < 0))
                    if diag_a >= 0:
                        ins = T.matmul(sb_.h[:], ident.h[:],
                                       mask4[diag_a // 128],
                                       start=False, stop=True)
                    return ins
                kb.op("T", mmsc)
                kb.did_w("T", sb_)
                pbuf = pT[kc % 2]
                kb.rd("S", sb_, bias_col); kb.wr("S", pbuf)
                kb.op("S", lambda S, sb_=sb_, pbuf=pbuf: S.activation(
                    pbuf.ap, sb_.h[:], AF.Exp, bias=bias_col.ap,
                    scale=0.125))
                kb.did_r("S", sb_); kb.did_w("S", pbuf, wide=True)

            def emit_pv(kc):
                pbuf = pT[kc % 2]
                kb.rd("T", vb3, pbuf)
                def mmpv(T, kc=kc, pbuf=pbuf, pv_bank=pv_bank, nkc=nkc,
                         v3=v3):
                    return T.matmul(pv_bank.h[0:65, :], v3[:, kc, 0:65],
                                    pbuf.ap, start=(kc == 0),
                                    stop=(kc == nkc - 1))
                kb.op("T", mmpv)
                kb.did_r("T", pbuf)

            for kc in range(nkc):
                emit_sc(kc)
                if kc >= 1:
                    emit_pv(kc - 1)
            emit_pv(nkc - 1)
            kb.did_w("T", pv_bank)
            kb.eop("V", lambda V, pv_bank=pv_bank: V.reciprocal(
                linv_row.ap, pv_bank.h[64:65, :]),
                reads=(pv_bank,), writes=(linv_row,))
            rep2 = kb.psum[5]
            kb.rd("T", linv_row, onesrow); kb.wr("T", rep2)
            kb.op("T", lambda T, rep2=rep2: T.matmul(
                rep2.h[0:64, :], onesrow.h[0:1, 0:64], linv_row.ap,
                start=True, stop=True))
            kb.did_w("T", rep2)
            kb.rd("S", rep2); kb.wr("S", linv_rep)
            kb.op("S", lambda S, rep2=rep2: S.activation(
                linv_rep.ap, rep2.h[0:64, :], AF.Copy))
            kb.did_r("S", rep2); kb.did_w("S", linv_rep)
            kb.rd("V", pv_bank, linv_rep); kb.wr("V", ctx_bf)
            kb.op("V", lambda V, pv_bank=pv_bank: V.tensor_tensor(
                ctx_bf.ap, pv_bank.h[0:64, :], linv_rep.ap, ALU.mult))
            kb.did_r("V", pv_bank); kb.did_w("V", ctx_bf)
            kb.rd("Y", ctx_bf)
            kb.op("Y", lambda Y, h=h, qc=qc: Y.dma_start(
                out=ctxT_dram.ap()[h * 64:(h + 1) * 64,
                                   qc * 512:(qc + 1) * 512],
                in_=ctx_bf.ap), 16)
            kb.did_r("Y", ctx_bf)

    kb.wait("Y", "Y")

    ctxc = [kb.sb(f"a_ctxc{i}", [128, 512], BF16) for i in range(2)]
    banks = kb.psum
    for sup in range(8):
        for c in range(8):
            buf = ctxc[c % 2]
            kb.wr("Y", buf)
            kb.op("Y", lambda Y, c=c, sup=sup, buf=buf: Y.dma_start(
                out=buf.ap,
                in_=ctxT_dram.ap()[c * 128:(c + 1) * 128,
                                   sup * 512:(sup + 1) * 512]), 16)
            kb.did_w("Y", buf)
            kb.rd("T", buf, wout_bf, onesrow, borow)
            if c == 0:
                for b in banks:
                    kb.wr("T", b)
            def mmo(T, c=c, buf=buf):
                for ti in range(4):
                    for fh in range(2):
                        ins = T.matmul(
                            banks[ti * 2 + fh].h[:],
                            buf[:, ti * 128:(ti + 1) * 128],
                            wout_bf[:, c * 1024 + fh * 512:
                                    c * 1024 + (fh + 1) * 512],
                            start=(c == 0), stop=False)
                if c == 7:
                    for ti in range(4):
                        for fh in range(2):
                            ins = T.matmul(
                                banks[ti * 2 + fh].h[:], onesrow.h[0:1, :],
                                borow.h[0:1, fh * 512:(fh + 1) * 512],
                                start=False, stop=True)
                return ins
            kb.op("T", mmo)
            kb.did_r("T", buf)
            if c == 7:
                for b in banks:
                    kb.did_w("T", b)
        for ti in range(4):
            tt = sup * 4 + ti
            xres = load_x(kb, tt)
            for fh in range(2):
                bank = banks[ti * 2 + fh]
                kb.rd("V", xres, bank)
                kb.op("V", lambda V, xres=xres, bank=bank, fh=fh:
                      V.tensor_tensor(xres[:, fh * 512:fh * 512 + 512],
                                      bank.h[:],
                                      xres[:, fh * 512:fh * 512 + 512],
                                      ALU.add))
                kb.did_r("V", bank)
            kb.did_w("V", xres, wide=True)
            store_x(kb, tt, xres)



DIN = 2048
NDT = DIN // 128
TC = 512
NCH = T_CORE // TC


def emit_l1mamba(kb, load_x, store_x, win, conv_w, conv_b, xproj, dtw, dtb,
                 alog, dpar, cw, z_dram, ident, lnscr, stats, dbg=None):
    def dump(name, buf, cols, ch=None):
        if dbg is None or name not in dbg:
            return
        if ch is not None and ch != 0:
            return
        kb.rd("Y", buf)
        kb.op("Y", lambda Y, name=name, buf=buf, cols=cols: Y.dma_start(
            out=dbg[name].ap()[:, 0:cols], in_=buf[:, 0:cols]), 16)
        kb.did_r("Y", buf)
    nc = kb.nc
    onesrow = kb.sb("m_ones", [1, 128], F32)
    onesb = kb.sb("m_onesb", [1, 128], BF16)
    kb.op("V", lambda V: V.memset(onesrow.ap, 1.0))
    kb.op("V", lambda V: V.memset(onesb.ap, 1.0))
    kb.did_w("V", onesrow, onesb)

    convw_c = kb.sb("m_convw", [128, 64], F32)
    convb_c = kb.sb("m_convb", [128, 16], F32)
    dtw_c = kb.sb("m_dtw", [128, 16], F32)
    dtb_c = kb.sb("m_dtb", [128, 16], F32)
    alog_c = kb.sb("m_alog", [128, 256], F32)
    nega = kb.sb("m_nega", [128, 256], F32)
    dpar_c = kb.sb("m_dpar", [128, 16], F32)
    xproj_b = kb.sb("m_xproj", [128, 16 * 33], BF16)
    kb.op("Y", lambda Y: Y.dma_start(
        out=convw_c.h[:].rearrange("p (i j) -> p i j", i=16),
        in_=conv_w.ap().rearrange("(i p) o j -> p i (o j)", p=128)), 16)
    kb.op("Y", lambda Y: Y.dma_start(
        out=convb_c.ap, in_=conv_b.ap().rearrange("(i p) -> p i", p=128),
        allow_slow_non_contiguous=True), 16)
    kb.op("Y", lambda Y: Y.dma_start(
        out=dtw_c.ap, in_=dtw.ap().rearrange("o (i p) -> p (o i)", p=128),
        allow_slow_non_contiguous=True), 16)
    kb.op("Y", lambda Y: Y.dma_start(
        out=dtb_c.ap, in_=dtb.ap().rearrange("(i p) -> p i", p=128),
        allow_slow_non_contiguous=True), 16)
    kb.op("Y", lambda Y: Y.dma_start(
        out=alog_c.h[:].rearrange("p (i s) -> p i s", i=16),
        in_=alog.ap().rearrange("(i p) s -> p i s", p=128)), 16)
    kb.op("Y", lambda Y: Y.dma_start(
        out=dpar_c.ap, in_=dpar.ap().rearrange("(i p) -> p i", p=128),
        allow_slow_non_contiguous=True), 16)
    kb.did_w("Y", convw_c, convb_c, dtw_c, dtb_c, alog_c, dpar_c)
    kb.op("G", lambda G: G.dma_start(
        out=xproj_b.h[:].rearrange("p (i k) -> p i k", i=16),
        in_=xproj.ap().rearrange("(i p) k -> p i k", p=128)), 16)
    kb.did_w("G", xproj_b)
    kb.eop("S", lambda S: S.activation(nega.ap, alog_c.ap, AF.Exp),
           reads=(alog_c,), writes=(nega,), wide=True)
    kb.eop("V", lambda V: V.tensor_scalar(nega.ap, nega.ap, -1.0, None,
                                          ALU.mult),
           reads=(nega,), writes=(nega,), wide=True)

    xln = kb.sb("m_xln", [128, D], BF16)
    xlnT = kb.sb("m_xlnT", [128, 8 * TC], BF16)
    psT = Buf(None)
    psT.h = kb.psum[7].h[:].bitcast(BF16)
    psT.readers = kb.psum[7].readers
    xbr = kb.sb("m_xbr", [128, NDT * TC], BF16)
    ybuf = xbr
    ytmp = kb.sb("m_ytmp", [128, TC], BF16)
    brep = kb.sb("m_brep", [128, 16 * TC], BF16)
    crep = kb.sb("m_crep", [128, 16 * TC], BF16)
    pad = kb.sb("m_pad", [128, TC + 3], BF16)
    convo = kb.sb("m_convo", [128, TC], BF16)
    hist = kb.sb("m_hist", [128, 16 * 3], BF16)
    hcarry = kb.sb("m_hcarry", [128, 256], F32)
    bcbuf = kb.sb("m_bc", [32, TC], BF16)
    dtraw = kb.sb("m_dtraw", [1, TC], F32)
    brow = kb.sb("m_brow", [1, TC], BF16)
    Rrep = kb.sb("m_Rrep", [128, TC], F32)
    dtb_f = kb.sb("m_dtfull", [128, TC], F32)
    ab_f2 = [kb.sb(f"m_abar{i}", [128, TC], F32) for i in range(2)]
    ab_f = ab_f2[0]
    bx_f = kb.sb("m_bx", [128, TC], BF16)
    sp_f = kb.sb("m_sp", [128, TC], F32)
    hbuf = kb.sb("m_h", [128, TC], BF16)
    ubuf = kb.sb("m_u", [128, TC], BF16)
    wsl = [kb.sb(f"m_wsl{i}", [128, 8 * 128], BF16) for i in range(2)]
    cwsl = [kb.sb(f"m_cwsl{i}", [128, D], BF16) for i in range(2)]
    zbuf2 = [kb.sb(f"m_zbuf{i}", [128, 512], BF16) for i in range(2)]
    kb.op("V", lambda V: V.memset(hist.ap, 0.0))
    kb.op("V", lambda V: V.memset(hcarry.ap, 0.0))
    kb.did_w("V", hist, hcarry)

    for ch in range(NCH):
        t0 = ch * TC
        NTPC = TC // 128
        xts = [load_x(kb, ch * NTPC + g) for g in range(NTPC)]
        for g in range(NTPC):
            emit_ln_stats(kb, xts[g], lnscr, stats, g, xcols=(0, D))
        emit_ln_finish(kb, stats, NTPC)
        x3 = xlnT.h[:].rearrange("p (c t) -> p c t", c=8)
        for g in range(NTPC):
            emit_ln_apply(kb, xts[g], xln, lnscr, stats, g, xcols=(0, D))
            kb.rd("T", xln, ident); kb.wr("T", psT)
            def tr(T, xln=xln, psT=psT):
                for c in range(8):
                    ins = T.transpose(psT.h[:, c * 128:(c + 1) * 128],
                                      xln[:, c * 128:(c + 1) * 128],
                                      ident.h[:])
                return ins
            kb.op("T", tr)
            kb.did_r("T", xln); kb.did_w("T", psT)
            kb.rd("V", psT); kb.wr("V", xlnT)
            kb.op("V", lambda V, g=g, x3=x3: V.tensor_copy(
                x3[:, :, g * 128:(g + 1) * 128],
                psT.h[:].rearrange("p (c t) -> p c t", c=8)))
            kb.did_r("V", psT)
        kb.did_w("V", xlnT, wide=True)

        for i in range(NDT):
            wbuf = wsl[i % 2]
            kb.wr("G", wbuf)
            kb.op("G", lambda G, i=i, wbuf=wbuf: G.dma_start(
                out=wbuf.h[:].rearrange("p (c f) -> p c f", c=8),
                in_=win.ap()[:, i * 128:(i + 1) * 128].rearrange(
                    "(c p) f -> p c f", p=128)), 16)
            kb.did_w("G", wbuf)
            for th in range(TC // 512):
                bank = kb.psum[th]
                kb.rd("T", xlnT, wbuf); kb.wr("T", bank)
                def mmi(T, i=i, th=th, bank=bank, wbuf=wbuf, x3=x3):
                    for c in range(8):
                        ins = T.matmul(bank.h[:],
                                       wbuf[:, c * 128:(c + 1) * 128],
                                       x3[:, c, th * 512:(th + 1) * 512],
                                       start=(c == 0), stop=(c == 7))
                    return ins
                kb.op("T", mmi)
                kb.did_w("T", bank); kb.did_r("T", wbuf, xlnT)
                kb.rd("S", bank); kb.wr("S", pad)
                kb.op("S", lambda S, th=th, bank=bank: S.activation(
                    pad[:, 3 + th * 512:3 + (th + 1) * 512], bank.h[:],
                    AF.Copy))
                kb.did_r("S", bank)
            kb.rd("S", hist)
            kb.op("S", lambda S, i=i: S.activation(
                pad[:, 0:3], hist[:, i * 3:(i + 1) * 3], AF.Copy))
            kb.did_r("S", hist); kb.did_w("S", pad)
            kb.rd("V", pad, convw_c); kb.wr("V", convo)
            kb.op("V", lambda V, i=i: V.tensor_single_scalar(
                convo.ap, pad[:, 0:TC], convw_c[:, i * 4:i * 4 + 1],
                ALU.mult))
            for j in range(1, 4):
                kb.op("V", lambda V, i=i, j=j: V.scalar_tensor_tensor(
                    convo.ap, pad[:, j:j + TC], convw_c[:, i * 4 + j:
                                                        i * 4 + j + 1],
                    convo.ap, ALU.mult, ALU.add))
            kb.did_w("V", convo, wide=True)
            kb.wr("V", hist)
            kb.op("V", lambda V, i=i: V.tensor_copy(
                hist[:, i * 3:(i + 1) * 3], pad[:, TC:TC + 3]))
            kb.did_w("V", hist)
            kb.did_r("V", pad)
            kb.rd("S", convo, convb_c); kb.wr("S", xbr)
            kb.op("S", lambda S, i=i: S.activation(
                xbr[:, i * TC:(i + 1) * TC], convo.ap, AF.Silu,
                bias=convb_c[:, i:i + 1]))
            kb.did_w("S", xbr, wide=True)

        dump("xbr", xbr, NDT * TC, ch)
        for i in range(NDT):
            wbuf = wsl[i % 2]
            kb.wr("G", wbuf)
            kb.op("G", lambda G, i=i, wbuf=wbuf: G.dma_start(
                out=wbuf.h[:].rearrange("p (c f) -> p c f", c=8),
                in_=win.ap()[:, 2048 + i * 128:2048 + (i + 1) * 128].rearrange(
                    "(c p) f -> p c f", p=128)), 16)
            kb.did_w("G", wbuf)
            for th in range(TC // 512):
                bank = kb.psum[2 + th]
                kb.rd("T", xlnT, wbuf); kb.wr("T", bank)
                def mmz(T, i=i, th=th, bank=bank, wbuf=wbuf, x3=x3):
                    for c in range(8):
                        ins = T.matmul(bank.h[:],
                                       wbuf[:, c * 128:(c + 1) * 128],
                                       x3[:, c, th * 512:(th + 1) * 512],
                                       start=(c == 0), stop=(c == 7))
                    return ins
                kb.op("T", mmz)
                kb.did_w("T", bank); kb.did_r("T", wbuf, xlnT)
                zbuf = zbuf2[i % 2]
                kb.rd("S", bank); kb.wr("S", zbuf)
                kb.op("S", lambda S, bank=bank, zbuf=zbuf: S.activation(
                    zbuf.ap, bank.h[:], AF.Silu))
                kb.did_r("S", bank); kb.did_w("S", zbuf)
                kb.rd("Y", zbuf)
                kb.op("Y", lambda Y, i=i, th=th, t0=t0, zbuf=zbuf:
                      Y.dma_start(
                    out=z_dram.ap()[i * 128:(i + 1) * 128,
                                    t0 + th * 512:t0 + (th + 1) * 512],
                    in_=zbuf.ap), 16)
                kb.did_r("Y", zbuf)

        for th in range(TC // 512):
            bank = kb.psum[4 + th]
            kb.rd("T", xbr, xproj_b); kb.wr("T", bank)
            def mms(T, th=th, bank=bank):
                for i in range(NDT):
                    ins = T.matmul(bank.h[0:33, :],
                                   xproj_b[:, i * 33:(i + 1) * 33],
                                   xbr[:, i * TC + th * 512:
                                       i * TC + (th + 1) * 512],
                                   start=(i == 0), stop=(i == NDT - 1))
                return ins
            kb.op("T", mms)
            kb.did_w("T", bank)
            kb.rd("V", bank); kb.wr("V", bcbuf, dtraw)
            kb.op("V", lambda V, th=th, bank=bank: V.tensor_copy(
                bcbuf[0:32, th * 512:(th + 1) * 512], bank.h[0:32, :]))
            kb.op("V", lambda V, th=th, bank=bank: V.tensor_copy(
                dtraw[0:1, th * 512:(th + 1) * 512], bank.h[32:33, :]))
            kb.did_r("V", bank); kb.did_w("V", bcbuf, dtraw)

        dump("bc", bcbuf, TC, ch)
        dump("dtraw", dtraw, TC, ch)
        for th in range(TC // 512):
            bank = kb.psum[4 + th]
            kb.rd("T", dtraw, onesrow); kb.wr("T", bank)
            kb.op("T", lambda T, th=th, bank=bank: T.matmul(
                bank.h[:], onesrow.h[0:1, :],
                dtraw[0:1, th * 512:(th + 1) * 512], start=True, stop=True))
            kb.did_w("T", bank)
            kb.rd("S", bank); kb.wr("S", Rrep)
            kb.op("S", lambda S, th=th, bank=bank: S.activation(
                Rrep[:, th * 512:(th + 1) * 512], bank.h[:], AF.Copy))
            kb.did_r("S", bank)
        kb.did_w("S", Rrep, wide=True)

        for s in range(16):
            for part, dst in [(0, brep), (16, crep)]:
                kb.rd("G", bcbuf)
                kb.op("G", lambda G, s=s, part=part: G.dma_start(
                    out=brow.ap, in_=bcbuf[part + s:part + s + 1, :]), 16)
                kb.did_w("G", brow)
                for th in range(TC // 512):
                    bank = kb.psum[6]
                    kb.rd("T", brow, onesb); kb.wr("T", bank)
                    kb.op("T", lambda T, th=th, bank=bank: T.matmul(
                        bank.h[:], onesb.h[0:1, :],
                        brow[0:1, th * 512:(th + 1) * 512],
                        start=True, stop=True))
                    kb.did_w("T", bank)
                    kb.rd("S", bank); kb.wr("S", dst)
                    kb.op("S", lambda S, s=s, th=th, bank=bank, dst=dst:
                          S.activation(dst[:, s * TC + th * 512:
                                           s * TC + (th + 1) * 512],
                                       bank.h[:], AF.Copy))
                    kb.did_r("S", bank)
        kb.did_w("S", brep, crep, wide=True)

        for i in range(NDT):
            kb.eop("V", lambda V, i=i: V.tensor_single_scalar(
                dtb_f.ap, Rrep.ap, dtw_c[:, i:i + 1], ALU.mult),
                reads=(Rrep, dtw_c), writes=(dtb_f,), wide=True)
            kb.eop("V", lambda V, i=i: V.tensor_single_scalar(
                dtb_f.ap, dtb_f.ap, dtb_c[:, i:i + 1], ALU.add),
                reads=(dtb_f, dtb_c), writes=(dtb_f,), wide=True)
            kb.eop("V", lambda V: V.tensor_scalar(
                ab_f.h[:].bitcast(I32), dtb_f.h[:].bitcast(
                    I32), 0x7FFFFFFF, None, ALU.bitwise_and),
                reads=(dtb_f,), writes=(ab_f,), wide=True)
            kb.eop("S", lambda S: S.activation(ab_f.ap, ab_f.ap, AF.Exp,
                                               scale=-1.0),
                   reads=(ab_f,), writes=(ab_f,), wide=True)
            eps1 = kb.constcol(1.0)
            kb.eop("S", lambda S: S.activation(ab_f.ap, ab_f.ap, AF.Ln,
                                               bias=eps1),
                   reads=(ab_f,), writes=(ab_f,), wide=True)
            kb.eop("V", lambda V: V.tensor_scalar(sp_f.ap, dtb_f.ap, 0.0,
                                                  None, ALU.max),
                   reads=(dtb_f,), writes=(sp_f,), wide=True)
            kb.eop("V", lambda V: V.tensor_tensor(dtb_f.ap, sp_f.ap,
                                                  ab_f.ap, ALU.add),
                   reads=(sp_f, ab_f), writes=(dtb_f,), wide=True)
            if i == 0:
                dump("dt0", dtb_f, TC, ch)
            kb.eop("V", lambda V, i=i: V.tensor_tensor(
                ubuf.ap, dtb_f.ap, xbr[:, i * TC:(i + 1) * TC], ALU.mult),
                reads=(dtb_f, xbr), writes=(ubuf,), wide=True)
            for s in range(16):
                abv = ab_f2[s % 2]
                kb.eop("S", lambda S, i=i, s=s, abv=abv: S.activation(
                    abv.ap, dtb_f.ap, AF.Exp,
                    scale=nega[:, i * 16 + s:i * 16 + s + 1]),
                    reads=(dtb_f, nega), writes=(abv,), wide=True)
                kb.eop("V", lambda V, s=s: V.tensor_tensor(
                    bx_f.ap, ubuf.ap, brep[:, s * TC:(s + 1) * TC],
                    ALU.mult),
                    reads=(ubuf, brep), writes=(bx_f,), wide=True)
                kb.eop("V", lambda V, i=i, s=s, abv=abv: V.tensor_tensor_scan(
                    hbuf.ap, abv.ap, bx_f.ap,
                    hcarry[:, i * 16 + s:i * 16 + s + 1], ALU.mult, ALU.add),
                    reads=(abv, bx_f, hcarry), writes=(hbuf,), wide=True)
                kb.eop("V", lambda V, i=i, s=s: V.tensor_copy(
                    hcarry[:, i * 16 + s:i * 16 + s + 1],
                    hbuf[:, TC - 1:TC]),
                    reads=(hbuf,), writes=(hcarry,))
                if i == 0 and s == 0:
                    dump("h00", hbuf, TC, ch)
                    dump("ab00", ab_f, TC, ch)
                    dump("bx00", bx_f, TC, ch)
                if s == 0:
                    kb.eop("V", lambda V, i=i, s=s: V.tensor_tensor(
                        ytmp.ap, hbuf.ap,
                        crep[:, s * TC:(s + 1) * TC], ALU.mult),
                        reads=(hbuf, crep), writes=(ytmp,), wide=True)
                else:
                    kb.eop("V", lambda V, i=i, s=s: V.tensor_tensor(
                        bx_f.ap, hbuf.ap,
                        crep[:, s * TC:(s + 1) * TC], ALU.mult),
                        reads=(hbuf, crep), writes=(bx_f,), wide=True)
                    kb.eop("V", lambda V, i=i: V.tensor_tensor(
                        ytmp.ap, ytmp.ap, bx_f.ap, ALU.add),
                        reads=(bx_f, ytmp), writes=(ytmp,), wide=True)
            kb.eop("V", lambda V, i=i: V.scalar_tensor_tensor(
                ybuf[:, i * TC:(i + 1) * TC], xbr[:, i * TC:(i + 1) * TC],
                dpar_c[:, i:i + 1], ytmp.ap,
                ALU.mult, ALU.add),
                reads=(xbr, dpar_c, ytmp), writes=(ybuf,), wide=True)
        kb.wait("Y", "Y")
        for i in range(NDT):
            for th in range(TC // 512):
                zbuf = zbuf2[i % 2]
                kb.wr("Y", zbuf)
                kb.op("Y", lambda Y, i=i, th=th, t0=t0, zbuf=zbuf:
                      Y.dma_start(
                    out=zbuf.ap,
                    in_=z_dram.ap()[i * 128:(i + 1) * 128,
                                    t0 + th * 512:t0 + (th + 1) * 512]), 16)
                kb.did_w("Y", zbuf)
                kb.eop("V", lambda V, i=i, th=th, zbuf=zbuf: V.tensor_tensor(
                    ybuf[:, i * TC + th * 512:i * TC + (th + 1) * 512],
                    ybuf[:, i * TC + th * 512:i * TC + (th + 1) * 512],
                    zbuf.ap, ALU.mult),
                    reads=(zbuf, ybuf), writes=(ybuf,), wide=True)

        banks = kb.psum
        for i in range(NDT):
            cbuf = cwsl[i % 2]
            kb.wr("G", cbuf)
            kb.op("G", lambda G, i=i, cbuf=cbuf: G.dma_start(
                out=cbuf.ap, in_=cw.ap()[i * 128:(i + 1) * 128, :]), 16)
            kb.did_w("G", cbuf)
            kb.rd("T", ybuf, cbuf)
            if i == 0:
                for b in banks:
                    kb.wr("T", b)
            def mmc(T, i=i, cbuf=cbuf):
                for tsub in range(4):
                    for fh in range(2):
                        ins = T.matmul(
                            banks[tsub * 2 + fh].h[:],
                            ybuf[:, i * TC + tsub * 128:
                                 i * TC + (tsub + 1) * 128],
                            cbuf[:, fh * 512:(fh + 1) * 512],
                            start=(i == 0), stop=(i == NDT - 1))
                return ins
            kb.op("T", mmc)
            kb.did_r("T", ybuf, cbuf)
            if i == NDT - 1:
                for b in banks:
                    kb.did_w("T", b)
        for tsub in range(4):
            tt = ch * (TC // 128) + tsub
            xres = load_x(kb, tt)
            for fh in range(2):
                bank = banks[tsub * 2 + fh]
                kb.rd("V", xres, bank)
                kb.op("V", lambda V, xres=xres, bank=bank, fh=fh:
                      V.tensor_tensor(xres[:, fh * 512:(fh + 1) * 512],
                                      bank.h[:],
                                      xres[:, fh * 512:(fh + 1) * 512],
                                      ALU.add))
                kb.did_r("V", bank)
            kb.did_w("V", xres, wide=True)
            store_x(kb, tt, xres)



RD48 = 48


def emit_lv_partial(kb, load_x, out_dram, lnscr, stats):
    lv = kb.sb("p_lv", [128, NT + 2], F32)
    onescol = kb.sb("p_ones", [128, 1], F32)
    kb.op("V", lambda V: V.memset(onescol.ap, 1.0))
    kb.did_w("V", onescol)
    hs = kb.sb("p_hs", [128, D], BF16)
    for grp in range(NT // 8):
        xts = [load_x(kb, grp * 8 + g) for g in range(8)]
        for g in range(8):
            emit_ln_stats(kb, xts[g], lnscr, stats, g, xcols=(0, D))
        emit_ln_finish(kb, stats, 8)
        for g in range(8):
            tt = grp * 8 + g
            emit_ln_apply(kb, xts[g], hs, lnscr, stats, g, xcols=(0, D))
            emit_lv_col(kb, hs, lnscr, stats, lv, tt)
    kb.eop("V", lambda V: V.tensor_reduce(lv[:, NT:NT + 1], lv[:, 0:NT],
                                          AX.X, ALU.add),
           reads=(lv,), writes=(lv,))
    bank = kb.psum[0]
    kb.rd("T", lv, onescol); kb.wr("T", bank)
    kb.op("T", lambda T, bank=bank: T.matmul(
        bank.h[0:1, 0:1], lv.h[:, NT:NT + 1], onescol.h[:, 0:1],
        start=True, stop=True))
    kb.did_w("T", bank)
    kb.eop("V", lambda V, bank=bank: V.tensor_copy(lv[0:1, NT + 1:NT + 2],
                                                   bank.h[0:1, 0:1]),
           reads=(bank,), writes=(lv,))
    kb.did_r("V", bank)
    kb.rd("Y", lv)
    kb.op("Y", lambda Y: Y.dma_start(out=out_dram.ap()[0:1, 0:1],
                                     in_=lv[0:1, NT + 1:NT + 2]), 16)
    kb.did_r("Y", lv)


def emit_lv_col(kb, hs, scratch, stats, lv, tt):
    st = stats.h
    s1c, s2c, mc, vc = st[:, 40:41], st[:, 41:42], st[:, 42:43], st[:, 43:44]
    kb.eop("S", lambda S: S.activation(scratch.ap, hs.ap, AF.Square,
                                       accum_out=s2c),
           reads=(hs,), writes=(scratch, stats), wide=True)
    kb.eop("V", lambda V: V.reduce_sum(s1c, hs.ap, AX.X),
           reads=(hs, stats), writes=(stats,))
    kb.eop("V", lambda V: V.tensor_scalar(mc, s1c, 1.0 / D, None, ALU.mult),
           reads=(stats,), writes=(stats,))
    kb.eop("V", lambda V: V.tensor_scalar(vc, s2c, 1.0 / D, None, ALU.mult),
           reads=(stats,), writes=(stats,))
    kb.eop("V", lambda V: V.tensor_tensor(mc, mc, mc, ALU.mult),
           reads=(stats,), writes=(stats,))
    kb.eop("V", lambda V: V.tensor_tensor(vc, vc, mc, ALU.subtract),
           reads=(stats,), writes=(stats,))
    one_ap = kb.constcol(1.0)
    kb.eop("S", lambda S, tt=tt: S.activation(lv[:, tt:tt + 1], vc, AF.Ln,
                                              bias=one_ap),
           reads=(stats,), writes=(lv,))


def emit_l1ffn(kb, load_x, store_x, qw_d, rw_d, ak_d, av_d, rel_d, w1_d,
               b1_d, w2_d, b2gwgb_d, partials8_d, w_dram, scr2_d, ident,
               lnscr, stats):
    onesrow = kb.sb("l_ones", [1, 128], F32)
    onescol = kb.sb("l_onesc", [128, 1], F32)
    onesrow_b = kb.sb("l_onesb", [1, 128], BF16)
    kb.op("V", lambda V: V.memset(onesrow.ap, 1.0))
    kb.op("V", lambda V: V.memset(onescol.ap, 1.0))
    kb.op("V", lambda V: V.memset(onesrow_b.ap, 1.0))
    kb.did_w("V", onesrow, onescol, onesrow_b)

    stg = kb.sb("l_stg", [128, D], BF16)
    qwT = kb.sb("l_qwT", [128, 8 * 128], BF16)
    rwT = kb.sb("l_rwT", [128, RD48], BF16)
    akT = kb.sb("l_akT", [128, T_CORE], BF16)
    rkaugT = kb.sb("l_rkaug", [49, T_CORE], BF16)
    w1T = kb.sb("l_w1T", [128, 8 * 256], BF16)
    b1row = kb.sb("l_b1row", [1, 256], F32)
    w2rep = kb.sb("l_w2rep", [128, 256], BF16)
    c3 = kb.sb("l_c3", [128, 4], F32)
    par8 = kb.sb("l_par8", [1, 8], F32)
    sc8 = kb.sb("l_sc8", [1, 8], F32)
    meanrep = kb.sb("l_meanrep", [128, 1], F32)

    kb.op("G", lambda G: G.dma_start(out=stg[:, 0:D], in_=qw_d.ap()), 16)
    kb.did_w("G", stg)
    kb.rd("Y", stg)
    kb.op("Y", lambda Y: Y.dma_start(out=scr2_d.ap()[0:128, 0:D],
                                     in_=stg[:, 0:D]), 16)
    kb.did_r("Y", stg)
    kb.wr("G", stg)
    kb.op("G", lambda G: G.dma_start(out=stg[0:48, 0:128], in_=rw_d.ap()), 16)
    kb.did_w("G", stg)
    kb.rd("Y", stg)
    kb.op("Y", lambda Y: Y.dma_start(out=scr2_d.ap()[128:176, 0:128],
                                     in_=stg[0:48, 0:128]), 16)
    kb.did_r("Y", stg)
    for i in range(2):
        kb.wr("G", stg)
        kb.op("G", lambda G, i=i: G.dma_start(
            out=stg[:, 0:D], in_=w1_d.ap()[i * 128:(i + 1) * 128, :]), 16)
        kb.did_w("G", stg)
        kb.rd("Y", stg)
        kb.op("Y", lambda Y, i=i: Y.dma_start(
            out=scr2_d.ap()[256 + i * 128:256 + (i + 1) * 128, 0:D],
            in_=stg[:, 0:D]), 16)
        kb.did_r("Y", stg)
    for i in range(32):
        kb.wr("G", stg)
        kb.op("G", lambda G, i=i: G.dma_start(
            out=stg[:, 0:128], in_=ak_d.ap()[i * 128:(i + 1) * 128, :]), 16)
        kb.did_w("G", stg)
        kb.rd("Y", stg)
        kb.op("Y", lambda Y, i=i: Y.dma_start(
            out=scr2_d.ap()[1024 + i * 128:1024 + (i + 1) * 128, 0:128],
            in_=stg[:, 0:128]), 16)
        kb.did_r("Y", stg)
    kb.wait("Y", "Y")
    for c in range(8):
        kb.op("Y", lambda Y, c=c: Y.dma_start(
            out=qwT[:, c * 128:(c + 1) * 128],
            in_=scr2_d.ap()[0:128, c * 128:(c + 1) * 128], transpose=True), 16)
    kb.op("Y", lambda Y: Y.dma_start(
        out=rwT[:, 0:RD48], in_=scr2_d.ap()[128:176, 0:128],
        transpose=True), 16)
    for c in range(8):
        kb.op("Y", lambda Y, c=c: Y.dma_start(
            out=w1T[:, c * 256:(c + 1) * 256],
            in_=scr2_d.ap()[256:512, c * 128:(c + 1) * 128], transpose=True),
            16)
    kb.op("Y", lambda Y: Y.dma_start(
        out=akT.ap, in_=scr2_d.ap()[1024:5120, 0:128], transpose=True), 16)
    kb.did_w("Y", qwT, rwT, w1T, akT)
    kb.op("Y", lambda Y: Y.dma_start(out=b1row.ap, in_=b1_d.ap().unsqueeze(0)),
          16)
    kb.op("Y", lambda Y: Y.dma_start(out=par8.ap,
                                     in_=partials8_d.ap().unsqueeze(0)), 16)
    kb.did_w("Y", b1row, par8)
    w2row = kb.sb("l_w2row", [1, 256], F32)
    sc3row = kb.sb("l_sc3row", [1, 4], F32)
    kb.op("Y", lambda Y: Y.dma_start(out=w2row.ap, in_=w2_d.ap()), 16)
    kb.op("Y", lambda Y: Y.dma_start(out=sc3row[0:1, 0:3],
                                     in_=b2gwgb_d.ap().unsqueeze(0)), 16)
    kb.did_w("Y", w2row, sc3row)
    bank = kb.psum[0]
    kb.rd("T", w2row, onesrow); kb.wr("T", bank)
    kb.op("T", lambda T, bank=bank: T.matmul(bank.h[:, 0:256],
                                             onesrow.h[0:1, :], w2row.ap,
                                             start=True, stop=True))
    kb.did_w("T", bank)
    kb.rd("S", bank); kb.wr("S", w2rep)
    kb.op("S", lambda S, bank=bank: S.activation(w2rep.ap, bank.h[:, 0:256],
                                                 AF.Copy))
    kb.did_r("S", bank); kb.did_w("S", w2rep)
    bank = kb.psum[1]
    kb.rd("T", sc3row, onesrow); kb.wr("T", bank)
    kb.op("T", lambda T, bank=bank: T.matmul(bank.h[:, 0:4], onesrow.h[0:1, :],
                                             sc3row.ap, start=True, stop=True))
    kb.did_w("T", bank)
    kb.eop("V", lambda V, bank=bank: V.tensor_copy(c3.ap, bank.h[:, 0:4]),
           reads=(bank,), writes=(c3,))
    kb.did_r("V", bank)
    kb.eop("V", lambda V: V.tensor_reduce(sc8[0:1, 0:1], par8.ap, AX.X,
                                          ALU.add),
           reads=(par8,), writes=(sc8,))
    kb.eop("V", lambda V: V.tensor_scalar(sc8[0:1, 1:2], sc8[0:1, 0:1],
                                          1.0 / 32768.0, 1e-6, ALU.mult,
                                          ALU.add),
           reads=(sc8,), writes=(sc8,))
    kb.eop("V", lambda V: V.reciprocal(sc8[0:1, 2:3], sc8[0:1, 1:2]),
           reads=(sc8,), writes=(sc8,))
    bank = kb.psum[2]
    kb.rd("T", sc8, onesrow); kb.wr("T", bank)
    kb.op("T", lambda T, bank=bank: T.matmul(bank.h[:, 0:1], onesrow.h[0:1, :],
                                             sc8[0:1, 2:3], start=True,
                                             stop=True))
    kb.did_w("T", bank)
    kb.eop("V", lambda V, bank=bank: V.tensor_copy(meanrep.ap,
                                                   bank.h[:, 0:1]),
           reads=(bank,), writes=(meanrep,))
    kb.did_r("V", bank)
    kb.op("G", lambda G: G.dma_start(out=rkaugT.h[48:49, :],
                                     in_=rel_d.ap().unsqueeze(0)), 16)
    kb.did_w("G", rkaugT)
    for nch in range(8):
        bank = kb.psum[3]
        kb.rd("T", rwT, akT); kb.wr("T", bank)
        kb.op("T", lambda T, nch=nch, bank=bank: T.matmul(
            bank.h[0:RD48, :], rwT[:, 0:RD48],
            akT[:, nch * 512:(nch + 1) * 512], start=True, stop=True))
        kb.did_w("T", bank)
        kb.rd("S", bank); kb.wr("S", rkaugT)
        kb.op("S", lambda S, nch=nch, bank=bank: S.activation(
            rkaugT.h[0:48, nch * 512:(nch + 1) * 512], bank.h[0:RD48, :],
            AF.Copy))
        kb.did_r("S", bank)
    kb.did_w("S", rkaugT, wide=True)

    hs_full = kb.sb("l_hs", [128, NT * D], BF16)
    hsT = kb.sb("l_hsT", [128, 8 * 512], BF16)
    psT = Buf(None)
    psT.h = kb.psum[7].h[:].bitcast(BF16)
    psT.readers = kb.psum[7].readers
    hstile = kb.sb("l_hstile", [128, D], BF16)
    qT_s = kb.sb("l_qTs", [128, 512], BF16)
    rqaugT = kb.sb("l_rqaug", [49, 512], BF16)
    ones512b = kb.sb("l_ones512b", [1, 512], BF16)
    kb.op("V", lambda V: V.memset(ones512b.ap, 1.0))
    kb.did_w("V", ones512b)
    kb.rd("Y", ones512b)
    kb.op("Y", lambda Y: Y.dma_start(out=rqaugT.h[48:49, :],
                                     in_=ones512b.ap), 16)
    kb.did_r("Y", ones512b)
    kb.did_w("Y", rqaugT)
    Sbuf = kb.sb("l_S", [128, T_CORE], BF16)
    m8 = kb.sb("l_m8", [128, 8], F32)
    mwork = kb.sb("l_mwork", [128, T_CORE], BF16)
    wrow = kb.sb("l_wrow", [128, T_CORE], BF16)
    l_all = kb.sb("l_lall", [128, NT], F32)
    sig_all = kb.sb("l_sigall", [128, NT], F32)
    lv_all = kb.sb("l_lvall", [128, NT], F32)
    h1buf = kb.sb("l_h1", [128, 256], BF16)
    u1 = kb.sb("l_u1", [128, NT], F32)

    x3 = hsT.h[:].rearrange("p (c t) -> p c t", c=8)
    for sup in range(8):
        xts = [load_x(kb, sup * 4 + g) for g in range(4)]
        for g in range(4):
            emit_ln_stats(kb, xts[g], lnscr, stats, g, xcols=(0, D))
        emit_ln_finish(kb, stats, 4)
        for g in range(4):
            tt = sup * 4 + g
            emit_ln_apply(kb, xts[g], hstile, lnscr, stats, g, xcols=(0, D))
            emit_lv_col(kb, hstile, lnscr, stats, lv_all, tt)
            kb.eop("V", lambda V, tt=tt: V.tensor_copy(
                hs_full[:, tt * D:(tt + 1) * D], hstile.ap),
                reads=(hstile,), writes=(hs_full,), wide=True)
            kb.rd("T", hstile, ident); kb.wr("T", psT)
            def tr(T, hstile=hstile, psT=psT):
                for c in range(8):
                    ins = T.transpose(psT.h[:, c * 128:(c + 1) * 128],
                                      hstile[:, c * 128:(c + 1) * 128],
                                      ident.h[:])
                return ins
            kb.op("T", tr)
            kb.did_r("T", hstile); kb.did_w("T", psT)
            kb.rd("V", psT); kb.wr("V", hsT)
            kb.op("V", lambda V, g=g, x3=x3: V.tensor_copy(
                x3[:, :, g * 128:(g + 1) * 128],
                psT.h[:].rearrange("p (c t) -> p c t", c=8)))
            kb.did_r("V", psT)
        kb.did_w("V", hsT, wide=True)
        bank = kb.psum[0]
        kb.rd("T", qwT, hsT); kb.wr("T", bank)
        def mmq(T, bank=bank, x3=x3):
            for c in range(8):
                ins = T.matmul(bank.h[:], qwT[:, c * 128:(c + 1) * 128],
                               x3[:, c, :], start=(c == 0), stop=(c == 7))
            return ins
        kb.op("T", mmq)
        kb.did_w("T", bank); kb.did_r("T", hsT)
        kb.rd("S", bank); kb.wr("S", qT_s)
        kb.op("S", lambda S, bank=bank: S.activation(qT_s.ap, bank.h[:],
                                                     AF.Copy))
        kb.did_r("S", bank); kb.did_w("S", qT_s)
        bank = kb.psum[1]
        kb.rd("T", rwT, qT_s); kb.wr("T", bank)
        kb.op("T", lambda T, bank=bank: T.matmul(
            bank.h[0:RD48, :], rwT[:, 0:RD48], qT_s.ap, start=True,
            stop=True))
        kb.did_w("T", bank)
        kb.rd("S", bank); kb.wr("S", rqaugT)
        kb.op("S", lambda S, bank=bank: S.activation(
            rqaugT.h[0:48, :], bank.h[0:RD48, :], AF.Copy,
            scale=1.0 / float(np.sqrt(48.0))))
        kb.did_r("S", bank); kb.did_w("S", rqaugT)
        for g in range(4):
            tt = sup * 4 + g
            bank = kb.psum[2]
            kb.rd("T", w1T, hsT, onesrow, b1row); kb.wr("T", bank)
            def mmh1(T, g=g, bank=bank, x3=x3):
                for c in range(8):
                    T.matmul(bank.h[:, 0:256],
                             x3[:, c, g * 128:(g + 1) * 128],
                             w1T[:, c * 256:(c + 1) * 256],
                             start=(c == 0), stop=False)
                return T.matmul(bank.h[:, 0:256], onesrow.h[0:1, :],
                                b1row.ap, start=False, stop=True)
            kb.op("T", mmh1)
            kb.did_w("T", bank)
            kb.rd("S", bank); kb.wr("S", h1buf)
            kb.op("S", lambda S, bank=bank: S.activation(
                h1buf.ap, bank.h[:, 0:256], AF.Gelu))
            kb.did_r("S", bank); kb.did_w("S", h1buf)
            kb.eop("V", lambda V: V.tensor_tensor(h1buf.ap, h1buf.ap,
                                                  w2rep.ap, ALU.mult),
                   reads=(h1buf, w2rep), writes=(h1buf,), wide=True)
            kb.eop("V", lambda V, tt=tt: V.reduce_sum(u1[:, tt:tt + 1],
                                                      h1buf.ap, AX.X),
                   reads=(h1buf, u1), writes=(u1,))
        for g in range(4):
            tt = sup * 4 + g
            for nch in range(8):
                bank = kb.psum[3 + nch % 2]
                kb.rd("T", rqaugT, rkaugT); kb.wr("T", bank)
                kb.op("T", lambda T, g=g, nch=nch, bank=bank: T.matmul(
                    bank.h[:], rqaugT.h[0:49, g * 128:(g + 1) * 128],
                    rkaugT.h[0:49, nch * 512:(nch + 1) * 512],
                    start=True, stop=True))
                kb.did_w("T", bank)
                kb.rd("S", bank); kb.wr("S", Sbuf)
                kb.op("S", lambda S, nch=nch, bank=bank: S.activation(
                    Sbuf[:, nch * 512:(nch + 1) * 512], bank.h[:], AF.Copy))
                kb.did_r("S", bank)
            kb.did_w("S", Sbuf, wide=True)
            kb.eop("V", lambda V: V.max(m8.ap, Sbuf.ap),
                   reads=(Sbuf,), writes=(m8,))
            kb.eop("V", lambda V: V.tensor_scalar(
                mwork.ap, Sbuf.ap, m8[:, 7:8], 30000.0, ALU.is_ge, ALU.mult),
                reads=(Sbuf, m8), writes=(mwork,), wide=True)
            kb.eop("V", lambda V: V.scalar_tensor_tensor(
                mwork.ap, Sbuf.ap, -30000.0, mwork.ap, ALU.add, ALU.add),
                reads=(Sbuf, mwork), writes=(mwork,), wide=True)
            kb.eop("V", lambda V: V.tensor_scalar(m8[:, 1:2], m8[:, 0:1],
                                                  -1.0, None, ALU.mult),
                   reads=(m8,), writes=(m8,))
            kb.rd("S", mwork, m8); kb.wr("S", wrow)
            kb.op("S", lambda S, tt=tt: S.activation(
                wrow.ap, mwork.ap, AF.Exp, bias=m8[:, 1:2],
                accum_out=l_all[:, tt:tt + 1]))
            kb.did_r("S", mwork); kb.did_w("S", wrow, wide=True)
            kb.did_w("S", l_all)
            kb.rd("Y", wrow)
            kb.op("Y", lambda Y, tt=tt: Y.dma_start(
                out=w_dram.ap()[tt * 128:(tt + 1) * 128, :], in_=wrow.ap), 16)
            kb.did_r("Y", wrow)

    kb.eop("V", lambda V: V.tensor_single_scalar(u1.ap, u1.ap, c3[:, 0:1],
                                                 ALU.add),
           reads=(u1, c3), writes=(u1,), wide=True)
    kb.eop("S", lambda S: S.activation(sig_all.ap, u1.ap, AF.Sigmoid),
           reads=(u1,), writes=(sig_all,), wide=True)
    kb.eop("V", lambda V: V.tensor_single_scalar(lv_all.ap, lv_all.ap,
                                                 meanrep[:, 0:1], ALU.mult),
           reads=(lv_all, meanrep), writes=(lv_all,), wide=True)
    kb.eop("V", lambda V: V.tensor_scalar(lv_all.ap, lv_all.ap, 0.5, None,
                                          ALU.mult),
           reads=(lv_all,), writes=(lv_all,))
    kb.eop("V", lambda V: V.scalar_tensor_tensor(
        sig_all.ap, sig_all.ap, 2.5, lv_all.ap, ALU.mult, ALU.add),
        reads=(sig_all, lv_all), writes=(sig_all,))
    kb.eop("V", lambda V: V.tensor_scalar(sig_all.ap, sig_all.ap, 0.0, 5.0,
                                          ALU.max, ALU.min),
           reads=(sig_all,), writes=(sig_all,))
    kb.eop("V", lambda V: V.tensor_single_scalar(sig_all.ap, sig_all.ap,
                                                 c3[:, 1:2], ALU.mult),
           reads=(sig_all, c3), writes=(sig_all,))
    kb.eop("V", lambda V: V.tensor_single_scalar(sig_all.ap, sig_all.ap,
                                                 c3[:, 2:3], ALU.add),
           reads=(sig_all, c3), writes=(sig_all,))
    kb.eop("S", lambda S: S.activation(sig_all.ap, sig_all.ap, AF.Sigmoid),
           reads=(sig_all,), writes=(sig_all,), wide=True)
    kb.eop("V", lambda V: V.reciprocal(l_all.ap, l_all.ap),
           reads=(l_all,), writes=(l_all,), wide=True)
    kb.eop("V", lambda V: V.tensor_tensor(sig_all.ap, sig_all.ap, l_all.ap,
                                          ALU.mult),
           reads=(sig_all, l_all), writes=(sig_all,))
    kb.eop("V", lambda V: V.tensor_scalar(sig_all.ap, sig_all.ap, 0.5, None,
                                          ALU.mult),
           reads=(sig_all,), writes=(sig_all,), wide=True)

    kb.wait("Y", "Y")
    wTs = [kb.sb(f"l_wT{i}", [128, 512], BF16) for i in range(2)]
    avb = [kb.sb(f"l_av{i}", [128, D], BF16) for i in range(2)]
    banks = kb.psum
    for sup in range(8):
        for n in range(32):
            wbuf = wTs[n % 2]
            abuf = avb[n % 2]
            kb.wr("Y", wbuf)
            kb.op("Y", lambda Y, sup=sup, n=n, wbuf=wbuf: Y.dma_start(
                out=wbuf.ap,
                in_=w_dram.ap()[sup * 512:(sup + 1) * 512,
                                n * 128:(n + 1) * 128], transpose=True), 16)
            kb.did_w("Y", wbuf)
            kb.wr("G", abuf)
            kb.op("G", lambda G, n=n, abuf=abuf: G.dma_start(
                out=abuf.ap, in_=av_d.ap()[n * 128:(n + 1) * 128, :]), 16)
            kb.did_w("G", abuf)
            kb.rd("T", wbuf, abuf)
            if n == 0:
                for b in banks:
                    kb.wr("T", b)
            def mma(T, n=n, wbuf=wbuf, abuf=abuf):
                for qt in range(4):
                    for hh in range(2):
                        ins = T.matmul(
                            banks[qt * 2 + hh].h[:],
                            wbuf[:, qt * 128:(qt + 1) * 128],
                            abuf[:, hh * 512:(hh + 1) * 512],
                            start=(n == 0), stop=(n == 31))
                return ins
            kb.op("T", mma)
            kb.did_r("T", wbuf, abuf)
            if n == 31:
                for b in banks:
                    kb.did_w("T", b)
        for qt in range(4):
            tt = sup * 4 + qt
            xres = load_x(kb, tt)
            for hh in range(2):
                bank = banks[qt * 2 + hh]
                kb.rd("V", xres, bank, sig_all, hs_full)
                kb.op("V", lambda V, xres=xres, bank=bank, hh=hh, tt=tt:
                      V.scalar_tensor_tensor(
                          xres[:, hh * 512:(hh + 1) * 512], bank.h[:],
                          sig_all[:, tt:tt + 1],
                          xres[:, hh * 512:(hh + 1) * 512],
                          ALU.mult, ALU.add))
                kb.op("V", lambda V, xres=xres, hh=hh, tt=tt:
                      V.scalar_tensor_tensor(
                          xres[:, hh * 512:(hh + 1) * 512],
                          hs_full[:, tt * D + hh * 512:
                                  tt * D + (hh + 1) * 512],
                          0.5, xres[:, hh * 512:(hh + 1) * 512],
                          ALU.mult, ALU.add))
                kb.did_r("V", bank, hs_full)
            kb.did_w("V", xres, wide=True)
            store_x(kb, tt, xres)



def build_l0ffn_launch():
    kb = KB()
    nc = kb.nc
    x_in = nc.dram_tensor("x_in", [T_CORE, D], F32, kind="ExternalInput")
    w_gate = nc.dram_tensor("w_gate", [I_FFN, D], F32, kind="ExternalInput")
    w_up = nc.dram_tensor("w_up", [I_FFN, D], F32, kind="ExternalInput")
    w_down = nc.dram_tensor("w_down", [D, I_FFN], F32, kind="ExternalInput")
    id_in = nc.dram_tensor("ident_in", [128, 128], BF16, kind="ExternalInput")
    x_out = nc.dram_tensor("x_out", [T_CORE, D], F32, kind="ExternalOutput")
    scr_q = nc.dram_tensor("ffn_scr", [6656, I_FFN], BF16)
    scr_guT = nc.dram_tensor("ffn_guT", [8, 128, 5632], BF16)
    scr_dnT = nc.dram_tensor("ffn_dnT", [NI, 128, D], BF16)
    import os
    _dbg = os.environ.get("FFN_DBG") == "1"
    hq_dram = nc.dram_tensor("hq_scr", [T_CORE, I_FFN], BF16,
                             kind="ExternalOutput" if _dbg else "Internal")

    ident = kb.sb("ident", [128, 128], BF16)
    kb.op("Y", lambda Y: Y.dma_start(out=ident.h[:], in_=id_in.ap()), 16)
    kb.did_w("Y", ident)

    qsc = quant_scratch(kb, "sh", I_FFN)
    big_f32 = kb.sb("big_f32", [128, I_FFN], F32)
    big_bf_a = kb.sb("big_bf_a", [128, I_FFN], BF16)
    big_bf_b = kb.sb("big_bf_b", [128, I_FFN], BF16)

    emit_l0ffn_weight_prep(kb, w_gate, w_up, w_down, scr_q, scr_guT, scr_dnT,
                           qsc, big_f32, big_bf_a, big_bf_b)

    xtiles = [kb.sb(f"xtile{i}", [128, D], F32) for i in range(9)]
    xslot = [0]

    def load_x(kb, tt):
        buf = xtiles[xslot[0] % len(xtiles)]
        xslot[0] += 1
        kb.wr("Y", buf)
        kb.op("Y", lambda Y, tt=tt, buf=buf: Y.dma_start(
            out=buf[:, 0:D], in_=x_in.ap()[tt * 128:(tt + 1) * 128, :]), 16)
        kb.did_w("Y", buf)
        return buf

    def store_x(kb, tt, buf):
        kb.rd("Y", buf)
        kb.op("Y", lambda Y, tt=tt, buf=buf: Y.dma_start(
            out=x_out.ap()[tt * 128:(tt + 1) * 128, :], in_=buf[:, 0:D]), 16)
        kb.did_r("Y", buf)

    emit_l0ffn(kb, load_x, store_x, scr_guT, scr_dnT, hq_dram, ident,
               qsc, big_f32, big_bf_a, big_bf_b)
    return kb.finalize()


def build_l0attn_launch():
    kb = KB()
    nc = kb.nc
    x_in = nc.dram_tensor("x_in", [T_CORE, D], F32, kind="ExternalInput")
    w_qkv = nc.dram_tensor("w_qkv", [D, 3072], F32, kind="ExternalInput")
    b_qkv = nc.dram_tensor("b_qkv", [3072], F32, kind="ExternalInput")
    w_out = nc.dram_tensor("w_out", [D, D], F32, kind="ExternalInput")
    b_out = nc.dram_tensor("b_out", [D], F32, kind="ExternalInput")
    id_in = nc.dram_tensor("ident_in", [128, 128], BF16, kind="ExternalInput")
    mask_in = nc.dram_tensor("mask_in", [4, 128, 512], BF16,
                             kind="ExternalInput")
    x_out = nc.dram_tensor("x_out", [T_CORE, D], F32, kind="ExternalOutput")
    qkT_dram = nc.dram_tensor("qkT_scr", [2048, T_CORE], BF16)
    v_dram = nc.dram_tensor("v_scr", [T_CORE, D], BF16)
    ctxT_dram = nc.dram_tensor("ctxT_scr", [D, T_CORE], BF16)

    ident = kb.sb("ident", [128, 128], BF16)
    maskb = kb.sb("maskb", [128, 4 * 512], BF16)
    kb.op("Y", lambda Y: Y.dma_start(out=ident.h[:], in_=id_in.ap()), 16)
    for a in range(4):
        kb.op("Y", lambda Y, a=a: Y.dma_start(
            out=maskb.h[:, a * 512:(a + 1) * 512], in_=mask_in.ap()[a]), 16)
    kb.did_w("Y", ident, maskb)
    mask4 = [maskb.h[:, a * 512:(a + 1) * 512] for a in range(4)]
    lnscr = kb.sb("lnscr", [128, D], BF16)
    stats = kb.sb("stats", [128, 48], F32)

    xtiles = [kb.sb(f"xtile{i}", [128, D], F32) for i in range(5)]
    xslot = [0]

    def load_x(kb, tt):
        buf = xtiles[xslot[0] % len(xtiles)]
        xslot[0] += 1
        kb.wr("Y", buf)
        kb.op("Y", lambda Y, tt=tt, buf=buf: Y.dma_start(
            out=buf[:, 0:D], in_=x_in.ap()[tt * 128:(tt + 1) * 128, :]), 16)
        kb.did_w("Y", buf)
        return buf

    def store_x(kb, tt, buf):
        kb.rd("Y", buf)
        kb.op("Y", lambda Y, tt=tt, buf=buf: Y.dma_start(
            out=x_out.ap()[tt * 128:(tt + 1) * 128, :], in_=buf[:, 0:D]), 16)
        kb.did_r("Y", buf)

    emit_l0attn(kb, load_x, store_x, w_qkv, b_qkv, w_out, b_out,
                qkT_dram, v_dram, ctxT_dram, ident, mask4, lnscr, stats)
    return kb.finalize()


def make_masks():
    m = np.zeros((4, 128, 512), np.float32)
    for a in range(4):
        for k in range(128):
            m[a, k, :a * 128 + k] = -1e9
    return m.astype(ml_dtypes.bfloat16)


def run_l0attn(x_full, w_qkv, b_qkv, w_out, b_out):
    nc = build_l0attn_launch()
    masks = make_masks()
    in_maps = [{"x_in": np.ascontiguousarray(x_full[c]), "w_qkv": w_qkv,
                "b_qkv": b_qkv, "w_out": w_out, "b_out": b_out,
                "ident_in": IDENT_BF16, "mask_in": masks} for c in range(8)]
    res = run_bass_kernel_spmd(nc, in_maps, core_ids=list(range(8)))
    return np.stack([res.results[c]["x_out"] for c in range(8)])


def build_mamba_launch():
    kb = KB()
    nc = kb.nc
    x_in = nc.dram_tensor("x_in", [T_CORE, D], F32, kind="ExternalInput")
    win = nc.dram_tensor("in_proj_w", [D, 2 * DIN], F32, kind="ExternalInput")
    conv_w = nc.dram_tensor("conv_w", [DIN, 1, 4], F32, kind="ExternalInput")
    conv_b = nc.dram_tensor("conv_b", [DIN], F32, kind="ExternalInput")
    xproj = nc.dram_tensor("x_proj_w", [DIN, 33], F32, kind="ExternalInput")
    dtw = nc.dram_tensor("dt_proj_w", [1, DIN], F32, kind="ExternalInput")
    dtb = nc.dram_tensor("dt_proj_b", [DIN], F32, kind="ExternalInput")
    alog = nc.dram_tensor("A_log", [DIN, 16], F32, kind="ExternalInput")
    dpar = nc.dram_tensor("D_param", [DIN], F32, kind="ExternalInput")
    cw = nc.dram_tensor("c_proj_w", [DIN, D], F32, kind="ExternalInput")
    id_in = nc.dram_tensor("ident_in", [128, 128], BF16, kind="ExternalInput")
    x_out = nc.dram_tensor("x_out", [T_CORE, D], F32, kind="ExternalOutput")
    z_dram = nc.dram_tensor("z_scr", [DIN, T_CORE], BF16)

    ident = kb.sb("ident", [128, 128], BF16)
    kb.op("Y", lambda Y: Y.dma_start(out=ident.h[:], in_=id_in.ap()), 16)
    kb.did_w("Y", ident)
    lnscr = kb.sb("lnscr", [128, D], BF16)
    stats = kb.sb("stats", [128, 48], F32)
    xtiles = [kb.sb(f"xtile{i}", [128, D], F32) for i in range(9)]
    xslot = [0]

    def load_x(kb, tt):
        buf = xtiles[xslot[0] % len(xtiles)]
        xslot[0] += 1
        kb.wr("Y", buf)
        kb.op("Y", lambda Y, tt=tt, buf=buf: Y.dma_start(
            out=buf[:, 0:D], in_=x_in.ap()[tt * 128:(tt + 1) * 128, :]), 16)
        kb.did_w("Y", buf)
        return buf

    def store_x(kb, tt, buf):
        kb.rd("Y", buf)
        kb.op("Y", lambda Y, tt=tt, buf=buf: Y.dma_start(
            out=x_out.ap()[tt * 128:(tt + 1) * 128, :], in_=buf[:, 0:D]), 16)
        kb.did_r("Y", buf)

    emit_l1mamba(kb, load_x, store_x, win, conv_w, conv_b, xproj, dtw, dtb,
                 alog, dpar, cw, z_dram, ident, lnscr, stats)
    return kb.finalize()


def run_mamba(x_full, z):
    nc = build_mamba_launch()
    in_maps = [{"x_in": np.ascontiguousarray(x_full[c]),
                "in_proj_w": z["L1_attn_in_proj_w"],
                "conv_w": z["L1_attn_conv_w"], "conv_b": z["L1_attn_conv_b"],
                "x_proj_w": z["L1_attn_x_proj_w"],
                "dt_proj_w": z["L1_attn_dt_proj_w"],
                "dt_proj_b": z["L1_attn_dt_proj_b"],
                "A_log": z["L1_attn_A_log"], "D_param": z["L1_attn_D_param"],
                "c_proj_w": z["L1_attn_c_proj_w"],
                "ident_in": IDENT_BF16} for c in range(8)]
    res = run_bass_kernel_spmd(nc, in_maps, core_ids=list(range(8)))
    return np.stack([res.results[c]["x_out"] for c in range(8)])


def _mk_loadstore(kb, x_in, x_out, ntiles=9):
    xtiles = [kb.sb(f"xtile{i}", [128, D], F32) for i in range(ntiles)]
    xslot = [0]

    def load_x(kb, tt):
        buf = xtiles[xslot[0] % len(xtiles)]
        xslot[0] += 1
        kb.wr("Y", buf)
        kb.op("Y", lambda Y, tt=tt, buf=buf: Y.dma_start(
            out=buf[:, 0:D], in_=x_in.ap()[tt * 128:(tt + 1) * 128, :]), 16)
        kb.did_w("Y", buf)
        return buf

    def store_x(kb, tt, buf):
        kb.rd("Y", buf)
        kb.op("Y", lambda Y, tt=tt, buf=buf: Y.dma_start(
            out=x_out.ap()[tt * 128:(tt + 1) * 128, :], in_=buf[:, 0:D]), 16)
        kb.did_r("Y", buf)
    return load_x, store_x


def build_lvpart_launch():
    kb = KB()
    nc = kb.nc
    x_in = nc.dram_tensor("x_in", [T_CORE, D], F32, kind="ExternalInput")
    out = nc.dram_tensor("lv_out", [1, 1], F32, kind="ExternalOutput")
    lnscr = kb.sb("lnscr", [128, D], BF16)
    stats = kb.sb("stats", [128, 48], F32)
    load_x, _ = _mk_loadstore(kb, x_in, None)
    emit_lv_partial(kb, load_x, out, lnscr, stats)
    return kb.finalize()


def build_l1ffn_launch():
    kb = KB()
    nc = kb.nc
    x_in = nc.dram_tensor("x_in", [T_CORE, D], F32, kind="ExternalInput")
    qw_d = nc.dram_tensor("q_proj_w", [128, D], F32, kind="ExternalInput")
    rw_d = nc.dram_tensor("router_w", [48, 128], F32, kind="ExternalInput")
    ak_d = nc.dram_tensor("aux_keys", [4096, 128], F32, kind="ExternalInput")
    av_d = nc.dram_tensor("aux_values", [4096, D], F32, kind="ExternalInput")
    rel_d = nc.dram_tensor("rel", [4096], F32, kind="ExternalInput")
    w1_d = nc.dram_tensor("unc_w1", [256, D], F32, kind="ExternalInput")
    b1_d = nc.dram_tensor("unc_b1", [256], F32, kind="ExternalInput")
    w2_d = nc.dram_tensor("unc_w2", [1, 256], F32, kind="ExternalInput")
    b2gwgb_d = nc.dram_tensor("b2gwgb", [3], F32, kind="ExternalInput")
    partials8_d = nc.dram_tensor("partials8", [8], F32, kind="ExternalInput")
    id_in = nc.dram_tensor("ident_in", [128, 128], BF16, kind="ExternalInput")
    x_out = nc.dram_tensor("x_out", [T_CORE, D], F32, kind="ExternalOutput")
    w_dram = nc.dram_tensor("w_scr", [T_CORE, 4096], BF16)
    scr2_d = nc.dram_tensor("scr2", [5120, D], BF16)

    ident = kb.sb("ident", [128, 128], BF16)
    kb.op("Y", lambda Y: Y.dma_start(out=ident.h[:], in_=id_in.ap()), 16)
    kb.did_w("Y", ident)
    lnscr = kb.sb("lnscr", [128, D], BF16)
    stats = kb.sb("stats", [128, 48], F32)
    load_x, store_x = _mk_loadstore(kb, x_in, x_out)
    emit_l1ffn(kb, load_x, store_x, qw_d, rw_d, ak_d, av_d, rel_d, w1_d,
               b1_d, w2_d, b2gwgb_d, partials8_d, w_dram, scr2_d, ident,
               lnscr, stats)
    return kb.finalize()


def run_l1ffn(x_full, z):
    ncA = build_lvpart_launch()
    in_mapsA = [{"x_in": np.ascontiguousarray(x_full[c])} for c in range(8)]
    resA = run_bass_kernel_spmd(ncA, in_mapsA, core_ids=list(range(8)))
    partials = np.concatenate([resA.results[c]["lv_out"].reshape(1)
                               for c in range(8)])
    ncB = build_l1ffn_launch()
    b2gwgb = np.stack([z["L1_ffn_unc_b2"].reshape(()),
                       z["L1_ffn_gate_w1"].reshape(()),
                       z["L1_ffn_gate_bias"].reshape(())]).astype(np.float32)
    in_mapsB = [{"x_in": np.ascontiguousarray(x_full[c]),
                 "q_proj_w": z["L1_ffn_q_proj_w"],
                 "router_w": z["L1_ffn_router_w"],
                 "aux_keys": z["L1_ffn_aux_keys"],
                 "aux_values": z["L1_ffn_aux_values"],
                 "rel": z["L1_ffn_reliability_mask"],
                 "unc_w1": z["L1_ffn_unc_w1"], "unc_b1": z["L1_ffn_unc_b1"],
                 "unc_w2": z["L1_ffn_unc_w2"], "b2gwgb": b2gwgb,
                 "partials8": partials, "ident_in": IDENT_BF16}
                for c in range(8)]
    resB = run_bass_kernel_spmd(ncB, in_mapsB, core_ids=list(range(8)))
    return np.stack([resB.results[c]["x_out"] for c in range(8)])


IDENT_BF16 = np.eye(128, dtype=ml_dtypes.bfloat16)


def run_l0ffn(x_full, w_gate, w_up, w_down):
    nc = build_l0ffn_launch()
    in_maps = [{"x_in": np.ascontiguousarray(x_full[c]),
                "w_gate": w_gate, "w_up": w_up, "w_down": w_down,
                "ident_in": IDENT_BF16} for c in range(8)]
    res = run_bass_kernel_spmd(nc, in_maps, core_ids=list(range(8)))
    return np.stack([res.results[c]["x_out"] for c in range(8)])



def _common_pre(kb, ntiles=9, pool_kb=150):
    nc = kb.nc
    kb.arena_init(pool_kb)
    id_in = nc.dram_tensor("ident_in", [128, 128], BF16, kind="ExternalInput")
    ident = kb.sb("ident", [128, 128], BF16)
    kb.op("Y", lambda Y: Y.dma_start(out=ident.h[:], in_=id_in.ap()), 16)
    kb.did_w("Y", ident)
    lnscr = kb.sb("lnscr", [128, D], BF16)
    stats = kb.sb("stats", [128, 48], F32)
    return ident, lnscr, stats


def _xio(kb, x_in_h=None, x_out_h=None, ntiles=9):
    xtiles = [kb.sb(f"xtile{i}", [128, D], F32) for i in range(ntiles)]
    state = {"in": x_in_h, "out": x_out_h, "slot": 0}

    def load_x(kb, tt):
        buf = xtiles[state["slot"] % len(xtiles)]
        state["slot"] += 1
        h = state["in"]
        kb.wr("Y", buf)
        kb.op("Y", lambda Y, tt=tt, buf=buf, h=h: Y.dma_start(
            out=buf[:, 0:D],
            in_=h.ap()[tt * 128:(tt + 1) * 128, :]), 16)
        kb.did_w("Y", buf)
        return buf

    def store_x(kb, tt, buf):
        h = state["out"]
        kb.rd("Y", buf)
        kb.op("Y", lambda Y, tt=tt, buf=buf, h=h: Y.dma_start(
            out=h.ap()[tt * 128:(tt + 1) * 128, :],
            in_=buf[:, 0:D]), 16)
        kb.did_r("Y", buf)
    return load_x, store_x, state


def build_launch1():
    kb = KB()
    nc = kb.nc
    x_in = nc.dram_tensor("x_in", [T_CORE, D], F32, kind="ExternalInput")
    w_gate = nc.dram_tensor("w_gate", [I_FFN, D], F32, kind="ExternalInput")
    w_up = nc.dram_tensor("w_up", [I_FFN, D], F32, kind="ExternalInput")
    w_down = nc.dram_tensor("w_down", [D, I_FFN], F32, kind="ExternalInput")
    w_qkv = nc.dram_tensor("w_qkv", [D, 3072], F32, kind="ExternalInput")
    b_qkv = nc.dram_tensor("b_qkv", [3072], F32, kind="ExternalInput")
    w_out = nc.dram_tensor("w_out", [D, D], F32, kind="ExternalInput")
    b_out = nc.dram_tensor("b_out", [D], F32, kind="ExternalInput")
    mask_in = nc.dram_tensor("mask_in", [4, 128, 512], BF16,
                             kind="ExternalInput")
    x_out3 = nc.dram_tensor("x_out3", [T_CORE, D], F32, kind="ExternalOutput")
    lv_out = nc.dram_tensor("lv_out", [1, 1], F32, kind="ExternalOutput")
    x_s1 = nc.dram_tensor("x_s1", [T_CORE, D], F32)
    x_s2 = nc.dram_tensor("x_s2", [T_CORE, D], F32)
    scr_q = nc.dram_tensor("ffn_scr", [6656, I_FFN], BF16)
    scr_guT = nc.dram_tensor("ffn_guT", [8, 128, 5632], BF16)
    scr_dnT = nc.dram_tensor("ffn_dnT", [NI, 128, D], BF16)
    hq_dram = nc.dram_tensor("hq_scr", [T_CORE, I_FFN], BF16)
    qkT_dram = nc.dram_tensor("qkT_scr", [2048, T_CORE], BF16)
    v_dram = nc.dram_tensor("v_scr", [T_CORE, D], BF16)
    ctxT_dram = nc.dram_tensor("ctxT_scr", [D, T_CORE], BF16)

    ident, lnscr, stats = _common_pre(kb)
    maskb = kb.sb("maskb", [128, 4 * 512], BF16)
    for a in range(4):
        kb.op("Y", lambda Y, a=a: Y.dma_start(
            out=maskb.h[:, a * 512:(a + 1) * 512], in_=mask_in.ap()[a]), 16)
    kb.did_w("Y", maskb)
    mask4 = [maskb.h[:, a * 512:(a + 1) * 512] for a in range(4)]
    load_x, store_x, state = _xio(kb)

    kb.arena_begin()
    qsc = quant_scratch(kb, "sh", I_FFN)
    big_f32 = kb.sb("big_f32", [128, I_FFN], F32)
    big_bf_a = kb.sb("big_bf_a", [128, I_FFN], BF16)
    big_bf_b = kb.sb("big_bf_b", [128, I_FFN], BF16)
    emit_l0ffn_weight_prep(kb, w_gate, w_up, w_down, scr_q, scr_guT, scr_dnT,
                           qsc, big_f32, big_bf_a, big_bf_b)
    state["in"], state["out"] = x_in, x_s1
    emit_l0ffn(kb, load_x, store_x, scr_guT, scr_dnT, hq_dram, ident,
               qsc, big_f32, big_bf_a, big_bf_b, grp_tiles=4)
    kb.arena_end(); kb.barrier()

    kb.arena_begin()
    state["in"], state["out"] = x_s1, x_s2
    emit_l0attn(kb, load_x, store_x, w_qkv, b_qkv, w_out, b_out,
                qkT_dram, v_dram, ctxT_dram, ident, mask4, lnscr, stats)
    kb.arena_end(); kb.barrier()

    kb.arena_begin()
    qsc = quant_scratch(kb, "sh2", I_FFN)
    big_f32 = kb.sb("big_f322", [128, I_FFN], F32)
    big_bf_a = kb.sb("big_bf_a2", [128, I_FFN], BF16)
    big_bf_b = kb.sb("big_bf_b2", [128, I_FFN], BF16)
    state["in"], state["out"] = x_s2, x_out3
    emit_l0ffn(kb, load_x, store_x, scr_guT, scr_dnT, hq_dram, ident,
               qsc, big_f32, big_bf_a, big_bf_b, grp_tiles=4)
    kb.arena_end(); kb.barrier()

    kb.arena_begin()
    state["in"] = x_out3
    emit_lv_partial(kb, load_x, lv_out, lnscr, stats)
    kb.arena_end()
    return kb.finalize()


def build_launch2():
    kb = KB()
    nc = kb.nc
    x_in = nc.dram_tensor("x_in", [T_CORE, D], F32, kind="ExternalInput")
    qw_d = nc.dram_tensor("q_proj_w", [128, D], F32, kind="ExternalInput")
    rw_d = nc.dram_tensor("router_w", [48, 128], F32, kind="ExternalInput")
    ak_d = nc.dram_tensor("aux_keys", [4096, 128], F32, kind="ExternalInput")
    av_d = nc.dram_tensor("aux_values", [4096, D], F32, kind="ExternalInput")
    rel_d = nc.dram_tensor("rel", [4096], F32, kind="ExternalInput")
    w1_d = nc.dram_tensor("unc_w1", [256, D], F32, kind="ExternalInput")
    b1_d = nc.dram_tensor("unc_b1", [256], F32, kind="ExternalInput")
    w2_d = nc.dram_tensor("unc_w2", [1, 256], F32, kind="ExternalInput")
    b2gwgb_d = nc.dram_tensor("b2gwgb", [3], F32, kind="ExternalInput")
    partials8_d = nc.dram_tensor("partials8", [8], F32, kind="ExternalInput")
    win = nc.dram_tensor("in_proj_w", [D, 2 * DIN], F32, kind="ExternalInput")
    conv_w = nc.dram_tensor("conv_w", [DIN, 1, 4], F32, kind="ExternalInput")
    conv_b = nc.dram_tensor("conv_b", [DIN], F32, kind="ExternalInput")
    xproj = nc.dram_tensor("x_proj_w", [DIN, 33], F32, kind="ExternalInput")
    dtw = nc.dram_tensor("dt_proj_w", [1, DIN], F32, kind="ExternalInput")
    dtb = nc.dram_tensor("dt_proj_b", [DIN], F32, kind="ExternalInput")
    alog = nc.dram_tensor("A_log", [DIN, 16], F32, kind="ExternalInput")
    dpar = nc.dram_tensor("D_param", [DIN], F32, kind="ExternalInput")
    cw = nc.dram_tensor("c_proj_w", [DIN, D], F32, kind="ExternalInput")
    x_out5 = nc.dram_tensor("x_out5", [T_CORE, D], F32, kind="ExternalOutput")
    lv_out = nc.dram_tensor("lv_out", [1, 1], F32, kind="ExternalOutput")
    x_s1 = nc.dram_tensor("x_s1", [T_CORE, D], F32)
    w_dram = nc.dram_tensor("w_scr", [T_CORE, 4096], BF16)
    scr2_d = nc.dram_tensor("scr2", [5120, D], BF16)
    z_dram = nc.dram_tensor("z_scr", [DIN, T_CORE], BF16)

    ident, lnscr, stats = _common_pre(kb)
    load_x, store_x, state = _xio(kb)

    kb.arena_begin()
    state["in"], state["out"] = x_in, x_s1
    emit_l1ffn(kb, load_x, store_x, qw_d, rw_d, ak_d, av_d, rel_d, w1_d,
               b1_d, w2_d, b2gwgb_d, partials8_d, w_dram, scr2_d, ident,
               lnscr, stats)
    kb.arena_end(); kb.barrier()

    kb.arena_begin()
    state["in"], state["out"] = x_s1, x_out5
    emit_l1mamba(kb, load_x, store_x, win, conv_w, conv_b, xproj, dtw, dtb,
                 alog, dpar, cw, z_dram, ident, lnscr, stats)
    kb.arena_end(); kb.barrier()

    kb.arena_begin()
    state["in"] = x_out5
    emit_lv_partial(kb, load_x, lv_out, lnscr, stats)
    kb.arena_end()
    return kb.finalize()


def build_launch3():
    kb = KB()
    nc = kb.nc
    x_in = nc.dram_tensor("x_in", [T_CORE, D], F32, kind="ExternalInput")
    qw_d = nc.dram_tensor("q_proj_w", [128, D], F32, kind="ExternalInput")
    rw_d = nc.dram_tensor("router_w", [48, 128], F32, kind="ExternalInput")
    ak_d = nc.dram_tensor("aux_keys", [4096, 128], F32, kind="ExternalInput")
    av_d = nc.dram_tensor("aux_values", [4096, D], F32, kind="ExternalInput")
    rel_d = nc.dram_tensor("rel", [4096], F32, kind="ExternalInput")
    w1_d = nc.dram_tensor("unc_w1", [256, D], F32, kind="ExternalInput")
    b1_d = nc.dram_tensor("unc_b1", [256], F32, kind="ExternalInput")
    w2_d = nc.dram_tensor("unc_w2", [1, 256], F32, kind="ExternalInput")
    b2gwgb_d = nc.dram_tensor("b2gwgb", [3], F32, kind="ExternalInput")
    partials8_d = nc.dram_tensor("partials8", [8], F32, kind="ExternalInput")
    x_out = nc.dram_tensor("x_out", [T_CORE, D], F32, kind="ExternalOutput")
    w_dram = nc.dram_tensor("w_scr", [T_CORE, 4096], BF16)
    scr2_d = nc.dram_tensor("scr2", [5120, D], BF16)
    ident, lnscr, stats = _common_pre(kb)
    load_x, store_x, state = _xio(kb)
    kb.arena_begin()
    state["in"], state["out"] = x_in, x_out
    emit_l1ffn(kb, load_x, store_x, qw_d, rw_d, ak_d, av_d, rel_d, w1_d,
               b1_d, w2_d, b2gwgb_d, partials8_d, w_dram, scr2_d, ident,
               lnscr, stats)
    kb.arena_end()
    return kb.finalize()


def kernel(**inputs):
    z = inputs
    x = np.ascontiguousarray(np.asarray(z["L0_attn_x"], np.float32))
    masks = make_masks()
    b2gwgb = np.stack([np.asarray(z["L1_ffn_unc_b2"]).reshape(-1)[0],
                       np.asarray(z["L1_ffn_gate_w1"]).reshape(-1)[0],
                       np.asarray(z["L1_ffn_gate_bias"]).reshape(-1)[0]]
                      ).astype(np.float32)

    nc1 = build_launch1()
    im = [{"x_in": x[c], "w_gate": z["L0_ffn_w_gate"],
           "w_up": z["L0_ffn_w_up"], "w_down": z["L0_ffn_w_down"],
           "w_qkv": z["L0_attn_w_qkv"], "b_qkv": z["L0_attn_b_qkv"],
           "w_out": z["L0_attn_w_out"], "b_out": z["L0_attn_b_out"],
           "mask_in": masks, "ident_in": IDENT_BF16} for c in range(8)]
    r1 = run_bass_kernel_spmd(nc1, im, core_ids=list(range(8)))
    x3 = [r1.results[c]["x_out3"] for c in range(8)]
    p1 = np.concatenate([r1.results[c]["lv_out"].reshape(1)
                         for c in range(8)]).astype(np.float32)

    nc2 = build_launch2()
    im = [{"x_in": x3[c], "q_proj_w": z["L1_ffn_q_proj_w"],
           "router_w": z["L1_ffn_router_w"], "aux_keys": z["L1_ffn_aux_keys"],
           "aux_values": z["L1_ffn_aux_values"],
           "rel": z["L1_ffn_reliability_mask"],
           "unc_w1": z["L1_ffn_unc_w1"], "unc_b1": z["L1_ffn_unc_b1"],
           "unc_w2": z["L1_ffn_unc_w2"], "b2gwgb": b2gwgb, "partials8": p1,
           "in_proj_w": z["L1_attn_in_proj_w"], "conv_w": z["L1_attn_conv_w"],
           "conv_b": z["L1_attn_conv_b"], "x_proj_w": z["L1_attn_x_proj_w"],
           "dt_proj_w": z["L1_attn_dt_proj_w"],
           "dt_proj_b": z["L1_attn_dt_proj_b"], "A_log": z["L1_attn_A_log"],
           "D_param": z["L1_attn_D_param"], "c_proj_w": z["L1_attn_c_proj_w"],
           "ident_in": IDENT_BF16} for c in range(8)]
    r2 = run_bass_kernel_spmd(nc2, im, core_ids=list(range(8)))
    x5 = [r2.results[c]["x_out5"] for c in range(8)]
    p2 = np.concatenate([r2.results[c]["lv_out"].reshape(1)
                         for c in range(8)]).astype(np.float32)

    nc3 = build_launch3()
    im = [{"x_in": x5[c], "q_proj_w": z["L1_ffn_q_proj_w"],
           "router_w": z["L1_ffn_router_w"], "aux_keys": z["L1_ffn_aux_keys"],
           "aux_values": z["L1_ffn_aux_values"],
           "rel": z["L1_ffn_reliability_mask"],
           "unc_w1": z["L1_ffn_unc_w1"], "unc_b1": z["L1_ffn_unc_b1"],
           "unc_w2": z["L1_ffn_unc_w2"], "b2gwgb": b2gwgb, "partials8": p2,
           "ident_in": IDENT_BF16} for c in range(8)]
    r3 = run_bass_kernel_spmd(nc3, im, core_ids=list(range(8)))
    out = np.stack([r3.results[c]["x_out"] for c in range(8)])
    return np.ascontiguousarray(out.astype(np.float32))
